# version 03
# baseline (speedup 1.0000x reference)
import numpy as np
import concourse.bass as bass
import concourse.mybir as mybir
import numpy as np
import concourse.bass as bass
import concourse.mybir as mybir

F32 = mybir.dt.float32
BF16 = mybir.dt.bfloat16
AF = mybir.ActivationFunctionType
ALU = mybir.AluOpType
AX = mybir.AxisListType

ENGINES = ("tensor", "vector", "scalar", "gpsimd", "sync")
DMA_QUEUES = ("sync", "gpsimd", "scalar")


class Plan:
    def __init__(self):
        self.n = 0
        self.op_engine = []
        self.op_isdma = []
        self.needs_inc = []
        self.deps = []
        self.last_writer = {}
        self.readers = {}
        self.eng_seq = {}
        self.counts = {e: 0 for e in ENGINES}
        self.dma_counts = {e: 0 for e in DMA_QUEUES}
        self.inc_rank = {}

    def record(self, engine, isdma, reads, writes, sreads, force_inc=False):
        i = self.n
        self.n += 1
        self.op_engine.append(engine)
        self.op_isdma.append(isdma)
        self.needs_inc.append(bool(force_inc))

        def same_pipe(j):
            return (not isdma and not self.op_isdma[j]
                    and engine == "tensor" and self.op_engine[j] == "tensor")

        dep = set()
        for r in list(reads) + list(sreads):
            for w in self.last_writer.get(r, ()):
                forced = r in sreads
                if not same_pipe(w) or forced:
                    dep.add(w)
            self.readers.setdefault(r, []).append(i)
        for r in writes:
            for w in self.last_writer.get(r, ()):
                if not same_pipe(w):
                    dep.add(w)
            for rd in self.readers.get(r, []):
                if rd != i and not same_pipe(rd):
                    dep.add(rd)
            prev = self.last_writer.get(r, ())
            self.last_writer[r] = [i]
            self.readers[r] = []
        self.deps.append(sorted(dep))
        for d in dep:
            self.needs_inc[d] = True
        return i

    def finalize(self):
        counters = {}
        self.rank = [None] * self.n
        for i in range(self.n):
            g = (self.op_engine[i], self.op_isdma[i])
            if self.needs_inc[i]:
                counters[g] = counters.get(g, 0) + 1
                self.rank[i] = counters[g]
        self.group_totals = counters
        self.flush_after = set()
        for i in range(self.n):
            for d in self.deps[i]:
                if self.op_isdma[d]:
                    self.flush_after.add(d)


class Sched:
    def __init__(self, nc, mode, eng, plan: Plan, sems):
        self.nc = nc
        self.mode = mode
        self.eng = eng
        self.plan = plan
        self.sems = sems
        self.idx = 0
        self.hwm = {}

    def _emit_waits(self, i):
        need = {}
        for d in self.plan.deps[i]:
            g = (self.plan.op_engine[d], self.plan.op_isdma[d])
            v = self.plan.rank[d] * (16 if g[1] else 1)
            if need.get(g, 0) < v:
                need[g] = v
        for g, v in sorted(need.items(), key=lambda kv: str(kv[0])):
            if self.hwm.get(g, 0) >= v:
                continue
            self.hwm[g] = v
            self.eng.wait_ge(self.sems[g], v)

    def op(self, engine, fn, reads=(), writes=(), sreads=(), force_inc=False):
        if self.mode == "plan":
            return self.plan.record(engine, False, reads, writes, sreads, force_inc)
        i = self.idx
        self.idx += 1
        assert self.plan.op_engine[i] == engine and not self.plan.op_isdma[i]
        if self.mode == engine:
            self._emit_waits(i)
            ins = fn(self.eng)
            if self.plan.needs_inc[i]:
                ins.then_inc(self.sems[(engine, False)], 1)
        return i

    def dma(self, queue, out, in_, reads=(), writes=(), force_inc=True, **kw):
        if self.mode == "plan":
            return self.plan.record(queue, True, reads, writes, (), force_inc)
        i = self.idx
        self.idx += 1
        assert self.plan.op_engine[i] == queue and self.plan.op_isdma[i]
        if self.mode == queue:
            self._emit_waits(i)
            ins = self.eng.dma_start(out=out, in_=in_, **kw)
            if self.plan.needs_inc[i]:
                ins.then_inc(self.sems[(queue, True)], 16)
            if i in self.plan.flush_after:
                g = (queue, True)
                v = 16 * self.plan.rank[i]
                if self.hwm.get(g, 0) < v:
                    self.hwm[g] = v
                    self.eng.wait_ge(self.sems[g], v)
        return i

    def final_wait(self):
        if self.mode == "plan":
            return
        if self.mode == "sync":
            for g, total in sorted(self.plan.group_totals.items(), key=lambda kv: str(kv[0])):
                v = total * (16 if g[1] else 1)
                if self.hwm.get(g, 0) < v:
                    self.eng.wait_ge(self.sems[g], v)


def play(nc, schedule_fn, *args, **kwargs):
    plan = Plan()
    schedule_fn(Sched(nc, "plan", None, plan, None), *args, **kwargs)
    plan.finalize()

    from contextlib import ExitStack
    with ExitStack() as ctx:
        sems = {}
        for e in ENGINES:
            sems[(e, False)] = ctx.enter_context(nc.semaphore(f"s_{e}"))
        for q in DMA_QUEUES:
            sems[(q, True)] = ctx.enter_context(nc.semaphore(f"s_dma_{q}"))
        with nc.Block() as block:
            @block.tensor
            def _(eng):
                schedule_fn(Sched(nc, "tensor", eng, plan, sems), *args, **kwargs)

            @block.vector
            def _(eng):
                schedule_fn(Sched(nc, "vector", eng, plan, sems), *args, **kwargs)

            @block.scalar
            def _(eng):
                schedule_fn(Sched(nc, "scalar", eng, plan, sems), *args, **kwargs)

            @block.gpsimd
            def _(eng):
                schedule_fn(Sched(nc, "gpsimd", eng, plan, sems), *args, **kwargs)

            @block.sync
            def _(eng):
                schedule_fn(Sched(nc, "sync", eng, plan, sems), *args, **kwargs)
    return plan


def ap(t, offset, dims):
    if isinstance(t, bass.AP):
        assert not isinstance(t.offset, str)
        offset = t.offset + offset
        t = t.tensor
    return bass.AP(t, offset, dims)


B, D, NH, W, HS, BDm = 8, 1024, 16, 64, 64, 16
DCH = 8
NHG = 4


def declare_dram_a(nc, TL, suffix="", dbg=False):
    TP = TL + 64
    t = {}
    ein = lambda n, s: nc.dram_tensor(n + suffix, s, F32, kind="ExternalInput").ap()
    t["xA"] = ein("xA", [B, D, TP])
    t["Wdisp"] = ein("Wdisp", [256, 1024])
    t["Wval"] = ein("Wval", [1024, 1024])
    t["Wcpj"] = ein("Wcpj", [1024, 1024])
    t["relpos"] = ein("relpos", [64, 16])
    t["ppw"] = ein("ppw", [16, 16])
    t["sfw"] = ein("sfw", [32, 16])
    t["sfb"] = ein("sfb", [32])
    t["bow"] = ein("bow", [1, 16])
    t["dow"] = ein("dow", [1, 16])
    t["dob"] = ein("dob", [1])
    t["maskA"] = ein("maskA", [128, 192])
    t["maskB"] = ein("maskB", [128, 192])
    t["x1"] = nc.dram_tensor("x1" + suffix, [B, D, TL], F32, kind="ExternalOutput").ap()
    t["wvalT"] = nc.dram_tensor("wvalT" + suffix, [1024, 1024], BF16, kind="Internal").ap()
    t["cprojT"] = nc.dram_tensor("cprojT" + suffix, [1024, 1024], BF16, kind="Internal").ap()
    t["shear"] = nc.dram_tensor("shearA" + suffix, [4, 2, 4, TL, TP], F32, kind="Internal").ap()
    if dbg:
        t["dbg_dproj"] = nc.dram_tensor("dbg_dproj" + suffix, [128, TP], BF16, kind="ExternalOutput").ap()
    if dbg:
        t["dbg_valt"] = nc.dram_tensor("dbg_valt" + suffix, [128, 1024], BF16, kind="ExternalOutput").ap()
    if dbg:
        t["dbg_raw"] = nc.dram_tensor("dbg_raw" + suffix, [128, 8 * TL], F32, kind="ExternalOutput").ap()
    if dbg:
        t["dbg_bondR"] = nc.dram_tensor("dbg_bondR" + suffix, [128, 768], F32, kind="ExternalOutput").ap()
    if dbg:
        t["dbg_dmgR"] = nc.dram_tensor("dbg_dmgR" + suffix, [128, 768], F32, kind="ExternalOutput").ap()
    if dbg:
        t["dbg_wcast"] = nc.dram_tensor("dbg_wcast" + suffix, [128, 768], BF16, kind="ExternalOutput").ap()
    if dbg:
        t["dbg_attn"] = nc.dram_tensor("dbg_attn" + suffix, [128, 8 * TL], BF16, kind="ExternalOutput").ap()
    return t


def alloc_bufs_a(ctx, nc, TL):
    TP = TL + 64
    TCHP = (TP + 127) // 128
    sb = {}
    s_ = lambda n, shape, dt: ctx.enter_context(nc.sbuf_tensor(n, shape, dt))
    sb["xb"] = s_("xb", [128, DCH * TP], BF16)
    sb["xio"] = s_("xio", [128, 2 * TP], F32)
    sb["valt"] = s_("valt", [128, 4 * TCHP * 1024], BF16)
    sb["dproj"] = s_("dproj", [128, 16 * TP], BF16)
    sb["gbuf"] = s_("gbuf", [128, 4 * 8 * TL], BF16)
    sb["raw"] = s_("raw", [128, 8 * TL], F32)
    sb["tanhb"] = s_("tanhb", [128, 8 * TL], F32)
    sb["bondR"] = s_("bondR", [128, 768], F32)
    sb["dmgR"] = s_("dmgR", [128, 768], F32)
    sb["S3"] = s_("S3", [128, 768], F32)
    sb["S4"] = s_("S4", [128, 768], F32)
    sb["wcast"] = s_("wcast", [128, 768], BF16)
    sb["negmax"] = s_("negmax", [128, 4], F32)
    sb["sumR"] = s_("sumR", [128, 4], F32)
    sb["recR"] = s_("recR", [128, 4], F32)
    sb["SuA"] = s_("SuA", [128, 128], BF16)
    sb["SuB"] = s_("SuB", [128, 128], BF16)
    sb["attn"] = s_("attn", [128, 4 * 8 * TL], BF16)
    sb["wrot"] = s_("wrot", [128, 2 * 512], BF16)
    sb["gpscr"] = s_("gpscr", [128, TL], BF16)
    sb["rb"] = s_("rb", [128, TL], BF16)
    sb["sqb"] = s_("sqb", [128, TL], BF16)

    sb["c_rep"] = s_("c_rep", [128, 64], F32)
    sb["posrep"] = s_("posrep", [128, 64], F32)
    sb["sfbcol"] = s_("sfbcol", [128, 1], F32)
    sb["dbh"] = s_("dbh", [128, 1], F32)
    sb["identb"] = s_("identb", [128, 128], BF16)
    sb["Wcomb"] = s_("Wcomb", [128, 8 * 512], BF16)
    sb["lhsT_dot"] = s_("lhsT_dot", [128, 8], BF16)
    sb["onescol"] = s_("onescol", [128, 1], BF16)
    sb["onesrow"] = s_("onesrow", [1, 128], F32)
    if 8 * TL < 1280:
        sb["smallS_t"] = s_("smallS_t", [128, 256], F32)
        sb["stg_t"] = s_("stg_t", [128, 256], BF16)

    ps = {}
    p_ = lambda n, shape, dt: ctx.enter_context(nc.psum_tensor(n, shape, dt))
    ps["PA"] = p_("PA", [128, 512], F32)
    ps["PB"] = p_("PB", [128, 512], F32)
    ps["PD"] = p_("PD", [128, 2048], F32)
    ps["PX"] = p_("PX", [128, 1024], BF16)
    ps["PV"] = p_("PV", [128, 512], F32)
    return sb, ps


def emit_a(s: Sched, t, sb, ps, TL, dbg=False):
    TP = TL + 64
    NTC = TL // 128
    TCHP = (TP + 127) // 128
    nc = s.nc
    PA, PB, PD, PX, PV = ps["PA"], ps["PB"], ps["PD"], ps["PX"], ps["PV"]
    xb, valt, dproj, gbuf, raw, tanhb = (sb[k] for k in ("xb", "valt", "dproj", "gbuf", "raw", "tanhb"))
    tanh_bc = tanhb[:].bitcast(BF16)
    tmp_keys = lambda: [("tmp", w) for w in range(8)]
    wldf = raw[:, 0:1024]
    wldf_keys = [("raw", 0), ("raw", 1)]
    wldb = tanh_bc[:, 0:1024]
    wldb_keys = [("tmp", 0), ("tmp", 1)]
    stg2 = gbuf[:, 0:256]
    stg2_keys = [("gbuf", 0)]
    blk8 = gbuf[:, 256:512]

    if 8 * TL >= 1280:
        smallS = raw[:, 1024:1280]
        stg = tanh_bc[:, 2048:2304]
        sKk = ("raw", 2)
        gKk = ("tmp", 4)
    else:
        smallS = sb["smallS_t"][:]
        stg = sb["stg_t"][:]
        sKk = "smallS_t"
        gKk = "stg_t"
    wrot_f32 = sb["wrot"][:].bitcast(F32)
    maskAs_v = wrot_f32[:, 0:192]
    maskBs_v = wrot_f32[:, 192:384]
    mask_keys = {"maskAs": [("wrot", 0)], "maskBs": [("wrot", 0), ("wrot", 1)]}
    valt_f32 = valt[:].bitcast(F32)
    stats = valt_f32[0:1, 0:4 * TL]
    packMR = valt_f32[0:1, 4 * TL:6 * TL]
    statsK = ("valt", 0, 0)
    packK = ("valt", 0, 1)
    xio = sb["xio"]
    xres_v = lambda slot: xio[:, slot * TP: slot * TP + TL]
    x1s_v = lambda slot: xio[:, slot * TP: slot * TP + TL]
    PDv = lambda k, n=512: PD[:, k * 512:k * 512 + n]

    def TT(engine, out, a, b_, op, reads, writes, sreads=()):
        return s.op(engine, lambda e: e.tensor_tensor(out, a, b_, op=op),
                    reads=reads, writes=writes, sreads=sreads)

    s.op("gpsimd", lambda e: e.memset(sb["identb"][:], 0.0), writes=["identb"])
    s.op("gpsimd", lambda e: e.affine_select(
        out=sb["identb"][:], in_=sb["identb"][:], compare_op=ALU.not_equal,
        fill=1.0, base=0, pattern=[[-1, 128]], channel_multiplier=1),
        reads=["identb"], writes=["identb"])
    s.op("vector", lambda e: e.memset(sb["onescol"][:], 1.0), writes=["onescol"])
    s.op("vector", lambda e: e.memset(ps["PD"][:], 0.0),
         writes=[("PD", k) for k in range(4)])
    s.op("vector", lambda e: e.memset(sb["onesrow"][:], 1.0), writes=["onesrow"])

    s.op("vector", lambda e: e.memset(wldf[:, 0:TP], 0.0), writes=wldf_keys)
    for bs in range(4):
        for half in range(2):
            for h in range(4):
                for tr in range(TL // 128):
                    s.dma("sync", t["shear"][bs, half, h, tr * 128:(tr + 1) * 128, :],
                          wldf[:, 0:TP], reads=wldf_keys, writes=[("shear", bs, half)])

    s.dma("scalar", smallS[0:16, 0:16], t["ppw"], writes=[sKk])
    s.dma("scalar", smallS[0:64, 16:32], t["relpos"], writes=[sKk])
    s.op("vector", lambda e: e.tensor_copy(stg[0:16, 0:16], smallS[0:16, 0:16]),
         reads=[sKk], writes=[gKk])
    s.op("vector", lambda e: e.tensor_copy(stg[0:64, 16:32], smallS[0:64, 16:32]),
         reads=[sKk], writes=[gKk])
    s.dma("scalar", smallS[0:32, 96:112], t["sfw"], writes=[sKk])
    s.op("vector", lambda e: e.tensor_copy(stg[0:32, 32:48], smallS[0:32, 96:112]),
         reads=[sKk], writes=[gKk])
    s.op("tensor", lambda e: e.transpose(PX[0:16, 0:16], stg[0:16, 0:16], sb["identb"][0:16, 0:16]),
         reads=[gKk, "identb"], writes=["PX"])
    s.op("tensor", lambda e: e.transpose(PX[0:16, 64:128], stg[0:64, 16:32], sb["identb"][0:64, 0:64]),
         reads=[gKk, "identb"], writes=["PX"])
    s.op("tensor", lambda e: e.transpose(PX[0:16, 128:160], stg[0:32, 32:48], sb["identb"][0:32, 0:32]),
         reads=[gKk, "identb"], writes=["PX"])
    s.op("vector", lambda e: e.tensor_copy(stg[0:16, 48:64], PX[0:16, 0:16]),
         reads=["PX"], writes=[gKk])
    s.op("vector", lambda e: e.tensor_copy(stg[0:16, 64:128], PX[0:16, 64:128]),
         reads=["PX"], writes=[gKk])
    s.op("vector", lambda e: e.tensor_copy(stg[0:16, 128:160], PX[0:16, 128:160]),
         reads=["PX"], writes=[gKk])
    s.op("tensor", lambda e: e.matmul(PA[0:16, 0:64], stg[0:16, 48:64],
                                      stg[0:16, 64:128], start=True, stop=True),
         reads=[gKk], writes=["PA"])
    s.op("vector", lambda e: e.tensor_copy(smallS[0:16, 32:96], PA[0:16, 0:64]),
         reads=["PA"], writes=[sKk])
    s.op("gpsimd", lambda e: e.memset(sb["posrep"][:], 0.0), writes=["posrep"])
    s.op("gpsimd", lambda e: e.memset(sb["sfbcol"][:], 0.0), writes=["sfbcol"])
    for h in range(4):
        s.dma("scalar", ap(sb["posrep"], (h * 32) * 64, [[64, 16], [1, 64]]),
              smallS[0:16, 32:96], reads=[sKk, "posrep"], writes=["posrep"])
        s.dma("scalar", ap(sb["sfbcol"], (h * 32), [[1, 32], [1, 1]]),
              t["sfb"].unsqueeze(1), reads=["sfbcol"], writes=["sfbcol"])
    s.op("vector", lambda e: e.tensor_scalar(sb["c_rep"][:], sb["posrep"][:],
                                             sb["sfbcol"][:], None, op0=ALU.add),
         reads=["posrep"], writes=["c_rep"], sreads=["sfbcol"])
    s.dma("scalar", smallS[0:1, 96:97], t["dob"].unsqueeze(1), reads=[sKk], writes=[sKk])
    s.op("tensor", lambda e: e.matmul(PB[:, 0:1], sb["onesrow"][:], smallS[0:1, 96:97],
                                      start=True, stop=True),
         reads=[sKk, "onesrow"], writes=["PB"])
    s.op("vector", lambda e: e.tensor_scalar_mul(sb["dbh"][:], PB[:, 0:1], 0.5),
         reads=["PB"], writes=["dbh"])
    s.op("gpsimd", lambda e: e.memset(sb["lhsT_dot"][:], 0.0), writes=["lhsT_dot"])
    s.dma("scalar", smallS[0:16, 112:113], t["bow"].rearrange("a b -> (a b)").unsqueeze(1),
          writes=[sKk])
    s.dma("scalar", smallS[0:16, 113:114], t["dow"].rearrange("a b -> (a b)").unsqueeze(1),
          writes=[sKk])
    s.op("vector", lambda e: e.tensor_copy(stg[0:16, 160:162], smallS[0:16, 112:114]),
         reads=[sKk], writes=[gKk])
    for h in range(4):
        s.dma("scalar", ap(sb["lhsT_dot"], (h * 32) * 8 + h, [[8, 16], [1, 1]]),
              stg[0:16, 160:161],
              reads=["lhsT_dot", gKk], writes=["lhsT_dot"])
        s.dma("scalar", ap(sb["lhsT_dot"], (h * 32 + 16) * 8 + 4 + h, [[8, 16], [1, 1]]),
              stg[0:16, 161:162],
              reads=["lhsT_dot", gKk], writes=["lhsT_dot"])
    s.op("gpsimd", lambda e: e.memset(blk8[:], 0.0), writes=[("gbuf", 0)])
    for h in range(8):
        s.dma("scalar", ap(gbuf, (h * 16) * (32 * TL) + 256 + h * 32, [[32 * TL, 16], [1, 32]]),
              stg[0:16, 128:160], reads=[gKk, ("gbuf", 0)], writes=[("gbuf", 0)])
    for oc in range(2):
        s.dma("sync", wldf, t["Wdisp"][oc * 128:(oc + 1) * 128, :],
              reads=wldf_keys, writes=wldf_keys)
        s.op("vector", lambda e: e.tensor_copy(wldb, wldf),
             reads=wldf_keys + wldb_keys, writes=wldb_keys)
        for dc in range(DCH):
            s.op("tensor", lambda e, dc=dc: e.matmul(PA[:, 0:256],
                 wldb[:, dc * 128:(dc + 1) * 128], blk8[:], start=True, stop=True),
                 reads=wldb_keys + [("gbuf", 0)], writes=["PA"])
            s.op("vector", lambda e, dc=dc, oc=oc: e.tensor_copy(
                 sb["Wcomb"][:, dc * 512 + oc * 256: dc * 512 + (oc + 1) * 256], PA[:, 0:256]),
                 reads=["PA"], writes=["Wcomb"])
    for (src, dst, key) in ((t["Wval"], t["wvalT"], "wvalT"), (t["Wcpj"], t["cprojT"], "cprojT")):
        for oc in range(8):
            s.dma("sync", wldf, src[oc * 128:(oc + 1) * 128, :],
                  reads=wldf_keys, writes=wldf_keys)
            s.op("vector", lambda e: e.tensor_copy(wldb, wldf),
                 reads=wldf_keys + wldb_keys, writes=wldb_keys)
            for dc in range(DCH):
                slot = dc % 2
                s.op("tensor", lambda e, dc=dc, slot=slot: e.transpose(
                     PX[:, slot * 128:(slot + 1) * 128],
                     wldb[:, dc * 128:(dc + 1) * 128], sb["identb"][:]),
                     reads=wldb_keys + ["identb"], writes=["PX"])
                s.op("vector", lambda e, slot=slot: e.tensor_copy(
                     stg2[:, slot * 128:(slot + 1) * 128] if False else
                     gbuf[:, slot * 128:(slot + 1) * 128],
                     PX[:, slot * 128:(slot + 1) * 128]),
                     reads=["PX"], writes=stg2_keys)
                s.dma("sync", dst[dc * 128:(dc + 1) * 128, oc * 128:(oc + 1) * 128],
                      gbuf[:, slot * 128:(slot + 1) * 128],
                      reads=stg2_keys, writes=[(key,)])

    for quad in range(2):
        for bs in range(4):
            b = quad * 4 + bs
            for dc in range(DCH):
                slot = dc % 2
                s.dma("sync", xio[:, slot * TP:(slot + 1) * TP],
                      t["xA"][b, dc * 128:(dc + 1) * 128, :],
                      reads=[("xio", slot)], writes=[("xio", slot)])
                s.op("gpsimd", lambda e, dc=dc, slot=slot: e.tensor_copy(
                     xb[:, dc * TP:(dc + 1) * TP], xio[:, slot * TP:(slot + 1) * TP]),
                     reads=[("xio", slot), ("xb", dc)], writes=[("xb", dc)])
            for oh in range(2):
                for dc in range(DCH):
                    slot = dc % 2
                    s.dma("sync", sb["wrot"][:, slot * 512:(slot + 1) * 512],
                          t["wvalT"][dc * 128:(dc + 1) * 128, oh * 512:(oh + 1) * 512],
                          reads=[("wrot", slot), ("wvalT",)], writes=[("wrot", slot)])
                    for tc in range(TCHP):
                        n = min(128, TP - tc * 128)
                        bank = PDv(tc) if tc < 4 else PA
                        s.op("tensor", lambda e, dc=dc, tc=tc, slot=slot, n=n, bank=bank: e.matmul(
                             bank[0:n, :],
                             xb[:, dc * TP + tc * 128: dc * TP + tc * 128 + n],
                             sb["wrot"][:, slot * 512:(slot + 1) * 512],
                             start=(dc == 0), stop=(dc == DCH - 1)),
                             reads=[("xb", dc), ("wrot", slot)],
                             writes=[("PD", tc) if tc < 4 else "PA"])
                for tc in range(TCHP):
                    n = min(128, TP - tc * 128)
                    bank = PDv(tc) if tc < 4 else PA
                    s.op("vector", lambda e, tc=tc, oh=oh, bs=bs, n=n, bank=bank: e.tensor_copy(
                         valt[0:n, (bs * TCHP + tc) * 1024 + oh * 512:
                              (bs * TCHP + tc) * 1024 + (oh + 1) * 512], bank[0:n, :]),
                         reads=[("PD", tc) if tc < 4 else "PA"],
                         writes=[("valt", bs, tc)])
            for hg in range(NHG):
                w0 = min(512, TP)
                for dc in range(DCH):
                    s.op("tensor", lambda e, dc=dc, hg=hg, w0=w0: e.matmul(
                         PDv(0, w0), sb["Wcomb"][:, dc * 512 + hg * 128: dc * 512 + (hg + 1) * 128],
                         xb[:, dc * TP: dc * TP + w0],
                         start=(dc == 0), stop=(dc == DCH - 1)),
                         reads=[("xb", dc), "Wcomb"], writes=[("PD", 0)])
                    if TP > 512:
                        s.op("tensor", lambda e, dc=dc, hg=hg: e.matmul(
                             PDv(1, TP - 512), sb["Wcomb"][:, dc * 512 + hg * 128: dc * 512 + (hg + 1) * 128],
                             xb[:, dc * TP + 512: (dc + 1) * TP],
                             start=(dc == 0), stop=(dc == DCH - 1)),
                             reads=[("xb", dc), "Wcomb"], writes=[("PD", 1)])
                s.op("vector", lambda e, bs=bs, hg=hg, w0=w0: e.tensor_copy(
                     dproj[:, (bs * 4 + hg) * TP: (bs * 4 + hg) * TP + w0], PDv(0, w0)),
                     reads=[("PD", 0)], writes=[("dproj", bs, hg)])
                if TP > 512:
                    s.op("vector", lambda e, bs=bs, hg=hg: e.tensor_copy(
                         dproj[:, (bs * 4 + hg) * TP + 512: (bs * 4 + hg + 1) * TP],
                         PDv(1, TP - 512)),
                         reads=[("PD", 1)], writes=[("dproj", bs, hg)])

        if dbg and quad == 0:
            s.dma("sync", t["dbg_dproj"], dproj[:, 0:TP], reads=[("dproj", 0, 0)],
                  writes=["dbgx"])
            s.dma("sync", t["dbg_valt"], valt[:, 0:1024], reads=[("valt", 0, 0)],
                  writes=["dbgx"])
        s.dma("sync", maskAs_v, t["maskA"], reads=[("wrot", 0)], writes=[("wrot", 0)])
        s.dma("sync", maskBs_v, t["maskB"], reads=[("wrot", 0), ("wrot", 1)],
              writes=[("wrot", 0), ("wrot", 1)])
        for hg in range(NHG):
            for wb in range(8):
                for bs in range(4):
                    dbase = (bs * 4 + hg) * TP
                    for wsub in range(8):
                        w = wb * 8 + wsub
                        in0 = ap(dproj, dbase + 1 + w, [[16 * TP, 128], [1, TL]])
                        in1 = ap(dproj, dbase + 64, [[16 * TP, 128], [1, TL]])
                        out = tanh_bc[:, wsub * TL:(wsub + 1) * TL]
                        if wsub % 2 == 0:
                            s.op("vector", lambda e, in0=in0, in1=in1, out=out, w=w:
                                 e.scalar_tensor_tensor(
                                 out, in0, sb["c_rep"][:, w:w + 1], in1,
                                 op0=ALU.add, op1=ALU.subtract),
                                 reads=[("dproj", bs, hg)], writes=[("tmp", wsub)],
                                 sreads=["c_rep"])
                        else:
                            cbc = ap(sb["c_rep"], w, [[64, 128], [0, TL]])
                            s.op("gpsimd", lambda e, in0=in0, cbc=cbc: e.tensor_tensor(
                                 sb["gpscr"][:], in0, cbc, op=ALU.add),
                                 reads=[("dproj", bs, hg), "c_rep", "gpscr"],
                                 writes=["gpscr"])
                            s.op("gpsimd", lambda e, in1=in1, out=out: e.tensor_tensor(
                                 out, sb["gpscr"][:], in1, op=ALU.subtract),
                                 reads=["gpscr", ("dproj", bs, hg)], writes=[("tmp", wsub)])
                    s.op("scalar", lambda e, bs=bs: e.activation(
                         gbuf[:, bs * 8 * TL:(bs + 1) * 8 * TL], tanh_bc[:, 0:8 * TL], AF.Gelu),
                         reads=tmp_keys(), writes=[("gbuf", bs)])
                for g4 in range(2):
                    for wsub in range(4):
                        for bs in range(4):
                            tp = None if bs == 0 else (0, 32 * bs)
                            s.op("tensor", lambda e, g4=g4, wsub=wsub, bs=bs, tp=tp: e.matmul(
                                 ap(PD, (32 * bs) * 2048 + wsub * 512, [[2048, 8], [1, TL]]),
                                 sb["lhsT_dot"][:],
                                 gbuf[:, (bs * 8 + g4 * 4 + wsub) * TL:(bs * 8 + g4 * 4 + wsub + 1) * TL],
                                 start=True, stop=True, tile_position=tp),
                                 reads=[("gbuf", bs), "lhsT_dot"],
                                 writes=[("PD", wsub)])
                    s.op("vector", lambda e, g4=g4: e.tensor_copy(
                         ap(raw, g4 * 4, [[8 * TL, 128], [1, 4], [8, TL]]),
                         ap(PD, 0, [[2048, 128], [512, 4], [1, TL]])),
                         reads=[("PD", k) for k in range(4)],
                         writes=[("raw", k) for k in range(8)])
                s.op("scalar", lambda e: e.activation(
                     tanhb[:], raw[:], AF.Tanh, bias=sb["dbh"][:, 0:1], scale=0.5),
                     reads=[("raw", k) for k in range(8)], writes=tmp_keys(),
                     sreads=["dbh"])
                if dbg and quad == 0 and hg == 0 and wb == 0:
                    s.dma("sync", t["dbg_raw"], raw[:], reads=[("raw", k) for k in range(8)],
                          writes=["dbgx"])
                for bs in range(4):
                    for half in range(2):
                        srcbuf = raw if half == 0 else tanhb
                        srckeys = [("raw", k) for k in range(8)] if half == 0 else tmp_keys()
                        src = ap(srcbuf, (32 * bs + 4 * half) * 8 * TL,
                                 [[8 * TL, 4], [8, TL], [1, 8]])
                        dst = ap(t["shear"], ((bs * 2 + half) * 4) * TL * TP + wb * 8 + 1,
                                 [[TL * TP, 4], [TP + 1, TL], [1, 8]])
                        s.dma("scalar", dst, src, reads=srckeys,
                              writes=[("shear", bs, half)])
            for bs in range(4):
                for tc in range(NTC):
                    roff = lambda half: ((bs * 2 + half) * 4) * TL * TP + tc * 128 * TP + tc * 128
                    s.dma("sync", sb["bondR"][:],
                          ap(t["shear"], roff(0), [[TP, 128], [TL * TP, 4], [1, 192]]),
                          reads=[("shear", bs, 0), "bondR"], writes=["bondR"])
                    s.dma("sync", sb["dmgR"][:],
                          ap(t["shear"], roff(1), [[TP, 128], [TL * TP, 4], [1, 192]]),
                          reads=[("shear", bs, 1), "dmgR"], writes=["dmgR"])
                    s.op("gpsimd", lambda e: e.tensor_scalar(
                         sb["S4"][:], sb["dmgR"][:], -5.0, None, op0=ALU.mult),
                         reads=["dmgR"], writes=["S4"])
                    s.op("gpsimd", lambda e: e.tensor_tensor(
                         sb["S3"][:], sb["S4"][:], sb["bondR"][:], op=ALU.add),
                         reads=["S4", "bondR"], writes=["S3"])
                    mv = maskAs_v if tc == 0 else maskBs_v
                    mks = mask_keys["maskAs" if tc == 0 else "maskBs"]
                    moff = 0 if tc == 0 else 192
                    s.op("gpsimd", lambda e, moff=moff: e.tensor_tensor(
                         sb["S4"][:], sb["S3"][:],
                         ap(wrot_f32, moff, [[512, 128], [0, 4], [1, 192]]), op=ALU.add),
                         reads=["S3"] + mks, writes=["S4"])
                    s.op("vector", lambda e: e.tensor_reduce(
                         sb["negmax"][:].unsqueeze(2),
                         sb["S4"][:].rearrange("p (h u) -> p h u", h=4),
                         axis=AX.X, op=ALU.max, negate=True),
                         reads=["S4"], writes=["negmax"])
                    s.op("gpsimd", lambda e: e.tensor_tensor(
                         sb["S3"][:], sb["S4"][:],
                         ap(sb["negmax"], 0, [[4, 128], [1, 4], [0, 192]]), op=ALU.add),
                         reads=["S4", "negmax"], writes=["S3"])
                    s.op("scalar", lambda e: e.activation(sb["S4"][:], sb["S3"][:], AF.Exp),
                         reads=["S3"], writes=["S4"])
                    s.op("vector", lambda e: e.tensor_reduce(
                         sb["sumR"][:].unsqueeze(2),
                         sb["S4"][:].rearrange("p (h u) -> p h u", h=4),
                         axis=AX.X, op=ALU.add),
                         reads=["S4"], writes=["sumR"])
                    s.op("vector", lambda e: e.reciprocal(sb["recR"][:], sb["sumR"][:]),
                         reads=["sumR"], writes=["recR"])
                    s.op("gpsimd", lambda e: e.tensor_tensor(
                         sb["wcast"][:], sb["S4"][:],
                         ap(sb["recR"], 0, [[4, 128], [1, 4], [0, 192]]), op=ALU.mult),
                         reads=["S4", "recR"], writes=["wcast"])
                    if dbg and quad == 0 and hg == 0 and bs == 0 and tc == 0:
                        s.dma("sync", t["dbg_bondR"], sb["bondR"][:], reads=["bondR"], writes=["dbgx"])
                        s.dma("sync", t["dbg_dmgR"], sb["dmgR"][:], reads=["dmgR"], writes=["dbgx"])
                        s.dma("sync", t["dbg_wcast"], sb["wcast"][:], reads=["wcast"], writes=["dbgx"])
                    for pair in range(2):
                        for hsub in range(2):
                            h = pair * 2 + hsub
                            s.op("tensor", lambda e, h=h: e.transpose(
                                 PX[:, 0:128], sb["wcast"][:, h * 192: h * 192 + 128],
                                 sb["identb"][:]),
                                 reads=["wcast", "identb"], writes=["PX"])
                            s.op("vector", lambda e: e.tensor_copy(sb["SuA"][:], PX[:, 0:128]),
                                 reads=["PX"], writes=["SuA"])
                            s.op("tensor", lambda e, h=h: e.transpose(
                                 PX[0:64, 128:256], sb["wcast"][:, h * 192 + 128: (h + 1) * 192],
                                 sb["identb"][:]),
                                 reads=["wcast", "identb"], writes=["PX"])
                            s.op("vector", lambda e: e.tensor_copy(sb["SuB"][0:64, :], PX[0:64, 128:256]),
                                 reads=["PX"], writes=["SuB"])
                            hglob = hg * 4 + h
                            vc0 = (bs * TCHP + tc) * 1024 + hglob * 64
                            vc1 = (bs * TCHP + tc + 1) * 1024 + hglob * 64
                            s.op("tensor", lambda e, vc0=vc0, hsub=hsub: e.matmul(
                                 PV[64 * hsub:64 * (hsub + 1), 0:128],
                                 valt[:, vc0:vc0 + 64], sb["SuA"][:],
                                 start=True, stop=False),
                                 reads=[("valt", bs, tc), "SuA"], writes=["PV"])
                            s.op("tensor", lambda e, vc1=vc1, hsub=hsub: e.matmul(
                                 PV[64 * hsub:64 * (hsub + 1), 0:128],
                                 valt[0:64, vc1:vc1 + 64], sb["SuB"][0:64, :],
                                 start=False, stop=True),
                                 reads=[("valt", bs, tc + 1), "SuB"], writes=["PV"])
                        c = hg * 2 + pair
                        s.op("vector", lambda e, bs=bs, c=c, tc=tc: e.tensor_copy(
                             sb["attn"][:, (bs * 8 + c) * TL + tc * 128:(bs * 8 + c) * TL + (tc + 1) * 128],
                             PV[:, 0:128]),
                             reads=["PV"], writes=[("attn", bs, c)])

        if dbg and quad == 0:
            s.dma("sync", t["dbg_attn"], sb["attn"][:, 0:8 * TL],
                  reads=[("attn", 0, c) for c in range(8)], writes=["dbgx"])
        for bs in range(4):
            b = quad * 4 + bs
            banks = [("PA", PA), ("PB", PB), (("PD", 0), PDv(0)), (("PD", 1), PDv(1))]
            for pas in range(2):
                for dc in range(DCH):
                    slot = dc % 2
                    s.dma("sync", sb["wrot"][:, slot * 512:(slot + 1) * 512],
                          t["cprojT"][dc * 128:(dc + 1) * 128, pas * 512:(pas + 1) * 512],
                          reads=[("wrot", slot), ("cprojT",)], writes=[("wrot", slot)])
                    for j, (bk, bt) in enumerate(banks):
                        s.op("tensor", lambda e, dc=dc, j=j, slot=slot, bt=bt: e.matmul(
                             bt[:, 0:TL], sb["wrot"][:, slot * 512 + j * 128: slot * 512 + (j + 1) * 128],
                             sb["attn"][:, (bs * 8 + dc) * TL:(bs * 8 + dc + 1) * TL],
                             start=(dc == 0), stop=(dc == DCH - 1)),
                             reads=[("wrot", slot), ("attn", bs, dc)], writes=[bk])
                for j, (bk, bt) in enumerate(banks):
                    oc = pas * 4 + j
                    slot = oc % 2
                    s.dma("sync", xres_v(slot),
                          t["xA"][b, oc * 128:(oc + 1) * 128, 64:TP],
                          reads=[("xio", slot)], writes=[("xio", slot)])
                    s.op("vector", lambda e, j=j, oc=oc, slot=slot, bt=bt: e.tensor_add(
                         raw[:, oc * TL:(oc + 1) * TL], bt[:, 0:TL],
                         xres_v(slot)),
                         reads=[bk, ("xio", slot)], writes=[("raw", oc)])
            for oc in range(DCH):
                s.op("vector", lambda e, oc=oc: e.tensor_copy(sb["rb"][:], raw[:, oc * TL:(oc + 1) * TL]),
                     reads=[("raw", oc)], writes=["rb"])
                s.op("vector", lambda e: e.tensor_mul(sb["sqb"][:], sb["rb"][:], sb["rb"][:]),
                     reads=["rb"], writes=["sqb"])
                s.op("tensor", lambda e, oc=oc: e.matmul(
                     ap(PD, 2 * 512, [[2048, 1], [1, TL]]), sb["onescol"][:], sb["rb"][:],
                     start=(oc == 0), stop=(oc == DCH - 1)),
                     reads=["rb", "onescol"], writes=[("PD", 2)])
                s.op("tensor", lambda e, oc=oc: e.matmul(
                     ap(PD, 3 * 512, [[2048, 1], [1, TL]]), sb["onescol"][:], sb["sqb"][:],
                     start=(oc == 0), stop=(oc == DCH - 1)),
                     reads=["sqb", "onescol"], writes=[("PD", 3)])
            s.op("vector", lambda e: e.tensor_scalar_mul(packMR[0:1, 0:TL], PDv(2)[0:1, 0:TL], 1.0 / D),
                 reads=[("PD", 2)], writes=[packK])
            s.op("vector", lambda e: e.tensor_scalar_mul(stats[0:1, 0:TL], PDv(3)[0:1, 0:TL], 1.0 / D),
                 reads=[("PD", 3)], writes=[statsK])
            s.op("vector", lambda e: e.tensor_mul(stats[0:1, TL:2 * TL],
                                                  packMR[0:1, 0:TL], packMR[0:1, 0:TL]),
                 reads=[packK], writes=[statsK])
            s.op("vector", lambda e: e.scalar_tensor_tensor(
                 stats[0:1, 2 * TL:3 * TL], stats[0:1, 0:TL], 1e-5,
                 stats[0:1, TL:2 * TL], op0=ALU.add, op1=ALU.subtract),
                 reads=[statsK], writes=[statsK])
            s.op("scalar", lambda e: e.activation(stats[0:1, 3 * TL:4 * TL],
                                                  stats[0:1, 2 * TL:3 * TL],
                                                  AF.Sqrt),
                 reads=[statsK], writes=[statsK])
            s.op("vector", lambda e: e.reciprocal(packMR[0:1, TL:2 * TL],
                                                  stats[0:1, 3 * TL:4 * TL]),
                 reads=[statsK], writes=[packK])
            s.op("tensor", lambda e: e.matmul(PA[:, 0:TL], sb["onesrow"][:],
                                              packMR[0:1, 0:TL], start=True, stop=True),
                 reads=[packK, "onesrow"], writes=["PA"])
            s.op("tensor", lambda e: e.matmul(PB[:, 0:TL], sb["onesrow"][:],
                                              packMR[0:1, TL:2 * TL], start=True, stop=True),
                 reads=[packK, "onesrow"], writes=["PB"])
            for oc in range(DCH):
                s.op("vector", lambda e, oc=oc: e.tensor_sub(
                     x1s_v(0), raw[:, oc * TL:(oc + 1) * TL], PA[:, 0:TL]),
                     reads=[("raw", oc), "PA"], writes=[("xio", 0)])
                s.op("vector", lambda e: e.tensor_mul(
                     x1s_v(1), x1s_v(0), PB[:, 0:TL]),
                     reads=[("xio", 0), "PB"], writes=[("xio", 1)])
                s.dma("sync", t["x1"][b, oc * 128:(oc + 1) * 128, :],
                      x1s_v(1),
                      reads=[("xio", 1)], writes=[("x1", b, oc)], force_inc=True)
    s.final_wait()


D, F_, E, R = 1024, 4096, 8, 16
DCH, FCH = 8, 32


def declare_dram_b(nc, TT, NB, suffix="", dbg=False):
    t = {}
    ein = lambda n, s: nc.dram_tensor(n + suffix, s, F32, kind="ExternalInput").ap()
    t["x1"] = ein("x1", [NB, D, TT])
    t["normw"] = ein("normw", [D])
    t["w1"] = ein("w1", [F_, D])
    t["w3"] = ein("w3", [F_, D])
    t["w2"] = ein("w2", [D, F_])
    t["gatew"] = ein("gatew", [E, D])
    t["a1"] = ein("a1", [E, R, D])
    t["b1"] = ein("b1", [E, F_, R])
    t["a3"] = ein("a3", [E, R, D])
    t["b3"] = ein("b3", [E, F_, R])
    t["a2"] = ein("a2", [E, R, F_])
    t["b2"] = ein("b2", [E, D, R])
    t["x2"] = nc.dram_tensor("x2" + suffix, [NB, D, TT], F32, kind="ExternalOutput").ap()
    t["w1T"] = nc.dram_tensor("w1T" + suffix, [D, F_], BF16, kind="Internal").ap()
    t["w3T"] = nc.dram_tensor("w3T" + suffix, [D, F_], BF16, kind="Internal").ap()
    t["w2T"] = nc.dram_tensor("w2T" + suffix, [F_, D], BF16, kind="Internal").ap()
    t["b1T"] = nc.dram_tensor("b1T" + suffix, [E, R, F_], BF16, kind="Internal").ap()
    t["b3T"] = nc.dram_tensor("b3T" + suffix, [E, R, F_], BF16, kind="Internal").ap()
    if dbg:
        t["dbg_nxb"] = nc.dram_tensor("dbg_nxb" + suffix, [128, DCH * TT], BF16, kind="ExternalOutput").ap()
    if dbg:
        t["dbg_wts"] = nc.dram_tensor("dbg_wts" + suffix, [128, 32], F32, kind="ExternalOutput").ap()
    if dbg:
        t["dbg_wrep"] = nc.dram_tensor("dbg_wrep" + suffix, [128, E * TT], BF16, kind="ExternalOutput").ap()
    if dbg:
        t["dbg_base1"] = nc.dram_tensor("dbg_base1" + suffix, [128, FCH * TT], BF16, kind="ExternalOutput").ap()
    if dbg:
        t["dbg_etr1"] = nc.dram_tensor("dbg_etr1" + suffix, [128, 2 * TT], BF16, kind="ExternalOutput").ap()
    if dbg:
        t["dbg_act"] = nc.dram_tensor("dbg_act" + suffix, [128, 4 * E * TT], BF16, kind="ExternalOutput").ap()
    if dbg:
        t["dbg_out"] = nc.dram_tensor("dbg_out" + suffix, [128, DCH * TT], F32, kind="ExternalOutput").ap()
    return t


def alloc_bufs_b(ctx, nc, TT):
    sb, ps = {}, {}
    s_ = lambda n, shape, dt: ctx.enter_context(nc.sbuf_tensor(n, shape, dt))
    p_ = lambda n, shape, dt: ctx.enter_context(nc.psum_tensor(n, shape, dt))
    sb["x1f"] = s_("x1f", [128, DCH * TT], F32)
    sb["nxb"] = s_("nxb", [128, DCH * TT], BF16)
    sb["base1"] = s_("base1", [128, FCH * TT], BF16)
    sb["base3"] = s_("base3", [128, FCH * TT], BF16)
    sb["actg"] = s_("actg", [128, 4 * E * TT], BF16)
    sb["wst"] = s_("wst", [128, 2 * 1024], BF16)
    sb["w2st"] = s_("w2st", [128, 4 * 1024], BF16)
    sb["bst"] = s_("bst", [128, 2048], BF16)
    sb["etr1P"] = s_("etr1P", [128, 2 * TT], BF16)
    sb["etr3P"] = s_("etr3P", [128, 2 * TT], BF16)
    sb["etr2P"] = s_("etr2P", [128, 2 * TT], BF16)
    sb["a1T"] = s_("a1T", [128, 2 * 1024], BF16)
    sb["a3T"] = s_("a3T", [128, 2 * 1024], BF16)
    sb["a2T"] = s_("a2T", [128, 4096], BF16)
    sb["b2s"] = s_("b2s", [128, 2 * 1024], BF16)
    sb["wrep"] = s_("wrep", [128, E * TT], BF16)

    sb["outacc"] = s_("outacc", [128, DCH * TT], F32)
    sb["gateT"] = s_("gateT", [128, 64], BF16)
    sb["normw"] = s_("normw_sb", [128, 8], F32)
    sb["gs1"] = s_("gs1", [128, 32], F32)
    sb["gs2"] = s_("gs2", [128, 32], F32)
    sb["gm"] = s_("gm", [128, 4], F32)
    sb["gm2"] = s_("gm2", [128, 4], F32)
    sb["gmr"] = s_("gmr", [128, 4], F32)
    sb["idxw"] = s_("idxw", [128, 32], F32)
    sb["idxwi"] = s_("idxwi", [128, 32], mybir.dt.int32)
    sb["wtsE"] = s_("wtsE", [128, 32], F32)
    sb["wEm"] = s_("wEm", [128, TT], F32)
    sb["identb"] = s_("identb", [128, 128], BF16)
    sb["identf"] = s_("identf", [128, 128], F32)
    sb["onescol"] = s_("onescol", [128, 1], BF16)
    sb["onesrow"] = s_("onesrow", [1, 128], F32)
    sb["zzb"] = s_("zzb", [1, 128], BF16)
    sb["selc"] = s_("selc", [8, 1024], BF16)
    sb["wEmb"] = s_("wEmb", [8, 512], BF16)
    sb["ldx"] = s_("ldx", [128, 1024], F32)
    sb["ldb"] = s_("ldb", [128, 1024], BF16)
    sb["stg2"] = s_("stg2", [128, 256], BF16)
    sb["rb"] = s_("rb", [128, TT], BF16)
    sb["sqb"] = s_("sqb", [128, TT], BF16)
    sb["xio"] = s_("xio", [128, 2 * TT], F32)
    ps["PA"] = p_("PA", [128, 512], F32)
    ps["PB"] = p_("PB", [128, 512], F32)
    ps["PD"] = p_("PD", [128, 2048], F32)
    ps["PV"] = p_("PV", [128, 512], F32)
    ps["PX"] = p_("PX", [128, 512], F32)
    return sb, ps


def emit_transpose_prep_b(s, t, sb, PXb, src, dst, RR, CC, kpref):
    ldf, ldb, stg2 = sb["ldf"], sb["ldb"], sb["stg2"]
    for rc in range(RR // 128):
        for cb in range(CC // 1024):
            s.dma("sync", ldf[:, 0:1024], src[rc * 128:(rc + 1) * 128, cb * 1024:(cb + 1) * 1024],
                  reads=[("ldx", 0), ("ldx", 1)], writes=[("ldx", 0), ("ldx", 1)])
            s.op("vector", lambda e: e.tensor_copy(ldb[:, 0:1024], ldf[:, 0:1024]),
                 reads=[("ldx", 0), ("ldx", 1), ("ldb",)], writes=[("ldb",)])
            for q in range(8):
                slot = q % 2
                s.op("tensor", lambda e, q=q, slot=slot: e.transpose(
                     PXb[:, slot * 128:(slot + 1) * 128],
                     ldb[:, q * 128:(q + 1) * 128], sb["identb"][:]),
                     reads=[("ldb",), "identb"], writes=["PX"])
                s.op("vector", lambda e, slot=slot: e.tensor_copy(
                     stg2[:, slot * 128:(slot + 1) * 128], PXb[:, slot * 128:(slot + 1) * 128]),
                     reads=["PX"], writes=[("stg2x",)])
                s.dma("sync", dst[cb * 1024 + q * 128: cb * 1024 + (q + 1) * 128,
                                  rc * 128:(rc + 1) * 128],
                      stg2[:, slot * 128:(slot + 1) * 128],
                      reads=[("stg2x",)], writes=[("dst", kpref)])


def emit_b(s: Sched, t, sb, ps, TT, NB, stop_stage=99, dbg=False):
    nc = s.nc
    PA, PB, PD, PV, PX = ps["PA"], ps["PB"], ps["PD"], ps["PV"], ps["PX"]
    PXb = PX[:].bitcast(BF16)
    NTC = TT // 128
    PDv = lambda k, n=512: PD[:, k * 512:k * 512 + n]
    x1f, nxb, base1, base3, actg = (sb[k] for k in ("x1f", "nxb", "base1", "base3", "actg"))
    sb["ldf"] = sb["ldx"]
    sb["x2s"] = sb["ldx"]
    sb["silu"] = sb["rb"]
    sb["h3w"] = sb["sqb"]
    outacc, wrep, xio = sb["outacc"], sb["wrep"], sb["xio"]
    g3 = lambda a: a.rearrange("p (c e) -> p c e", e=8)

    s.op("gpsimd", lambda e: e.memset(sb["identb"][:], 0.0), writes=["identb"])
    s.op("gpsimd", lambda e: e.affine_select(
        out=sb["identb"][:], in_=sb["identb"][:], compare_op=ALU.not_equal,
        fill=1.0, base=0, pattern=[[-1, 128]], channel_multiplier=1),
        reads=["identb"], writes=["identb"])
    s.op("gpsimd", lambda e: e.memset(sb["identf"][:], 0.0), writes=["identf"])
    s.op("gpsimd", lambda e: e.affine_select(
        out=sb["identf"][:], in_=sb["identf"][:], compare_op=ALU.not_equal,
        fill=1.0, base=0, pattern=[[-1, 128]], channel_multiplier=1),
        reads=["identf"], writes=["identf"])
    s.op("vector", lambda e: e.memset(sb["onescol"][:], 1.0), writes=["onescol"])
    s.op("vector", lambda e: e.memset(sb["onesrow"][:], 1.0), writes=["onesrow"])
    s.op("vector", lambda e: e.memset(sb["zzb"][:], 0.0), writes=["zzb"])
    s.op("vector", lambda e: e.memset(sb["bst"][:], 0.0),
         writes=[("bst", w_, h, e4) for w_ in range(2) for h in range(2) for e4 in range(4)])
    s.op("vector", lambda e: e.memset(sb["b2s"][:], 0.0), writes=["b2s"])
    s.op("vector", lambda e: e.memset(PD[:], 0.0), writes=[("PD", k) for k in range(4)])
    s.op("vector", lambda e: e.memset(PV[:], 0.0), writes=["PV"])
    s.op("vector", lambda e: e.memset(PX[:], 0.0), writes=["PX"])
    s.op("gpsimd", lambda e: e.iota(sb["idxwi"][:], base=8, pattern=[[0, 4], [-1, 8]],
                                    channel_multiplier=0), writes=["idxwi"])
    s.op("vector", lambda e: e.tensor_copy(sb["idxw"][:], sb["idxwi"][:]),
         reads=["idxwi"], writes=["idxw"])
    s.op("gpsimd", lambda e: e.memset(sb["selc"][:], 0.0), writes=["selc"])
    s.op("gpsimd", lambda e: e.affine_select(
        out=sb["selc"][:], in_=sb["selc"][:], compare_op=ALU.not_equal,
        fill=1.0, base=0, pattern=[[-1, 8], [0, 128]], channel_multiplier=1),
        reads=["selc"], writes=["selc"])
    for dc in range(DCH):
        s.dma("scalar", sb["normw"][:, dc:dc + 1],
              t["normw"][dc * 128:(dc + 1) * 128].unsqueeze(1),
              reads=["normw_s"], writes=["normw_s"])
    s.dma("sync", sb["ldf"][0:8, 0:1024], t["gatew"], reads=[("ldx", 0), ("ldx", 1)], writes=[("ldx", 0), ("ldx", 1)])
    s.op("vector", lambda e: e.tensor_copy(sb["ldb"][0:8, 0:1024], sb["ldf"][0:8, 0:1024]),
         reads=[("ldx", 0), ("ldx", 1), ("ldb",)], writes=[("ldb",)])
    for dc in range(DCH):
        s.op("tensor", lambda e, dc=dc: e.transpose(
             PXb[:, 0:8], sb["ldb"][0:8, dc * 128:(dc + 1) * 128], sb["identb"][0:8, 0:8]),
             reads=[("ldb",), "identb"], writes=["PX"])
        s.op("vector", lambda e, dc=dc: e.tensor_copy(sb["gateT"][:, dc * 8:(dc + 1) * 8],
                                                      PXb[:, 0:8]),
             reads=["PX"], writes=["gateT"])
    emit_transpose_prep_b(s, t, sb, PXb, t["w1"], t["w1T"], F_, D, "w1")
    emit_transpose_prep_b(s, t, sb, PXb, t["w3"], t["w3T"], F_, D, "w3")
    emit_transpose_prep_b(s, t, sb, PXb, t["w2"], t["w2T"], D, F_, "w2")
    for (bsrc, bdst, kp) in ((t["b1"], t["b1T"], "b1"), (t["b3"], t["b3T"], "b3")):
        for e_ in range(E):
            for fb in range(4):
                src = ap(bsrc, e_ * F_ * R + fb * 1024 * R, [[16, 128], [2048, 8], [1, 16]])
                s.dma("sync", ap(sb["ldf"], 0, [[1024, 128], [1, 128]]), src,
                      reads=[("ldx", 0), ("ldx", 1)], writes=[("ldx", 0), ("ldx", 1)])
                s.op("vector", lambda e2: e2.tensor_copy(
                     ap(sb["ldb"], 0, [[1024, 128], [1, 128]]),
                     ap(sb["ldf"], 0, [[1024, 128], [1, 128]])),
                     reads=[("ldx", 0), ("ldx", 1), ("ldb",)], writes=[("ldb",)])
                for q in range(8):
                    slot = q % 2
                    tile = ap(sb["ldb"], q * 16, [[1024, 128], [1, 16]])
                    s.op("tensor", lambda e2, tile=tile, slot=slot: e2.transpose(
                         PXb[0:16, slot * 128:slot * 128 + 128], tile, sb["identb"][:]),
                         reads=[("ldb",), "identb"], writes=["PX"])
                    s.op("vector", lambda e2, slot=slot: e2.tensor_copy(
                         sb["stg2"][0:16, slot * 128:slot * 128 + 128],
                         PXb[0:16, slot * 128:slot * 128 + 128]),
                         reads=["PX"], writes=[("stg2x",)])
                    s.dma("sync", bdst[e_, :, fb * 1024 + q * 128: fb * 1024 + (q + 1) * 128],
                          sb["stg2"][0:16, slot * 128:slot * 128 + 128],
                          reads=[("stg2x",)], writes=[("dst", kp)])
    for (asrc, adst) in ((t["a1"], "a1T"), (t["a3"], "a3T")):
        for half in range(2):
            for e4 in range(4):
                s.dma("sync", ap(sb["ldf"], (e4 * 32) * 1024, [[1024, 16], [1, 1024]]),
                      asrc[half * 4 + e4], reads=[("ldx", 0), ("ldx", 1)], writes=[("ldx", 0), ("ldx", 1)])
            s.op("vector", lambda e2: e2.memset(sb["ldb"][:, 0:1024], 0.0),
                 reads=[("ldb",)], writes=[("ldb",)])
            for e4 in range(4):
                s.op("vector", lambda e2, e4=e4: e2.tensor_copy(
                     ap(sb["ldb"], (e4 * 32) * 1024, [[1024, 16], [1, 1024]]),
                     ap(sb["ldf"], (e4 * 32) * 1024, [[1024, 16], [1, 1024]])),
                     reads=[("ldx", 0), ("ldx", 1), ("ldb",)], writes=[("ldb",)])
            for dc in range(DCH):
                s.op("tensor", lambda e2, dc=dc: e2.transpose(
                     PXb[:, 0:128], sb["ldb"][:, dc * 128:(dc + 1) * 128], sb["identb"][:]),
                     reads=[("ldb",), "identb"], writes=["PX"])
                s.op("vector", lambda e2, dc=dc, half=half, adst=adst: e2.tensor_copy(
                     sb[adst][:, half * 1024 + dc * 128: half * 1024 + (dc + 1) * 128],
                     PXb[:, 0:128]),
                     reads=["PX"], writes=[adst])
    for e_ in range(E):
        for cb in range(4):
            s.dma("sync", ap(sb["ldf"], 0, [[1024, 16], [1, 1024]]),
                  t["a2"][e_, :, cb * 1024:(cb + 1) * 1024],
                  reads=[("ldx", 0), ("ldx", 1)], writes=[("ldx", 0), ("ldx", 1)])
            s.op("vector", lambda e2: e2.tensor_copy(
                 ap(sb["ldb"], 0, [[1024, 16], [1, 1024]]),
                 ap(sb["ldf"], 0, [[1024, 16], [1, 1024]])),
                 reads=[("ldx", 0), ("ldx", 1), ("ldb",)], writes=[("ldb",)])
            for q in range(8):
                fc = cb * 8 + q
                slot = q % 2
                s.op("tensor", lambda e2, q=q, slot=slot: e2.transpose(
                     PXb[:, slot * 128:slot * 128 + 16],
                     ap(sb["ldb"], q * 128, [[1024, 16], [1, 128]]),
                     sb["identb"][0:16, 0:16]),
                     reads=[("ldb",), "identb"], writes=["PX"])
                s.op("vector", lambda e2, fc=fc, e_=e_, slot=slot: e2.tensor_copy(
                     sb["a2T"][:, fc * 128 + e_ * 16: fc * 128 + (e_ + 1) * 16],
                     PXb[:, slot * 128:slot * 128 + 16]),
                     reads=["PX"], writes=["a2T"])
    for half in range(2):
        for e4 in range(4):
            e_ = half * 4 + e4
            for dgb in range(2):
                src = ap(t["b2"], e_ * D * R + dgb * 512 * R, [[16, 128], [2048, 4], [1, 16]])
                s.dma("sync", ap(sb["ldf"], 0, [[1024, 128], [1, 64]]), src,
                      reads=[("ldx", 0), ("ldx", 1)], writes=[("ldx", 0), ("ldx", 1)])
                s.op("vector", lambda e2: e2.tensor_copy(
                     ap(sb["ldb"], 0, [[1024, 128], [1, 64]]),
                     ap(sb["ldf"], 0, [[1024, 128], [1, 64]])),
                     reads=[("ldx", 0), ("ldx", 1), ("ldb",)], writes=[("ldb",)])
                for q in range(4):
                    slot = q % 2
                    tile = ap(sb["ldb"], q * 16, [[1024, 128], [1, 16]])
                    s.op("tensor", lambda e2, tile=tile, slot=slot: e2.transpose(
                         PXb[0:16, slot * 128:slot * 128 + 128], tile, sb["identb"][:]),
                         reads=[("ldb",), "identb"], writes=["PX"])
                    s.op("vector", lambda e2, e4=e4, half=half, dgb=dgb, q=q, slot=slot:
                         e2.tensor_copy(
                         sb["b2s"][e4 * 32:e4 * 32 + 16,
                                   half * 1024 + dgb * 512 + q * 128:
                                   half * 1024 + dgb * 512 + (q + 1) * 128],
                         PXb[0:16, slot * 128:slot * 128 + 128]),
                         reads=["PX"], writes=["b2s"])

    if stop_stage <= 1:
        s.dma("sync", t["dbg_wts"][:, 0:4], sb["gateT"][:, 0:8].bitcast(F32),
              reads=["gateT"], writes=["dbg"], force_inc=True)
        s.final_wait()
        return
    for b in range(NB):
        for dc in range(DCH):
            s.dma("sync", x1f[:, dc * TT:(dc + 1) * TT], t["x1"][b, dc * 128:(dc + 1) * 128, :],
                  reads=[("x1f", dc)], writes=[("x1f", dc)])
            s.op("vector", lambda e, dc=dc: e.tensor_copy(
                 sb["rb"][:], x1f[:, dc * TT:(dc + 1) * TT]),
                 reads=[("x1f", dc), "rb"], writes=["rb"])
            s.op("vector", lambda e: e.tensor_mul(sb["sqb"][:], sb["rb"][:], sb["rb"][:]),
                 reads=["rb"], writes=["sqb"])
            s.op("tensor", lambda e, dc=dc: e.matmul(
                 ap(PD, 2 * 512, [[2048, 1], [1, TT]]), sb["onescol"][:], sb["sqb"][:],
                 start=(dc == 0), stop=(dc == DCH - 1)),
                 reads=["sqb", "onescol"], writes=[("PD", 2)])
        s.op("vector", lambda e: e.tensor_scalar(xio[0:1, 0:TT], PDv(2)[0:1, 0:TT],
                                                 1.0 / D, 1e-6, op0=ALU.mult, op1=ALU.add),
             reads=[("PD", 2)], writes=[("xio", 0)])
        s.op("scalar", lambda e: e.activation(xio[0:1, TT:2 * TT], xio[0:1, 0:TT], AF.Sqrt),
             reads=[("xio", 0)], writes=[("xio", 1)])
        s.op("vector", lambda e: e.reciprocal(xio[0:1, 0:TT], xio[0:1, TT:2 * TT]),
             reads=[("xio", 1)], writes=[("xio", 0)])
        s.op("tensor", lambda e: e.matmul(PA[:, 0:TT], sb["onesrow"][:], xio[0:1, 0:TT],
                                          start=True, stop=True),
             reads=[("xio", 0), "onesrow"], writes=["PA"])
        for dc in range(DCH):
            s.op("vector", lambda e, dc=dc: e.scalar_tensor_tensor(
                 nxb[:, dc * TT:(dc + 1) * TT], x1f[:, dc * TT:(dc + 1) * TT],
                 sb["normw"][:, dc:dc + 1], PA[:, 0:TT], op0=ALU.mult, op1=ALU.mult),
                 reads=[("x1f", dc), "PA", ("nxb", dc)], sreads=["normw_s"],
                 writes=[("nxb", dc)])
        if stop_stage <= 2:
            continue
        for dc in range(DCH):
            s.op("tensor", lambda e, dc=dc: e.matmul(
                 PB[0:8, 0:TT], sb["gateT"][:, dc * 8:(dc + 1) * 8], nxb[:, dc * TT:(dc + 1) * TT],
                 start=(dc == 0), stop=(dc == DCH - 1)),
                 reads=[("nxb", dc), "gateT"], writes=["PB"])
        s.op("vector", lambda e: e.tensor_copy(sb["wEm"][0:8, 0:TT], PB[0:8, 0:TT]),
             reads=["PB"], writes=["wEm"])
        for tc in range(NTC):
            s.op("tensor", lambda e, tc=tc: e.transpose(
                 PV[:, tc * 8:(tc + 1) * 8], sb["wEm"][0:8, tc * 128:(tc + 1) * 128],
                 sb["identf"][0:8, 0:8]),
                 reads=["wEm", "identf"], writes=["PV"])
        s.op("vector", lambda e: e.tensor_copy(sb["gs1"][:, 0:NTC * 8], PV[:, 0:NTC * 8]),
             reads=["PV"], writes=["gs1"])
        s.op("vector", lambda e: e.tensor_reduce(sb["gm"][:, 0:NTC].unsqueeze(2),
             g3(sb["gs1"][:, 0:NTC * 8]), axis=AX.X, op=ALU.max, negate=True),
             reads=["gs1"], writes=["gm"])
        s.op("vector", lambda e: e.tensor_tensor(sb["gs2"][:, 0:NTC * 8], sb["gs1"][:, 0:NTC * 8],
             ap(sb["gm"], 0, [[4, 128], [1, NTC], [0, 8]]), op=ALU.add),
             reads=["gs1", "gm"], writes=["gs2"])
        s.op("scalar", lambda e: e.activation(sb["gs1"][:, 0:NTC * 8], sb["gs2"][:, 0:NTC * 8],
                                              AF.Exp),
             reads=["gs2"], writes=["gs1"])
        s.op("vector", lambda e: e.tensor_reduce(sb["gm"][:, 0:NTC].unsqueeze(2),
             g3(sb["gs1"][:, 0:NTC * 8]), axis=AX.X, op=ALU.add),
             reads=["gs1"], writes=["gm"])
        s.op("vector", lambda e: e.reciprocal(sb["gmr"][:, 0:NTC], sb["gm"][:, 0:NTC]),
             reads=["gm"], writes=["gmr"])
        s.op("vector", lambda e: e.tensor_tensor(sb["gs2"][:, 0:NTC * 8], sb["gs1"][:, 0:NTC * 8],
             ap(sb["gmr"], 0, [[4, 128], [1, NTC], [0, 8]]), op=ALU.mult),
             reads=["gs1", "gmr"], writes=["gs2"])
        s.op("vector", lambda e: e.tensor_reduce(sb["gm"][:, 0:NTC].unsqueeze(2),
             g3(sb["gs2"][:, 0:NTC * 8]), axis=AX.X, op=ALU.max),
             reads=["gs2"], writes=["gm"])
        s.op("vector", lambda e: e.tensor_tensor(sb["gs1"][:, 0:NTC * 8], sb["gs2"][:, 0:NTC * 8],
             ap(sb["gm"], 0, [[4, 128], [1, NTC], [0, 8]]), op=ALU.is_equal),
             reads=["gs2", "gm"], writes=["gs1"])
        s.op("gpsimd", lambda e: e.tensor_tensor(sb["wtsE"][:, 0:NTC * 8], sb["gs1"][:, 0:NTC * 8],
             sb["idxw"][:, 0:NTC * 8], op=ALU.mult),
             reads=["gs1", "idxw"], writes=["wtsE"])
        s.op("vector", lambda e: e.tensor_reduce(sb["gm2"][:, 0:NTC].unsqueeze(2),
             g3(sb["wtsE"][:, 0:NTC * 8]), axis=AX.X, op=ALU.max),
             reads=["wtsE"], writes=["gm2"])
        s.op("vector", lambda e: e.tensor_tensor(sb["gs1"][:, 0:NTC * 8], sb["wtsE"][:, 0:NTC * 8],
             ap(sb["gm2"], 0, [[4, 128], [1, NTC], [0, 8]]), op=ALU.is_equal),
             reads=["wtsE", "gm2"], writes=["gs1"])
        s.op("vector", lambda e: e.scalar_tensor_tensor(sb["wtsE"][:, 0:NTC * 8],
             sb["gs1"][:, 0:NTC * 8], -1e30, sb["gs2"][:, 0:NTC * 8],
             op0=ALU.mult, op1=ALU.add),
             reads=["gs1", "gs2"], writes=["wtsE"])
        s.op("vector", lambda e: e.tensor_reduce(sb["gm2"][:, 0:NTC].unsqueeze(2),
             g3(sb["wtsE"][:, 0:NTC * 8]), axis=AX.X, op=ALU.max),
             reads=["wtsE"], writes=["gm2"])
        s.op("vector", lambda e: e.tensor_tensor(sb["wtsE"][:, 0:NTC * 8], sb["wtsE"][:, 0:NTC * 8],
             ap(sb["gm2"], 0, [[4, 128], [1, NTC], [0, 8]]), op=ALU.is_equal),
             reads=["wtsE", "gm2"], writes=["wtsE"])
        s.op("gpsimd", lambda e: e.tensor_tensor(sb["gs2"][:, 0:NTC * 8], sb["wtsE"][:, 0:NTC * 8],
             sb["idxw"][:, 0:NTC * 8], op=ALU.mult),
             reads=["wtsE", "idxw"], writes=["gs2"])
        s.op("vector", lambda e: e.tensor_reduce(sb["gmr"][:, 0:NTC].unsqueeze(2),
             g3(sb["gs2"][:, 0:NTC * 8]), axis=AX.X, op=ALU.max),
             reads=["gs2"], writes=["gmr"])
        s.op("vector", lambda e: e.tensor_tensor(sb["wtsE"][:, 0:NTC * 8], sb["gs2"][:, 0:NTC * 8],
             ap(sb["gmr"], 0, [[4, 128], [1, NTC], [0, 8]]), op=ALU.is_equal),
             reads=["gs2", "gmr"], writes=["wtsE"])
        s.op("gpsimd", lambda e: e.tensor_tensor(sb["gs1"][:, 0:NTC * 8], sb["gs1"][:, 0:NTC * 8],
             ap(sb["gm"], 0, [[4, 128], [1, NTC], [0, 8]]), op=ALU.mult),
             reads=["gs1", "gm"], writes=["gs1"])
        s.op("gpsimd", lambda e: e.tensor_tensor(sb["wtsE"][:, 0:NTC * 8], sb["wtsE"][:, 0:NTC * 8],
             ap(sb["gm2"], 0, [[4, 128], [1, NTC], [0, 8]]), op=ALU.mult),
             reads=["wtsE", "gm2"], writes=["wtsE"])
        s.op("gpsimd", lambda e: e.tensor_tensor(sb["wtsE"][:, 0:NTC * 8], sb["wtsE"][:, 0:NTC * 8],
             sb["gs1"][:, 0:NTC * 8], op=ALU.add),
             reads=["wtsE", "gs1"], writes=["wtsE"])
        s.op("vector", lambda e: e.tensor_tensor(sb["gm"][:, 0:NTC], sb["gm"][:, 0:NTC],
             sb["gm2"][:, 0:NTC], op=ALU.add),
             reads=["gm", "gm2"], writes=["gm"])
        s.op("vector", lambda e: e.reciprocal(sb["gmr"][:, 0:NTC], sb["gm"][:, 0:NTC]),
             reads=["gm"], writes=["gmr"])
        s.op("vector", lambda e: e.tensor_tensor(sb["wtsE"][:, 0:NTC * 8], sb["wtsE"][:, 0:NTC * 8],
             ap(sb["gmr"], 0, [[4, 128], [1, NTC], [0, 8]]), op=ALU.mult),
             reads=["wtsE", "gmr"], writes=["wtsE"])
        for tc in range(NTC):
            s.op("tensor", lambda e, tc=tc: e.transpose(
                 PX[0:8, tc * 128:(tc + 1) * 128],
                 sb["wtsE"][:, tc * 8:(tc + 1) * 8], sb["identf"][:]),
                 reads=["wtsE", "identf"], writes=["PX"])
        s.op("vector", lambda e: e.tensor_copy(sb["wEmb"][0:8, 0:TT], PX[0:8, 0:TT]),
             reads=["PX"], writes=["wEmb"])
        for e_ in range(E):
            s.op("tensor", lambda e, e_=e_: e.matmul(
                 PA[:, 0:TT], sb["selc"][:, e_ * 128:(e_ + 1) * 128], sb["wEmb"][0:8, 0:TT],
                 start=True, stop=True),
                 reads=["wEmb", "selc"], writes=["PA"])
            s.op("vector", lambda e, e_=e_: e.tensor_copy(
                 wrep[:, e_ * TT:(e_ + 1) * TT], PA[:, 0:TT]),
                 reads=["PA"], writes=[("wrep", e_)])
        if dbg and b == 0:
            s.dma("sync", t["dbg_nxb"], sb["nxb"][:], reads=[("nxb", dc) for dc in range(DCH)], writes=["dbg"])
            s.dma("sync", t["dbg_wts"][:, 0:NTC * 8], sb["wtsE"][:, 0:NTC * 8], reads=["wtsE"], writes=["dbg"])
            s.dma("sync", t["dbg_wrep"], sb["wrep"][:], reads=[("wrep", e_) for e_ in range(E)], writes=["dbg"])
        if stop_stage <= 3:
            continue
        for w_, (wT, bt) in enumerate(((t["w1T"], base1), (t["w3T"], base3))):
            for fc in range(FCH):
                slot = (fc * 2 + w_) % 2
                s.dma("sync", sb["wst"][:, slot * 1024:(slot + 1) * 1024],
                      ap(wT, fc * 128, [[F_, 128], [128 * F_, 8], [1, 128]]),
                      reads=[("wst", slot)], writes=[("wst", slot)])
                for dc in range(DCH):
                    s.op("tensor", lambda e, dc=dc, slot=slot: e.matmul(
                         PB[:, 0:TT], sb["wst"][:, slot * 1024 + dc * 128: slot * 1024 + (dc + 1) * 128],
                         nxb[:, dc * TT:(dc + 1) * TT],
                         start=(dc == 0), stop=(dc == DCH - 1)),
                         reads=[("wst", slot), ("nxb", dc)], writes=["PB"])
                s.op("vector", lambda e, fc=fc, bt=bt: e.tensor_copy(
                     bt[:, fc * TT:(fc + 1) * TT], PB[:, 0:TT]),
                     reads=["PB"], writes=[("base", w_, fc)])
        for half in range(2):
            for (aname, ename) in (("a1T", "etr1P"), ("a3T", "etr3P")):
                for dc in range(DCH):
                    s.op("tensor", lambda e, dc=dc, half=half, aname=aname: e.matmul(
                         PB[:, 0:TT],
                         sb[aname][:, half * 1024 + dc * 128: half * 1024 + (dc + 1) * 128],
                         nxb[:, dc * TT:(dc + 1) * TT],
                         start=(dc == 0), stop=(dc == DCH - 1)),
                         reads=[aname, ("nxb", dc)], writes=["PB"])
                s.op("vector", lambda e, half=half, ename=ename: e.tensor_scalar_mul(
                     sb[ename][:, half * TT:(half + 1) * TT], PB[:, 0:TT], 2.0),
                     reads=["PB"], writes=[(ename, half)])
        if dbg and b == 0:
            s.dma("sync", t["dbg_base1"], sb["base1"][:], reads=[("base", 0, fc) for fc in range(FCH)], writes=["dbg"])
            s.dma("sync", t["dbg_etr1"], sb["etr1P"][:], reads=[("etr1P", 0), ("etr1P", 1)], writes=["dbg"])
        for bank, bk in ((PV, "PV"), (PX, "PX")):
            s.op("tensor", lambda e, bank=bank: e.matmul(
                 bank[:, 0:TT], sb["zzb"][0:1, 0:128], sb["wEmb"][0:1, 0:TT],
                 start=True, stop=False, skip_group_check=True),
                 reads=["zzb", "wEmb"], writes=[bk])
        if stop_stage <= 4:
            continue
        for fcg in range(8):
            for w_, bT in ((0, t["b1T"]), (1, t["b3T"])):
                for e_ in range(E):
                    half, e4 = e_ // 4, e_ % 4
                    s.dma("scalar",
                          ap(sb["bst"], (e4 * 32) * 2048 + w_ * 1024 + half * 512,
                             [[2048, 16], [1, 512]]),
                          bT[e_, :, fcg * 512:(fcg + 1) * 512],
                          reads=[("bst", w_, half, e4)], writes=[("bst", w_, half, e4)])
            for fci in range(4):
                fc = fcg * 4 + fci
                for e_ in range(E):
                    half, e4 = e_ // 4, e_ % 4
                    lhs1 = ap(sb["bst"], (e4 * 32) * 2048 + 0 * 1024 + half * 512 + fci * 128,
                              [[2048, 32], [1, 128]])
                    lhs3 = ap(sb["bst"], (e4 * 32) * 2048 + 1 * 1024 + half * 512 + fci * 128,
                              [[2048, 32], [1, 128]])
                    rhs1 = ap(sb["etr1P"], (e4 * 32) * (2 * TT) + half * TT, [[2 * TT, 32], [1, TT]])
                    rhs3 = ap(sb["etr3P"], (e4 * 32) * (2 * TT) + half * TT, [[2 * TT, 32], [1, TT]])
                    tp = (32 * e4, 0) if e4 > 0 else None
                    s.op("tensor", lambda e, lhs1=lhs1, rhs1=rhs1, tp=tp: e.matmul(
                         PA[:, 0:TT], lhs1, rhs1, start=True, stop=False,
                         tile_position=tp, skip_group_check=True),
                         reads=[("bst", 0, e_ // 4, e4), ("etr1P", e_ // 4)], writes=["PA"])
                    s.op("tensor", lambda e, fc=fc: e.matmul(
                         PA[:, 0:TT], sb["identb"][:], base1[:, fc * TT:(fc + 1) * TT],
                         start=False, stop=True, skip_group_check=True),
                         reads=["identb", ("base", 0, fc)], writes=["PA"])
                    s.op("scalar", lambda e: e.activation(sb["silu"][:], PA[:, 0:TT], AF.Silu),
                         reads=["PA"], writes=["rb"])
                    s.op("tensor", lambda e, lhs3=lhs3, rhs3=rhs3, tp=tp: e.matmul(
                         PB[:, 0:TT], lhs3, rhs3, start=True, stop=False,
                         tile_position=tp, skip_group_check=True),
                         reads=[("bst", 1, e_ // 4, e4), ("etr3P", e_ // 4)], writes=["PB"])
                    s.op("tensor", lambda e, fc=fc: e.matmul(
                         PB[:, 0:TT], sb["identb"][:], base3[:, fc * TT:(fc + 1) * TT],
                         start=False, stop=True, skip_group_check=True),
                         reads=["identb", ("base", 1, fc)], writes=["PB"])
                    s.op("vector", lambda e, e_=e_: e.tensor_tensor(
                         sb["h3w"][:], PB[:, 0:TT], wrep[:, e_ * TT:(e_ + 1) * TT], op=ALU.mult),
                         reads=["PB", ("wrep", e_), "sqb"], writes=["sqb"])
                    s.op("gpsimd", lambda e, fci=fci, e_=e_: e.tensor_tensor(
                         actg[:, (fci * 8 + e_) * TT:(fci * 8 + e_ + 1) * TT],
                         sb["h3w"][:], sb["silu"][:], op=ALU.mult),
                         reads=["sqb", "rb"], writes=[("actg", fci, e_)])
                    bank = PV if half == 0 else PX
                    bk = "PV" if half == 0 else "PX"
                    s.op("tensor", lambda e, fc=fc, e_=e_, fci=fci, bank=bank, e4=e4: e.matmul(
                         ap(bank, (e4 * 32) * 512, [[512, 16], [1, TT]]),
                         sb["a2T"][:, fc * 128 + (e_ % 8) * 16: fc * 128 + (e_ % 8 + 1) * 16],
                         actg[:, (fci * 8 + e_) * TT:(fci * 8 + e_ + 1) * TT],
                         start=False, stop=(fc == FCH - 1 and e4 == 3), skip_group_check=True,
                         tile_position=((0, 32 * e4) if e4 > 0 else None)),
                         reads=["a2T", ("actg", fci, e_)], writes=[bk])
            for dcg in range(2):
                for dc4 in range(4):
                    dc = dcg * 4 + dc4
                    for fci in range(4):
                        fc = fcg * 4 + fci
                        slot = fci
                        if dcg == 0 and dc4 == 0:
                            s.dma("sync", sb["w2st"][:, slot * 1024:(slot + 1) * 1024],
                                  ap(t["w2T"], fc * 128 * D, [[D, 128], [1, 1024]]),
                                  reads=[("w2st", slot)], writes=[("w2st", slot)])
                        for e_ in range(E):
                            s.op("tensor", lambda e, dc=dc, fci=fci, e_=e_, slot=slot, dc4=dc4:
                                 e.matmul(
                                 PDv(dc4)[:, 0:TT] if TT == 512 else PDv(dc4)[:, 0:TT],
                                 sb["w2st"][:, slot * 1024 + dc * 128: slot * 1024 + (dc + 1) * 128],
                                 actg[:, (fci * 8 + e_) * TT:(fci * 8 + e_ + 1) * TT],
                                 start=(fci == 0 and e_ == 0), stop=(fci == 3 and e_ == E - 1),
                                 skip_group_check=True),
                                 reads=[("w2st", slot), ("actg", fci, e_)], writes=[("PD", dc4)])
                    if fcg == 0:
                        s.op("vector", lambda e, dc=dc, dc4=dc4: e.tensor_copy(
                             outacc[:, dc * TT:(dc + 1) * TT], PDv(dc4)[:, 0:TT]),
                             reads=[("PD", dc4)], writes=[("outacc", dc)])
                    else:
                        s.op("vector", lambda e, dc=dc, dc4=dc4: e.tensor_add(
                             outacc[:, dc * TT:(dc + 1) * TT], PDv(dc4)[:, 0:TT],
                             outacc[:, dc * TT:(dc + 1) * TT]),
                             reads=[("PD", dc4), ("outacc", dc)], writes=[("outacc", dc)])
        if dbg and b == 0:
            s.dma("sync", t["dbg_act"], sb["actg"][:],
                  reads=[("actg", fci, e_) for fci in range(4) for e_ in range(E)], writes=["dbg"])
        if stop_stage <= 5:
            continue
        for half, (bank, bk) in enumerate(((PV, "PV"), (PX, "PX"))):
            s.op("vector", lambda e, half=half, bank=bank: e.tensor_scalar_mul(
                 sb["etr2P"][:, half * TT:(half + 1) * TT], bank[:, 0:TT], 2.0),
                 reads=[bk], writes=[("etr2P", half)])
        if stop_stage <= 7:
            continue
        for dc in range(DCH):
            for half in range(2):
                s.op("tensor", lambda e, dc=dc, half=half: e.matmul(
                     PA[:, 0:TT],
                     sb["b2s"][:, half * 1024 + dc * 128: half * 1024 + (dc + 1) * 128],
                     ap(sb["etr2P"], half * TT, [[2 * TT, 128], [1, TT]]),
                     start=(half == 0), stop=(half == 1), skip_group_check=True),
                     reads=["b2s", ("etr2P", half)], writes=["PA"])
            s.op("vector", lambda e, dc=dc: e.tensor_add(
                 outacc[:, dc * TT:(dc + 1) * TT], PA[:, 0:TT], outacc[:, dc * TT:(dc + 1) * TT]),
                 reads=["PA", ("outacc", dc)], writes=[("outacc", dc)])
        if dbg and b == 0:
            s.dma("sync", t["dbg_out"], sb["outacc"][:],
                  reads=[("outacc", dc) for dc in range(DCH)], writes=["dbg"])
        if stop_stage <= 6:
            continue
        for dc in range(DCH):
            s.op("vector", lambda e, dc=dc: e.tensor_add(
                 outacc[:, dc * TT:(dc + 1) * TT], outacc[:, dc * TT:(dc + 1) * TT],
                 x1f[:, dc * TT:(dc + 1) * TT]),
                 reads=[("outacc", dc), ("x1f", dc)], writes=[("outacc", dc)])
            s.op("vector", lambda e, dc=dc: e.tensor_copy(
                 sb["rb"][:], outacc[:, dc * TT:(dc + 1) * TT]),
                 reads=[("outacc", dc), "rb"], writes=["rb"])
            s.op("vector", lambda e: e.tensor_mul(sb["sqb"][:], sb["rb"][:], sb["rb"][:]),
                 reads=["rb"], writes=["sqb"])
            s.op("tensor", lambda e, dc=dc: e.matmul(
                 ap(PD, 2 * 512, [[2048, 1], [1, TT]]), sb["onescol"][:], sb["rb"][:],
                 start=(dc == 0), stop=(dc == DCH - 1)),
                 reads=["rb", "onescol"], writes=[("PD", 2)])
            s.op("tensor", lambda e, dc=dc: e.matmul(
                 ap(PD, 3 * 512, [[2048, 1], [1, TT]]), sb["onescol"][:], sb["sqb"][:],
                 start=(dc == 0), stop=(dc == DCH - 1)),
                 reads=["sqb", "onescol"], writes=[("PD", 3)])
        if stop_stage <= 8:
            continue
        s.op("vector", lambda e: e.tensor_scalar_mul(xio[0:1, 0:TT], PDv(2)[0:1, 0:TT], 1.0 / D),
             reads=[("PD", 2)], writes=[("xio", 0)])
        s.op("vector", lambda e: e.tensor_scalar_mul(xio[0:1, TT:2 * TT], PDv(3)[0:1, 0:TT], 1.0 / D),
             reads=[("PD", 3)], writes=[("xio", 1)])
        s.op("vector", lambda e: e.tensor_mul(sb["wEm"][0:1, 0:TT],
             xio[0:1, 0:TT], xio[0:1, 0:TT]),
             reads=[("xio", 0)], writes=["wEm"])
        s.op("vector", lambda e: e.scalar_tensor_tensor(
             xio[0:1, TT:2 * TT], xio[0:1, TT:2 * TT], 1e-5, sb["wEm"][0:1, 0:TT],
             op0=ALU.add, op1=ALU.subtract),
             reads=[("xio", 1), "wEm"], writes=[("xio", 1)])
        s.op("scalar", lambda e: e.activation(sb["wEm"][0:1, 0:TT], xio[0:1, TT:2 * TT], AF.Sqrt),
             reads=[("xio", 1)], writes=["wEm"])
        s.op("vector", lambda e: e.reciprocal(xio[0:1, TT:2 * TT], sb["wEm"][0:1, 0:TT]),
             reads=["wEm"], writes=[("xio", 1)])
        s.op("tensor", lambda e: e.matmul(PA[:, 0:TT], sb["onesrow"][:], xio[0:1, 0:TT],
                                          start=True, stop=True),
             reads=[("xio", 0), "onesrow"], writes=["PA"])
        s.op("tensor", lambda e: e.matmul(PB[:, 0:TT], sb["onesrow"][:], xio[0:1, TT:2 * TT],
                                          start=True, stop=True),
             reads=[("xio", 1), "onesrow"], writes=["PB"])
        if stop_stage <= 9:
            continue
        for dc in range(DCH):
            s.op("vector", lambda e, dc=dc: e.tensor_sub(
                 sb["x2s"][:, 0:TT], outacc[:, dc * TT:(dc + 1) * TT], PA[:, 0:TT]),
                 reads=[("outacc", dc), "PA", ("ldx", 0)], writes=[("ldx", 0)])
            s.op("vector", lambda e: e.tensor_mul(
                 sb["x2s"][:, TT:2 * TT], sb["x2s"][:, 0:TT], PB[:, 0:TT]),
                 reads=[("ldx", 0), "PB", ("ldx", 1)], writes=[("ldx", 1)])
            if stop_stage > 10 and (stop_stage > 11 or dc == 0):
                s.dma("sync", t["x2"][b, dc * 128:(dc + 1) * 128, :], sb["x2s"][:, TT:2 * TT],
                      reads=[("ldx", 1)], writes=[("x2", b, dc)], force_inc=True)
    if stop_stage < 99:
        s.dma("sync", t["dbg_wts"][:, 0:8], sb["gs1"][:, 0:8],
              reads=["gs1"], writes=["dbg2"], force_inc=True)
    s.final_wait()


D, NH, HD, SL = 1024, 16, 64, 8
DCH = 8


def declare_dram_c(nc, TL, suffix="", dbg=False):
    t = {}
    ein = lambda n, s: nc.dram_tensor(n + suffix, s, F32, kind="ExternalInput").ap()
    t["x2"] = ein("x2", [SL, D, TL])
    t["Wy"] = ein("Wy", [1, NH, HD, HD])
    t["Wq"] = ein("Wq", [1, NH, HD, HD])
    t["Wk"] = ein("Wk", [1, NH, HD, HD])
    t["wb"] = ein("wb", [1, NH, HD, 4])
    t["outw"] = ein("outw", [D, D])
    t["lng"] = ein("lng", [D])
    t["lnb"] = ein("lnb", [D])
    t["x3"] = nc.dram_tensor("x3" + suffix, [SL, D, TL], F32, kind="ExternalOutput").ap()
    t["outwT"] = nc.dram_tensor("outwT" + suffix, [D, D], BF16, kind="Internal").ap()
    if dbg:
        t["dbg_bxs"] = nc.dram_tensor("dbg_bxs" + suffix, [128, SL * 256], F32, kind="ExternalOutput").ap()
    if dbg:
        t["dbg_ys"] = nc.dram_tensor("dbg_ys" + suffix, [128, SL * DCH * 128], BF16, kind="ExternalOutput").ap()
    if dbg:
        t["dbg_attn"] = nc.dram_tensor("dbg_attn" + suffix, [128, SL * DCH * 128], F32, kind="ExternalOutput").ap()
    if dbg:
        t["dbg_sm"] = nc.dram_tensor("dbg_sm" + suffix, [128, 192], F32, kind="ExternalOutput").ap()
    if dbg:
        t["dbg_rbuf"] = nc.dram_tensor("dbg_rbuf" + suffix, [128, DCH * 128], F32, kind="ExternalOutput").ap()
    if dbg:
        t["dbg_ln"] = nc.dram_tensor("dbg_ln" + suffix, [128, 128], F32, kind="ExternalOutput").ap()
    return t


def alloc_bufs_c(ctx, nc, TL):
    sb, ps = {}, {}
    s_ = lambda n, shape, dt: ctx.enter_context(nc.sbuf_tensor(n, shape, dt))
    p_ = lambda n, shape, dt: ctx.enter_context(nc.psum_tensor(n, shape, dt))
    sb["x2f"] = s_("x2f", [128, SL * DCH * 128], F32)
    sb["dTf"] = s_("dTf", [128, 128], F32)
    sb["ysfm"] = s_("ysfm", [128, SL * DCH * 128], BF16)
    sb["Xp"] = s_("Xp", [128, SL * HD], F32)
    sb["Kp"] = s_("Kp", [128, SL * HD], F32)
    sb["Uall"] = s_("Uall", [128, 256 * SL], F32)
    sb["bxs"] = s_("bxs", [128, SL * 256], F32)
    sb["bxd"] = s_("bxd", [128, 256], F32)
    sb["ya"] = s_("ya", [128, 256], F32)
    sb["ua"] = s_("ua", [128, 256], F32)
    sb["sm"] = s_("sm", [128, 192], F32)
    sb["beta"] = s_("beta", [128, 4], F32)
    sb["dbuf"] = s_("dbuf", [128, 64], F32)
    sb["ybf"] = s_("ybf", [128, 64], BF16)
    sb["nmx"] = s_("nmx", [128, 3], F32)
    sb["smr"] = s_("smr", [128, 3], F32)
    sb["tmp1"] = s_("tmp1", [128, SL * HD], F32)
    sb["tmp2"] = s_("tmp2", [128, 256 * SL], F32)
    sb["prj1"] = s_("prj1", [128, 128], F32)
    sb["prj2"] = s_("prj2", [128, 128], F32)
    sb["Wcat1"] = s_("Wcat1", [128, NH * 128], F32)
    sb["Wcat2"] = s_("Wcat2", [128, NH * 68], F32)
    sb["owst"] = s_("owst", [128, 2 * 1024], BF16)
    sb["identb"] = s_("identb", [128, 128], BF16)
    sb["identf"] = s_("identf", [128, 128], F32)
    sb["rbuf"] = s_("rbuf", [128, DCH * 128], F32)
    sb["lngc"] = s_("lngc", [128, 8], F32)
    sb["lnbc"] = s_("lnbc", [128, 8], F32)
    sb["rb"] = s_("rb", [128, 128], BF16)
    sb["sqb"] = s_("sqb", [128, 128], BF16)
    sb["stats"] = s_("stats", [1, 512], F32)
    sb["onescol"] = s_("onescol", [128, 1], BF16)
    sb["onesrow"] = s_("onesrow", [1, 128], F32)
    sb["ldf"] = s_("ldf", [128, 1024], F32)
    sb["ldb"] = s_("ldb", [128, 1024], BF16)
    sb["stg2"] = s_("stg2", [128, 256], BF16)
    sb["x3s"] = s_("x3s", [128, 2 * 128], F32)
    sb["attnb"] = s_("attnb", [128, SL * DCH * 128], F32)
    sb["gx"] = s_("gx", [128, 8], F32)
    ps["PA"] = p_("PA", [128, 512], F32)
    ps["PB"] = p_("PB", [128, 512], F32)
    ps["PD"] = p_("PD", [128, 2048], F32)
    ps["PV"] = p_("PV", [128, 512], F32)
    ps["PX"] = p_("PX", [128, 512], F32)
    return sb, ps


def emit_c(s: Sched, t, sb, ps, TL, dbg=False):
    nc = s.nc
    PA, PB, PD, PV, PX = ps["PA"], ps["PB"], ps["PD"], ps["PV"], ps["PX"]
    PXb = PX[:].bitcast(BF16)
    NTC = TL // 128
    PDv = lambda k, n=512: PD[:, k * 512:k * 512 + n]
    x2f, ysfm = sb["x2f"], sb["ysfm"]
    Xp, Kp, Uall, bxs = sb["Xp"], sb["Kp"], sb["Uall"], sb["bxs"]

    s.op("gpsimd", lambda e: e.memset(sb["identb"][:], 0.0), writes=["identb"])
    s.op("gpsimd", lambda e: e.affine_select(
        out=sb["identb"][:], in_=sb["identb"][:], compare_op=ALU.not_equal,
        fill=1.0, base=0, pattern=[[-1, 128]], channel_multiplier=1),
        reads=["identb"], writes=["identb"])
    s.op("gpsimd", lambda e: e.memset(sb["identf"][:], 0.0), writes=["identf"])
    s.op("gpsimd", lambda e: e.affine_select(
        out=sb["identf"][:], in_=sb["identf"][:], compare_op=ALU.not_equal,
        fill=1.0, base=0, pattern=[[-1, 128]], channel_multiplier=1),
        reads=["identf"], writes=["identf"])
    s.op("vector", lambda e: e.memset(sb["onescol"][:], 1.0), writes=["onescol"])
    s.op("vector", lambda e: e.memset(sb["Wcat1"][:], 0.0), writes=["Wcat1"])
    s.op("vector", lambda e: e.memset(sb["Wcat2"][:], 0.0), writes=["Wcat2"])
    s.op("vector", lambda e: e.memset(sb["onesrow"][:], 1.0), writes=["onesrow"])
    s.op("vector", lambda e: e.memset(PD[:], 0.0), writes=[("PD", k) for k in range(4)])
    for dc in range(DCH):
        s.dma("scalar", sb["lngc"][:, dc:dc + 1], t["lng"][dc * 128:(dc + 1) * 128].unsqueeze(1),
              reads=["lngc"], writes=["lngc"])
        s.dma("scalar", sb["lnbc"][:, dc:dc + 1], t["lnb"][dc * 128:(dc + 1) * 128].unsqueeze(1),
              reads=["lnbc"], writes=["lnbc"])
    for rc in range(8):
        s.dma("sync", sb["ldf"][:, 0:1024], t["outw"][rc * 128:(rc + 1) * 128, :],
              reads=["ldf"], writes=["ldf"])
        s.op("vector", lambda e: e.tensor_copy(sb["ldb"][:, 0:1024], sb["ldf"][:, 0:1024]),
             reads=["ldf", "ldb"], writes=["ldb"])
        for q in range(8):
            slot = q % 2
            s.op("tensor", lambda e, q=q, slot=slot: e.transpose(
                 PXb[:, slot * 128:(slot + 1) * 128],
                 sb["ldb"][:, q * 128:(q + 1) * 128], sb["identb"][:]),
                 reads=["ldb", "identb"], writes=["PX"])
            s.op("vector", lambda e, slot=slot: e.tensor_copy(
                 sb["stg2"][:, slot * 128:(slot + 1) * 128], PXb[:, slot * 128:(slot + 1) * 128]),
                 reads=["PX"], writes=[("stg2", slot)])
            s.dma("sync", t["outwT"][q * 128:(q + 1) * 128, rc * 128:(rc + 1) * 128],
                  sb["stg2"][:, slot * 128:(slot + 1) * 128],
                  reads=[("stg2", slot)], writes=["outwT_d"])
    for (wsrc, part) in ((t["Wy"], 0), (t["Wq"], 1), (t["Wk"], 2)):
        for ch in range(8):
            src = ap(wsrc, ch * 128 * 64, [[64, 128], [1, 64]])
            s.dma("sync", sb["ldf"][:, 0:64], src, reads=["ldf"], writes=["ldf"])
            for hsub in range(2):
                h = ch * 2 + hsub
                s.op("tensor", lambda e, hsub=hsub: e.transpose(
                     PX[0:64, 0:64], sb["ldf"][hsub * 64:(hsub + 1) * 64, 0:64],
                     sb["identf"][hsub * 64:(hsub + 1) * 64, hsub * 64:(hsub + 1) * 64]),
                     reads=["ldf", "identf"], writes=["PX"])
                for cpy in range(2):
                    if part < 2:
                        dst = sb["Wcat1"][cpy * 64:(cpy + 1) * 64,
                                          h * 128 + part * 64: h * 128 + (part + 1) * 64]
                        key = "Wcat1"
                    else:
                        dst = sb["Wcat2"][cpy * 64:(cpy + 1) * 64, h * 68: h * 68 + 64]
                        key = "Wcat2"
                    s.op("vector", lambda e, dst=dst: e.tensor_copy(dst, PX[0:64, 0:64]),
                         reads=["PX"], writes=[key])
    for ch in range(4):
        src = ap(t["wb"], ch * 2 * 64 * 4, [[4, 128], [1, 4]])
        s.dma("sync", sb["ldf"][:, 0:4], src, reads=["ldf"], writes=["ldf"])
        for hsub in range(2):
            h = ch * 2 + hsub
            for cpy in range(2):
                s.op("vector", lambda e, hsub=hsub, h=h, cpy=cpy: e.tensor_copy(
                     sb["Wcat2"][cpy * 64:(cpy + 1) * 64, h * 68 + 64: h * 68 + 68],
                     sb["ldf"][hsub * 64:(hsub + 1) * 64, 0:4]),
                     reads=["ldf"], writes=["Wcat2"])

    for tc in range(NTC):
        for b in range(SL):
            s.dma("sync", x2f[:, b * 1024:(b + 1) * 1024],
                  ap(t["x2"], b * D * TL + tc * 128, [[TL, 128], [128 * TL, 8], [1, 128]]),
                  reads=[("x2f", b)], writes=[("x2f", b)])
        for h in range(NH):
            hb = (h % 2) * 64
            dc = h // 2
            idsl = sb["identf"][hb:hb + 64, hb:hb + 64]
            s.op("vector", lambda e: e.memset(sb["bxs"][:], 0.0),
                 reads=["bxs"], writes=["bxs"])
            s.op("vector", lambda e: e.memset(sb["bxd"][:], 0.0),
                 reads=["bxd"], writes=["bxd"])
            for st in range(SL):
                xsl = lambda arr: arr[hb:hb + 64, (st * 8 + dc) * 128:(st * 8 + dc) * 128 + 128]
                s.op("tensor", lambda e, st=st: e.transpose(
                     PDv(0, 64)[:, 0:64], xsl(x2f), idsl),
                     reads=[("x2f", st), "identf"], writes=[("PD", 0)])
                s.op("vector", lambda e, st=st: e.tensor_copy(
                     Xp[:, st * HD:(st + 1) * HD], PDv(0, 64)[:, 0:64]),
                     reads=[("PD", 0)], writes=["Xp"])
                s.op("tensor", lambda e, st=st: e.matmul(
                     PA[:, 0:128], sb["Wcat1"][hb:hb + 64, h * 128:(h + 1) * 128],
                     xsl(x2f), start=True, stop=True),
                     reads=[("x2f", st), "Wcat1"], writes=["PA"])
                s.op("tensor", lambda e, st=st: e.matmul(
                     PB[0:68, 0:128], sb["Wcat2"][hb:hb + 64, h * 68:(h + 1) * 68],
                     xsl(x2f), start=True, stop=True),
                     reads=[("x2f", st), "Wcat2"], writes=["PB"])
                s.op("vector", lambda e: e.tensor_copy(sb["prj1"][:], PA[:, 0:128]),
                     reads=["PA"], writes=["prj1"])
                s.op("vector", lambda e: e.tensor_copy(sb["prj2"][0:68, :], PB[0:68, 0:128]),
                     reads=["PB"], writes=["prj2"])
                s.op("tensor", lambda e: e.transpose(PDv(0)[:, 0:128], sb["prj1"][:],
                                                     sb["identf"][:]),
                     reads=["prj1", "identf"], writes=[("PD", 0)])
                s.op("tensor", lambda e: e.transpose(PDv(1, 68)[:, 0:68], sb["prj2"][0:68, :],
                                                     sb["identf"][0:68, 0:68]),
                     reads=["prj2", "identf"], writes=[("PD", 1)])
                s.op("vector", lambda e, st=st: e.tensor_copy(
                     bxs[:, st * 256: st * 256 + 128], PDv(0)[:, 0:128]),
                     reads=[("PD", 0)], writes=["bxs"])
                s.op("vector", lambda e, st=st: e.tensor_copy(
                     bxs[:, st * 256 + 128: st * 256 + 196], PDv(1, 68)[:, 0:68]),
                     reads=[("PD", 1)], writes=["bxs"])
            if dbg and tc == 0 and h == 0:
                s.dma("sync", t["dbg_bxs"], bxs[:], reads=["bxs"], writes=["dbgk"])
            for st in range(SL):
                if st > 0:
                    s.op("gpsimd", lambda e, st=st: e.tensor_tensor(
                         sb["tmp1"][:, 0:st * HD], Kp[:, 0:st * HD],
                         ap(Xp, st * HD, [[SL * HD, 128], [0, st], [1, HD]]), op=ALU.mult),
                         reads=["Kp", "Xp", "tmp1"], writes=["tmp1"])
                    s.op("vector", lambda e, st=st: e.tensor_reduce(
                         ap(sb["gx"], 0, [[8, 128], [1, st], [0, 1]]),
                         sb["tmp1"][:, 0:st * HD].rearrange("p (s d) -> p s d", d=HD),
                         axis=AX.X, op=ALU.add),
                         reads=["tmp1"], writes=["gx"])
                    s.op("gpsimd", lambda e, st=st: e.tensor_tensor(
                         sb["tmp2"][:, 0:256 * st],
                         ap(Uall, 0, [[256 * SL, 128], [SL, 256], [1, st]]),
                         ap(sb["gx"], 0, [[8, 128], [0, 256], [1, st]]), op=ALU.mult),
                         reads=["Uall", "gx", "tmp2"], writes=["tmp2"])
                    s.op("vector", lambda e, st=st: e.tensor_reduce(
                         ap(sb["ua"], 0, [[256, 128], [1, 256], [0, 1]]),
                         sb["tmp2"][:, 0:256 * st].rearrange("p (o s) -> p o s", o=256),
                         axis=AX.X, op=ALU.add),
                         reads=["tmp2"], writes=["ua"])
                    s.op("vector", lambda e, st=st: e.tensor_add(
                         sb["ya"][:], sb["ua"][:, 0:256], bxs[:, st * 256:(st + 1) * 256]),
                         reads=["ua", "bxs", "ya"], writes=["ya"])
                else:
                    s.op("vector", lambda e: e.tensor_copy(sb["ya"][:], bxs[:, 0:256]),
                         reads=["bxs", "ya"], writes=["ya"])
                s.op("vector", lambda e: e.tensor_reduce(
                     sb["nmx"][:].unsqueeze(2),
                     sb["ya"][:, 0:192].rearrange("p (c d) -> p c d", d=HD),
                     axis=AX.X, op=ALU.max, negate=True),
                     reads=["ya"], writes=["nmx"])
                s.op("gpsimd", lambda e: e.tensor_tensor(
                     sb["sm"][:], sb["ya"][:, 0:192],
                     ap(sb["nmx"], 0, [[3, 128], [1, 3], [0, HD]]), op=ALU.add),
                     reads=["ya", "nmx", "sm"], writes=["sm"])
                s.op("scalar", lambda e: e.activation(sb["sm"][:], sb["sm"][:], AF.Exp),
                     reads=["sm"], writes=["sm"])
                s.op("vector", lambda e: e.tensor_reduce(
                     sb["smr"][:].unsqueeze(2),
                     sb["sm"][:].rearrange("p (c d) -> p c d", d=HD),
                     axis=AX.X, op=ALU.add),
                     reads=["sm"], writes=["smr"])
                s.op("vector", lambda e: e.reciprocal(sb["smr"][:], sb["smr"][:]),
                     reads=["smr"], writes=["smr"])
                s.op("gpsimd", lambda e: e.tensor_tensor(
                     sb["sm"][:], sb["sm"][:],
                     ap(sb["smr"], 0, [[3, 128], [1, 3], [0, HD]]), op=ALU.mult),
                     reads=["sm", "smr"], writes=["sm"])
                s.op("scalar", lambda e: e.activation(sb["beta"][:], sb["ya"][:, 192:196],
                                                      AF.Sigmoid),
                     reads=["ya"], writes=["beta"])
                if dbg and tc == 0 and h == 0 and st == 0:
                    s.dma("sync", t["dbg_sm"], sb["sm"][:], reads=["sm"], writes=["dbgk"])
                s.op("vector", lambda e, st=st: e.tensor_copy(
                     Kp[:, st * HD:(st + 1) * HD], sb["sm"][:, 128:192]),
                     reads=["sm", "Kp"], writes=["Kp"])
                s.op("vector", lambda e: e.tensor_sub(sb["dbuf"][:], sb["sm"][:, 64:128],
                                                      sb["sm"][:, 128:192]),
                     reads=["sm", "dbuf"], writes=["dbuf"])
                s.op("vector", lambda e: e.tensor_copy(sb["ybf"][:], sb["sm"][:, 0:64]),
                     reads=["sm", "ybf"], writes=["ybf"])
                s.op("tensor", lambda e: e.transpose(PXb[0:64, 0:128], sb["ybf"][:],
                                                     sb["identb"][:]),
                     reads=["ybf", "identb"], writes=["PX"])
                s.op("vector", lambda e, st=st: e.tensor_copy(
                     ysfm[hb:hb + 64, (st * 8 + dc) * 128:(st * 8 + dc) * 128 + 128],
                     PXb[0:64, 0:128]),
                     reads=["PX"], writes=[("ysfm", st, dc)])
                s.op("tensor", lambda e: e.transpose(PX[0:64, 128:256], sb["dbuf"][:], sb["identf"][:]),
                     reads=["dbuf", "identf"], writes=["PX"])
                s.op("vector", lambda e: e.tensor_copy(sb["dTf"][0:64, 0:128], PX[0:64, 128:256]),
                     reads=["PX"], writes=["dTf"])
                s.op("tensor", lambda e, h=h: e.matmul(
                     PA[:, 0:128], sb["Wcat1"][0:64, h * 128:(h + 1) * 128],
                     sb["dTf"][0:64, 0:128], start=True, stop=True),
                     reads=["dTf", "Wcat1"], writes=["PA"])
                s.op("tensor", lambda e, h=h: e.matmul(
                     PB[0:68, 0:128], sb["Wcat2"][0:64, h * 68:(h + 1) * 68],
                     sb["dTf"][0:64, 0:128], start=True, stop=True),
                     reads=["dTf", "Wcat2"], writes=["PB"])
                s.op("vector", lambda e: e.tensor_copy(sb["prj1"][:], PA[:, 0:128]),
                     reads=["PA"], writes=["prj1"])
                s.op("vector", lambda e: e.tensor_copy(sb["prj2"][0:68, :], PB[0:68, 0:128]),
                     reads=["PB"], writes=["prj2"])
                s.op("tensor", lambda e: e.transpose(PDv(0)[:, 0:128], sb["prj1"][:],
                                                     sb["identf"][:]),
                     reads=["prj1", "identf"], writes=[("PD", 0)])
                s.op("tensor", lambda e: e.transpose(PDv(1, 68)[:, 0:68], sb["prj2"][0:68, :],
                                                     sb["identf"][0:68, 0:68]),
                     reads=["prj2", "identf"], writes=[("PD", 1)])
                s.op("vector", lambda e: e.tensor_copy(sb["bxd"][:, 0:128], PDv(0)[:, 0:128]),
                     reads=[("PD", 0)], writes=["bxd"])
                s.op("vector", lambda e: e.tensor_copy(sb["bxd"][:, 128:196], PDv(1, 68)[:, 0:68]),
                     reads=[("PD", 1)], writes=["bxd"])
                if st > 0:
                    s.op("gpsimd", lambda e, st=st: e.tensor_tensor(
                         sb["tmp1"][:, 0:st * HD], Kp[:, 0:st * HD],
                         ap(sb["dbuf"], 0, [[64, 128], [0, st], [1, HD]]), op=ALU.mult),
                         reads=["Kp", "dbuf", "tmp1"], writes=["tmp1"])
                    s.op("vector", lambda e, st=st: e.tensor_reduce(
                         ap(sb["gx"], 0, [[8, 128], [1, st], [0, 1]]),
                         sb["tmp1"][:, 0:st * HD].rearrange("p (s d) -> p s d", d=HD),
                         axis=AX.X, op=ALU.add),
                         reads=["tmp1"], writes=["gx"])
                    s.op("gpsimd", lambda e, st=st: e.tensor_tensor(
                         sb["tmp2"][:, 0:256 * st],
                         ap(Uall, 0, [[256 * SL, 128], [SL, 256], [1, st]]),
                         ap(sb["gx"], 0, [[8, 128], [0, 256], [1, st]]), op=ALU.mult),
                         reads=["Uall", "gx", "tmp2"], writes=["tmp2"])
                    s.op("vector", lambda e, st=st: e.tensor_reduce(
                         ap(sb["ua"], 0, [[256, 128], [1, 256], [0, 1]]),
                         sb["tmp2"][:, 0:256 * st].rearrange("p (o s) -> p o s", o=256),
                         axis=AX.X, op=ALU.add),
                         reads=["tmp2"], writes=["ua"])
                    s.op("vector", lambda e: e.tensor_add(sb["ya"][:], sb["ua"][:, 0:256],
                                                          sb["bxd"][:]),
                         reads=["ua", "bxd", "ya"], writes=["ya"])
                else:
                    s.op("vector", lambda e: e.tensor_copy(sb["ya"][:], sb["bxd"][:]),
                         reads=["bxd", "ya"], writes=["ya"])
                s.op("gpsimd", lambda e, st=st: e.tensor_tensor(
                     ap(Uall, st, [[256 * SL, 128], [64 * SL, 4], [SL, 64]]),
                     ap(sb["ya"], 0, [[256, 128], [64, 4], [1, 64]]),
                     ap(sb["beta"], 0, [[4, 128], [1, 4], [0, 64]]),
                     op=ALU.mult),
                     reads=["ya", "beta", "Uall"], writes=["Uall"])
        if dbg and tc == 0:
            s.dma("sync", t["dbg_ys"], ysfm[:],
                  reads=[("ysfm", b_, d_) for b_ in range(SL) for d_ in range(DCH)], writes=["dbgk"])
        for oc in range(DCH):
            slot = oc % 2
            s.dma("sync", sb["owst"][:, slot * 1024:(slot + 1) * 1024],
                  ap(t["outwT"], oc * 128, [[D, 128], [128 * D, 8], [1, 128]]),
                  reads=[("owst", slot)], writes=[("owst", slot)])
            for b in range(SL):
                pk = ("PD", 2 + (b % 2))
                bank = PDv(2 + (b % 2))
                for dc2 in range(DCH):
                    s.op("tensor", lambda e, b=b, dc2=dc2, slot=slot, bank=bank: e.matmul(
                         bank[:, (b // 2) * 128:(b // 2) * 128 + 128],
                         sb["owst"][:, slot * 1024 + dc2 * 128: slot * 1024 + (dc2 + 1) * 128],
                         ysfm[:, (b * 8 + dc2) * 128:(b * 8 + dc2) * 128 + 128],
                         start=(dc2 == 0), stop=(dc2 == DCH - 1)),
                         reads=[("owst", slot), ("ysfm", b, dc2)], writes=[pk])
                s.op("vector", lambda e, oc=oc, b=b, bank=bank: e.tensor_copy(
                     sb["attnb"][:, (b * 8 + oc) * 128:(b * 8 + oc) * 128 + 128],
                     bank[:, (b // 2) * 128:(b // 2) * 128 + 128]),
                     reads=[pk], writes=[("attnb", b, oc)])
        if dbg and tc == 0:
            s.dma("sync", t["dbg_attn"], sb["attnb"][:],
                  reads=[("attnb", b_, o_) for b_ in range(SL) for o_ in range(DCH)], writes=["dbgk"])
        for b in range(SL):
            for dc2 in range(DCH):
                s.op("vector", lambda e, b=b, dc2=dc2: e.tensor_copy(
                     sb["rb"][:], x2f[:, b * 1024 + dc2 * 128: b * 1024 + (dc2 + 1) * 128]),
                     reads=[("x2f", b), "rb"], writes=["rb"])
                s.op("vector", lambda e: e.tensor_mul(sb["sqb"][:], sb["rb"][:], sb["rb"][:]),
                     reads=["rb"], writes=["sqb"])
                s.op("tensor", lambda e, dc2=dc2: e.matmul(
                     PA[0:1, 0:128], sb["onescol"][:], sb["rb"][:],
                     start=(dc2 == 0), stop=(dc2 == DCH - 1), skip_group_check=True),
                     reads=["rb", "onescol"], writes=["PA"])
                s.op("tensor", lambda e, dc2=dc2: e.matmul(
                     PB[0:1, 0:128], sb["onescol"][:], sb["sqb"][:],
                     start=(dc2 == 0), stop=(dc2 == DCH - 1), skip_group_check=True),
                     reads=["sqb", "onescol"], writes=["PB"])
            s.op("vector", lambda e: e.tensor_scalar_mul(sb["stats"][0:1, 0:128],
                 PA[0:1, 0:128], 1.0 / D), reads=["PA"], writes=["st0"])
            s.op("vector", lambda e: e.tensor_scalar_mul(sb["stats"][0:1, 128:256],
                 PB[0:1, 0:128], 1.0 / D), reads=["PB"], writes=["st1"])
            s.op("vector", lambda e: e.tensor_mul(sb["stats"][0:1, 256:384],
                 sb["stats"][0:1, 0:128], sb["stats"][0:1, 0:128]),
                 reads=["st0"], writes=["st2"])
            s.op("vector", lambda e: e.scalar_tensor_tensor(
                 sb["stats"][0:1, 128:256], sb["stats"][0:1, 128:256], 1e-5,
                 sb["stats"][0:1, 256:384], op0=ALU.add, op1=ALU.subtract),
                 reads=["st1", "st2"], writes=["st1"])
            s.op("scalar", lambda e: e.activation(sb["stats"][0:1, 256:384],
                 sb["stats"][0:1, 128:256], AF.Sqrt),
                 reads=["st1"], writes=["st2"])
            s.op("vector", lambda e: e.reciprocal(sb["stats"][0:1, 128:256],
                 sb["stats"][0:1, 256:384]),
                 reads=["st2"], writes=["st1"])
            s.op("tensor", lambda e: e.matmul(PV[:, 256:384], sb["onesrow"][:],
                 sb["stats"][0:1, 0:128], start=True, stop=True),
                 reads=["st0", "onesrow"], writes=["PV"])
            s.op("tensor", lambda e: e.matmul(PV[:, 384:512], sb["onesrow"][:],
                 sb["stats"][0:1, 128:256], start=True, stop=True),
                 reads=["st1", "onesrow"], writes=["PV"])
            for dc2 in range(DCH):
                s.op("vector", lambda e, b=b, dc2=dc2: e.tensor_sub(
                     sb["x3s"][:, 0:128],
                     x2f[:, b * 1024 + dc2 * 128: b * 1024 + (dc2 + 1) * 128],
                     PV[:, 256:384]),
                     reads=[("x2f", b), "PV", "x3s"], writes=["x3s"])
                s.op("vector", lambda e: e.tensor_mul(sb["x3s"][:, 128:256],
                     sb["x3s"][:, 0:128], PV[:, 384:512]),
                     reads=["x3s", "PV"], writes=["x3s2"])
                s.op("vector", lambda e, dc2=dc2: e.tensor_scalar(
                     sb["x3s"][:, 0:128], sb["x3s"][:, 128:256],
                     sb["lngc"][:, dc2:dc2 + 1], sb["lnbc"][:, dc2:dc2 + 1],
                     op0=ALU.mult, op1=ALU.add),
                     reads=["x3s2"], sreads=["lngc", "lnbc"], writes=["x3s"])
                if dbg and tc == 0 and b == 0 and dc2 == 0:
                    s.dma("sync", t["dbg_ln"], sb["x3s"][:, 0:128], reads=["x3s"], writes=["dbgk"])
                s.op("gpsimd", lambda e, b=b, dc2=dc2: e.tensor_tensor(
                     sb["x3s"][:, 128:256], sb["x3s"][:, 0:128],
                     sb["attnb"][:, (b * 8 + dc2) * 128:(b * 8 + dc2) * 128 + 128],
                     op=ALU.add),
                     reads=["x3s", ("attnb", b, dc2)], writes=["x3s2"])
                s.op("gpsimd", lambda e, b=b, dc2=dc2: e.tensor_tensor(
                     sb["rbuf"][:, dc2 * 128:(dc2 + 1) * 128], sb["x3s"][:, 128:256],
                     x2f[:, b * 1024 + dc2 * 128: b * 1024 + (dc2 + 1) * 128],
                     op=ALU.add),
                     reads=["x3s2", ("x2f", b), ("rbuf", dc2)], writes=[("rbuf", dc2)])
                s.op("vector", lambda e, dc2=dc2: e.tensor_copy(
                     sb["rb"][:], sb["rbuf"][:, dc2 * 128:(dc2 + 1) * 128]),
                     reads=[("rbuf", dc2), "rb"], writes=["rb"])
                s.op("vector", lambda e: e.tensor_mul(sb["sqb"][:], sb["rb"][:], sb["rb"][:]),
                     reads=["rb"], writes=["sqb"])
                s.op("tensor", lambda e, dc2=dc2: e.matmul(
                     PA[0:1, 0:128], sb["onescol"][:], sb["rb"][:],
                     start=(dc2 == 0), stop=(dc2 == DCH - 1), skip_group_check=True),
                     reads=["rb", "onescol"], writes=["PA"])
                s.op("tensor", lambda e, dc2=dc2: e.matmul(
                     PB[0:1, 0:128], sb["onescol"][:], sb["sqb"][:],
                     start=(dc2 == 0), stop=(dc2 == DCH - 1), skip_group_check=True),
                     reads=["sqb", "onescol"], writes=["PB"])
            if dbg and tc == 0 and b == 0:
                s.dma("sync", t["dbg_rbuf"], sb["rbuf"][:],
                      reads=[("rbuf", d_) for d_ in range(DCH)], writes=["dbgk"])
            s.op("vector", lambda e: e.tensor_scalar_mul(sb["stats"][0:1, 0:128],
                 PA[0:1, 0:128], 1.0 / D), reads=["PA"], writes=["st0"])
            s.op("vector", lambda e: e.tensor_scalar_mul(sb["stats"][0:1, 128:256],
                 PB[0:1, 0:128], 1.0 / D), reads=["PB"], writes=["st1"])
            s.op("vector", lambda e: e.tensor_mul(sb["stats"][0:1, 256:384],
                 sb["stats"][0:1, 0:128], sb["stats"][0:1, 0:128]),
                 reads=["st0"], writes=["st2"])
            s.op("vector", lambda e: e.scalar_tensor_tensor(
                 sb["stats"][0:1, 128:256], sb["stats"][0:1, 128:256], 1e-5,
                 sb["stats"][0:1, 256:384], op0=ALU.add, op1=ALU.subtract),
                 reads=["st1", "st2"], writes=["st1"])
            s.op("scalar", lambda e: e.activation(sb["stats"][0:1, 256:384],
                 sb["stats"][0:1, 128:256], AF.Sqrt),
                 reads=["st1"], writes=["st2"])
            s.op("vector", lambda e: e.reciprocal(sb["stats"][0:1, 128:256],
                 sb["stats"][0:1, 256:384]),
                 reads=["st2"], writes=["st1"])
            s.op("tensor", lambda e: e.matmul(PX[:, 256:384], sb["onesrow"][:],
                 sb["stats"][0:1, 0:128], start=True, stop=True),
                 reads=["st0", "onesrow"], writes=["PX"])
            s.op("tensor", lambda e: e.matmul(PX[:, 384:512], sb["onesrow"][:],
                 sb["stats"][0:1, 128:256], start=True, stop=True),
                 reads=["st1", "onesrow"], writes=["PX"])
            for dc2 in range(DCH):
                s.op("vector", lambda e, dc2=dc2: e.tensor_sub(
                     sb["x3s"][:, 0:128], sb["rbuf"][:, dc2 * 128:(dc2 + 1) * 128],
                     PX[:, 256:384]),
                     reads=[("rbuf", dc2), "PX", "x3s"], writes=["x3s"])
                s.op("vector", lambda e: e.tensor_mul(sb["x3s"][:, 128:256],
                     sb["x3s"][:, 0:128], PX[:, 384:512]),
                     reads=["x3s", "PX"], writes=["x3s2"])
                s.dma("sync", ap(t["x3"], b * D * TL + dc2 * 128 * TL + tc * 128,
                                 [[TL, 128], [1, 128]]),
                      sb["x3s"][:, 128:256],
                      reads=["x3s2"], writes=[("x3", tc, b, dc2)], force_inc=True)
    s.final_wait()


D, E, FE, FS = 1024, 8, 512, 1024
DCH = 8


def declare_dram_d(nc, TT, NB, suffix="", dbg=False):
    t = {}
    ein = lambda n, s: nc.dram_tensor(n + suffix, s, F32, kind="ExternalInput").ap()
    t["x3"] = ein("x3", [NB, D, TT])
    t["routerw"] = ein("routerw", [D, E])
    t["wgate"] = ein("wgate", [E, D, FE])
    t["wup"] = ein("wup", [E, D, FE])
    t["wdown"] = ein("wdown", [E, FE, D])
    t["wsgate"] = ein("wsgate", [D, FS])
    t["wsup"] = ein("wsup", [D, FS])
    t["wsdown"] = ein("wsdown", [FS, D])
    t["x4"] = nc.dram_tensor("x4" + suffix, [NB, D, TT], F32, kind="ExternalOutput").ap()
    if dbg:
        t["dbg_wts"] = nc.dram_tensor("dbg_wts" + suffix, [128, 32], F32, kind="ExternalOutput").ap()
    if dbg:
        t["dbg_act"] = nc.dram_tensor("dbg_act" + suffix, [128, 512], BF16, kind="ExternalOutput").ap()
    if dbg:
        t["dbg_out"] = nc.dram_tensor("dbg_out" + suffix, [128, DCH * 512], F32, kind="ExternalOutput").ap()
    t["wgate_b"] = nc.dram_tensor("wgate_b" + suffix, [E, D, FE], BF16, kind="Internal").ap()
    t["wup_b"] = nc.dram_tensor("wup_b" + suffix, [E, D, FE], BF16, kind="Internal").ap()
    t["wdown_b"] = nc.dram_tensor("wdown_b" + suffix, [E, FE, D], BF16, kind="Internal").ap()
    t["wsgate_b"] = nc.dram_tensor("wsgate_b" + suffix, [D, FS], BF16, kind="Internal").ap()
    t["wsup_b"] = nc.dram_tensor("wsup_b" + suffix, [D, FS], BF16, kind="Internal").ap()
    t["wsdown_b"] = nc.dram_tensor("wsdown_b" + suffix, [FS, D], BF16, kind="Internal").ap()
    return t


def alloc_bufs_d(ctx, nc, TT):
    sb, ps = {}, {}
    s_ = lambda n, shape, dt: ctx.enter_context(nc.sbuf_tensor(n, shape, dt))
    p_ = lambda n, shape, dt: ctx.enter_context(nc.psum_tensor(n, shape, dt))
    NTC = TT // 128
    sb["x3f"] = s_("x3f", [128, DCH * TT], F32)
    sb["xb"] = s_("xb", [128, DCH * TT], BF16)
    sb["actg"] = s_("actg", [128, E * 4 * TT], BF16)
    sb["acts"] = s_("acts", [128, 8 * TT], BF16)
    sb["wst"] = s_("wst", [128, 4 * 1024], BF16)
    sb["wdst"] = s_("wdst", [128, 2 * 2048], BF16)
    sb["wrep"] = s_("wrep", [128, E * TT], BF16)
    sb["outacc"] = s_("outacc", [128, DCH * TT], F32)
    sb["routT"] = s_("routT", [128, 64], BF16)
    sb["gs1"] = s_("gs1", [128, 32], F32)
    sb["gs2"] = s_("gs2", [128, 32], F32)
    sb["wts"] = s_("wts", [128, 32], F32)
    sb["gm"] = s_("gm", [128, 4], F32)
    sb["gm2"] = s_("gm2", [128, 4], F32)
    sb["gmr"] = s_("gmr", [128, 4], F32)
    sb["idxu"] = s_("idxu", [128, 32], F32)
    sb["idxui"] = s_("idxui", [128, 32], mybir.dt.int32)
    sb["wEm"] = s_("wEm", [128, TT], F32)
    sb["wEmb"] = s_("wEmb", [8, 512], BF16)
    sb["selc"] = s_("selc", [8, 1024], BF16)
    sb["identf"] = s_("identf", [128, 128], F32)
    sb["onescol"] = s_("onescol", [128, 1], BF16)
    sb["onesrow"] = s_("onesrow", [1, 128], F32)
    sb["silu"] = s_("silu", [128, TT], BF16)
    sb["h3w"] = s_("h3w", [128, TT], BF16)
    sb["ldf"] = s_("ldf", [128, 2048], F32)
    sb["ldb"] = s_("ldb", [128, 2048], BF16)
    sb["rb"] = s_("rb", [128, TT], BF16)
    sb["sqb"] = s_("sqb", [128, TT], BF16)
    sb["xio"] = s_("xio", [128, 2 * TT], F32)
    sb["x4s"] = s_("x4s", [128, 2 * TT], F32)
    ps["PA"] = p_("PA", [128, 512], F32)
    ps["PB"] = p_("PB", [128, 512], F32)
    ps["PD"] = p_("PD", [128, 2048], F32)
    ps["PV"] = p_("PV", [128, 512], F32)
    ps["PX"] = p_("PX", [128, 512], F32)
    return sb, ps


def emit_d(s: Sched, t, sb, ps, TT, NB, dbg=False):
    nc = s.nc
    PA, PB, PD, PV, PX = ps["PA"], ps["PB"], ps["PD"], ps["PV"], ps["PX"]
    NTC = TT // 128
    PDv = lambda k, n=512: PD[:, k * 512:k * 512 + n]
    x3f, xb, actg, acts = sb["x3f"], sb["xb"], sb["actg"], sb["acts"]
    outacc, wrep, xio = sb["outacc"], sb["wrep"], sb["xio"]
    g3 = lambda a: a.rearrange("p (c e) -> p c e", e=8)

    s.op("gpsimd", lambda e: e.memset(sb["identf"][:], 0.0), writes=["identf"])
    s.op("gpsimd", lambda e: e.affine_select(
        out=sb["identf"][:], in_=sb["identf"][:], compare_op=ALU.not_equal,
        fill=1.0, base=0, pattern=[[-1, 128]], channel_multiplier=1),
        reads=["identf"], writes=["identf"])
    s.op("vector", lambda e: e.memset(sb["onescol"][:], 1.0), writes=["onescol"])
    s.op("vector", lambda e: e.memset(sb["onesrow"][:], 1.0), writes=["onesrow"])
    s.op("vector", lambda e: e.memset(PD[:], 0.0), writes=[("PD", k) for k in range(4)])
    s.op("gpsimd", lambda e: e.iota(sb["idxui"][:], base=1, pattern=[[0, 4], [1, 8]],
                                    channel_multiplier=0), writes=["idxui"])
    s.op("vector", lambda e: e.tensor_copy(sb["idxu"][:], sb["idxui"][:]),
         reads=["idxui"], writes=["idxu"])
    s.op("gpsimd", lambda e: e.memset(sb["selc"][:], 0.0), writes=["selc"])
    s.op("gpsimd", lambda e: e.affine_select(
        out=sb["selc"][:], in_=sb["selc"][:], compare_op=ALU.not_equal,
        fill=1.0, base=0, pattern=[[-1, 8], [0, 128]], channel_multiplier=1),
        reads=["selc"], writes=["selc"])
    for (fsrc, fdst, kp) in (("wgate", "wgate_b", "cg"), ("wup", "wup_b", "cu"),
                             ("wdown", "wdown_b", "cd"), ("wsgate", "wsgate_b", "csg"),
                             ("wsup", "wsup_b", "csu"), ("wsdown", "wsdown_b", "csd")):
        total = 1
        for d_ in t[fsrc].shape:
            total *= d_
        nblk = total // (128 * 2048)
        for blk in range(nblk):
            s.dma("sync", sb["ldf"][:, 0:2048],
                  ap(t[fsrc], blk * 128 * 2048, [[2048, 128], [1, 2048]]),
                  reads=["ldf"], writes=["ldf"])
            s.op("vector", lambda e: e.tensor_copy(sb["ldb"][:, 0:2048], sb["ldf"][:, 0:2048]),
                 reads=["ldf", "ldb"], writes=["ldb"])
            s.dma("sync", ap(t[fdst], blk * 128 * 2048, [[2048, 128], [1, 2048]]),
                  sb["ldb"][:, 0:2048],
                  reads=["ldb"], writes=[("wb",)])
    for dc in range(DCH):
        s.dma("sync", sb["ldf"][:, 0:8], t["routerw"][dc * 128:(dc + 1) * 128, :],
              reads=["ldf"], writes=["ldf"])
        s.op("vector", lambda e, dc=dc: e.tensor_copy(sb["routT"][:, dc * 8:(dc + 1) * 8],
                                                      sb["ldf"][:, 0:8]),
             reads=["ldf"], writes=["routT"])

    for b in range(NB):
        for dc in range(DCH):
            s.dma("sync", x3f[:, dc * TT:(dc + 1) * TT], t["x3"][b, dc * 128:(dc + 1) * 128, :],
                  reads=[("x3f", dc)], writes=[("x3f", dc)])
            s.op("vector", lambda e, dc=dc: e.tensor_copy(
                 xb[:, dc * TT:(dc + 1) * TT], x3f[:, dc * TT:(dc + 1) * TT]),
                 reads=[("x3f", dc), ("xb", dc)], writes=[("xb", dc)])
        for dc in range(DCH):
            s.op("tensor", lambda e, dc=dc: e.matmul(
                 PB[0:8, 0:TT], sb["routT"][:, dc * 8:(dc + 1) * 8], xb[:, dc * TT:(dc + 1) * TT],
                 start=(dc == 0), stop=(dc == DCH - 1)),
                 reads=[("xb", dc), "routT"], writes=["PB"])
        s.op("vector", lambda e: e.tensor_copy(sb["wEm"][0:8, 0:TT], PB[0:8, 0:TT]),
             reads=["PB"], writes=["wEm"])
        for tc in range(NTC):
            s.op("tensor", lambda e, tc=tc: e.transpose(
                 PV[:, tc * 8:(tc + 1) * 8], sb["wEm"][0:8, tc * 128:(tc + 1) * 128],
                 sb["identf"][0:8, 0:8]),
                 reads=["wEm", "identf"], writes=["PV"])
        s.op("vector", lambda e: e.tensor_copy(sb["gs1"][:, 0:NTC * 8], PV[:, 0:NTC * 8]),
             reads=["PV"], writes=["gs1"])
        s.op("vector", lambda e: e.tensor_reduce(sb["gm"][:, 0:NTC].unsqueeze(2),
             g3(sb["gs1"][:, 0:NTC * 8]), axis=AX.X, op=ALU.max, negate=True),
             reads=["gs1"], writes=["gm"])
        s.op("vector", lambda e: e.tensor_tensor(sb["gs2"][:, 0:NTC * 8], sb["gs1"][:, 0:NTC * 8],
             ap(sb["gm"], 0, [[4, 128], [1, NTC], [0, 8]]), op=ALU.add),
             reads=["gs1", "gm"], writes=["gs2"])
        s.op("scalar", lambda e: e.activation(sb["gs1"][:, 0:NTC * 8], sb["gs2"][:, 0:NTC * 8],
                                              AF.Exp),
             reads=["gs2"], writes=["gs1"])
        s.op("vector", lambda e: e.tensor_reduce(sb["gm"][:, 0:NTC].unsqueeze(2),
             g3(sb["gs1"][:, 0:NTC * 8]), axis=AX.X, op=ALU.add),
             reads=["gs1"], writes=["gm"])
        s.op("vector", lambda e: e.reciprocal(sb["gmr"][:, 0:NTC], sb["gm"][:, 0:NTC]),
             reads=["gm"], writes=["gmr"])
        s.op("vector", lambda e: e.tensor_tensor(sb["wts"][:, 0:NTC * 8], sb["gs1"][:, 0:NTC * 8],
             ap(sb["gmr"], 0, [[4, 128], [1, NTC], [0, 8]]), op=ALU.mult),
             reads=["gs1", "gmr"], writes=["wts"])
        s.op("vector", lambda e: e.tensor_copy(sb["gs2"][:, 0:NTC * 8], sb["wts"][:, 0:NTC * 8]),
             reads=["wts"], writes=["gs2"])
        for it in range(2):
            s.op("vector", lambda e: e.tensor_reduce(sb["gm"][:, 0:NTC].unsqueeze(2),
                 g3(sb["gs2"][:, 0:NTC * 8]), axis=AX.X, op=ALU.min),
                 reads=["gs2"], writes=["gm"])
            s.op("vector", lambda e: e.tensor_tensor(sb["gs1"][:, 0:NTC * 8],
                 sb["gs2"][:, 0:NTC * 8],
                 ap(sb["gm"], 0, [[4, 128], [1, NTC], [0, 8]]), op=ALU.is_equal),
                 reads=["gs2", "gm"], writes=["gs1"])
            s.op("gpsimd", lambda e: e.tensor_tensor(sb["gs1"][:, 0:NTC * 8],
                 sb["gs1"][:, 0:NTC * 8], sb["idxu"][:, 0:NTC * 8], op=ALU.mult),
                 reads=["gs1", "idxu"], writes=["gs1"])
            s.op("vector", lambda e: e.tensor_reduce(sb["gm2"][:, 0:NTC].unsqueeze(2),
                 g3(sb["gs1"][:, 0:NTC * 8]), axis=AX.X, op=ALU.max),
                 reads=["gs1"], writes=["gm2"])
            s.op("vector", lambda e: e.tensor_tensor(sb["gs1"][:, 0:NTC * 8],
                 sb["gs1"][:, 0:NTC * 8],
                 ap(sb["gm2"], 0, [[4, 128], [1, NTC], [0, 8]]), op=ALU.is_equal),
                 reads=["gs1", "gm2"], writes=["gs1"])
            s.op("vector", lambda e: e.scalar_tensor_tensor(sb["gs2"][:, 0:NTC * 8],
                 sb["gs1"][:, 0:NTC * 8], 1e30, sb["gs2"][:, 0:NTC * 8],
                 op0=ALU.mult, op1=ALU.add),
                 reads=["gs1", "gs2"], writes=["gs2"])
            s.op("gpsimd", lambda e: e.tensor_tensor(
                 sb["gs1"][:, 0:NTC * 8], sb["gs1"][:, 0:NTC * 8], sb["wts"][:, 0:NTC * 8],
                 op=ALU.mult),
                 reads=["gs1", "wts"], writes=["gs1"])
            s.op("vector", lambda e: e.tensor_sub(sb["wts"][:, 0:NTC * 8],
                 sb["wts"][:, 0:NTC * 8], sb["gs1"][:, 0:NTC * 8]),
                 reads=["wts", "gs1"], writes=["wts"])
        if dbg and b == 0:
            s.dma("sync", t["dbg_wts"][:, 0:NTC * 8], sb["wts"][:, 0:NTC * 8],
                  reads=["wts"], writes=["dbgk"])
        for tc in range(NTC):
            s.op("tensor", lambda e, tc=tc: e.transpose(
                 PX[0:8, tc * 128:(tc + 1) * 128],
                 sb["wts"][:, tc * 8:(tc + 1) * 8], sb["identf"][:]),
                 reads=["wts", "identf"], writes=["PX"])
        s.op("vector", lambda e: e.tensor_copy(sb["wEmb"][0:8, 0:TT], PX[0:8, 0:TT]),
             reads=["PX"], writes=["wEmb"])
        for e_ in range(E):
            s.op("tensor", lambda e, e_=e_: e.matmul(
                 PA[:, 0:TT], sb["selc"][:, e_ * 128:(e_ + 1) * 128], sb["wEmb"][0:8, 0:TT],
                 start=True, stop=True),
                 reads=["wEmb", "selc"], writes=["PA"])
            s.op("vector", lambda e, e_=e_: e.tensor_copy(
                 wrep[:, e_ * TT:(e_ + 1) * TT], PA[:, 0:TT]),
                 reads=["PA"], writes=[("wrep", e_)])
        for e_ in range(E):
            for fc in range(4):
                slot = fc % 2
                s.dma("sync", sb["wst"][:, slot * 1024:(slot + 1) * 1024],
                      ap(t["wgate_b"], e_ * D * FE + fc * 128, [[FE, 128], [128 * FE, 8], [1, 128]]),
                      reads=[("wst", slot), ("wb",)], writes=[("wst", slot)])
                s.dma("sync", sb["wst"][:, (2 + slot) * 1024:(3 + slot) * 1024],
                      ap(t["wup_b"], e_ * D * FE + fc * 128, [[FE, 128], [128 * FE, 8], [1, 128]]),
                      reads=[("wst", 2 + slot), ("wb",)], writes=[("wst", 2 + slot)])
                for dc in range(DCH):
                    s.op("tensor", lambda e, dc=dc, slot=slot: e.matmul(
                         PA[:, 0:TT], sb["wst"][:, slot * 1024 + dc * 128: slot * 1024 + (dc + 1) * 128],
                         xb[:, dc * TT:(dc + 1) * TT],
                         start=(dc == 0), stop=(dc == DCH - 1)),
                         reads=[("wst", slot), ("xb", dc)], writes=["PA"])
                    s.op("tensor", lambda e, dc=dc, slot=slot: e.matmul(
                         PB[:, 0:TT], sb["wst"][:, (2 + slot) * 1024 + dc * 128: (2 + slot) * 1024 + (dc + 1) * 128],
                         xb[:, dc * TT:(dc + 1) * TT],
                         start=(dc == 0), stop=(dc == DCH - 1)),
                         reads=[("wst", 2 + slot), ("xb", dc)], writes=["PB"])
                s.op("scalar", lambda e: e.activation(sb["silu"][:], PA[:, 0:TT], AF.Silu),
                     reads=["PA"], writes=["silu"])
                s.op("vector", lambda e, e_=e_: e.tensor_tensor(
                     sb["h3w"][:], PB[:, 0:TT], wrep[:, e_ * TT:(e_ + 1) * TT], op=ALU.mult),
                     reads=["PB", ("wrep", e_), "h3w"], writes=["h3w"])
                s.op("gpsimd", lambda e, e_=e_, fc=fc: e.tensor_tensor(
                     actg[:, (e_ * 4 + fc) * TT:(e_ * 4 + fc + 1) * TT],
                     sb["h3w"][:], sb["silu"][:], op=ALU.mult),
                     reads=["h3w", "silu"], writes=[("actg", e_, fc)])
        if dbg and b == 0:
            s.dma("sync", t["dbg_act"][:, 0:TT], actg[:, 0:TT], reads=[("actg", 0, 0)], writes=["dbgk"])
        for fc in range(8):
            slot = fc % 2
            s.dma("sync", sb["wst"][:, slot * 1024:(slot + 1) * 1024],
                  ap(t["wsgate_b"], fc * 128, [[FS, 128], [128 * FS, 8], [1, 128]]),
                  reads=[("wst", slot), ("wb",)], writes=[("wst", slot)])
            s.dma("sync", sb["wst"][:, (2 + slot) * 1024:(3 + slot) * 1024],
                  ap(t["wsup_b"], fc * 128, [[FS, 128], [128 * FS, 8], [1, 128]]),
                  reads=[("wst", 2 + slot), ("wb",)], writes=[("wst", 2 + slot)])
            for dc in range(DCH):
                s.op("tensor", lambda e, dc=dc, slot=slot: e.matmul(
                     PA[:, 0:TT], sb["wst"][:, slot * 1024 + dc * 128: slot * 1024 + (dc + 1) * 128],
                     xb[:, dc * TT:(dc + 1) * TT],
                     start=(dc == 0), stop=(dc == DCH - 1)),
                     reads=[("wst", slot), ("xb", dc)], writes=["PA"])
                s.op("tensor", lambda e, dc=dc, slot=slot: e.matmul(
                     PB[:, 0:TT], sb["wst"][:, (2 + slot) * 1024 + dc * 128: (2 + slot) * 1024 + (dc + 1) * 128],
                     xb[:, dc * TT:(dc + 1) * TT],
                     start=(dc == 0), stop=(dc == DCH - 1)),
                     reads=[("wst", 2 + slot), ("xb", dc)], writes=["PB"])
            s.op("scalar", lambda e: e.activation(sb["silu"][:], PA[:, 0:TT], AF.Silu),
                 reads=["PA"], writes=["silu"])
            s.op("vector", lambda e: e.tensor_copy(sb["h3w"][:], PB[:, 0:TT]),
                 reads=["PB", "h3w"], writes=["h3w"])
            s.op("gpsimd", lambda e, fc=fc: e.tensor_tensor(
                 acts[:, fc * TT:(fc + 1) * TT], sb["h3w"][:], sb["silu"][:], op=ALU.mult),
                 reads=["h3w", "silu"], writes=[("acts", fc)])
        for dcg in range(2):
            for e_ in range(E):
                slot = e_ % 2
                s.dma("sync", sb["wdst"][:, slot * 2048:(slot + 1) * 2048],
                      ap(t["wdown_b"], e_ * FE * D + dcg * 512,
                         [[D, 128], [128 * D, 4], [1, 512]]),
                      reads=[("wdst", slot), ("wb",)], writes=[("wdst", slot)])
                for dc4 in range(4):
                    for fc in range(4):
                        s.op("tensor", lambda e, e_=e_, fc=fc, dc4=dc4, slot=slot: e.matmul(
                             PDv(dc4)[:, 0:TT],
                             sb["wdst"][:, slot * 2048 + fc * 512 + dc4 * 128:
                                        slot * 2048 + fc * 512 + (dc4 + 1) * 128],
                             actg[:, (e_ * 4 + fc) * TT:(e_ * 4 + fc + 1) * TT],
                             start=(e_ == 0 and fc == 0), stop=False, skip_group_check=True),
                             reads=[("wdst", slot), ("actg", e_, fc)], writes=[("PD", dc4)])
            for fs in range(8):
                slot = fs % 2
                s.dma("sync", sb["wdst"][:, slot * 2048: slot * 2048 + 512],
                      ap(t["wsdown_b"], fs * 128 * D + dcg * 512,
                         [[D, 128], [1, 512]]),
                      reads=[("wdst", slot), ("wb",)], writes=[("wdst", slot)])
                for dc4 in range(4):
                    s.op("tensor", lambda e, fs=fs, dc4=dc4, slot=slot: e.matmul(
                         PDv(dc4)[:, 0:TT],
                         sb["wdst"][:, slot * 2048 + dc4 * 128: slot * 2048 + (dc4 + 1) * 128],
                         acts[:, fs * TT:(fs + 1) * TT],
                         start=False, stop=(fs == 7 and dc4 == 3), skip_group_check=True),
                         reads=[("wdst", slot), ("acts", fs)], writes=[("PD", dc4)])
            for dc4 in range(4):
                dc = dcg * 4 + dc4
                s.op("vector", lambda e, dc=dc, dc4=dc4: e.tensor_copy(
                     outacc[:, dc * TT:(dc + 1) * TT], PDv(dc4)[:, 0:TT]),
                     reads=[("PD", dc4)], writes=[("outacc", dc)])
        if dbg and b == 0:
            s.dma("sync", t["dbg_out"][:, 0:DCH * TT], outacc[:],
                  reads=[("outacc", d_) for d_ in range(DCH)], writes=["dbgk"])
        for dc in range(DCH):
            s.op("vector", lambda e, dc=dc: e.tensor_add(
                 outacc[:, dc * TT:(dc + 1) * TT], outacc[:, dc * TT:(dc + 1) * TT],
                 x3f[:, dc * TT:(dc + 1) * TT]),
                 reads=[("outacc", dc), ("x3f", dc)], writes=[("outacc", dc)])
            s.op("vector", lambda e, dc=dc: e.tensor_copy(
                 sb["rb"][:], outacc[:, dc * TT:(dc + 1) * TT]),
                 reads=[("outacc", dc), "rb"], writes=["rb"])
            s.op("vector", lambda e: e.tensor_mul(sb["sqb"][:], sb["rb"][:], sb["rb"][:]),
                 reads=["rb"], writes=["sqb"])
            s.op("tensor", lambda e, dc=dc: e.matmul(
                 ap(PD, 2 * 512, [[2048, 1], [1, TT]]), sb["onescol"][:], sb["rb"][:],
                 start=(dc == 0), stop=(dc == DCH - 1), skip_group_check=True),
                 reads=["rb", "onescol"], writes=[("PD", 2)])
            s.op("tensor", lambda e, dc=dc: e.matmul(
                 ap(PD, 3 * 512, [[2048, 1], [1, TT]]), sb["onescol"][:], sb["sqb"][:],
                 start=(dc == 0), stop=(dc == DCH - 1), skip_group_check=True),
                 reads=["sqb", "onescol"], writes=[("PD", 3)])
        s.op("vector", lambda e: e.tensor_scalar_mul(xio[0:1, 0:TT], PDv(2)[0:1, 0:TT], 1.0 / D),
             reads=[("PD", 2)], writes=[("xio", 0)])
        s.op("vector", lambda e: e.tensor_scalar_mul(xio[0:1, TT:2 * TT], PDv(3)[0:1, 0:TT], 1.0 / D),
             reads=[("PD", 3)], writes=[("xio", 1)])
        s.op("vector", lambda e: e.tensor_mul(sb["wEm"][0:1, 0:TT],
             xio[0:1, 0:TT], xio[0:1, 0:TT]),
             reads=[("xio", 0)], writes=["wEm"])
        s.op("vector", lambda e: e.scalar_tensor_tensor(
             xio[0:1, TT:2 * TT], xio[0:1, TT:2 * TT], 1e-5, sb["wEm"][0:1, 0:TT],
             op0=ALU.add, op1=ALU.subtract),
             reads=[("xio", 1), "wEm"], writes=[("xio", 1)])
        s.op("scalar", lambda e: e.activation(sb["wEm"][0:1, 0:TT], xio[0:1, TT:2 * TT], AF.Sqrt),
             reads=[("xio", 1)], writes=["wEm"])
        s.op("vector", lambda e: e.reciprocal(xio[0:1, TT:2 * TT], sb["wEm"][0:1, 0:TT]),
             reads=["wEm"], writes=[("xio", 1)])
        s.op("tensor", lambda e: e.matmul(PA[:, 0:TT], sb["onesrow"][:], xio[0:1, 0:TT],
                                          start=True, stop=True),
             reads=[("xio", 0), "onesrow"], writes=["PA"])
        s.op("tensor", lambda e: e.matmul(PB[:, 0:TT], sb["onesrow"][:], xio[0:1, TT:2 * TT],
                                          start=True, stop=True),
             reads=[("xio", 1), "onesrow"], writes=["PB"])
        for dc in range(DCH):
            s.op("vector", lambda e, dc=dc: e.tensor_sub(
                 sb["x4s"][:, 0:TT], outacc[:, dc * TT:(dc + 1) * TT], PA[:, 0:TT]),
                 reads=[("outacc", dc), "PA", ("x4s", 0)], writes=[("x4s", 0)])
            s.op("vector", lambda e: e.tensor_mul(
                 sb["x4s"][:, TT:2 * TT], sb["x4s"][:, 0:TT], PB[:, 0:TT]),
                 reads=[("x4s", 0), "PB", ("x4s", 1)], writes=[("x4s", 1)])
            s.dma("sync", t["x4"][b, dc * 128:(dc + 1) * 128, :], sb["x4s"][:, TT:2 * TT],
                  reads=[("x4s", 1)], writes=[("x4", b, dc)], force_inc=True)
    s.final_wait()


import numpy as np
from contextlib import ExitStack
from concourse.bass_utils import run_bass_kernel_spmd

NCORES = 8
TLC = 512


def _run_phase(build_fn, in_maps):
    nc, names = build_fn()
    res = run_bass_kernel_spmd(nc, in_maps, core_ids=list(range(NCORES)))
    return res.results


def kernel(**inp):
    f32 = lambda a: np.ascontiguousarray(np.asarray(a, dtype=np.float32))
    x = f32(inp["L0_attn_x"])
    xt = np.ascontiguousarray(x.transpose(0, 2, 1))
    xpad = np.concatenate([np.zeros((8, 1024, 64), np.float32), xt], axis=2)

    ts_ = np.arange(128)[:, None]
    us_ = np.arange(192)[None, :]
    band = np.where((us_ >= ts_ + 1) & (us_ <= ts_ + 64), 0.0, -1e30).astype(np.float32)
    mask0 = (band + np.where(us_ < 64, -1e30, 0.0)).astype(np.float32)

    def build_a():
        nc = bass.Bass("TRN2", target_bir_lowering=False)
        t = declare_dram_a(nc, TLC)
        with ExitStack() as ctx:
            sb, ps = alloc_bufs_a(ctx, nc, TLC)
            play(nc, emit_a, t, sb, ps, TLC)
        return nc, None

    wA = dict(Wdisp=f32(inp["L0_attn_W_disp"]), Wval=f32(inp["L0_attn_W_val"]),
              Wcpj=f32(inp["L0_attn_c_proj_w"]), relpos=f32(inp["L0_attn_rel_pos_emb"]),
              ppw=f32(inp["L0_attn_pos_proj_w"]), sfw=f32(inp["L0_attn_strain_fused_w"]),
              sfb=f32(inp["L0_attn_strain_fused_b"]), bow=f32(inp["L0_attn_bond_out_w"]),
              dow=f32(inp["L0_attn_damage_out_w"]), dob=f32(inp["L0_attn_damage_out_b"]))
    in_maps = []
    for c in range(NCORES):
        m = dict(wA)
        m["xA"] = np.ascontiguousarray(xpad[:, :, c * TLC: c * TLC + TLC + 64])
        m["maskA"] = mask0 if c == 0 else band
        m["maskB"] = band
        in_maps.append(m)
    resA = _run_phase(build_a, in_maps)
    x1 = [r["x1"] for r in resA]

    def build_b():
        nc = bass.Bass("TRN2", target_bir_lowering=False)
        t = declare_dram_b(nc, TLC, 8)
        with ExitStack() as ctx:
            sb, ps = alloc_bufs_b(ctx, nc, TLC)
            play(nc, emit_b, t, sb, ps, TLC, 8)
        return nc, None

    wB = dict(normw=f32(inp["L0_ffn_norm_w"]), w1=f32(inp["L0_ffn_w1"]),
              w3=f32(inp["L0_ffn_w3"]), w2=f32(inp["L0_ffn_w2"]),
              gatew=f32(inp["L0_ffn_gate_w"]), a1=f32(inp["L0_ffn_a1"]),
              b1=f32(inp["L0_ffn_b1"]), a3=f32(inp["L0_ffn_a3"]),
              b3=f32(inp["L0_ffn_b3"]), a2=f32(inp["L0_ffn_a2"]),
              b2=f32(inp["L0_ffn_b2"]))
    in_maps = [dict(wB, x1=x1[c]) for c in range(NCORES)]
    resB = _run_phase(build_b, in_maps)
    x2 = [r["x2"] for r in resB]

    def build_c():
        nc = bass.Bass("TRN2", target_bir_lowering=False)
        t = declare_dram_c(nc, TLC)
        with ExitStack() as ctx:
            sb, ps = alloc_bufs_c(ctx, nc, TLC)
            play(nc, emit_c, t, sb, ps, TLC)
        return nc, None

    wC = dict(Wy=f32(inp["L1_attn_W_y"]), Wq=f32(inp["L1_attn_W_q"]),
              Wk=f32(inp["L1_attn_W_k"]), wb=f32(inp["L1_attn_w_b"]),
              outw=f32(inp["L1_attn_out_w"]), lng=f32(inp["L1_attn_ln_g"]),
              lnb=f32(inp["L1_attn_ln_b"]))
    in_maps = [dict(wC, x2=x2[c]) for c in range(NCORES)]
    resC = _run_phase(build_c, in_maps)
    x3 = [r["x3"] for r in resC]

    def build_d():
        nc = bass.Bass("TRN2", target_bir_lowering=False)
        t = declare_dram_d(nc, TLC, 8)
        with ExitStack() as ctx:
            sb, ps = alloc_bufs_d(ctx, nc, TLC)
            play(nc, emit_d, t, sb, ps, TLC, 8)
        return nc, None

    wD = dict(routerw=f32(inp["L1_ffn_router_w"]), wgate=f32(inp["L1_ffn_w_gate"]),
              wup=f32(inp["L1_ffn_w_up"]), wdown=f32(inp["L1_ffn_w_down"]),
              wsgate=f32(inp["L1_ffn_ws_gate"]), wsup=f32(inp["L1_ffn_ws_up"]),
              wsdown=f32(inp["L1_ffn_ws_down"]))
    in_maps = [dict(wD, x3=x3[c]) for c in range(NCORES)]
    resD = _run_phase(build_d, in_maps)

    out = np.concatenate([r["x4"] for r in resD], axis=2)
    return np.ascontiguousarray(out.transpose(0, 2, 1))


# version 04
# speedup vs baseline: 1.0040x; 1.0040x over previous
import numpy as np
import concourse.bass as bass
import concourse.mybir as mybir
import numpy as np
import concourse.bass as bass
import concourse.mybir as mybir

F32 = mybir.dt.float32
BF16 = mybir.dt.bfloat16
AF = mybir.ActivationFunctionType
ALU = mybir.AluOpType
AX = mybir.AxisListType

ENGINES = ("tensor", "vector", "scalar", "gpsimd", "sync")
DMA_QUEUES = ("sync", "gpsimd", "scalar")


class Plan:
    def __init__(self):
        self.n = 0
        self.op_engine = []
        self.op_isdma = []
        self.needs_inc = []
        self.deps = []
        self.last_writer = {}
        self.readers = {}
        self.eng_seq = {}
        self.counts = {e: 0 for e in ENGINES}
        self.dma_counts = {e: 0 for e in DMA_QUEUES}
        self.inc_rank = {}

    def record(self, engine, isdma, reads, writes, sreads, force_inc=False):
        i = self.n
        self.n += 1
        self.op_engine.append(engine)
        self.op_isdma.append(isdma)
        self.needs_inc.append(bool(force_inc))

        def same_pipe(j):
            return (not isdma and not self.op_isdma[j]
                    and engine == "tensor" and self.op_engine[j] == "tensor")

        dep = set()
        for r in list(reads) + list(sreads):
            for w in self.last_writer.get(r, ()):
                forced = r in sreads
                if not same_pipe(w) or forced:
                    dep.add(w)
            self.readers.setdefault(r, []).append(i)
        for r in writes:
            for w in self.last_writer.get(r, ()):
                if not same_pipe(w):
                    dep.add(w)
            for rd in self.readers.get(r, []):
                if rd != i and not same_pipe(rd):
                    dep.add(rd)
            prev = self.last_writer.get(r, ())
            self.last_writer[r] = [i]
            self.readers[r] = []
        self.deps.append(sorted(dep))
        for d in dep:
            self.needs_inc[d] = True
        return i

    def finalize(self):
        counters = {}
        self.rank = [None] * self.n
        for i in range(self.n):
            g = (self.op_engine[i], self.op_isdma[i])
            if self.needs_inc[i]:
                counters[g] = counters.get(g, 0) + 1
                self.rank[i] = counters[g]
        self.group_totals = counters
        self.flush_after = set()
        for i in range(self.n):
            for d in self.deps[i]:
                if self.op_isdma[d]:
                    self.flush_after.add(d)


class Sched:
    def __init__(self, nc, mode, eng, plan: Plan, sems):
        self.nc = nc
        self.mode = mode
        self.eng = eng
        self.plan = plan
        self.sems = sems
        self.idx = 0
        self.hwm = {}

    def _emit_waits(self, i):
        need = {}
        for d in self.plan.deps[i]:
            g = (self.plan.op_engine[d], self.plan.op_isdma[d])
            v = self.plan.rank[d] * (16 if g[1] else 1)
            if need.get(g, 0) < v:
                need[g] = v
        for g, v in sorted(need.items(), key=lambda kv: str(kv[0])):
            if self.hwm.get(g, 0) >= v:
                continue
            self.hwm[g] = v
            self.eng.wait_ge(self.sems[g], v)

    def op(self, engine, fn, reads=(), writes=(), sreads=(), force_inc=False):
        if self.mode == "plan":
            return self.plan.record(engine, False, reads, writes, sreads, force_inc)
        i = self.idx
        self.idx += 1
        assert self.plan.op_engine[i] == engine and not self.plan.op_isdma[i]
        if self.mode == engine:
            self._emit_waits(i)
            ins = fn(self.eng)
            if self.plan.needs_inc[i]:
                ins.then_inc(self.sems[(engine, False)], 1)
        return i

    def dma(self, queue, out, in_, reads=(), writes=(), force_inc=True, **kw):
        if self.mode == "plan":
            return self.plan.record(queue, True, reads, writes, (), force_inc)
        i = self.idx
        self.idx += 1
        assert self.plan.op_engine[i] == queue and self.plan.op_isdma[i]
        if self.mode == queue:
            self._emit_waits(i)
            ins = self.eng.dma_start(out=out, in_=in_, **kw)
            if self.plan.needs_inc[i]:
                ins.then_inc(self.sems[(queue, True)], 16)
            if i in self.plan.flush_after:
                g = (queue, True)
                v = 16 * self.plan.rank[i]
                if self.hwm.get(g, 0) < v:
                    self.hwm[g] = v
                    self.eng.wait_ge(self.sems[g], v)
        return i

    def final_wait(self):
        if self.mode == "plan":
            return
        if self.mode == "sync":
            for g, total in sorted(self.plan.group_totals.items(), key=lambda kv: str(kv[0])):
                v = total * (16 if g[1] else 1)
                if self.hwm.get(g, 0) < v:
                    self.eng.wait_ge(self.sems[g], v)


def play(nc, schedule_fn, *args, **kwargs):
    plan = Plan()
    schedule_fn(Sched(nc, "plan", None, plan, None), *args, **kwargs)
    plan.finalize()

    from contextlib import ExitStack
    with ExitStack() as ctx:
        sems = {}
        for e in ENGINES:
            sems[(e, False)] = ctx.enter_context(nc.semaphore(f"s_{e}"))
        for q in DMA_QUEUES:
            sems[(q, True)] = ctx.enter_context(nc.semaphore(f"s_dma_{q}"))
        with nc.Block() as block:
            @block.tensor
            def _(eng):
                schedule_fn(Sched(nc, "tensor", eng, plan, sems), *args, **kwargs)

            @block.vector
            def _(eng):
                schedule_fn(Sched(nc, "vector", eng, plan, sems), *args, **kwargs)

            @block.scalar
            def _(eng):
                schedule_fn(Sched(nc, "scalar", eng, plan, sems), *args, **kwargs)

            @block.gpsimd
            def _(eng):
                schedule_fn(Sched(nc, "gpsimd", eng, plan, sems), *args, **kwargs)

            @block.sync
            def _(eng):
                schedule_fn(Sched(nc, "sync", eng, plan, sems), *args, **kwargs)
    return plan


def ap(t, offset, dims):
    if isinstance(t, bass.AP):
        assert not isinstance(t.offset, str)
        offset = t.offset + offset
        t = t.tensor
    return bass.AP(t, offset, dims)


B, D, NH, W, HS, BDm = 8, 1024, 16, 64, 64, 16
DCH = 8
NHG = 4


def declare_dram_a(nc, TL, suffix="", dbg=False, x1=None):
    TP = TL + 64
    t = {}
    ein = lambda n, s: nc.dram_tensor(n + suffix, s, F32, kind="ExternalInput").ap()
    t["xA"] = ein("xA", [B, D, TP])
    t["Wdisp"] = ein("Wdisp", [256, 1024])
    t["Wval"] = ein("Wval", [1024, 1024])
    t["Wcpj"] = ein("Wcpj", [1024, 1024])
    t["relpos"] = ein("relpos", [64, 16])
    t["ppw"] = ein("ppw", [16, 16])
    t["sfw"] = ein("sfw", [32, 16])
    t["sfb"] = ein("sfb", [32])
    t["bow"] = ein("bow", [1, 16])
    t["dow"] = ein("dow", [1, 16])
    t["dob"] = ein("dob", [1])
    t["maskA"] = ein("maskA", [128, 192])
    t["maskB"] = ein("maskB", [128, 192])
    t["x1"] = x1 if x1 is not None else nc.dram_tensor("x1" + suffix, [B, D, TL], F32, kind="ExternalOutput").ap()
    t["wvalT"] = nc.dram_tensor("wvalT" + suffix, [1024, 1024], BF16, kind="Internal").ap()
    t["cprojT"] = nc.dram_tensor("cprojT" + suffix, [1024, 1024], BF16, kind="Internal").ap()
    t["shear"] = nc.dram_tensor("shearA" + suffix, [4, 2, 4, TL, TP], F32, kind="Internal").ap()
    if dbg:
        t["dbg_dproj"] = nc.dram_tensor("dbg_dproj" + suffix, [128, TP], BF16, kind="ExternalOutput").ap()
    if dbg:
        t["dbg_valt"] = nc.dram_tensor("dbg_valt" + suffix, [128, 1024], BF16, kind="ExternalOutput").ap()
    if dbg:
        t["dbg_raw"] = nc.dram_tensor("dbg_raw" + suffix, [128, 8 * TL], F32, kind="ExternalOutput").ap()
    if dbg:
        t["dbg_bondR"] = nc.dram_tensor("dbg_bondR" + suffix, [128, 768], F32, kind="ExternalOutput").ap()
    if dbg:
        t["dbg_dmgR"] = nc.dram_tensor("dbg_dmgR" + suffix, [128, 768], F32, kind="ExternalOutput").ap()
    if dbg:
        t["dbg_wcast"] = nc.dram_tensor("dbg_wcast" + suffix, [128, 768], BF16, kind="ExternalOutput").ap()
    if dbg:
        t["dbg_attn"] = nc.dram_tensor("dbg_attn" + suffix, [128, 8 * TL], BF16, kind="ExternalOutput").ap()
    return t


def alloc_bufs_a(ctx, nc, TL, suffix=""):
    TP = TL + 64
    TCHP = (TP + 127) // 128
    sb = {}
    s_ = lambda n, shape, dt: ctx.enter_context(nc.sbuf_tensor(n + suffix, shape, dt))
    sb["xb"] = s_("xb", [128, DCH * TP], BF16)
    sb["xio"] = s_("xio", [128, 2 * TP], F32)
    sb["valt"] = s_("valt", [128, 4 * TCHP * 1024], BF16)
    sb["dproj"] = s_("dproj", [128, 16 * TP], BF16)
    sb["gbuf"] = s_("gbuf", [128, 4 * 8 * TL], BF16)
    sb["raw"] = s_("raw", [128, 8 * TL], F32)
    sb["tanhb"] = s_("tanhb", [128, 8 * TL], F32)
    sb["bondR"] = s_("bondR", [128, 768], F32)
    sb["dmgR"] = s_("dmgR", [128, 768], F32)
    sb["S3"] = s_("S3", [128, 768], F32)
    sb["S4"] = s_("S4", [128, 768], F32)
    sb["wcast"] = s_("wcast", [128, 768], BF16)
    sb["negmax"] = s_("negmax", [128, 4], F32)
    sb["sumR"] = s_("sumR", [128, 4], F32)
    sb["recR"] = s_("recR", [128, 4], F32)
    sb["SuA"] = s_("SuA", [128, 128], BF16)
    sb["SuB"] = s_("SuB", [128, 128], BF16)
    sb["attn"] = s_("attn", [128, 4 * 8 * TL], BF16)
    sb["wrot"] = s_("wrot", [128, 2 * 512], BF16)
    sb["gpscr"] = s_("gpscr", [128, TL], BF16)
    sb["rb"] = s_("rb", [128, TL], BF16)
    sb["sqb"] = s_("sqb", [128, TL], BF16)

    sb["c_rep"] = s_("c_rep", [128, 64], F32)
    sb["posrep"] = s_("posrep", [128, 64], F32)
    sb["sfbcol"] = s_("sfbcol", [128, 1], F32)
    sb["dbh"] = s_("dbh", [128, 1], F32)
    sb["identb"] = s_("identb", [128, 128], BF16)
    sb["Wcomb"] = s_("Wcomb", [128, 8 * 512], BF16)
    sb["lhsT_dot"] = s_("lhsT_dot", [128, 8], BF16)
    sb["onescol"] = s_("onescol", [128, 1], BF16)
    sb["onesrow"] = s_("onesrow", [1, 128], F32)
    if 8 * TL < 1280:
        sb["smallS_t"] = s_("smallS_t", [128, 256], F32)
        sb["stg_t"] = s_("stg_t", [128, 256], BF16)

    ps = {}
    p_ = lambda n, shape, dt: ctx.enter_context(nc.psum_tensor(n + suffix, shape, dt))
    ps["PA"] = p_("PA", [128, 512], F32)
    ps["PB"] = p_("PB", [128, 512], F32)
    ps["PD"] = p_("PD", [128, 2048], F32)
    ps["PX"] = p_("PX", [128, 1024], BF16)
    ps["PV"] = p_("PV", [128, 512], F32)
    return sb, ps


def emit_a(s: Sched, t, sb, ps, TL, dbg=False, final=True):
    TP = TL + 64
    NTC = TL // 128
    TCHP = (TP + 127) // 128
    nc = s.nc
    PA, PB, PD, PX, PV = ps["PA"], ps["PB"], ps["PD"], ps["PX"], ps["PV"]
    xb, valt, dproj, gbuf, raw, tanhb = (sb[k] for k in ("xb", "valt", "dproj", "gbuf", "raw", "tanhb"))
    tanh_bc = tanhb[:].bitcast(BF16)
    tmp_keys = lambda: [("tmp", w) for w in range(8)]
    wldf = raw[:, 0:1024]
    wldf_keys = [("raw", 0), ("raw", 1)]
    wldb = tanh_bc[:, 0:1024]
    wldb_keys = [("tmp", 0), ("tmp", 1)]
    stg2 = gbuf[:, 0:256]
    stg2_keys = [("gbuf", 0)]
    blk8 = gbuf[:, 256:512]

    if 8 * TL >= 1280:
        smallS = raw[:, 1024:1280]
        stg = tanh_bc[:, 2048:2304]
        sKk = ("raw", 2)
        gKk = ("tmp", 4)
    else:
        smallS = sb["smallS_t"][:]
        stg = sb["stg_t"][:]
        sKk = "smallS_t"
        gKk = "stg_t"
    wrot_f32 = sb["wrot"][:].bitcast(F32)
    maskAs_v = wrot_f32[:, 0:192]
    maskBs_v = wrot_f32[:, 192:384]
    mask_keys = {"maskAs": [("wrot", 0)], "maskBs": [("wrot", 0), ("wrot", 1)]}
    valt_f32 = valt[:].bitcast(F32)
    stats = valt_f32[0:1, 0:4 * TL]
    packMR = valt_f32[0:1, 4 * TL:6 * TL]
    statsK = ("valt", 0, 0)
    packK = ("valt", 0, 1)
    xio = sb["xio"]
    xres_v = lambda slot: xio[:, slot * TP: slot * TP + TL]
    x1s_v = lambda slot: xio[:, slot * TP: slot * TP + TL]
    PDv = lambda k, n=512: PD[:, k * 512:k * 512 + n]

    def TT(engine, out, a, b_, op, reads, writes, sreads=()):
        return s.op(engine, lambda e: e.tensor_tensor(out, a, b_, op=op),
                    reads=reads, writes=writes, sreads=sreads)

    s.op("gpsimd", lambda e: e.memset(sb["identb"][:], 0.0), writes=["identb"])
    s.op("gpsimd", lambda e: e.affine_select(
        out=sb["identb"][:], in_=sb["identb"][:], compare_op=ALU.not_equal,
        fill=1.0, base=0, pattern=[[-1, 128]], channel_multiplier=1),
        reads=["identb"], writes=["identb"])
    s.op("vector", lambda e: e.memset(sb["onescol"][:], 1.0), writes=["onescol"])
    s.op("vector", lambda e: e.memset(ps["PD"][:], 0.0),
         writes=[("PD", k) for k in range(4)])
    s.op("vector", lambda e: e.memset(sb["onesrow"][:], 1.0), writes=["onesrow"])

    s.op("vector", lambda e: e.memset(wldf[:, 0:TP], 0.0), writes=wldf_keys)
    for bs in range(4):
        for half in range(2):
            for h in range(4):
                for tr in range(TL // 128):
                    s.dma("sync", t["shear"][bs, half, h, tr * 128:(tr + 1) * 128, :],
                          wldf[:, 0:TP], reads=wldf_keys, writes=[("shear", bs, half)])

    s.dma("scalar", smallS[0:16, 0:16], t["ppw"], writes=[sKk])
    s.dma("scalar", smallS[0:64, 16:32], t["relpos"], writes=[sKk])
    s.op("vector", lambda e: e.tensor_copy(stg[0:16, 0:16], smallS[0:16, 0:16]),
         reads=[sKk], writes=[gKk])
    s.op("vector", lambda e: e.tensor_copy(stg[0:64, 16:32], smallS[0:64, 16:32]),
         reads=[sKk], writes=[gKk])
    s.dma("scalar", smallS[0:32, 96:112], t["sfw"], writes=[sKk])
    s.op("vector", lambda e: e.tensor_copy(stg[0:32, 32:48], smallS[0:32, 96:112]),
         reads=[sKk], writes=[gKk])
    s.op("tensor", lambda e: e.transpose(PX[0:16, 0:16], stg[0:16, 0:16], sb["identb"][0:16, 0:16]),
         reads=[gKk, "identb"], writes=["PX"])
    s.op("tensor", lambda e: e.transpose(PX[0:16, 64:128], stg[0:64, 16:32], sb["identb"][0:64, 0:64]),
         reads=[gKk, "identb"], writes=["PX"])
    s.op("tensor", lambda e: e.transpose(PX[0:16, 128:160], stg[0:32, 32:48], sb["identb"][0:32, 0:32]),
         reads=[gKk, "identb"], writes=["PX"])
    s.op("vector", lambda e: e.tensor_copy(stg[0:16, 48:64], PX[0:16, 0:16]),
         reads=["PX"], writes=[gKk])
    s.op("vector", lambda e: e.tensor_copy(stg[0:16, 64:128], PX[0:16, 64:128]),
         reads=["PX"], writes=[gKk])
    s.op("vector", lambda e: e.tensor_copy(stg[0:16, 128:160], PX[0:16, 128:160]),
         reads=["PX"], writes=[gKk])
    s.op("tensor", lambda e: e.matmul(PA[0:16, 0:64], stg[0:16, 48:64],
                                      stg[0:16, 64:128], start=True, stop=True),
         reads=[gKk], writes=["PA"])
    s.op("vector", lambda e: e.tensor_copy(smallS[0:16, 32:96], PA[0:16, 0:64]),
         reads=["PA"], writes=[sKk])
    s.op("gpsimd", lambda e: e.memset(sb["posrep"][:], 0.0), writes=["posrep"])
    s.op("gpsimd", lambda e: e.memset(sb["sfbcol"][:], 0.0), writes=["sfbcol"])
    for h in range(4):
        s.dma("scalar", ap(sb["posrep"], (h * 32) * 64, [[64, 16], [1, 64]]),
              smallS[0:16, 32:96], reads=[sKk, "posrep"], writes=["posrep"])
        s.dma("scalar", ap(sb["sfbcol"], (h * 32), [[1, 32], [1, 1]]),
              t["sfb"].unsqueeze(1), reads=["sfbcol"], writes=["sfbcol"])
    s.op("vector", lambda e: e.tensor_scalar(sb["c_rep"][:], sb["posrep"][:],
                                             sb["sfbcol"][:], None, op0=ALU.add),
         reads=["posrep"], writes=["c_rep"], sreads=["sfbcol"])
    s.dma("scalar", smallS[0:1, 96:97], t["dob"].unsqueeze(1), reads=[sKk], writes=[sKk])
    s.op("tensor", lambda e: e.matmul(PB[:, 0:1], sb["onesrow"][:], smallS[0:1, 96:97],
                                      start=True, stop=True),
         reads=[sKk, "onesrow"], writes=["PB"])
    s.op("vector", lambda e: e.tensor_scalar_mul(sb["dbh"][:], PB[:, 0:1], 0.5),
         reads=["PB"], writes=["dbh"])
    s.op("gpsimd", lambda e: e.memset(sb["lhsT_dot"][:], 0.0), writes=["lhsT_dot"])
    s.dma("scalar", smallS[0:16, 112:113], t["bow"].rearrange("a b -> (a b)").unsqueeze(1),
          writes=[sKk])
    s.dma("scalar", smallS[0:16, 113:114], t["dow"].rearrange("a b -> (a b)").unsqueeze(1),
          writes=[sKk])
    s.op("vector", lambda e: e.tensor_copy(stg[0:16, 160:162], smallS[0:16, 112:114]),
         reads=[sKk], writes=[gKk])
    for h in range(4):
        s.dma("scalar", ap(sb["lhsT_dot"], (h * 32) * 8 + h, [[8, 16], [1, 1]]),
              stg[0:16, 160:161],
              reads=["lhsT_dot", gKk], writes=["lhsT_dot"])
        s.dma("scalar", ap(sb["lhsT_dot"], (h * 32 + 16) * 8 + 4 + h, [[8, 16], [1, 1]]),
              stg[0:16, 161:162],
              reads=["lhsT_dot", gKk], writes=["lhsT_dot"])
    s.op("gpsimd", lambda e: e.memset(blk8[:], 0.0), writes=[("gbuf", 0)])
    for h in range(8):
        s.dma("scalar", ap(gbuf, (h * 16) * (32 * TL) + 256 + h * 32, [[32 * TL, 16], [1, 32]]),
              stg[0:16, 128:160], reads=[gKk, ("gbuf", 0)], writes=[("gbuf", 0)])
    for oc in range(2):
        s.dma("sync", wldf, t["Wdisp"][oc * 128:(oc + 1) * 128, :],
              reads=wldf_keys, writes=wldf_keys)
        s.op("vector", lambda e: e.tensor_copy(wldb, wldf),
             reads=wldf_keys + wldb_keys, writes=wldb_keys)
        for dc in range(DCH):
            s.op("tensor", lambda e, dc=dc: e.matmul(PA[:, 0:256],
                 wldb[:, dc * 128:(dc + 1) * 128], blk8[:], start=True, stop=True),
                 reads=wldb_keys + [("gbuf", 0)], writes=["PA"])
            s.op("vector", lambda e, dc=dc, oc=oc: e.tensor_copy(
                 sb["Wcomb"][:, dc * 512 + oc * 256: dc * 512 + (oc + 1) * 256], PA[:, 0:256]),
                 reads=["PA"], writes=["Wcomb"])
    for (src, dst, key) in ((t["Wval"], t["wvalT"], "wvalT"), (t["Wcpj"], t["cprojT"], "cprojT")):
        for oc in range(8):
            s.dma("sync", wldf, src[oc * 128:(oc + 1) * 128, :],
                  reads=wldf_keys, writes=wldf_keys)
            s.op("vector", lambda e: e.tensor_copy(wldb, wldf),
                 reads=wldf_keys + wldb_keys, writes=wldb_keys)
            for dc in range(DCH):
                slot = dc % 2
                s.op("tensor", lambda e, dc=dc, slot=slot: e.transpose(
                     PX[:, slot * 128:(slot + 1) * 128],
                     wldb[:, dc * 128:(dc + 1) * 128], sb["identb"][:]),
                     reads=wldb_keys + ["identb"], writes=["PX"])
                s.op("vector", lambda e, slot=slot: e.tensor_copy(
                     stg2[:, slot * 128:(slot + 1) * 128] if False else
                     gbuf[:, slot * 128:(slot + 1) * 128],
                     PX[:, slot * 128:(slot + 1) * 128]),
                     reads=["PX"], writes=stg2_keys)
                s.dma("sync", dst[dc * 128:(dc + 1) * 128, oc * 128:(oc + 1) * 128],
                      gbuf[:, slot * 128:(slot + 1) * 128],
                      reads=stg2_keys, writes=[(key,)])

    for quad in range(2):
        for bs in range(4):
            b = quad * 4 + bs
            for dc in range(DCH):
                slot = dc % 2
                s.dma("sync", xio[:, slot * TP:(slot + 1) * TP],
                      t["xA"][b, dc * 128:(dc + 1) * 128, :],
                      reads=[("xio", slot)], writes=[("xio", slot)])
                s.op("gpsimd", lambda e, dc=dc, slot=slot: e.tensor_copy(
                     xb[:, dc * TP:(dc + 1) * TP], xio[:, slot * TP:(slot + 1) * TP]),
                     reads=[("xio", slot), ("xb", dc)], writes=[("xb", dc)])
            for oh in range(2):
                for dc in range(DCH):
                    slot = dc % 2
                    s.dma("sync", sb["wrot"][:, slot * 512:(slot + 1) * 512],
                          t["wvalT"][dc * 128:(dc + 1) * 128, oh * 512:(oh + 1) * 512],
                          reads=[("wrot", slot), ("wvalT",)], writes=[("wrot", slot)])
                    for tc in range(TCHP):
                        n = min(128, TP - tc * 128)
                        bank = PDv(tc) if tc < 4 else PA
                        s.op("tensor", lambda e, dc=dc, tc=tc, slot=slot, n=n, bank=bank: e.matmul(
                             bank[0:n, :],
                             xb[:, dc * TP + tc * 128: dc * TP + tc * 128 + n],
                             sb["wrot"][:, slot * 512:(slot + 1) * 512],
                             start=(dc == 0), stop=(dc == DCH - 1)),
                             reads=[("xb", dc), ("wrot", slot)],
                             writes=[("PD", tc) if tc < 4 else "PA"])
                for tc in range(TCHP):
                    n = min(128, TP - tc * 128)
                    bank = PDv(tc) if tc < 4 else PA
                    s.op("vector", lambda e, tc=tc, oh=oh, bs=bs, n=n, bank=bank: e.tensor_copy(
                         valt[0:n, (bs * TCHP + tc) * 1024 + oh * 512:
                              (bs * TCHP + tc) * 1024 + (oh + 1) * 512], bank[0:n, :]),
                         reads=[("PD", tc) if tc < 4 else "PA"],
                         writes=[("valt", bs, tc)])
            for hg in range(NHG):
                w0 = min(512, TP)
                for dc in range(DCH):
                    s.op("tensor", lambda e, dc=dc, hg=hg, w0=w0: e.matmul(
                         PDv(0, w0), sb["Wcomb"][:, dc * 512 + hg * 128: dc * 512 + (hg + 1) * 128],
                         xb[:, dc * TP: dc * TP + w0],
                         start=(dc == 0), stop=(dc == DCH - 1)),
                         reads=[("xb", dc), "Wcomb"], writes=[("PD", 0)])
                    if TP > 512:
                        s.op("tensor", lambda e, dc=dc, hg=hg: e.matmul(
                             PDv(1, TP - 512), sb["Wcomb"][:, dc * 512 + hg * 128: dc * 512 + (hg + 1) * 128],
                             xb[:, dc * TP + 512: (dc + 1) * TP],
                             start=(dc == 0), stop=(dc == DCH - 1)),
                             reads=[("xb", dc), "Wcomb"], writes=[("PD", 1)])
                s.op("vector", lambda e, bs=bs, hg=hg, w0=w0: e.tensor_copy(
                     dproj[:, (bs * 4 + hg) * TP: (bs * 4 + hg) * TP + w0], PDv(0, w0)),
                     reads=[("PD", 0)], writes=[("dproj", bs, hg)])
                if TP > 512:
                    s.op("vector", lambda e, bs=bs, hg=hg: e.tensor_copy(
                         dproj[:, (bs * 4 + hg) * TP + 512: (bs * 4 + hg + 1) * TP],
                         PDv(1, TP - 512)),
                         reads=[("PD", 1)], writes=[("dproj", bs, hg)])

        if dbg and quad == 0:
            s.dma("sync", t["dbg_dproj"], dproj[:, 0:TP], reads=[("dproj", 0, 0)],
                  writes=["dbgx"])
            s.dma("sync", t["dbg_valt"], valt[:, 0:1024], reads=[("valt", 0, 0)],
                  writes=["dbgx"])
        s.dma("sync", maskAs_v, t["maskA"], reads=[("wrot", 0)], writes=[("wrot", 0)])
        s.dma("sync", maskBs_v, t["maskB"], reads=[("wrot", 0), ("wrot", 1)],
              writes=[("wrot", 0), ("wrot", 1)])
        for hg in range(NHG):
            for wb in range(8):
                for bs in range(4):
                    dbase = (bs * 4 + hg) * TP
                    for wsub in range(8):
                        w = wb * 8 + wsub
                        in0 = ap(dproj, dbase + 1 + w, [[16 * TP, 128], [1, TL]])
                        in1 = ap(dproj, dbase + 64, [[16 * TP, 128], [1, TL]])
                        out = tanh_bc[:, wsub * TL:(wsub + 1) * TL]
                        if wsub % 2 == 0:
                            s.op("vector", lambda e, in0=in0, in1=in1, out=out, w=w:
                                 e.scalar_tensor_tensor(
                                 out, in0, sb["c_rep"][:, w:w + 1], in1,
                                 op0=ALU.add, op1=ALU.subtract),
                                 reads=[("dproj", bs, hg)], writes=[("tmp", wsub)],
                                 sreads=["c_rep"])
                        else:
                            cbc = ap(sb["c_rep"], w, [[64, 128], [0, TL]])
                            s.op("gpsimd", lambda e, in0=in0, cbc=cbc: e.tensor_tensor(
                                 sb["gpscr"][:], in0, cbc, op=ALU.add),
                                 reads=[("dproj", bs, hg), "c_rep", "gpscr"],
                                 writes=["gpscr"])
                            s.op("gpsimd", lambda e, in1=in1, out=out: e.tensor_tensor(
                                 out, sb["gpscr"][:], in1, op=ALU.subtract),
                                 reads=["gpscr", ("dproj", bs, hg)], writes=[("tmp", wsub)])
                    s.op("scalar", lambda e, bs=bs: e.activation(
                         gbuf[:, bs * 8 * TL:(bs + 1) * 8 * TL], tanh_bc[:, 0:8 * TL], AF.Gelu),
                         reads=tmp_keys(), writes=[("gbuf", bs)])
                for g4 in range(2):
                    for wsub in range(4):
                        for bs in range(4):
                            tp = None if bs == 0 else (0, 32 * bs)
                            s.op("tensor", lambda e, g4=g4, wsub=wsub, bs=bs, tp=tp: e.matmul(
                                 ap(PD, (32 * bs) * 2048 + wsub * 512, [[2048, 8], [1, TL]]),
                                 sb["lhsT_dot"][:],
                                 gbuf[:, (bs * 8 + g4 * 4 + wsub) * TL:(bs * 8 + g4 * 4 + wsub + 1) * TL],
                                 start=True, stop=True, tile_position=tp),
                                 reads=[("gbuf", bs), "lhsT_dot"],
                                 writes=[("PD", wsub)])
                    s.op("vector", lambda e, g4=g4: e.tensor_copy(
                         ap(raw, g4 * 4, [[8 * TL, 128], [1, 4], [8, TL]]),
                         ap(PD, 0, [[2048, 128], [512, 4], [1, TL]])),
                         reads=[("PD", k) for k in range(4)],
                         writes=[("raw", k) for k in range(8)])
                s.op("scalar", lambda e: e.activation(
                     tanhb[:], raw[:], AF.Tanh, bias=sb["dbh"][:, 0:1], scale=0.5),
                     reads=[("raw", k) for k in range(8)], writes=tmp_keys(),
                     sreads=["dbh"])
                if dbg and quad == 0 and hg == 0 and wb == 0:
                    s.dma("sync", t["dbg_raw"], raw[:], reads=[("raw", k) for k in range(8)],
                          writes=["dbgx"])
                for bs in range(4):
                    for half in range(2):
                        srcbuf = raw if half == 0 else tanhb
                        srckeys = [("raw", k) for k in range(8)] if half == 0 else tmp_keys()
                        src = ap(srcbuf, (32 * bs + 4 * half) * 8 * TL,
                                 [[8 * TL, 4], [8, TL], [1, 8]])
                        dst = ap(t["shear"], ((bs * 2 + half) * 4) * TL * TP + wb * 8 + 1,
                                 [[TL * TP, 4], [TP + 1, TL], [1, 8]])
                        s.dma("scalar", dst, src, reads=srckeys,
                              writes=[("shear", bs, half)])
            for bs in range(4):
                for tc in range(NTC):
                    roff = lambda half: ((bs * 2 + half) * 4) * TL * TP + tc * 128 * TP + tc * 128
                    s.dma("sync", sb["bondR"][:],
                          ap(t["shear"], roff(0), [[TP, 128], [TL * TP, 4], [1, 192]]),
                          reads=[("shear", bs, 0), "bondR"], writes=["bondR"])
                    s.dma("sync", sb["dmgR"][:],
                          ap(t["shear"], roff(1), [[TP, 128], [TL * TP, 4], [1, 192]]),
                          reads=[("shear", bs, 1), "dmgR"], writes=["dmgR"])
                    s.op("gpsimd", lambda e: e.tensor_scalar(
                         sb["S4"][:], sb["dmgR"][:], -5.0, None, op0=ALU.mult),
                         reads=["dmgR"], writes=["S4"])
                    s.op("gpsimd", lambda e: e.tensor_tensor(
                         sb["S3"][:], sb["S4"][:], sb["bondR"][:], op=ALU.add),
                         reads=["S4", "bondR"], writes=["S3"])
                    mv = maskAs_v if tc == 0 else maskBs_v
                    mks = mask_keys["maskAs" if tc == 0 else "maskBs"]
                    moff = 0 if tc == 0 else 192
                    s.op("gpsimd", lambda e, moff=moff: e.tensor_tensor(
                         sb["S4"][:], sb["S3"][:],
                         ap(wrot_f32, moff, [[512, 128], [0, 4], [1, 192]]), op=ALU.add),
                         reads=["S3"] + mks, writes=["S4"])
                    s.op("vector", lambda e: e.tensor_reduce(
                         sb["negmax"][:].unsqueeze(2),
                         sb["S4"][:].rearrange("p (h u) -> p h u", h=4),
                         axis=AX.X, op=ALU.max, negate=True),
                         reads=["S4"], writes=["negmax"])
                    s.op("gpsimd", lambda e: e.tensor_tensor(
                         sb["S3"][:], sb["S4"][:],
                         ap(sb["negmax"], 0, [[4, 128], [1, 4], [0, 192]]), op=ALU.add),
                         reads=["S4", "negmax"], writes=["S3"])
                    s.op("scalar", lambda e: e.activation(sb["S4"][:], sb["S3"][:], AF.Exp),
                         reads=["S3"], writes=["S4"])
                    s.op("vector", lambda e: e.tensor_reduce(
                         sb["sumR"][:].unsqueeze(2),
                         sb["S4"][:].rearrange("p (h u) -> p h u", h=4),
                         axis=AX.X, op=ALU.add),
                         reads=["S4"], writes=["sumR"])
                    s.op("vector", lambda e: e.reciprocal(sb["recR"][:], sb["sumR"][:]),
                         reads=["sumR"], writes=["recR"])
                    s.op("gpsimd", lambda e: e.tensor_tensor(
                         sb["wcast"][:], sb["S4"][:],
                         ap(sb["recR"], 0, [[4, 128], [1, 4], [0, 192]]), op=ALU.mult),
                         reads=["S4", "recR"], writes=["wcast"])
                    if dbg and quad == 0 and hg == 0 and bs == 0 and tc == 0:
                        s.dma("sync", t["dbg_bondR"], sb["bondR"][:], reads=["bondR"], writes=["dbgx"])
                        s.dma("sync", t["dbg_dmgR"], sb["dmgR"][:], reads=["dmgR"], writes=["dbgx"])
                        s.dma("sync", t["dbg_wcast"], sb["wcast"][:], reads=["wcast"], writes=["dbgx"])
                    for pair in range(2):
                        for hsub in range(2):
                            h = pair * 2 + hsub
                            s.op("tensor", lambda e, h=h: e.transpose(
                                 PX[:, 0:128], sb["wcast"][:, h * 192: h * 192 + 128],
                                 sb["identb"][:]),
                                 reads=["wcast", "identb"], writes=["PX"])
                            s.op("vector", lambda e: e.tensor_copy(sb["SuA"][:], PX[:, 0:128]),
                                 reads=["PX"], writes=["SuA"])
                            s.op("tensor", lambda e, h=h: e.transpose(
                                 PX[0:64, 128:256], sb["wcast"][:, h * 192 + 128: (h + 1) * 192],
                                 sb["identb"][:]),
                                 reads=["wcast", "identb"], writes=["PX"])
                            s.op("vector", lambda e: e.tensor_copy(sb["SuB"][0:64, :], PX[0:64, 128:256]),
                                 reads=["PX"], writes=["SuB"])
                            hglob = hg * 4 + h
                            vc0 = (bs * TCHP + tc) * 1024 + hglob * 64
                            vc1 = (bs * TCHP + tc + 1) * 1024 + hglob * 64
                            s.op("tensor", lambda e, vc0=vc0, hsub=hsub: e.matmul(
                                 PV[64 * hsub:64 * (hsub + 1), 0:128],
                                 valt[:, vc0:vc0 + 64], sb["SuA"][:],
                                 start=True, stop=False),
                                 reads=[("valt", bs, tc), "SuA"], writes=["PV"])
                            s.op("tensor", lambda e, vc1=vc1, hsub=hsub: e.matmul(
                                 PV[64 * hsub:64 * (hsub + 1), 0:128],
                                 valt[0:64, vc1:vc1 + 64], sb["SuB"][0:64, :],
                                 start=False, stop=True),
                                 reads=[("valt", bs, tc + 1), "SuB"], writes=["PV"])
                        c = hg * 2 + pair
                        s.op("vector", lambda e, bs=bs, c=c, tc=tc: e.tensor_copy(
                             sb["attn"][:, (bs * 8 + c) * TL + tc * 128:(bs * 8 + c) * TL + (tc + 1) * 128],
                             PV[:, 0:128]),
                             reads=["PV"], writes=[("attn", bs, c)])

        if dbg and quad == 0:
            s.dma("sync", t["dbg_attn"], sb["attn"][:, 0:8 * TL],
                  reads=[("attn", 0, c) for c in range(8)], writes=["dbgx"])
        for bs in range(4):
            b = quad * 4 + bs
            banks = [("PA", PA), ("PB", PB), (("PD", 0), PDv(0)), (("PD", 1), PDv(1))]
            for pas in range(2):
                for dc in range(DCH):
                    slot = dc % 2
                    s.dma("sync", sb["wrot"][:, slot * 512:(slot + 1) * 512],
                          t["cprojT"][dc * 128:(dc + 1) * 128, pas * 512:(pas + 1) * 512],
                          reads=[("wrot", slot), ("cprojT",)], writes=[("wrot", slot)])
                    for j, (bk, bt) in enumerate(banks):
                        s.op("tensor", lambda e, dc=dc, j=j, slot=slot, bt=bt: e.matmul(
                             bt[:, 0:TL], sb["wrot"][:, slot * 512 + j * 128: slot * 512 + (j + 1) * 128],
                             sb["attn"][:, (bs * 8 + dc) * TL:(bs * 8 + dc + 1) * TL],
                             start=(dc == 0), stop=(dc == DCH - 1)),
                             reads=[("wrot", slot), ("attn", bs, dc)], writes=[bk])
                for j, (bk, bt) in enumerate(banks):
                    oc = pas * 4 + j
                    slot = oc % 2
                    s.dma("sync", xres_v(slot),
                          t["xA"][b, oc * 128:(oc + 1) * 128, 64:TP],
                          reads=[("xio", slot)], writes=[("xio", slot)])
                    s.op("vector", lambda e, j=j, oc=oc, slot=slot, bt=bt: e.tensor_add(
                         raw[:, oc * TL:(oc + 1) * TL], bt[:, 0:TL],
                         xres_v(slot)),
                         reads=[bk, ("xio", slot)], writes=[("raw", oc)])
            for oc in range(DCH):
                s.op("vector", lambda e, oc=oc: e.tensor_copy(sb["rb"][:], raw[:, oc * TL:(oc + 1) * TL]),
                     reads=[("raw", oc)], writes=["rb"])
                s.op("vector", lambda e: e.tensor_mul(sb["sqb"][:], sb["rb"][:], sb["rb"][:]),
                     reads=["rb"], writes=["sqb"])
                s.op("tensor", lambda e, oc=oc: e.matmul(
                     ap(PD, 2 * 512, [[2048, 1], [1, TL]]), sb["onescol"][:], sb["rb"][:],
                     start=(oc == 0), stop=(oc == DCH - 1)),
                     reads=["rb", "onescol"], writes=[("PD", 2)])
                s.op("tensor", lambda e, oc=oc: e.matmul(
                     ap(PD, 3 * 512, [[2048, 1], [1, TL]]), sb["onescol"][:], sb["sqb"][:],
                     start=(oc == 0), stop=(oc == DCH - 1)),
                     reads=["sqb", "onescol"], writes=[("PD", 3)])
            s.op("vector", lambda e: e.tensor_scalar_mul(packMR[0:1, 0:TL], PDv(2)[0:1, 0:TL], 1.0 / D),
                 reads=[("PD", 2)], writes=[packK])
            s.op("vector", lambda e: e.tensor_scalar_mul(stats[0:1, 0:TL], PDv(3)[0:1, 0:TL], 1.0 / D),
                 reads=[("PD", 3)], writes=[statsK])
            s.op("vector", lambda e: e.tensor_mul(stats[0:1, TL:2 * TL],
                                                  packMR[0:1, 0:TL], packMR[0:1, 0:TL]),
                 reads=[packK], writes=[statsK])
            s.op("vector", lambda e: e.scalar_tensor_tensor(
                 stats[0:1, 2 * TL:3 * TL], stats[0:1, 0:TL], 1e-5,
                 stats[0:1, TL:2 * TL], op0=ALU.add, op1=ALU.subtract),
                 reads=[statsK], writes=[statsK])
            s.op("scalar", lambda e: e.activation(stats[0:1, 3 * TL:4 * TL],
                                                  stats[0:1, 2 * TL:3 * TL],
                                                  AF.Sqrt),
                 reads=[statsK], writes=[statsK])
            s.op("vector", lambda e: e.reciprocal(packMR[0:1, TL:2 * TL],
                                                  stats[0:1, 3 * TL:4 * TL]),
                 reads=[statsK], writes=[packK])
            s.op("tensor", lambda e: e.matmul(PA[:, 0:TL], sb["onesrow"][:],
                                              packMR[0:1, 0:TL], start=True, stop=True),
                 reads=[packK, "onesrow"], writes=["PA"])
            s.op("tensor", lambda e: e.matmul(PB[:, 0:TL], sb["onesrow"][:],
                                              packMR[0:1, TL:2 * TL], start=True, stop=True),
                 reads=[packK, "onesrow"], writes=["PB"])
            for oc in range(DCH):
                s.op("vector", lambda e, oc=oc: e.tensor_sub(
                     x1s_v(0), raw[:, oc * TL:(oc + 1) * TL], PA[:, 0:TL]),
                     reads=[("raw", oc), "PA"], writes=[("xio", 0)])
                s.op("vector", lambda e: e.tensor_mul(
                     x1s_v(1), x1s_v(0), PB[:, 0:TL]),
                     reads=[("xio", 0), "PB"], writes=[("xio", 1)])
                s.dma("sync", t["x1"][b, oc * 128:(oc + 1) * 128, :],
                      x1s_v(1),
                      reads=[("xio", 1)], writes=[("x1", b, oc)], force_inc=True)
    if final:
        s.final_wait()


D, F_, E, R = 1024, 4096, 8, 16
DCH, FCH = 8, 32


def declare_dram_b(nc, TT, NB, suffix="", dbg=False, x1=None, x2=None):
    t = {}
    ein = lambda n, s: nc.dram_tensor(n + suffix, s, F32, kind="ExternalInput").ap()
    t["x1"] = x1 if x1 is not None else ein("x1", [NB, D, TT])
    t["normw"] = ein("normw", [D])
    t["w1"] = ein("w1", [F_, D])
    t["w3"] = ein("w3", [F_, D])
    t["w2"] = ein("w2", [D, F_])
    t["gatew"] = ein("gatew", [E, D])
    t["a1"] = ein("a1", [E, R, D])
    t["b1"] = ein("b1", [E, F_, R])
    t["a3"] = ein("a3", [E, R, D])
    t["b3"] = ein("b3", [E, F_, R])
    t["a2"] = ein("a2", [E, R, F_])
    t["b2"] = ein("b2", [E, D, R])
    t["x2"] = x2 if x2 is not None else nc.dram_tensor("x2" + suffix, [NB, D, TT], F32, kind="ExternalOutput").ap()
    t["w1T"] = nc.dram_tensor("w1T" + suffix, [D, F_], BF16, kind="Internal").ap()
    t["w3T"] = nc.dram_tensor("w3T" + suffix, [D, F_], BF16, kind="Internal").ap()
    t["w2T"] = nc.dram_tensor("w2T" + suffix, [F_, D], BF16, kind="Internal").ap()
    t["b1T"] = nc.dram_tensor("b1T" + suffix, [E, R, F_], BF16, kind="Internal").ap()
    t["b3T"] = nc.dram_tensor("b3T" + suffix, [E, R, F_], BF16, kind="Internal").ap()
    if dbg:
        t["dbg_nxb"] = nc.dram_tensor("dbg_nxb" + suffix, [128, DCH * TT], BF16, kind="ExternalOutput").ap()
    if dbg:
        t["dbg_wts"] = nc.dram_tensor("dbg_wts" + suffix, [128, 32], F32, kind="ExternalOutput").ap()
    if dbg:
        t["dbg_wrep"] = nc.dram_tensor("dbg_wrep" + suffix, [128, E * TT], BF16, kind="ExternalOutput").ap()
    if dbg:
        t["dbg_base1"] = nc.dram_tensor("dbg_base1" + suffix, [128, FCH * TT], BF16, kind="ExternalOutput").ap()
    if dbg:
        t["dbg_etr1"] = nc.dram_tensor("dbg_etr1" + suffix, [128, 2 * TT], BF16, kind="ExternalOutput").ap()
    if dbg:
        t["dbg_act"] = nc.dram_tensor("dbg_act" + suffix, [128, 4 * E * TT], BF16, kind="ExternalOutput").ap()
    if dbg:
        t["dbg_out"] = nc.dram_tensor("dbg_out" + suffix, [128, DCH * TT], F32, kind="ExternalOutput").ap()
    return t


def alloc_bufs_b(ctx, nc, TT, suffix=""):
    sb, ps = {}, {}
    s_ = lambda n, shape, dt: ctx.enter_context(nc.sbuf_tensor(n + suffix, shape, dt))
    p_ = lambda n, shape, dt: ctx.enter_context(nc.psum_tensor(n + suffix, shape, dt))
    sb["x1f"] = s_("x1f", [128, DCH * TT], F32)
    sb["nxb"] = s_("nxb", [128, DCH * TT], BF16)
    sb["base1"] = s_("base1", [128, FCH * TT], BF16)
    sb["base3"] = s_("base3", [128, FCH * TT], BF16)
    sb["actg"] = s_("actg", [128, 4 * E * TT], BF16)
    sb["wst"] = s_("wst", [128, 2 * 1024], BF16)
    sb["w2st"] = s_("w2st", [128, 4 * 1024], BF16)
    sb["bst"] = s_("bst", [128, 2048], BF16)
    sb["etr1P"] = s_("etr1P", [128, 2 * TT], BF16)
    sb["etr3P"] = s_("etr3P", [128, 2 * TT], BF16)
    sb["etr2P"] = s_("etr2P", [128, 2 * TT], BF16)
    sb["a1T"] = s_("a1T", [128, 2 * 1024], BF16)
    sb["a3T"] = s_("a3T", [128, 2 * 1024], BF16)
    sb["a2T"] = s_("a2T", [128, 4096], BF16)
    sb["b2s"] = s_("b2s", [128, 2 * 1024], BF16)
    sb["wrep"] = s_("wrep", [128, E * TT], BF16)

    sb["outacc"] = s_("outacc", [128, DCH * TT], F32)
    sb["gateT"] = s_("gateT", [128, 64], BF16)
    sb["normw"] = s_("normw_sb", [128, 8], F32)
    sb["gs1"] = s_("gs1", [128, 32], F32)
    sb["gs2"] = s_("gs2", [128, 32], F32)
    sb["gm"] = s_("gm", [128, 4], F32)
    sb["gm2"] = s_("gm2", [128, 4], F32)
    sb["gmr"] = s_("gmr", [128, 4], F32)
    sb["idxw"] = s_("idxw", [128, 32], F32)
    sb["idxwi"] = s_("idxwi", [128, 32], mybir.dt.int32)
    sb["wtsE"] = s_("wtsE", [128, 32], F32)
    sb["wEm"] = s_("wEm", [128, TT], F32)
    sb["identb"] = s_("identb", [128, 128], BF16)
    sb["identf"] = s_("identf", [128, 128], F32)
    sb["onescol"] = s_("onescol", [128, 1], BF16)
    sb["onesrow"] = s_("onesrow", [1, 128], F32)
    sb["zzb"] = s_("zzb", [1, 128], BF16)
    sb["selc"] = s_("selc", [8, 1024], BF16)
    sb["wEmb"] = s_("wEmb", [8, 512], BF16)
    sb["ldx"] = s_("ldx", [128, 1024], F32)
    sb["ldb"] = s_("ldb", [128, 1024], BF16)
    sb["stg2"] = s_("stg2", [128, 256], BF16)
    sb["rb"] = s_("rb", [128, TT], BF16)
    sb["sqb"] = s_("sqb", [128, TT], BF16)
    sb["xio"] = s_("xio", [128, 2 * TT], F32)
    ps["PA"] = p_("PA", [128, 512], F32)
    ps["PB"] = p_("PB", [128, 512], F32)
    ps["PD"] = p_("PD", [128, 2048], F32)
    ps["PV"] = p_("PV", [128, 512], F32)
    ps["PX"] = p_("PX", [128, 512], F32)
    return sb, ps


def emit_transpose_prep_b(s, t, sb, PXb, src, dst, RR, CC, kpref):
    ldf, ldb, stg2 = sb["ldf"], sb["ldb"], sb["stg2"]
    for rc in range(RR // 128):
        for cb in range(CC // 1024):
            s.dma("sync", ldf[:, 0:1024], src[rc * 128:(rc + 1) * 128, cb * 1024:(cb + 1) * 1024],
                  reads=[("ldx", 0), ("ldx", 1)], writes=[("ldx", 0), ("ldx", 1)])
            s.op("vector", lambda e: e.tensor_copy(ldb[:, 0:1024], ldf[:, 0:1024]),
                 reads=[("ldx", 0), ("ldx", 1), ("ldb",)], writes=[("ldb",)])
            for q in range(8):
                slot = q % 2
                s.op("tensor", lambda e, q=q, slot=slot: e.transpose(
                     PXb[:, slot * 128:(slot + 1) * 128],
                     ldb[:, q * 128:(q + 1) * 128], sb["identb"][:]),
                     reads=[("ldb",), "identb"], writes=["PX"])
                s.op("vector", lambda e, slot=slot: e.tensor_copy(
                     stg2[:, slot * 128:(slot + 1) * 128], PXb[:, slot * 128:(slot + 1) * 128]),
                     reads=["PX"], writes=[("stg2x",)])
                s.dma("sync", dst[cb * 1024 + q * 128: cb * 1024 + (q + 1) * 128,
                                  rc * 128:(rc + 1) * 128],
                      stg2[:, slot * 128:(slot + 1) * 128],
                      reads=[("stg2x",)], writes=[("dst", kpref)])


def emit_b(s: Sched, t, sb, ps, TT, NB, stop_stage=99, dbg=False, final=True):
    nc = s.nc
    PA, PB, PD, PV, PX = ps["PA"], ps["PB"], ps["PD"], ps["PV"], ps["PX"]
    PXb = PX[:].bitcast(BF16)
    NTC = TT // 128
    PDv = lambda k, n=512: PD[:, k * 512:k * 512 + n]
    x1f, nxb, base1, base3, actg = (sb[k] for k in ("x1f", "nxb", "base1", "base3", "actg"))
    sb["ldf"] = sb["ldx"]
    sb["x2s"] = sb["ldx"]
    sb["silu"] = sb["rb"]
    sb["h3w"] = sb["sqb"]
    outacc, wrep, xio = sb["outacc"], sb["wrep"], sb["xio"]
    g3 = lambda a: a.rearrange("p (c e) -> p c e", e=8)

    s.op("gpsimd", lambda e: e.memset(sb["identb"][:], 0.0), writes=["identb"])
    s.op("gpsimd", lambda e: e.affine_select(
        out=sb["identb"][:], in_=sb["identb"][:], compare_op=ALU.not_equal,
        fill=1.0, base=0, pattern=[[-1, 128]], channel_multiplier=1),
        reads=["identb"], writes=["identb"])
    s.op("gpsimd", lambda e: e.memset(sb["identf"][:], 0.0), writes=["identf"])
    s.op("gpsimd", lambda e: e.affine_select(
        out=sb["identf"][:], in_=sb["identf"][:], compare_op=ALU.not_equal,
        fill=1.0, base=0, pattern=[[-1, 128]], channel_multiplier=1),
        reads=["identf"], writes=["identf"])
    s.op("vector", lambda e: e.memset(sb["onescol"][:], 1.0), writes=["onescol"])
    s.op("vector", lambda e: e.memset(sb["onesrow"][:], 1.0), writes=["onesrow"])
    s.op("vector", lambda e: e.memset(sb["zzb"][:], 0.0), writes=["zzb"])
    s.op("vector", lambda e: e.memset(sb["bst"][:], 0.0),
         writes=[("bst", w_, h, e4) for w_ in range(2) for h in range(2) for e4 in range(4)])
    s.op("vector", lambda e: e.memset(sb["b2s"][:], 0.0), writes=["b2s"])
    s.op("vector", lambda e: e.memset(PD[:], 0.0), writes=[("PD", k) for k in range(4)])
    s.op("vector", lambda e: e.memset(PV[:], 0.0), writes=["PV"])
    s.op("vector", lambda e: e.memset(PX[:], 0.0), writes=["PX"])
    s.op("gpsimd", lambda e: e.iota(sb["idxwi"][:], base=8, pattern=[[0, 4], [-1, 8]],
                                    channel_multiplier=0), writes=["idxwi"])
    s.op("vector", lambda e: e.tensor_copy(sb["idxw"][:], sb["idxwi"][:]),
         reads=["idxwi"], writes=["idxw"])
    s.op("gpsimd", lambda e: e.memset(sb["selc"][:], 0.0), writes=["selc"])
    s.op("gpsimd", lambda e: e.affine_select(
        out=sb["selc"][:], in_=sb["selc"][:], compare_op=ALU.not_equal,
        fill=1.0, base=0, pattern=[[-1, 8], [0, 128]], channel_multiplier=1),
        reads=["selc"], writes=["selc"])
    for dc in range(DCH):
        s.dma("scalar", sb["normw"][:, dc:dc + 1],
              t["normw"][dc * 128:(dc + 1) * 128].unsqueeze(1),
              reads=["normw_s"], writes=["normw_s"])
    s.dma("sync", sb["ldf"][0:8, 0:1024], t["gatew"], reads=[("ldx", 0), ("ldx", 1)], writes=[("ldx", 0), ("ldx", 1)])
    s.op("vector", lambda e: e.tensor_copy(sb["ldb"][0:8, 0:1024], sb["ldf"][0:8, 0:1024]),
         reads=[("ldx", 0), ("ldx", 1), ("ldb",)], writes=[("ldb",)])
    for dc in range(DCH):
        s.op("tensor", lambda e, dc=dc: e.transpose(
             PXb[:, 0:8], sb["ldb"][0:8, dc * 128:(dc + 1) * 128], sb["identb"][0:8, 0:8]),
             reads=[("ldb",), "identb"], writes=["PX"])
        s.op("vector", lambda e, dc=dc: e.tensor_copy(sb["gateT"][:, dc * 8:(dc + 1) * 8],
                                                      PXb[:, 0:8]),
             reads=["PX"], writes=["gateT"])
    emit_transpose_prep_b(s, t, sb, PXb, t["w1"], t["w1T"], F_, D, "w1")
    emit_transpose_prep_b(s, t, sb, PXb, t["w3"], t["w3T"], F_, D, "w3")
    emit_transpose_prep_b(s, t, sb, PXb, t["w2"], t["w2T"], D, F_, "w2")
    for (bsrc, bdst, kp) in ((t["b1"], t["b1T"], "b1"), (t["b3"], t["b3T"], "b3")):
        for e_ in range(E):
            for fb in range(4):
                src = ap(bsrc, e_ * F_ * R + fb * 1024 * R, [[16, 128], [2048, 8], [1, 16]])
                s.dma("sync", ap(sb["ldf"], 0, [[1024, 128], [1, 128]]), src,
                      reads=[("ldx", 0), ("ldx", 1)], writes=[("ldx", 0), ("ldx", 1)])
                s.op("vector", lambda e2: e2.tensor_copy(
                     ap(sb["ldb"], 0, [[1024, 128], [1, 128]]),
                     ap(sb["ldf"], 0, [[1024, 128], [1, 128]])),
                     reads=[("ldx", 0), ("ldx", 1), ("ldb",)], writes=[("ldb",)])
                for q in range(8):
                    slot = q % 2
                    tile = ap(sb["ldb"], q * 16, [[1024, 128], [1, 16]])
                    s.op("tensor", lambda e2, tile=tile, slot=slot: e2.transpose(
                         PXb[0:16, slot * 128:slot * 128 + 128], tile, sb["identb"][:]),
                         reads=[("ldb",), "identb"], writes=["PX"])
                    s.op("vector", lambda e2, slot=slot: e2.tensor_copy(
                         sb["stg2"][0:16, slot * 128:slot * 128 + 128],
                         PXb[0:16, slot * 128:slot * 128 + 128]),
                         reads=["PX"], writes=[("stg2x",)])
                    s.dma("sync", bdst[e_, :, fb * 1024 + q * 128: fb * 1024 + (q + 1) * 128],
                          sb["stg2"][0:16, slot * 128:slot * 128 + 128],
                          reads=[("stg2x",)], writes=[("dst", kp)])
    for (asrc, adst) in ((t["a1"], "a1T"), (t["a3"], "a3T")):
        for half in range(2):
            for e4 in range(4):
                s.dma("sync", ap(sb["ldf"], (e4 * 32) * 1024, [[1024, 16], [1, 1024]]),
                      asrc[half * 4 + e4], reads=[("ldx", 0), ("ldx", 1)], writes=[("ldx", 0), ("ldx", 1)])
            s.op("vector", lambda e2: e2.memset(sb["ldb"][:, 0:1024], 0.0),
                 reads=[("ldb",)], writes=[("ldb",)])
            for e4 in range(4):
                s.op("vector", lambda e2, e4=e4: e2.tensor_copy(
                     ap(sb["ldb"], (e4 * 32) * 1024, [[1024, 16], [1, 1024]]),
                     ap(sb["ldf"], (e4 * 32) * 1024, [[1024, 16], [1, 1024]])),
                     reads=[("ldx", 0), ("ldx", 1), ("ldb",)], writes=[("ldb",)])
            for dc in range(DCH):
                s.op("tensor", lambda e2, dc=dc: e2.transpose(
                     PXb[:, 0:128], sb["ldb"][:, dc * 128:(dc + 1) * 128], sb["identb"][:]),
                     reads=[("ldb",), "identb"], writes=["PX"])
                s.op("vector", lambda e2, dc=dc, half=half, adst=adst: e2.tensor_copy(
                     sb[adst][:, half * 1024 + dc * 128: half * 1024 + (dc + 1) * 128],
                     PXb[:, 0:128]),
                     reads=["PX"], writes=[adst])
    for e_ in range(E):
        for cb in range(4):
            s.dma("sync", ap(sb["ldf"], 0, [[1024, 16], [1, 1024]]),
                  t["a2"][e_, :, cb * 1024:(cb + 1) * 1024],
                  reads=[("ldx", 0), ("ldx", 1)], writes=[("ldx", 0), ("ldx", 1)])
            s.op("vector", lambda e2: e2.tensor_copy(
                 ap(sb["ldb"], 0, [[1024, 16], [1, 1024]]),
                 ap(sb["ldf"], 0, [[1024, 16], [1, 1024]])),
                 reads=[("ldx", 0), ("ldx", 1), ("ldb",)], writes=[("ldb",)])
            for q in range(8):
                fc = cb * 8 + q
                slot = q % 2
                s.op("tensor", lambda e2, q=q, slot=slot: e2.transpose(
                     PXb[:, slot * 128:slot * 128 + 16],
                     ap(sb["ldb"], q * 128, [[1024, 16], [1, 128]]),
                     sb["identb"][0:16, 0:16]),
                     reads=[("ldb",), "identb"], writes=["PX"])
                s.op("vector", lambda e2, fc=fc, e_=e_, slot=slot: e2.tensor_copy(
                     sb["a2T"][:, fc * 128 + e_ * 16: fc * 128 + (e_ + 1) * 16],
                     PXb[:, slot * 128:slot * 128 + 16]),
                     reads=["PX"], writes=["a2T"])
    for half in range(2):
        for e4 in range(4):
            e_ = half * 4 + e4
            for dgb in range(2):
                src = ap(t["b2"], e_ * D * R + dgb * 512 * R, [[16, 128], [2048, 4], [1, 16]])
                s.dma("sync", ap(sb["ldf"], 0, [[1024, 128], [1, 64]]), src,
                      reads=[("ldx", 0), ("ldx", 1)], writes=[("ldx", 0), ("ldx", 1)])
                s.op("vector", lambda e2: e2.tensor_copy(
                     ap(sb["ldb"], 0, [[1024, 128], [1, 64]]),
                     ap(sb["ldf"], 0, [[1024, 128], [1, 64]])),
                     reads=[("ldx", 0), ("ldx", 1), ("ldb",)], writes=[("ldb",)])
                for q in range(4):
                    slot = q % 2
                    tile = ap(sb["ldb"], q * 16, [[1024, 128], [1, 16]])
                    s.op("tensor", lambda e2, tile=tile, slot=slot: e2.transpose(
                         PXb[0:16, slot * 128:slot * 128 + 128], tile, sb["identb"][:]),
                         reads=[("ldb",), "identb"], writes=["PX"])
                    s.op("vector", lambda e2, e4=e4, half=half, dgb=dgb, q=q, slot=slot:
                         e2.tensor_copy(
                         sb["b2s"][e4 * 32:e4 * 32 + 16,
                                   half * 1024 + dgb * 512 + q * 128:
                                   half * 1024 + dgb * 512 + (q + 1) * 128],
                         PXb[0:16, slot * 128:slot * 128 + 128]),
                         reads=["PX"], writes=["b2s"])

    if stop_stage <= 1:
        s.dma("sync", t["dbg_wts"][:, 0:4], sb["gateT"][:, 0:8].bitcast(F32),
              reads=["gateT"], writes=["dbg"], force_inc=True)
        s.final_wait()
        return
    for b in range(NB):
        for dc in range(DCH):
            s.dma("sync", x1f[:, dc * TT:(dc + 1) * TT], t["x1"][b, dc * 128:(dc + 1) * 128, :],
                  reads=[("x1f", dc)], writes=[("x1f", dc)])
            s.op("vector", lambda e, dc=dc: e.tensor_copy(
                 sb["rb"][:], x1f[:, dc * TT:(dc + 1) * TT]),
                 reads=[("x1f", dc), "rb"], writes=["rb"])
            s.op("vector", lambda e: e.tensor_mul(sb["sqb"][:], sb["rb"][:], sb["rb"][:]),
                 reads=["rb"], writes=["sqb"])
            s.op("tensor", lambda e, dc=dc: e.matmul(
                 ap(PD, 2 * 512, [[2048, 1], [1, TT]]), sb["onescol"][:], sb["sqb"][:],
                 start=(dc == 0), stop=(dc == DCH - 1)),
                 reads=["sqb", "onescol"], writes=[("PD", 2)])
        s.op("vector", lambda e: e.tensor_scalar(xio[0:1, 0:TT], PDv(2)[0:1, 0:TT],
                                                 1.0 / D, 1e-6, op0=ALU.mult, op1=ALU.add),
             reads=[("PD", 2)], writes=[("xio", 0)])
        s.op("scalar", lambda e: e.activation(xio[0:1, TT:2 * TT], xio[0:1, 0:TT], AF.Sqrt),
             reads=[("xio", 0)], writes=[("xio", 1)])
        s.op("vector", lambda e: e.reciprocal(xio[0:1, 0:TT], xio[0:1, TT:2 * TT]),
             reads=[("xio", 1)], writes=[("xio", 0)])
        s.op("tensor", lambda e: e.matmul(PA[:, 0:TT], sb["onesrow"][:], xio[0:1, 0:TT],
                                          start=True, stop=True),
             reads=[("xio", 0), "onesrow"], writes=["PA"])
        for dc in range(DCH):
            s.op("vector", lambda e, dc=dc: e.scalar_tensor_tensor(
                 nxb[:, dc * TT:(dc + 1) * TT], x1f[:, dc * TT:(dc + 1) * TT],
                 sb["normw"][:, dc:dc + 1], PA[:, 0:TT], op0=ALU.mult, op1=ALU.mult),
                 reads=[("x1f", dc), "PA", ("nxb", dc)], sreads=["normw_s"],
                 writes=[("nxb", dc)])
        if stop_stage <= 2:
            continue
        for dc in range(DCH):
            s.op("tensor", lambda e, dc=dc: e.matmul(
                 PB[0:8, 0:TT], sb["gateT"][:, dc * 8:(dc + 1) * 8], nxb[:, dc * TT:(dc + 1) * TT],
                 start=(dc == 0), stop=(dc == DCH - 1)),
                 reads=[("nxb", dc), "gateT"], writes=["PB"])
        s.op("vector", lambda e: e.tensor_copy(sb["wEm"][0:8, 0:TT], PB[0:8, 0:TT]),
             reads=["PB"], writes=["wEm"])
        for tc in range(NTC):
            s.op("tensor", lambda e, tc=tc: e.transpose(
                 PV[:, tc * 8:(tc + 1) * 8], sb["wEm"][0:8, tc * 128:(tc + 1) * 128],
                 sb["identf"][0:8, 0:8]),
                 reads=["wEm", "identf"], writes=["PV"])
        s.op("vector", lambda e: e.tensor_copy(sb["gs1"][:, 0:NTC * 8], PV[:, 0:NTC * 8]),
             reads=["PV"], writes=["gs1"])
        s.op("vector", lambda e: e.tensor_reduce(sb["gm"][:, 0:NTC].unsqueeze(2),
             g3(sb["gs1"][:, 0:NTC * 8]), axis=AX.X, op=ALU.max, negate=True),
             reads=["gs1"], writes=["gm"])
        s.op("vector", lambda e: e.tensor_tensor(sb["gs2"][:, 0:NTC * 8], sb["gs1"][:, 0:NTC * 8],
             ap(sb["gm"], 0, [[4, 128], [1, NTC], [0, 8]]), op=ALU.add),
             reads=["gs1", "gm"], writes=["gs2"])
        s.op("scalar", lambda e: e.activation(sb["gs1"][:, 0:NTC * 8], sb["gs2"][:, 0:NTC * 8],
                                              AF.Exp),
             reads=["gs2"], writes=["gs1"])
        s.op("vector", lambda e: e.tensor_reduce(sb["gm"][:, 0:NTC].unsqueeze(2),
             g3(sb["gs1"][:, 0:NTC * 8]), axis=AX.X, op=ALU.add),
             reads=["gs1"], writes=["gm"])
        s.op("vector", lambda e: e.reciprocal(sb["gmr"][:, 0:NTC], sb["gm"][:, 0:NTC]),
             reads=["gm"], writes=["gmr"])
        s.op("vector", lambda e: e.tensor_tensor(sb["gs2"][:, 0:NTC * 8], sb["gs1"][:, 0:NTC * 8],
             ap(sb["gmr"], 0, [[4, 128], [1, NTC], [0, 8]]), op=ALU.mult),
             reads=["gs1", "gmr"], writes=["gs2"])
        s.op("vector", lambda e: e.tensor_reduce(sb["gm"][:, 0:NTC].unsqueeze(2),
             g3(sb["gs2"][:, 0:NTC * 8]), axis=AX.X, op=ALU.max),
             reads=["gs2"], writes=["gm"])
        s.op("vector", lambda e: e.tensor_tensor(sb["gs1"][:, 0:NTC * 8], sb["gs2"][:, 0:NTC * 8],
             ap(sb["gm"], 0, [[4, 128], [1, NTC], [0, 8]]), op=ALU.is_equal),
             reads=["gs2", "gm"], writes=["gs1"])
        s.op("gpsimd", lambda e: e.tensor_tensor(sb["wtsE"][:, 0:NTC * 8], sb["gs1"][:, 0:NTC * 8],
             sb["idxw"][:, 0:NTC * 8], op=ALU.mult),
             reads=["gs1", "idxw"], writes=["wtsE"])
        s.op("vector", lambda e: e.tensor_reduce(sb["gm2"][:, 0:NTC].unsqueeze(2),
             g3(sb["wtsE"][:, 0:NTC * 8]), axis=AX.X, op=ALU.max),
             reads=["wtsE"], writes=["gm2"])
        s.op("vector", lambda e: e.tensor_tensor(sb["gs1"][:, 0:NTC * 8], sb["wtsE"][:, 0:NTC * 8],
             ap(sb["gm2"], 0, [[4, 128], [1, NTC], [0, 8]]), op=ALU.is_equal),
             reads=["wtsE", "gm2"], writes=["gs1"])
        s.op("vector", lambda e: e.scalar_tensor_tensor(sb["wtsE"][:, 0:NTC * 8],
             sb["gs1"][:, 0:NTC * 8], -1e30, sb["gs2"][:, 0:NTC * 8],
             op0=ALU.mult, op1=ALU.add),
             reads=["gs1", "gs2"], writes=["wtsE"])
        s.op("vector", lambda e: e.tensor_reduce(sb["gm2"][:, 0:NTC].unsqueeze(2),
             g3(sb["wtsE"][:, 0:NTC * 8]), axis=AX.X, op=ALU.max),
             reads=["wtsE"], writes=["gm2"])
        s.op("vector", lambda e: e.tensor_tensor(sb["wtsE"][:, 0:NTC * 8], sb["wtsE"][:, 0:NTC * 8],
             ap(sb["gm2"], 0, [[4, 128], [1, NTC], [0, 8]]), op=ALU.is_equal),
             reads=["wtsE", "gm2"], writes=["wtsE"])
        s.op("gpsimd", lambda e: e.tensor_tensor(sb["gs2"][:, 0:NTC * 8], sb["wtsE"][:, 0:NTC * 8],
             sb["idxw"][:, 0:NTC * 8], op=ALU.mult),
             reads=["wtsE", "idxw"], writes=["gs2"])
        s.op("vector", lambda e: e.tensor_reduce(sb["gmr"][:, 0:NTC].unsqueeze(2),
             g3(sb["gs2"][:, 0:NTC * 8]), axis=AX.X, op=ALU.max),
             reads=["gs2"], writes=["gmr"])
        s.op("vector", lambda e: e.tensor_tensor(sb["wtsE"][:, 0:NTC * 8], sb["gs2"][:, 0:NTC * 8],
             ap(sb["gmr"], 0, [[4, 128], [1, NTC], [0, 8]]), op=ALU.is_equal),
             reads=["gs2", "gmr"], writes=["wtsE"])
        s.op("gpsimd", lambda e: e.tensor_tensor(sb["gs1"][:, 0:NTC * 8], sb["gs1"][:, 0:NTC * 8],
             ap(sb["gm"], 0, [[4, 128], [1, NTC], [0, 8]]), op=ALU.mult),
             reads=["gs1", "gm"], writes=["gs1"])
        s.op("gpsimd", lambda e: e.tensor_tensor(sb["wtsE"][:, 0:NTC * 8], sb["wtsE"][:, 0:NTC * 8],
             ap(sb["gm2"], 0, [[4, 128], [1, NTC], [0, 8]]), op=ALU.mult),
             reads=["wtsE", "gm2"], writes=["wtsE"])
        s.op("gpsimd", lambda e: e.tensor_tensor(sb["wtsE"][:, 0:NTC * 8], sb["wtsE"][:, 0:NTC * 8],
             sb["gs1"][:, 0:NTC * 8], op=ALU.add),
             reads=["wtsE", "gs1"], writes=["wtsE"])
        s.op("vector", lambda e: e.tensor_tensor(sb["gm"][:, 0:NTC], sb["gm"][:, 0:NTC],
             sb["gm2"][:, 0:NTC], op=ALU.add),
             reads=["gm", "gm2"], writes=["gm"])
        s.op("vector", lambda e: e.reciprocal(sb["gmr"][:, 0:NTC], sb["gm"][:, 0:NTC]),
             reads=["gm"], writes=["gmr"])
        s.op("vector", lambda e: e.tensor_tensor(sb["wtsE"][:, 0:NTC * 8], sb["wtsE"][:, 0:NTC * 8],
             ap(sb["gmr"], 0, [[4, 128], [1, NTC], [0, 8]]), op=ALU.mult),
             reads=["wtsE", "gmr"], writes=["wtsE"])
        for tc in range(NTC):
            s.op("tensor", lambda e, tc=tc: e.transpose(
                 PX[0:8, tc * 128:(tc + 1) * 128],
                 sb["wtsE"][:, tc * 8:(tc + 1) * 8], sb["identf"][:]),
                 reads=["wtsE", "identf"], writes=["PX"])
        s.op("vector", lambda e: e.tensor_copy(sb["wEmb"][0:8, 0:TT], PX[0:8, 0:TT]),
             reads=["PX"], writes=["wEmb"])
        for e_ in range(E):
            s.op("tensor", lambda e, e_=e_: e.matmul(
                 PA[:, 0:TT], sb["selc"][:, e_ * 128:(e_ + 1) * 128], sb["wEmb"][0:8, 0:TT],
                 start=True, stop=True),
                 reads=["wEmb", "selc"], writes=["PA"])
            s.op("vector", lambda e, e_=e_: e.tensor_copy(
                 wrep[:, e_ * TT:(e_ + 1) * TT], PA[:, 0:TT]),
                 reads=["PA"], writes=[("wrep", e_)])
        if dbg and b == 0:
            s.dma("sync", t["dbg_nxb"], sb["nxb"][:], reads=[("nxb", dc) for dc in range(DCH)], writes=["dbg"])
            s.dma("sync", t["dbg_wts"][:, 0:NTC * 8], sb["wtsE"][:, 0:NTC * 8], reads=["wtsE"], writes=["dbg"])
            s.dma("sync", t["dbg_wrep"], sb["wrep"][:], reads=[("wrep", e_) for e_ in range(E)], writes=["dbg"])
        if stop_stage <= 3:
            continue
        for w_, (wT, bt) in enumerate(((t["w1T"], base1), (t["w3T"], base3))):
            for fc in range(FCH):
                slot = (fc * 2 + w_) % 2
                s.dma("sync", sb["wst"][:, slot * 1024:(slot + 1) * 1024],
                      ap(wT, fc * 128, [[F_, 128], [128 * F_, 8], [1, 128]]),
                      reads=[("wst", slot)], writes=[("wst", slot)])
                for dc in range(DCH):
                    s.op("tensor", lambda e, dc=dc, slot=slot: e.matmul(
                         PB[:, 0:TT], sb["wst"][:, slot * 1024 + dc * 128: slot * 1024 + (dc + 1) * 128],
                         nxb[:, dc * TT:(dc + 1) * TT],
                         start=(dc == 0), stop=(dc == DCH - 1)),
                         reads=[("wst", slot), ("nxb", dc)], writes=["PB"])
                s.op("vector", lambda e, fc=fc, bt=bt: e.tensor_copy(
                     bt[:, fc * TT:(fc + 1) * TT], PB[:, 0:TT]),
                     reads=["PB"], writes=[("base", w_, fc)])
        for half in range(2):
            for (aname, ename) in (("a1T", "etr1P"), ("a3T", "etr3P")):
                for dc in range(DCH):
                    s.op("tensor", lambda e, dc=dc, half=half, aname=aname: e.matmul(
                         PB[:, 0:TT],
                         sb[aname][:, half * 1024 + dc * 128: half * 1024 + (dc + 1) * 128],
                         nxb[:, dc * TT:(dc + 1) * TT],
                         start=(dc == 0), stop=(dc == DCH - 1)),
                         reads=[aname, ("nxb", dc)], writes=["PB"])
                s.op("vector", lambda e, half=half, ename=ename: e.tensor_scalar_mul(
                     sb[ename][:, half * TT:(half + 1) * TT], PB[:, 0:TT], 2.0),
                     reads=["PB"], writes=[(ename, half)])
        if dbg and b == 0:
            s.dma("sync", t["dbg_base1"], sb["base1"][:], reads=[("base", 0, fc) for fc in range(FCH)], writes=["dbg"])
            s.dma("sync", t["dbg_etr1"], sb["etr1P"][:], reads=[("etr1P", 0), ("etr1P", 1)], writes=["dbg"])
        for bank, bk in ((PV, "PV"), (PX, "PX")):
            s.op("tensor", lambda e, bank=bank: e.matmul(
                 bank[:, 0:TT], sb["zzb"][0:1, 0:128], sb["wEmb"][0:1, 0:TT],
                 start=True, stop=False, skip_group_check=True),
                 reads=["zzb", "wEmb"], writes=[bk])
        if stop_stage <= 4:
            continue
        for fcg in range(8):
            for w_, bT in ((0, t["b1T"]), (1, t["b3T"])):
                for e_ in range(E):
                    half, e4 = e_ // 4, e_ % 4
                    s.dma("scalar",
                          ap(sb["bst"], (e4 * 32) * 2048 + w_ * 1024 + half * 512,
                             [[2048, 16], [1, 512]]),
                          bT[e_, :, fcg * 512:(fcg + 1) * 512],
                          reads=[("bst", w_, half, e4)], writes=[("bst", w_, half, e4)])
            for fci in range(4):
                fc = fcg * 4 + fci
                for e_ in range(E):
                    half, e4 = e_ // 4, e_ % 4
                    lhs1 = ap(sb["bst"], (e4 * 32) * 2048 + 0 * 1024 + half * 512 + fci * 128,
                              [[2048, 32], [1, 128]])
                    lhs3 = ap(sb["bst"], (e4 * 32) * 2048 + 1 * 1024 + half * 512 + fci * 128,
                              [[2048, 32], [1, 128]])
                    rhs1 = ap(sb["etr1P"], (e4 * 32) * (2 * TT) + half * TT, [[2 * TT, 32], [1, TT]])
                    rhs3 = ap(sb["etr3P"], (e4 * 32) * (2 * TT) + half * TT, [[2 * TT, 32], [1, TT]])
                    tp = (32 * e4, 0) if e4 > 0 else None
                    s.op("tensor", lambda e, lhs1=lhs1, rhs1=rhs1, tp=tp: e.matmul(
                         PA[:, 0:TT], lhs1, rhs1, start=True, stop=False,
                         tile_position=tp, skip_group_check=True),
                         reads=[("bst", 0, e_ // 4, e4), ("etr1P", e_ // 4)], writes=["PA"])
                    s.op("tensor", lambda e, fc=fc: e.matmul(
                         PA[:, 0:TT], sb["identb"][:], base1[:, fc * TT:(fc + 1) * TT],
                         start=False, stop=True, skip_group_check=True),
                         reads=["identb", ("base", 0, fc)], writes=["PA"])
                    s.op("scalar", lambda e: e.activation(sb["silu"][:], PA[:, 0:TT], AF.Silu),
                         reads=["PA"], writes=["rb"])
                    s.op("tensor", lambda e, lhs3=lhs3, rhs3=rhs3, tp=tp: e.matmul(
                         PB[:, 0:TT], lhs3, rhs3, start=True, stop=False,
                         tile_position=tp, skip_group_check=True),
                         reads=[("bst", 1, e_ // 4, e4), ("etr3P", e_ // 4)], writes=["PB"])
                    s.op("tensor", lambda e, fc=fc: e.matmul(
                         PB[:, 0:TT], sb["identb"][:], base3[:, fc * TT:(fc + 1) * TT],
                         start=False, stop=True, skip_group_check=True),
                         reads=["identb", ("base", 1, fc)], writes=["PB"])
                    s.op("vector", lambda e, e_=e_: e.tensor_tensor(
                         sb["h3w"][:], PB[:, 0:TT], wrep[:, e_ * TT:(e_ + 1) * TT], op=ALU.mult),
                         reads=["PB", ("wrep", e_), "sqb"], writes=["sqb"])
                    s.op("gpsimd", lambda e, fci=fci, e_=e_: e.tensor_tensor(
                         actg[:, (fci * 8 + e_) * TT:(fci * 8 + e_ + 1) * TT],
                         sb["h3w"][:], sb["silu"][:], op=ALU.mult),
                         reads=["sqb", "rb"], writes=[("actg", fci, e_)])
                    bank = PV if half == 0 else PX
                    bk = "PV" if half == 0 else "PX"
                    s.op("tensor", lambda e, fc=fc, e_=e_, fci=fci, bank=bank, e4=e4: e.matmul(
                         ap(bank, (e4 * 32) * 512, [[512, 16], [1, TT]]),
                         sb["a2T"][:, fc * 128 + (e_ % 8) * 16: fc * 128 + (e_ % 8 + 1) * 16],
                         actg[:, (fci * 8 + e_) * TT:(fci * 8 + e_ + 1) * TT],
                         start=False, stop=(fc == FCH - 1 and e4 == 3), skip_group_check=True,
                         tile_position=((0, 32 * e4) if e4 > 0 else None)),
                         reads=["a2T", ("actg", fci, e_)], writes=[bk])
            for dcg in range(2):
                for dc4 in range(4):
                    dc = dcg * 4 + dc4
                    for fci in range(4):
                        fc = fcg * 4 + fci
                        slot = fci
                        if dcg == 0 and dc4 == 0:
                            s.dma("sync", sb["w2st"][:, slot * 1024:(slot + 1) * 1024],
                                  ap(t["w2T"], fc * 128 * D, [[D, 128], [1, 1024]]),
                                  reads=[("w2st", slot)], writes=[("w2st", slot)])
                        for e_ in range(E):
                            s.op("tensor", lambda e, dc=dc, fci=fci, e_=e_, slot=slot, dc4=dc4:
                                 e.matmul(
                                 PDv(dc4)[:, 0:TT] if TT == 512 else PDv(dc4)[:, 0:TT],
                                 sb["w2st"][:, slot * 1024 + dc * 128: slot * 1024 + (dc + 1) * 128],
                                 actg[:, (fci * 8 + e_) * TT:(fci * 8 + e_ + 1) * TT],
                                 start=(fci == 0 and e_ == 0), stop=(fci == 3 and e_ == E - 1),
                                 skip_group_check=True),
                                 reads=[("w2st", slot), ("actg", fci, e_)], writes=[("PD", dc4)])
                    if fcg == 0:
                        s.op("vector", lambda e, dc=dc, dc4=dc4: e.tensor_copy(
                             outacc[:, dc * TT:(dc + 1) * TT], PDv(dc4)[:, 0:TT]),
                             reads=[("PD", dc4)], writes=[("outacc", dc)])
                    else:
                        s.op("vector", lambda e, dc=dc, dc4=dc4: e.tensor_add(
                             outacc[:, dc * TT:(dc + 1) * TT], PDv(dc4)[:, 0:TT],
                             outacc[:, dc * TT:(dc + 1) * TT]),
                             reads=[("PD", dc4), ("outacc", dc)], writes=[("outacc", dc)])
        if dbg and b == 0:
            s.dma("sync", t["dbg_act"], sb["actg"][:],
                  reads=[("actg", fci, e_) for fci in range(4) for e_ in range(E)], writes=["dbg"])
        if stop_stage <= 5:
            continue
        for half, (bank, bk) in enumerate(((PV, "PV"), (PX, "PX"))):
            s.op("vector", lambda e, half=half, bank=bank: e.tensor_scalar_mul(
                 sb["etr2P"][:, half * TT:(half + 1) * TT], bank[:, 0:TT], 2.0),
                 reads=[bk], writes=[("etr2P", half)])
        if stop_stage <= 7:
            continue
        for dc in range(DCH):
            for half in range(2):
                s.op("tensor", lambda e, dc=dc, half=half: e.matmul(
                     PA[:, 0:TT],
                     sb["b2s"][:, half * 1024 + dc * 128: half * 1024 + (dc + 1) * 128],
                     ap(sb["etr2P"], half * TT, [[2 * TT, 128], [1, TT]]),
                     start=(half == 0), stop=(half == 1), skip_group_check=True),
                     reads=["b2s", ("etr2P", half)], writes=["PA"])
            s.op("vector", lambda e, dc=dc: e.tensor_add(
                 outacc[:, dc * TT:(dc + 1) * TT], PA[:, 0:TT], outacc[:, dc * TT:(dc + 1) * TT]),
                 reads=["PA", ("outacc", dc)], writes=[("outacc", dc)])
        if dbg and b == 0:
            s.dma("sync", t["dbg_out"], sb["outacc"][:],
                  reads=[("outacc", dc) for dc in range(DCH)], writes=["dbg"])
        if stop_stage <= 6:
            continue
        for dc in range(DCH):
            s.op("vector", lambda e, dc=dc: e.tensor_add(
                 outacc[:, dc * TT:(dc + 1) * TT], outacc[:, dc * TT:(dc + 1) * TT],
                 x1f[:, dc * TT:(dc + 1) * TT]),
                 reads=[("outacc", dc), ("x1f", dc)], writes=[("outacc", dc)])
            s.op("vector", lambda e, dc=dc: e.tensor_copy(
                 sb["rb"][:], outacc[:, dc * TT:(dc + 1) * TT]),
                 reads=[("outacc", dc), "rb"], writes=["rb"])
            s.op("vector", lambda e: e.tensor_mul(sb["sqb"][:], sb["rb"][:], sb["rb"][:]),
                 reads=["rb"], writes=["sqb"])
            s.op("tensor", lambda e, dc=dc: e.matmul(
                 ap(PD, 2 * 512, [[2048, 1], [1, TT]]), sb["onescol"][:], sb["rb"][:],
                 start=(dc == 0), stop=(dc == DCH - 1)),
                 reads=["rb", "onescol"], writes=[("PD", 2)])
            s.op("tensor", lambda e, dc=dc: e.matmul(
                 ap(PD, 3 * 512, [[2048, 1], [1, TT]]), sb["onescol"][:], sb["sqb"][:],
                 start=(dc == 0), stop=(dc == DCH - 1)),
                 reads=["sqb", "onescol"], writes=[("PD", 3)])
        if stop_stage <= 8:
            continue
        s.op("vector", lambda e: e.tensor_scalar_mul(xio[0:1, 0:TT], PDv(2)[0:1, 0:TT], 1.0 / D),
             reads=[("PD", 2)], writes=[("xio", 0)])
        s.op("vector", lambda e: e.tensor_scalar_mul(xio[0:1, TT:2 * TT], PDv(3)[0:1, 0:TT], 1.0 / D),
             reads=[("PD", 3)], writes=[("xio", 1)])
        s.op("vector", lambda e: e.tensor_mul(sb["wEm"][0:1, 0:TT],
             xio[0:1, 0:TT], xio[0:1, 0:TT]),
             reads=[("xio", 0)], writes=["wEm"])
        s.op("vector", lambda e: e.scalar_tensor_tensor(
             xio[0:1, TT:2 * TT], xio[0:1, TT:2 * TT], 1e-5, sb["wEm"][0:1, 0:TT],
             op0=ALU.add, op1=ALU.subtract),
             reads=[("xio", 1), "wEm"], writes=[("xio", 1)])
        s.op("scalar", lambda e: e.activation(sb["wEm"][0:1, 0:TT], xio[0:1, TT:2 * TT], AF.Sqrt),
             reads=[("xio", 1)], writes=["wEm"])
        s.op("vector", lambda e: e.reciprocal(xio[0:1, TT:2 * TT], sb["wEm"][0:1, 0:TT]),
             reads=["wEm"], writes=[("xio", 1)])
        s.op("tensor", lambda e: e.matmul(PA[:, 0:TT], sb["onesrow"][:], xio[0:1, 0:TT],
                                          start=True, stop=True),
             reads=[("xio", 0), "onesrow"], writes=["PA"])
        s.op("tensor", lambda e: e.matmul(PB[:, 0:TT], sb["onesrow"][:], xio[0:1, TT:2 * TT],
                                          start=True, stop=True),
             reads=[("xio", 1), "onesrow"], writes=["PB"])
        if stop_stage <= 9:
            continue
        for dc in range(DCH):
            s.op("vector", lambda e, dc=dc: e.tensor_sub(
                 sb["x2s"][:, 0:TT], outacc[:, dc * TT:(dc + 1) * TT], PA[:, 0:TT]),
                 reads=[("outacc", dc), "PA", ("ldx", 0)], writes=[("ldx", 0)])
            s.op("vector", lambda e: e.tensor_mul(
                 sb["x2s"][:, TT:2 * TT], sb["x2s"][:, 0:TT], PB[:, 0:TT]),
                 reads=[("ldx", 0), "PB", ("ldx", 1)], writes=[("ldx", 1)])
            if stop_stage > 10 and (stop_stage > 11 or dc == 0):
                s.dma("sync", t["x2"][b, dc * 128:(dc + 1) * 128, :], sb["x2s"][:, TT:2 * TT],
                      reads=[("ldx", 1)], writes=[("x2", b, dc)], force_inc=True)
    if stop_stage < 99:
        s.dma("sync", t["dbg_wts"][:, 0:8], sb["gs1"][:, 0:8],
              reads=["gs1"], writes=["dbg2"], force_inc=True)
    if final:
        s.final_wait()


D, NH, HD, SL = 1024, 16, 64, 8
DCH = 8


def declare_dram_c(nc, TL, suffix="", dbg=False, x2=None, x3=None):
    t = {}
    ein = lambda n, s: nc.dram_tensor(n + suffix, s, F32, kind="ExternalInput").ap()
    t["x2"] = x2 if x2 is not None else ein("x2", [SL, D, TL])
    t["Wy"] = ein("Wy", [1, NH, HD, HD])
    t["Wq"] = ein("Wq", [1, NH, HD, HD])
    t["Wk"] = ein("Wk", [1, NH, HD, HD])
    t["wb"] = ein("wb", [1, NH, HD, 4])
    t["outw"] = ein("outw", [D, D])
    t["lng"] = ein("lng", [D])
    t["lnb"] = ein("lnb", [D])
    t["x3"] = x3 if x3 is not None else nc.dram_tensor("x3" + suffix, [SL, D, TL], F32, kind="ExternalOutput").ap()
    t["outwT"] = nc.dram_tensor("outwT" + suffix, [D, D], BF16, kind="Internal").ap()
    if dbg:
        t["dbg_bxs"] = nc.dram_tensor("dbg_bxs" + suffix, [128, SL * 256], F32, kind="ExternalOutput").ap()
    if dbg:
        t["dbg_ys"] = nc.dram_tensor("dbg_ys" + suffix, [128, SL * DCH * 128], BF16, kind="ExternalOutput").ap()
    if dbg:
        t["dbg_attn"] = nc.dram_tensor("dbg_attn" + suffix, [128, SL * DCH * 128], F32, kind="ExternalOutput").ap()
    if dbg:
        t["dbg_sm"] = nc.dram_tensor("dbg_sm" + suffix, [128, 192], F32, kind="ExternalOutput").ap()
    if dbg:
        t["dbg_rbuf"] = nc.dram_tensor("dbg_rbuf" + suffix, [128, DCH * 128], F32, kind="ExternalOutput").ap()
    if dbg:
        t["dbg_ln"] = nc.dram_tensor("dbg_ln" + suffix, [128, 128], F32, kind="ExternalOutput").ap()
    return t


def alloc_bufs_c(ctx, nc, TL, suffix=""):
    sb, ps = {}, {}
    s_ = lambda n, shape, dt: ctx.enter_context(nc.sbuf_tensor(n + suffix, shape, dt))
    p_ = lambda n, shape, dt: ctx.enter_context(nc.psum_tensor(n + suffix, shape, dt))
    sb["x2f"] = s_("x2f", [128, SL * DCH * 128], F32)
    sb["dTf"] = s_("dTf", [128, 128], F32)
    sb["ysfm"] = s_("ysfm", [128, SL * DCH * 128], BF16)
    sb["Xp"] = s_("Xp", [128, SL * HD], F32)
    sb["Kp"] = s_("Kp", [128, SL * HD], F32)
    sb["Uall"] = s_("Uall", [128, 256 * SL], F32)
    sb["bxs"] = s_("bxs", [128, SL * 256], F32)
    sb["bxd"] = s_("bxd", [128, 256], F32)
    sb["ya"] = s_("ya", [128, 256], F32)
    sb["ua"] = s_("ua", [128, 256], F32)
    sb["sm"] = s_("sm", [128, 192], F32)
    sb["beta"] = s_("beta", [128, 4], F32)
    sb["dbuf"] = s_("dbuf", [128, 64], F32)
    sb["ybf"] = s_("ybf", [128, 64], BF16)
    sb["nmx"] = s_("nmx", [128, 3], F32)
    sb["smr"] = s_("smr", [128, 3], F32)
    sb["tmp1"] = s_("tmp1", [128, SL * HD], F32)
    sb["tmp2"] = s_("tmp2", [128, 256 * SL], F32)
    sb["prj1"] = s_("prj1", [128, 128], F32)
    sb["prj2"] = s_("prj2", [128, 128], F32)
    sb["Wcat1"] = s_("Wcat1", [128, NH * 128], F32)
    sb["Wcat2"] = s_("Wcat2", [128, NH * 68], F32)
    sb["owst"] = s_("owst", [128, 2 * 1024], BF16)
    sb["identb"] = s_("identb", [128, 128], BF16)
    sb["identf"] = s_("identf", [128, 128], F32)
    sb["rbuf"] = s_("rbuf", [128, DCH * 128], F32)
    sb["lngc"] = s_("lngc", [128, 8], F32)
    sb["lnbc"] = s_("lnbc", [128, 8], F32)
    sb["rb"] = s_("rb", [128, 128], BF16)
    sb["sqb"] = s_("sqb", [128, 128], BF16)
    sb["stats"] = s_("stats", [1, 512], F32)
    sb["onescol"] = s_("onescol", [128, 1], BF16)
    sb["onesrow"] = s_("onesrow", [1, 128], F32)
    sb["ldf"] = s_("ldf", [128, 1024], F32)
    sb["ldb"] = s_("ldb", [128, 1024], BF16)
    sb["stg2"] = s_("stg2", [128, 256], BF16)
    sb["x3s"] = s_("x3s", [128, 2 * 128], F32)
    sb["attnb"] = s_("attnb", [128, SL * DCH * 128], F32)
    sb["gx"] = s_("gx", [128, 8], F32)
    ps["PA"] = p_("PA", [128, 512], F32)
    ps["PB"] = p_("PB", [128, 512], F32)
    ps["PD"] = p_("PD", [128, 2048], F32)
    ps["PV"] = p_("PV", [128, 512], F32)
    ps["PX"] = p_("PX", [128, 512], F32)
    return sb, ps


def emit_c(s: Sched, t, sb, ps, TL, dbg=False, final=True):
    nc = s.nc
    PA, PB, PD, PV, PX = ps["PA"], ps["PB"], ps["PD"], ps["PV"], ps["PX"]
    PXb = PX[:].bitcast(BF16)
    NTC = TL // 128
    PDv = lambda k, n=512: PD[:, k * 512:k * 512 + n]
    x2f, ysfm = sb["x2f"], sb["ysfm"]
    Xp, Kp, Uall, bxs = sb["Xp"], sb["Kp"], sb["Uall"], sb["bxs"]

    s.op("gpsimd", lambda e: e.memset(sb["identb"][:], 0.0), writes=["identb"])
    s.op("gpsimd", lambda e: e.affine_select(
        out=sb["identb"][:], in_=sb["identb"][:], compare_op=ALU.not_equal,
        fill=1.0, base=0, pattern=[[-1, 128]], channel_multiplier=1),
        reads=["identb"], writes=["identb"])
    s.op("gpsimd", lambda e: e.memset(sb["identf"][:], 0.0), writes=["identf"])
    s.op("gpsimd", lambda e: e.affine_select(
        out=sb["identf"][:], in_=sb["identf"][:], compare_op=ALU.not_equal,
        fill=1.0, base=0, pattern=[[-1, 128]], channel_multiplier=1),
        reads=["identf"], writes=["identf"])
    s.op("vector", lambda e: e.memset(sb["onescol"][:], 1.0), writes=["onescol"])
    s.op("vector", lambda e: e.memset(sb["Wcat1"][:], 0.0), writes=["Wcat1"])
    s.op("vector", lambda e: e.memset(sb["Wcat2"][:], 0.0), writes=["Wcat2"])
    s.op("vector", lambda e: e.memset(sb["onesrow"][:], 1.0), writes=["onesrow"])
    s.op("vector", lambda e: e.memset(PD[:], 0.0), writes=[("PD", k) for k in range(4)])
    for dc in range(DCH):
        s.dma("scalar", sb["lngc"][:, dc:dc + 1], t["lng"][dc * 128:(dc + 1) * 128].unsqueeze(1),
              reads=["lngc"], writes=["lngc"])
        s.dma("scalar", sb["lnbc"][:, dc:dc + 1], t["lnb"][dc * 128:(dc + 1) * 128].unsqueeze(1),
              reads=["lnbc"], writes=["lnbc"])
    for rc in range(8):
        s.dma("sync", sb["ldf"][:, 0:1024], t["outw"][rc * 128:(rc + 1) * 128, :],
              reads=["ldf"], writes=["ldf"])
        s.op("vector", lambda e: e.tensor_copy(sb["ldb"][:, 0:1024], sb["ldf"][:, 0:1024]),
             reads=["ldf", "ldb"], writes=["ldb"])
        for q in range(8):
            slot = q % 2
            s.op("tensor", lambda e, q=q, slot=slot: e.transpose(
                 PXb[:, slot * 128:(slot + 1) * 128],
                 sb["ldb"][:, q * 128:(q + 1) * 128], sb["identb"][:]),
                 reads=["ldb", "identb"], writes=["PX"])
            s.op("vector", lambda e, slot=slot: e.tensor_copy(
                 sb["stg2"][:, slot * 128:(slot + 1) * 128], PXb[:, slot * 128:(slot + 1) * 128]),
                 reads=["PX"], writes=[("stg2", slot)])
            s.dma("sync", t["outwT"][q * 128:(q + 1) * 128, rc * 128:(rc + 1) * 128],
                  sb["stg2"][:, slot * 128:(slot + 1) * 128],
                  reads=[("stg2", slot)], writes=["outwT_d"])
    for (wsrc, part) in ((t["Wy"], 0), (t["Wq"], 1), (t["Wk"], 2)):
        for ch in range(8):
            src = ap(wsrc, ch * 128 * 64, [[64, 128], [1, 64]])
            s.dma("sync", sb["ldf"][:, 0:64], src, reads=["ldf"], writes=["ldf"])
            for hsub in range(2):
                h = ch * 2 + hsub
                s.op("tensor", lambda e, hsub=hsub: e.transpose(
                     PX[0:64, 0:64], sb["ldf"][hsub * 64:(hsub + 1) * 64, 0:64],
                     sb["identf"][hsub * 64:(hsub + 1) * 64, hsub * 64:(hsub + 1) * 64]),
                     reads=["ldf", "identf"], writes=["PX"])
                for cpy in range(2):
                    if part < 2:
                        dst = sb["Wcat1"][cpy * 64:(cpy + 1) * 64,
                                          h * 128 + part * 64: h * 128 + (part + 1) * 64]
                        key = "Wcat1"
                    else:
                        dst = sb["Wcat2"][cpy * 64:(cpy + 1) * 64, h * 68: h * 68 + 64]
                        key = "Wcat2"
                    s.op("vector", lambda e, dst=dst: e.tensor_copy(dst, PX[0:64, 0:64]),
                         reads=["PX"], writes=[key])
    for ch in range(4):
        src = ap(t["wb"], ch * 2 * 64 * 4, [[4, 128], [1, 4]])
        s.dma("sync", sb["ldf"][:, 0:4], src, reads=["ldf"], writes=["ldf"])
        for hsub in range(2):
            h = ch * 2 + hsub
            for cpy in range(2):
                s.op("vector", lambda e, hsub=hsub, h=h, cpy=cpy: e.tensor_copy(
                     sb["Wcat2"][cpy * 64:(cpy + 1) * 64, h * 68 + 64: h * 68 + 68],
                     sb["ldf"][hsub * 64:(hsub + 1) * 64, 0:4]),
                     reads=["ldf"], writes=["Wcat2"])

    for tc in range(NTC):
        for b in range(SL):
            s.dma("sync", x2f[:, b * 1024:(b + 1) * 1024],
                  ap(t["x2"], b * D * TL + tc * 128, [[TL, 128], [128 * TL, 8], [1, 128]]),
                  reads=[("x2f", b)], writes=[("x2f", b)])
        for h in range(NH):
            hb = (h % 2) * 64
            dc = h // 2
            idsl = sb["identf"][hb:hb + 64, hb:hb + 64]
            s.op("vector", lambda e: e.memset(sb["bxs"][:], 0.0),
                 reads=["bxs"], writes=["bxs"])
            s.op("vector", lambda e: e.memset(sb["bxd"][:], 0.0),
                 reads=["bxd"], writes=["bxd"])
            for st in range(SL):
                xsl = lambda arr: arr[hb:hb + 64, (st * 8 + dc) * 128:(st * 8 + dc) * 128 + 128]
                s.op("tensor", lambda e, st=st: e.transpose(
                     PDv(0, 64)[:, 0:64], xsl(x2f), idsl),
                     reads=[("x2f", st), "identf"], writes=[("PD", 0)])
                s.op("vector", lambda e, st=st: e.tensor_copy(
                     Xp[:, st * HD:(st + 1) * HD], PDv(0, 64)[:, 0:64]),
                     reads=[("PD", 0)], writes=["Xp"])
                s.op("tensor", lambda e, st=st: e.matmul(
                     PA[:, 0:128], sb["Wcat1"][hb:hb + 64, h * 128:(h + 1) * 128],
                     xsl(x2f), start=True, stop=True),
                     reads=[("x2f", st), "Wcat1"], writes=["PA"])
                s.op("tensor", lambda e, st=st: e.matmul(
                     PB[0:68, 0:128], sb["Wcat2"][hb:hb + 64, h * 68:(h + 1) * 68],
                     xsl(x2f), start=True, stop=True),
                     reads=[("x2f", st), "Wcat2"], writes=["PB"])
                s.op("vector", lambda e: e.tensor_copy(sb["prj1"][:], PA[:, 0:128]),
                     reads=["PA"], writes=["prj1"])
                s.op("vector", lambda e: e.tensor_copy(sb["prj2"][0:68, :], PB[0:68, 0:128]),
                     reads=["PB"], writes=["prj2"])
                s.op("tensor", lambda e: e.transpose(PDv(0)[:, 0:128], sb["prj1"][:],
                                                     sb["identf"][:]),
                     reads=["prj1", "identf"], writes=[("PD", 0)])
                s.op("tensor", lambda e: e.transpose(PDv(1, 68)[:, 0:68], sb["prj2"][0:68, :],
                                                     sb["identf"][0:68, 0:68]),
                     reads=["prj2", "identf"], writes=[("PD", 1)])
                s.op("vector", lambda e, st=st: e.tensor_copy(
                     bxs[:, st * 256: st * 256 + 128], PDv(0)[:, 0:128]),
                     reads=[("PD", 0)], writes=["bxs"])
                s.op("vector", lambda e, st=st: e.tensor_copy(
                     bxs[:, st * 256 + 128: st * 256 + 196], PDv(1, 68)[:, 0:68]),
                     reads=[("PD", 1)], writes=["bxs"])
            if dbg and tc == 0 and h == 0:
                s.dma("sync", t["dbg_bxs"], bxs[:], reads=["bxs"], writes=["dbgk"])
            for st in range(SL):
                if st > 0:
                    s.op("gpsimd", lambda e, st=st: e.tensor_tensor(
                         sb["tmp1"][:, 0:st * HD], Kp[:, 0:st * HD],
                         ap(Xp, st * HD, [[SL * HD, 128], [0, st], [1, HD]]), op=ALU.mult),
                         reads=["Kp", "Xp", "tmp1"], writes=["tmp1"])
                    s.op("vector", lambda e, st=st: e.tensor_reduce(
                         ap(sb["gx"], 0, [[8, 128], [1, st], [0, 1]]),
                         sb["tmp1"][:, 0:st * HD].rearrange("p (s d) -> p s d", d=HD),
                         axis=AX.X, op=ALU.add),
                         reads=["tmp1"], writes=["gx"])
                    s.op("gpsimd", lambda e, st=st: e.tensor_tensor(
                         sb["tmp2"][:, 0:256 * st],
                         ap(Uall, 0, [[256 * SL, 128], [SL, 256], [1, st]]),
                         ap(sb["gx"], 0, [[8, 128], [0, 256], [1, st]]), op=ALU.mult),
                         reads=["Uall", "gx", "tmp2"], writes=["tmp2"])
                    s.op("vector", lambda e, st=st: e.tensor_reduce(
                         ap(sb["ua"], 0, [[256, 128], [1, 256], [0, 1]]),
                         sb["tmp2"][:, 0:256 * st].rearrange("p (o s) -> p o s", o=256),
                         axis=AX.X, op=ALU.add),
                         reads=["tmp2"], writes=["ua"])
                    s.op("vector", lambda e, st=st: e.tensor_add(
                         sb["ya"][:], sb["ua"][:, 0:256], bxs[:, st * 256:(st + 1) * 256]),
                         reads=["ua", "bxs", "ya"], writes=["ya"])
                else:
                    s.op("vector", lambda e: e.tensor_copy(sb["ya"][:], bxs[:, 0:256]),
                         reads=["bxs", "ya"], writes=["ya"])
                s.op("vector", lambda e: e.tensor_reduce(
                     sb["nmx"][:].unsqueeze(2),
                     sb["ya"][:, 0:192].rearrange("p (c d) -> p c d", d=HD),
                     axis=AX.X, op=ALU.max, negate=True),
                     reads=["ya"], writes=["nmx"])
                s.op("gpsimd", lambda e: e.tensor_tensor(
                     sb["sm"][:], sb["ya"][:, 0:192],
                     ap(sb["nmx"], 0, [[3, 128], [1, 3], [0, HD]]), op=ALU.add),
                     reads=["ya", "nmx", "sm"], writes=["sm"])
                s.op("scalar", lambda e: e.activation(sb["sm"][:], sb["sm"][:], AF.Exp),
                     reads=["sm"], writes=["sm"])
                s.op("vector", lambda e: e.tensor_reduce(
                     sb["smr"][:].unsqueeze(2),
                     sb["sm"][:].rearrange("p (c d) -> p c d", d=HD),
                     axis=AX.X, op=ALU.add),
                     reads=["sm"], writes=["smr"])
                s.op("vector", lambda e: e.reciprocal(sb["smr"][:], sb["smr"][:]),
                     reads=["smr"], writes=["smr"])
                s.op("gpsimd", lambda e: e.tensor_tensor(
                     sb["sm"][:], sb["sm"][:],
                     ap(sb["smr"], 0, [[3, 128], [1, 3], [0, HD]]), op=ALU.mult),
                     reads=["sm", "smr"], writes=["sm"])
                s.op("scalar", lambda e: e.activation(sb["beta"][:], sb["ya"][:, 192:196],
                                                      AF.Sigmoid),
                     reads=["ya"], writes=["beta"])
                if dbg and tc == 0 and h == 0 and st == 0:
                    s.dma("sync", t["dbg_sm"], sb["sm"][:], reads=["sm"], writes=["dbgk"])
                s.op("vector", lambda e, st=st: e.tensor_copy(
                     Kp[:, st * HD:(st + 1) * HD], sb["sm"][:, 128:192]),
                     reads=["sm", "Kp"], writes=["Kp"])
                s.op("vector", lambda e: e.tensor_sub(sb["dbuf"][:], sb["sm"][:, 64:128],
                                                      sb["sm"][:, 128:192]),
                     reads=["sm", "dbuf"], writes=["dbuf"])
                s.op("vector", lambda e: e.tensor_copy(sb["ybf"][:], sb["sm"][:, 0:64]),
                     reads=["sm", "ybf"], writes=["ybf"])
                s.op("tensor", lambda e: e.transpose(PXb[0:64, 0:128], sb["ybf"][:],
                                                     sb["identb"][:]),
                     reads=["ybf", "identb"], writes=["PX"])
                s.op("vector", lambda e, st=st: e.tensor_copy(
                     ysfm[hb:hb + 64, (st * 8 + dc) * 128:(st * 8 + dc) * 128 + 128],
                     PXb[0:64, 0:128]),
                     reads=["PX"], writes=[("ysfm", st, dc)])
                s.op("tensor", lambda e: e.transpose(PX[0:64, 128:256], sb["dbuf"][:], sb["identf"][:]),
                     reads=["dbuf", "identf"], writes=["PX"])
                s.op("vector", lambda e: e.tensor_copy(sb["dTf"][0:64, 0:128], PX[0:64, 128:256]),
                     reads=["PX"], writes=["dTf"])
                s.op("tensor", lambda e, h=h: e.matmul(
                     PA[:, 0:128], sb["Wcat1"][0:64, h * 128:(h + 1) * 128],
                     sb["dTf"][0:64, 0:128], start=True, stop=True),
                     reads=["dTf", "Wcat1"], writes=["PA"])
                s.op("tensor", lambda e, h=h: e.matmul(
                     PB[0:68, 0:128], sb["Wcat2"][0:64, h * 68:(h + 1) * 68],
                     sb["dTf"][0:64, 0:128], start=True, stop=True),
                     reads=["dTf", "Wcat2"], writes=["PB"])
                s.op("vector", lambda e: e.tensor_copy(sb["prj1"][:], PA[:, 0:128]),
                     reads=["PA"], writes=["prj1"])
                s.op("vector", lambda e: e.tensor_copy(sb["prj2"][0:68, :], PB[0:68, 0:128]),
                     reads=["PB"], writes=["prj2"])
                s.op("tensor", lambda e: e.transpose(PDv(0)[:, 0:128], sb["prj1"][:],
                                                     sb["identf"][:]),
                     reads=["prj1", "identf"], writes=[("PD", 0)])
                s.op("tensor", lambda e: e.transpose(PDv(1, 68)[:, 0:68], sb["prj2"][0:68, :],
                                                     sb["identf"][0:68, 0:68]),
                     reads=["prj2", "identf"], writes=[("PD", 1)])
                s.op("vector", lambda e: e.tensor_copy(sb["bxd"][:, 0:128], PDv(0)[:, 0:128]),
                     reads=[("PD", 0)], writes=["bxd"])
                s.op("vector", lambda e: e.tensor_copy(sb["bxd"][:, 128:196], PDv(1, 68)[:, 0:68]),
                     reads=[("PD", 1)], writes=["bxd"])
                if st > 0:
                    s.op("gpsimd", lambda e, st=st: e.tensor_tensor(
                         sb["tmp1"][:, 0:st * HD], Kp[:, 0:st * HD],
                         ap(sb["dbuf"], 0, [[64, 128], [0, st], [1, HD]]), op=ALU.mult),
                         reads=["Kp", "dbuf", "tmp1"], writes=["tmp1"])
                    s.op("vector", lambda e, st=st: e.tensor_reduce(
                         ap(sb["gx"], 0, [[8, 128], [1, st], [0, 1]]),
                         sb["tmp1"][:, 0:st * HD].rearrange("p (s d) -> p s d", d=HD),
                         axis=AX.X, op=ALU.add),
                         reads=["tmp1"], writes=["gx"])
                    s.op("gpsimd", lambda e, st=st: e.tensor_tensor(
                         sb["tmp2"][:, 0:256 * st],
                         ap(Uall, 0, [[256 * SL, 128], [SL, 256], [1, st]]),
                         ap(sb["gx"], 0, [[8, 128], [0, 256], [1, st]]), op=ALU.mult),
                         reads=["Uall", "gx", "tmp2"], writes=["tmp2"])
                    s.op("vector", lambda e, st=st: e.tensor_reduce(
                         ap(sb["ua"], 0, [[256, 128], [1, 256], [0, 1]]),
                         sb["tmp2"][:, 0:256 * st].rearrange("p (o s) -> p o s", o=256),
                         axis=AX.X, op=ALU.add),
                         reads=["tmp2"], writes=["ua"])
                    s.op("vector", lambda e: e.tensor_add(sb["ya"][:], sb["ua"][:, 0:256],
                                                          sb["bxd"][:]),
                         reads=["ua", "bxd", "ya"], writes=["ya"])
                else:
                    s.op("vector", lambda e: e.tensor_copy(sb["ya"][:], sb["bxd"][:]),
                         reads=["bxd", "ya"], writes=["ya"])
                s.op("gpsimd", lambda e, st=st: e.tensor_tensor(
                     ap(Uall, st, [[256 * SL, 128], [64 * SL, 4], [SL, 64]]),
                     ap(sb["ya"], 0, [[256, 128], [64, 4], [1, 64]]),
                     ap(sb["beta"], 0, [[4, 128], [1, 4], [0, 64]]),
                     op=ALU.mult),
                     reads=["ya", "beta", "Uall"], writes=["Uall"])
        if dbg and tc == 0:
            s.dma("sync", t["dbg_ys"], ysfm[:],
                  reads=[("ysfm", b_, d_) for b_ in range(SL) for d_ in range(DCH)], writes=["dbgk"])
        for oc in range(DCH):
            slot = oc % 2
            s.dma("sync", sb["owst"][:, slot * 1024:(slot + 1) * 1024],
                  ap(t["outwT"], oc * 128, [[D, 128], [128 * D, 8], [1, 128]]),
                  reads=[("owst", slot)], writes=[("owst", slot)])
            for b in range(SL):
                pk = ("PD", 2 + (b % 2))
                bank = PDv(2 + (b % 2))
                for dc2 in range(DCH):
                    s.op("tensor", lambda e, b=b, dc2=dc2, slot=slot, bank=bank: e.matmul(
                         bank[:, (b // 2) * 128:(b // 2) * 128 + 128],
                         sb["owst"][:, slot * 1024 + dc2 * 128: slot * 1024 + (dc2 + 1) * 128],
                         ysfm[:, (b * 8 + dc2) * 128:(b * 8 + dc2) * 128 + 128],
                         start=(dc2 == 0), stop=(dc2 == DCH - 1)),
                         reads=[("owst", slot), ("ysfm", b, dc2)], writes=[pk])
                s.op("vector", lambda e, oc=oc, b=b, bank=bank: e.tensor_copy(
                     sb["attnb"][:, (b * 8 + oc) * 128:(b * 8 + oc) * 128 + 128],
                     bank[:, (b // 2) * 128:(b // 2) * 128 + 128]),
                     reads=[pk], writes=[("attnb", b, oc)])
        if dbg and tc == 0:
            s.dma("sync", t["dbg_attn"], sb["attnb"][:],
                  reads=[("attnb", b_, o_) for b_ in range(SL) for o_ in range(DCH)], writes=["dbgk"])
        for b in range(SL):
            for dc2 in range(DCH):
                s.op("vector", lambda e, b=b, dc2=dc2: e.tensor_copy(
                     sb["rb"][:], x2f[:, b * 1024 + dc2 * 128: b * 1024 + (dc2 + 1) * 128]),
                     reads=[("x2f", b), "rb"], writes=["rb"])
                s.op("vector", lambda e: e.tensor_mul(sb["sqb"][:], sb["rb"][:], sb["rb"][:]),
                     reads=["rb"], writes=["sqb"])
                s.op("tensor", lambda e, dc2=dc2: e.matmul(
                     PA[0:1, 0:128], sb["onescol"][:], sb["rb"][:],
                     start=(dc2 == 0), stop=(dc2 == DCH - 1), skip_group_check=True),
                     reads=["rb", "onescol"], writes=["PA"])
                s.op("tensor", lambda e, dc2=dc2: e.matmul(
                     PB[0:1, 0:128], sb["onescol"][:], sb["sqb"][:],
                     start=(dc2 == 0), stop=(dc2 == DCH - 1), skip_group_check=True),
                     reads=["sqb", "onescol"], writes=["PB"])
            s.op("vector", lambda e: e.tensor_scalar_mul(sb["stats"][0:1, 0:128],
                 PA[0:1, 0:128], 1.0 / D), reads=["PA"], writes=["st0"])
            s.op("vector", lambda e: e.tensor_scalar_mul(sb["stats"][0:1, 128:256],
                 PB[0:1, 0:128], 1.0 / D), reads=["PB"], writes=["st1"])
            s.op("vector", lambda e: e.tensor_mul(sb["stats"][0:1, 256:384],
                 sb["stats"][0:1, 0:128], sb["stats"][0:1, 0:128]),
                 reads=["st0"], writes=["st2"])
            s.op("vector", lambda e: e.scalar_tensor_tensor(
                 sb["stats"][0:1, 128:256], sb["stats"][0:1, 128:256], 1e-5,
                 sb["stats"][0:1, 256:384], op0=ALU.add, op1=ALU.subtract),
                 reads=["st1", "st2"], writes=["st1"])
            s.op("scalar", lambda e: e.activation(sb["stats"][0:1, 256:384],
                 sb["stats"][0:1, 128:256], AF.Sqrt),
                 reads=["st1"], writes=["st2"])
            s.op("vector", lambda e: e.reciprocal(sb["stats"][0:1, 128:256],
                 sb["stats"][0:1, 256:384]),
                 reads=["st2"], writes=["st1"])
            s.op("tensor", lambda e: e.matmul(PV[:, 256:384], sb["onesrow"][:],
                 sb["stats"][0:1, 0:128], start=True, stop=True),
                 reads=["st0", "onesrow"], writes=["PV"])
            s.op("tensor", lambda e: e.matmul(PV[:, 384:512], sb["onesrow"][:],
                 sb["stats"][0:1, 128:256], start=True, stop=True),
                 reads=["st1", "onesrow"], writes=["PV"])
            for dc2 in range(DCH):
                s.op("vector", lambda e, b=b, dc2=dc2: e.tensor_sub(
                     sb["x3s"][:, 0:128],
                     x2f[:, b * 1024 + dc2 * 128: b * 1024 + (dc2 + 1) * 128],
                     PV[:, 256:384]),
                     reads=[("x2f", b), "PV", "x3s"], writes=["x3s"])
                s.op("vector", lambda e: e.tensor_mul(sb["x3s"][:, 128:256],
                     sb["x3s"][:, 0:128], PV[:, 384:512]),
                     reads=["x3s", "PV"], writes=["x3s2"])
                s.op("vector", lambda e, dc2=dc2: e.tensor_scalar(
                     sb["x3s"][:, 0:128], sb["x3s"][:, 128:256],
                     sb["lngc"][:, dc2:dc2 + 1], sb["lnbc"][:, dc2:dc2 + 1],
                     op0=ALU.mult, op1=ALU.add),
                     reads=["x3s2"], sreads=["lngc", "lnbc"], writes=["x3s"])
                if dbg and tc == 0 and b == 0 and dc2 == 0:
                    s.dma("sync", t["dbg_ln"], sb["x3s"][:, 0:128], reads=["x3s"], writes=["dbgk"])
                s.op("gpsimd", lambda e, b=b, dc2=dc2: e.tensor_tensor(
                     sb["x3s"][:, 128:256], sb["x3s"][:, 0:128],
                     sb["attnb"][:, (b * 8 + dc2) * 128:(b * 8 + dc2) * 128 + 128],
                     op=ALU.add),
                     reads=["x3s", ("attnb", b, dc2)], writes=["x3s2"])
                s.op("gpsimd", lambda e, b=b, dc2=dc2: e.tensor_tensor(
                     sb["rbuf"][:, dc2 * 128:(dc2 + 1) * 128], sb["x3s"][:, 128:256],
                     x2f[:, b * 1024 + dc2 * 128: b * 1024 + (dc2 + 1) * 128],
                     op=ALU.add),
                     reads=["x3s2", ("x2f", b), ("rbuf", dc2)], writes=[("rbuf", dc2)])
                s.op("vector", lambda e, dc2=dc2: e.tensor_copy(
                     sb["rb"][:], sb["rbuf"][:, dc2 * 128:(dc2 + 1) * 128]),
                     reads=[("rbuf", dc2), "rb"], writes=["rb"])
                s.op("vector", lambda e: e.tensor_mul(sb["sqb"][:], sb["rb"][:], sb["rb"][:]),
                     reads=["rb"], writes=["sqb"])
                s.op("tensor", lambda e, dc2=dc2: e.matmul(
                     PA[0:1, 0:128], sb["onescol"][:], sb["rb"][:],
                     start=(dc2 == 0), stop=(dc2 == DCH - 1), skip_group_check=True),
                     reads=["rb", "onescol"], writes=["PA"])
                s.op("tensor", lambda e, dc2=dc2: e.matmul(
                     PB[0:1, 0:128], sb["onescol"][:], sb["sqb"][:],
                     start=(dc2 == 0), stop=(dc2 == DCH - 1), skip_group_check=True),
                     reads=["sqb", "onescol"], writes=["PB"])
            if dbg and tc == 0 and b == 0:
                s.dma("sync", t["dbg_rbuf"], sb["rbuf"][:],
                      reads=[("rbuf", d_) for d_ in range(DCH)], writes=["dbgk"])
            s.op("vector", lambda e: e.tensor_scalar_mul(sb["stats"][0:1, 0:128],
                 PA[0:1, 0:128], 1.0 / D), reads=["PA"], writes=["st0"])
            s.op("vector", lambda e: e.tensor_scalar_mul(sb["stats"][0:1, 128:256],
                 PB[0:1, 0:128], 1.0 / D), reads=["PB"], writes=["st1"])
            s.op("vector", lambda e: e.tensor_mul(sb["stats"][0:1, 256:384],
                 sb["stats"][0:1, 0:128], sb["stats"][0:1, 0:128]),
                 reads=["st0"], writes=["st2"])
            s.op("vector", lambda e: e.scalar_tensor_tensor(
                 sb["stats"][0:1, 128:256], sb["stats"][0:1, 128:256], 1e-5,
                 sb["stats"][0:1, 256:384], op0=ALU.add, op1=ALU.subtract),
                 reads=["st1", "st2"], writes=["st1"])
            s.op("scalar", lambda e: e.activation(sb["stats"][0:1, 256:384],
                 sb["stats"][0:1, 128:256], AF.Sqrt),
                 reads=["st1"], writes=["st2"])
            s.op("vector", lambda e: e.reciprocal(sb["stats"][0:1, 128:256],
                 sb["stats"][0:1, 256:384]),
                 reads=["st2"], writes=["st1"])
            s.op("tensor", lambda e: e.matmul(PX[:, 256:384], sb["onesrow"][:],
                 sb["stats"][0:1, 0:128], start=True, stop=True),
                 reads=["st0", "onesrow"], writes=["PX"])
            s.op("tensor", lambda e: e.matmul(PX[:, 384:512], sb["onesrow"][:],
                 sb["stats"][0:1, 128:256], start=True, stop=True),
                 reads=["st1", "onesrow"], writes=["PX"])
            for dc2 in range(DCH):
                s.op("vector", lambda e, dc2=dc2: e.tensor_sub(
                     sb["x3s"][:, 0:128], sb["rbuf"][:, dc2 * 128:(dc2 + 1) * 128],
                     PX[:, 256:384]),
                     reads=[("rbuf", dc2), "PX", "x3s"], writes=["x3s"])
                s.op("vector", lambda e: e.tensor_mul(sb["x3s"][:, 128:256],
                     sb["x3s"][:, 0:128], PX[:, 384:512]),
                     reads=["x3s", "PX"], writes=["x3s2"])
                s.dma("sync", ap(t["x3"], b * D * TL + dc2 * 128 * TL + tc * 128,
                                 [[TL, 128], [1, 128]]),
                      sb["x3s"][:, 128:256],
                      reads=["x3s2"], writes=[("x3", tc, b, dc2)], force_inc=True)
    if final:
        s.final_wait()


D, E, FE, FS = 1024, 8, 512, 1024
DCH = 8


def declare_dram_d(nc, TT, NB, suffix="", dbg=False, x3=None):
    t = {}
    ein = lambda n, s: nc.dram_tensor(n + suffix, s, F32, kind="ExternalInput").ap()
    t["x3"] = x3 if x3 is not None else ein("x3", [NB, D, TT])
    t["routerw"] = ein("routerw", [D, E])
    t["wgate"] = ein("wgate", [E, D, FE])
    t["wup"] = ein("wup", [E, D, FE])
    t["wdown"] = ein("wdown", [E, FE, D])
    t["wsgate"] = ein("wsgate", [D, FS])
    t["wsup"] = ein("wsup", [D, FS])
    t["wsdown"] = ein("wsdown", [FS, D])
    t["x4"] = nc.dram_tensor("x4" + suffix, [NB, D, TT], F32, kind="ExternalOutput").ap()
    if dbg:
        t["dbg_wts"] = nc.dram_tensor("dbg_wts" + suffix, [128, 32], F32, kind="ExternalOutput").ap()
    if dbg:
        t["dbg_act"] = nc.dram_tensor("dbg_act" + suffix, [128, 512], BF16, kind="ExternalOutput").ap()
    if dbg:
        t["dbg_out"] = nc.dram_tensor("dbg_out" + suffix, [128, DCH * 512], F32, kind="ExternalOutput").ap()
    t["wgate_b"] = nc.dram_tensor("wgate_b" + suffix, [E, D, FE], BF16, kind="Internal").ap()
    t["wup_b"] = nc.dram_tensor("wup_b" + suffix, [E, D, FE], BF16, kind="Internal").ap()
    t["wdown_b"] = nc.dram_tensor("wdown_b" + suffix, [E, FE, D], BF16, kind="Internal").ap()
    t["wsgate_b"] = nc.dram_tensor("wsgate_b" + suffix, [D, FS], BF16, kind="Internal").ap()
    t["wsup_b"] = nc.dram_tensor("wsup_b" + suffix, [D, FS], BF16, kind="Internal").ap()
    t["wsdown_b"] = nc.dram_tensor("wsdown_b" + suffix, [FS, D], BF16, kind="Internal").ap()
    return t


def alloc_bufs_d(ctx, nc, TT, suffix=""):
    sb, ps = {}, {}
    s_ = lambda n, shape, dt: ctx.enter_context(nc.sbuf_tensor(n + suffix, shape, dt))
    p_ = lambda n, shape, dt: ctx.enter_context(nc.psum_tensor(n + suffix, shape, dt))
    NTC = TT // 128
    sb["x3f"] = s_("x3f", [128, DCH * TT], F32)
    sb["xb"] = s_("xb", [128, DCH * TT], BF16)
    sb["actg"] = s_("actg", [128, E * 4 * TT], BF16)
    sb["acts"] = s_("acts", [128, 8 * TT], BF16)
    sb["wst"] = s_("wst", [128, 4 * 1024], BF16)
    sb["wdst"] = s_("wdst", [128, 2 * 2048], BF16)
    sb["wrep"] = s_("wrep", [128, E * TT], BF16)
    sb["outacc"] = s_("outacc", [128, DCH * TT], F32)
    sb["routT"] = s_("routT", [128, 64], BF16)
    sb["gs1"] = s_("gs1", [128, 32], F32)
    sb["gs2"] = s_("gs2", [128, 32], F32)
    sb["wts"] = s_("wts", [128, 32], F32)
    sb["gm"] = s_("gm", [128, 4], F32)
    sb["gm2"] = s_("gm2", [128, 4], F32)
    sb["gmr"] = s_("gmr", [128, 4], F32)
    sb["idxu"] = s_("idxu", [128, 32], F32)
    sb["idxui"] = s_("idxui", [128, 32], mybir.dt.int32)
    sb["wEm"] = s_("wEm", [128, TT], F32)
    sb["wEmb"] = s_("wEmb", [8, 512], BF16)
    sb["selc"] = s_("selc", [8, 1024], BF16)
    sb["identf"] = s_("identf", [128, 128], F32)
    sb["onescol"] = s_("onescol", [128, 1], BF16)
    sb["onesrow"] = s_("onesrow", [1, 128], F32)
    sb["silu"] = s_("silu", [128, TT], BF16)
    sb["h3w"] = s_("h3w", [128, TT], BF16)
    sb["ldf"] = s_("ldf", [128, 2048], F32)
    sb["ldb"] = s_("ldb", [128, 2048], BF16)
    sb["rb"] = s_("rb", [128, TT], BF16)
    sb["sqb"] = s_("sqb", [128, TT], BF16)
    sb["xio"] = s_("xio", [128, 2 * TT], F32)
    sb["x4s"] = s_("x4s", [128, 2 * TT], F32)
    ps["PA"] = p_("PA", [128, 512], F32)
    ps["PB"] = p_("PB", [128, 512], F32)
    ps["PD"] = p_("PD", [128, 2048], F32)
    ps["PV"] = p_("PV", [128, 512], F32)
    ps["PX"] = p_("PX", [128, 512], F32)
    return sb, ps


def emit_d(s: Sched, t, sb, ps, TT, NB, dbg=False):
    nc = s.nc
    PA, PB, PD, PV, PX = ps["PA"], ps["PB"], ps["PD"], ps["PV"], ps["PX"]
    NTC = TT // 128
    PDv = lambda k, n=512: PD[:, k * 512:k * 512 + n]
    x3f, xb, actg, acts = sb["x3f"], sb["xb"], sb["actg"], sb["acts"]
    outacc, wrep, xio = sb["outacc"], sb["wrep"], sb["xio"]
    g3 = lambda a: a.rearrange("p (c e) -> p c e", e=8)

    s.op("gpsimd", lambda e: e.memset(sb["identf"][:], 0.0), writes=["identf"])
    s.op("gpsimd", lambda e: e.affine_select(
        out=sb["identf"][:], in_=sb["identf"][:], compare_op=ALU.not_equal,
        fill=1.0, base=0, pattern=[[-1, 128]], channel_multiplier=1),
        reads=["identf"], writes=["identf"])
    s.op("vector", lambda e: e.memset(sb["onescol"][:], 1.0), writes=["onescol"])
    s.op("vector", lambda e: e.memset(sb["onesrow"][:], 1.0), writes=["onesrow"])
    s.op("vector", lambda e: e.memset(PD[:], 0.0), writes=[("PD", k) for k in range(4)])
    s.op("gpsimd", lambda e: e.iota(sb["idxui"][:], base=1, pattern=[[0, 4], [1, 8]],
                                    channel_multiplier=0), writes=["idxui"])
    s.op("vector", lambda e: e.tensor_copy(sb["idxu"][:], sb["idxui"][:]),
         reads=["idxui"], writes=["idxu"])
    s.op("gpsimd", lambda e: e.memset(sb["selc"][:], 0.0), writes=["selc"])
    s.op("gpsimd", lambda e: e.affine_select(
        out=sb["selc"][:], in_=sb["selc"][:], compare_op=ALU.not_equal,
        fill=1.0, base=0, pattern=[[-1, 8], [0, 128]], channel_multiplier=1),
        reads=["selc"], writes=["selc"])
    for (fsrc, fdst, kp) in (("wgate", "wgate_b", "cg"), ("wup", "wup_b", "cu"),
                             ("wdown", "wdown_b", "cd"), ("wsgate", "wsgate_b", "csg"),
                             ("wsup", "wsup_b", "csu"), ("wsdown", "wsdown_b", "csd")):
        total = 1
        for d_ in t[fsrc].shape:
            total *= d_
        nblk = total // (128 * 2048)
        for blk in range(nblk):
            s.dma("sync", sb["ldf"][:, 0:2048],
                  ap(t[fsrc], blk * 128 * 2048, [[2048, 128], [1, 2048]]),
                  reads=["ldf"], writes=["ldf"])
            s.op("vector", lambda e: e.tensor_copy(sb["ldb"][:, 0:2048], sb["ldf"][:, 0:2048]),
                 reads=["ldf", "ldb"], writes=["ldb"])
            s.dma("sync", ap(t[fdst], blk * 128 * 2048, [[2048, 128], [1, 2048]]),
                  sb["ldb"][:, 0:2048],
                  reads=["ldb"], writes=[("wb",)])
    for dc in range(DCH):
        s.dma("sync", sb["ldf"][:, 0:8], t["routerw"][dc * 128:(dc + 1) * 128, :],
              reads=["ldf"], writes=["ldf"])
        s.op("vector", lambda e, dc=dc: e.tensor_copy(sb["routT"][:, dc * 8:(dc + 1) * 8],
                                                      sb["ldf"][:, 0:8]),
             reads=["ldf"], writes=["routT"])

    for b in range(NB):
        for dc in range(DCH):
            s.dma("sync", x3f[:, dc * TT:(dc + 1) * TT], t["x3"][b, dc * 128:(dc + 1) * 128, :],
                  reads=[("x3f", dc)], writes=[("x3f", dc)])
            s.op("vector", lambda e, dc=dc: e.tensor_copy(
                 xb[:, dc * TT:(dc + 1) * TT], x3f[:, dc * TT:(dc + 1) * TT]),
                 reads=[("x3f", dc), ("xb", dc)], writes=[("xb", dc)])
        for dc in range(DCH):
            s.op("tensor", lambda e, dc=dc: e.matmul(
                 PB[0:8, 0:TT], sb["routT"][:, dc * 8:(dc + 1) * 8], xb[:, dc * TT:(dc + 1) * TT],
                 start=(dc == 0), stop=(dc == DCH - 1)),
                 reads=[("xb", dc), "routT"], writes=["PB"])
        s.op("vector", lambda e: e.tensor_copy(sb["wEm"][0:8, 0:TT], PB[0:8, 0:TT]),
             reads=["PB"], writes=["wEm"])
        for tc in range(NTC):
            s.op("tensor", lambda e, tc=tc: e.transpose(
                 PV[:, tc * 8:(tc + 1) * 8], sb["wEm"][0:8, tc * 128:(tc + 1) * 128],
                 sb["identf"][0:8, 0:8]),
                 reads=["wEm", "identf"], writes=["PV"])
        s.op("vector", lambda e: e.tensor_copy(sb["gs1"][:, 0:NTC * 8], PV[:, 0:NTC * 8]),
             reads=["PV"], writes=["gs1"])
        s.op("vector", lambda e: e.tensor_reduce(sb["gm"][:, 0:NTC].unsqueeze(2),
             g3(sb["gs1"][:, 0:NTC * 8]), axis=AX.X, op=ALU.max, negate=True),
             reads=["gs1"], writes=["gm"])
        s.op("vector", lambda e: e.tensor_tensor(sb["gs2"][:, 0:NTC * 8], sb["gs1"][:, 0:NTC * 8],
             ap(sb["gm"], 0, [[4, 128], [1, NTC], [0, 8]]), op=ALU.add),
             reads=["gs1", "gm"], writes=["gs2"])
        s.op("scalar", lambda e: e.activation(sb["gs1"][:, 0:NTC * 8], sb["gs2"][:, 0:NTC * 8],
                                              AF.Exp),
             reads=["gs2"], writes=["gs1"])
        s.op("vector", lambda e: e.tensor_reduce(sb["gm"][:, 0:NTC].unsqueeze(2),
             g3(sb["gs1"][:, 0:NTC * 8]), axis=AX.X, op=ALU.add),
             reads=["gs1"], writes=["gm"])
        s.op("vector", lambda e: e.reciprocal(sb["gmr"][:, 0:NTC], sb["gm"][:, 0:NTC]),
             reads=["gm"], writes=["gmr"])
        s.op("vector", lambda e: e.tensor_tensor(sb["wts"][:, 0:NTC * 8], sb["gs1"][:, 0:NTC * 8],
             ap(sb["gmr"], 0, [[4, 128], [1, NTC], [0, 8]]), op=ALU.mult),
             reads=["gs1", "gmr"], writes=["wts"])
        s.op("vector", lambda e: e.tensor_copy(sb["gs2"][:, 0:NTC * 8], sb["wts"][:, 0:NTC * 8]),
             reads=["wts"], writes=["gs2"])
        for it in range(2):
            s.op("vector", lambda e: e.tensor_reduce(sb["gm"][:, 0:NTC].unsqueeze(2),
                 g3(sb["gs2"][:, 0:NTC * 8]), axis=AX.X, op=ALU.min),
                 reads=["gs2"], writes=["gm"])
            s.op("vector", lambda e: e.tensor_tensor(sb["gs1"][:, 0:NTC * 8],
                 sb["gs2"][:, 0:NTC * 8],
                 ap(sb["gm"], 0, [[4, 128], [1, NTC], [0, 8]]), op=ALU.is_equal),
                 reads=["gs2", "gm"], writes=["gs1"])
            s.op("gpsimd", lambda e: e.tensor_tensor(sb["gs1"][:, 0:NTC * 8],
                 sb["gs1"][:, 0:NTC * 8], sb["idxu"][:, 0:NTC * 8], op=ALU.mult),
                 reads=["gs1", "idxu"], writes=["gs1"])
            s.op("vector", lambda e: e.tensor_reduce(sb["gm2"][:, 0:NTC].unsqueeze(2),
                 g3(sb["gs1"][:, 0:NTC * 8]), axis=AX.X, op=ALU.max),
                 reads=["gs1"], writes=["gm2"])
            s.op("vector", lambda e: e.tensor_tensor(sb["gs1"][:, 0:NTC * 8],
                 sb["gs1"][:, 0:NTC * 8],
                 ap(sb["gm2"], 0, [[4, 128], [1, NTC], [0, 8]]), op=ALU.is_equal),
                 reads=["gs1", "gm2"], writes=["gs1"])
            s.op("vector", lambda e: e.scalar_tensor_tensor(sb["gs2"][:, 0:NTC * 8],
                 sb["gs1"][:, 0:NTC * 8], 1e30, sb["gs2"][:, 0:NTC * 8],
                 op0=ALU.mult, op1=ALU.add),
                 reads=["gs1", "gs2"], writes=["gs2"])
            s.op("gpsimd", lambda e: e.tensor_tensor(
                 sb["gs1"][:, 0:NTC * 8], sb["gs1"][:, 0:NTC * 8], sb["wts"][:, 0:NTC * 8],
                 op=ALU.mult),
                 reads=["gs1", "wts"], writes=["gs1"])
            s.op("vector", lambda e: e.tensor_sub(sb["wts"][:, 0:NTC * 8],
                 sb["wts"][:, 0:NTC * 8], sb["gs1"][:, 0:NTC * 8]),
                 reads=["wts", "gs1"], writes=["wts"])
        if dbg and b == 0:
            s.dma("sync", t["dbg_wts"][:, 0:NTC * 8], sb["wts"][:, 0:NTC * 8],
                  reads=["wts"], writes=["dbgk"])
        for tc in range(NTC):
            s.op("tensor", lambda e, tc=tc: e.transpose(
                 PX[0:8, tc * 128:(tc + 1) * 128],
                 sb["wts"][:, tc * 8:(tc + 1) * 8], sb["identf"][:]),
                 reads=["wts", "identf"], writes=["PX"])
        s.op("vector", lambda e: e.tensor_copy(sb["wEmb"][0:8, 0:TT], PX[0:8, 0:TT]),
             reads=["PX"], writes=["wEmb"])
        for e_ in range(E):
            s.op("tensor", lambda e, e_=e_: e.matmul(
                 PA[:, 0:TT], sb["selc"][:, e_ * 128:(e_ + 1) * 128], sb["wEmb"][0:8, 0:TT],
                 start=True, stop=True),
                 reads=["wEmb", "selc"], writes=["PA"])
            s.op("vector", lambda e, e_=e_: e.tensor_copy(
                 wrep[:, e_ * TT:(e_ + 1) * TT], PA[:, 0:TT]),
                 reads=["PA"], writes=[("wrep", e_)])
        for e_ in range(E):
            for fc in range(4):
                slot = fc % 2
                s.dma("sync", sb["wst"][:, slot * 1024:(slot + 1) * 1024],
                      ap(t["wgate_b"], e_ * D * FE + fc * 128, [[FE, 128], [128 * FE, 8], [1, 128]]),
                      reads=[("wst", slot), ("wb",)], writes=[("wst", slot)])
                s.dma("sync", sb["wst"][:, (2 + slot) * 1024:(3 + slot) * 1024],
                      ap(t["wup_b"], e_ * D * FE + fc * 128, [[FE, 128], [128 * FE, 8], [1, 128]]),
                      reads=[("wst", 2 + slot), ("wb",)], writes=[("wst", 2 + slot)])
                for dc in range(DCH):
                    s.op("tensor", lambda e, dc=dc, slot=slot: e.matmul(
                         PA[:, 0:TT], sb["wst"][:, slot * 1024 + dc * 128: slot * 1024 + (dc + 1) * 128],
                         xb[:, dc * TT:(dc + 1) * TT],
                         start=(dc == 0), stop=(dc == DCH - 1)),
                         reads=[("wst", slot), ("xb", dc)], writes=["PA"])
                    s.op("tensor", lambda e, dc=dc, slot=slot: e.matmul(
                         PB[:, 0:TT], sb["wst"][:, (2 + slot) * 1024 + dc * 128: (2 + slot) * 1024 + (dc + 1) * 128],
                         xb[:, dc * TT:(dc + 1) * TT],
                         start=(dc == 0), stop=(dc == DCH - 1)),
                         reads=[("wst", 2 + slot), ("xb", dc)], writes=["PB"])
                s.op("scalar", lambda e: e.activation(sb["silu"][:], PA[:, 0:TT], AF.Silu),
                     reads=["PA"], writes=["silu"])
                s.op("vector", lambda e, e_=e_: e.tensor_tensor(
                     sb["h3w"][:], PB[:, 0:TT], wrep[:, e_ * TT:(e_ + 1) * TT], op=ALU.mult),
                     reads=["PB", ("wrep", e_), "h3w"], writes=["h3w"])
                s.op("gpsimd", lambda e, e_=e_, fc=fc: e.tensor_tensor(
                     actg[:, (e_ * 4 + fc) * TT:(e_ * 4 + fc + 1) * TT],
                     sb["h3w"][:], sb["silu"][:], op=ALU.mult),
                     reads=["h3w", "silu"], writes=[("actg", e_, fc)])
        if dbg and b == 0:
            s.dma("sync", t["dbg_act"][:, 0:TT], actg[:, 0:TT], reads=[("actg", 0, 0)], writes=["dbgk"])
        for fc in range(8):
            slot = fc % 2
            s.dma("sync", sb["wst"][:, slot * 1024:(slot + 1) * 1024],
                  ap(t["wsgate_b"], fc * 128, [[FS, 128], [128 * FS, 8], [1, 128]]),
                  reads=[("wst", slot), ("wb",)], writes=[("wst", slot)])
            s.dma("sync", sb["wst"][:, (2 + slot) * 1024:(3 + slot) * 1024],
                  ap(t["wsup_b"], fc * 128, [[FS, 128], [128 * FS, 8], [1, 128]]),
                  reads=[("wst", 2 + slot), ("wb",)], writes=[("wst", 2 + slot)])
            for dc in range(DCH):
                s.op("tensor", lambda e, dc=dc, slot=slot: e.matmul(
                     PA[:, 0:TT], sb["wst"][:, slot * 1024 + dc * 128: slot * 1024 + (dc + 1) * 128],
                     xb[:, dc * TT:(dc + 1) * TT],
                     start=(dc == 0), stop=(dc == DCH - 1)),
                     reads=[("wst", slot), ("xb", dc)], writes=["PA"])
                s.op("tensor", lambda e, dc=dc, slot=slot: e.matmul(
                     PB[:, 0:TT], sb["wst"][:, (2 + slot) * 1024 + dc * 128: (2 + slot) * 1024 + (dc + 1) * 128],
                     xb[:, dc * TT:(dc + 1) * TT],
                     start=(dc == 0), stop=(dc == DCH - 1)),
                     reads=[("wst", 2 + slot), ("xb", dc)], writes=["PB"])
            s.op("scalar", lambda e: e.activation(sb["silu"][:], PA[:, 0:TT], AF.Silu),
                 reads=["PA"], writes=["silu"])
            s.op("vector", lambda e: e.tensor_copy(sb["h3w"][:], PB[:, 0:TT]),
                 reads=["PB", "h3w"], writes=["h3w"])
            s.op("gpsimd", lambda e, fc=fc: e.tensor_tensor(
                 acts[:, fc * TT:(fc + 1) * TT], sb["h3w"][:], sb["silu"][:], op=ALU.mult),
                 reads=["h3w", "silu"], writes=[("acts", fc)])
        for dcg in range(2):
            for e_ in range(E):
                slot = e_ % 2
                s.dma("sync", sb["wdst"][:, slot * 2048:(slot + 1) * 2048],
                      ap(t["wdown_b"], e_ * FE * D + dcg * 512,
                         [[D, 128], [128 * D, 4], [1, 512]]),
                      reads=[("wdst", slot), ("wb",)], writes=[("wdst", slot)])
                for dc4 in range(4):
                    for fc in range(4):
                        s.op("tensor", lambda e, e_=e_, fc=fc, dc4=dc4, slot=slot: e.matmul(
                             PDv(dc4)[:, 0:TT],
                             sb["wdst"][:, slot * 2048 + fc * 512 + dc4 * 128:
                                        slot * 2048 + fc * 512 + (dc4 + 1) * 128],
                             actg[:, (e_ * 4 + fc) * TT:(e_ * 4 + fc + 1) * TT],
                             start=(e_ == 0 and fc == 0), stop=False, skip_group_check=True),
                             reads=[("wdst", slot), ("actg", e_, fc)], writes=[("PD", dc4)])
            for fs in range(8):
                slot = fs % 2
                s.dma("sync", sb["wdst"][:, slot * 2048: slot * 2048 + 512],
                      ap(t["wsdown_b"], fs * 128 * D + dcg * 512,
                         [[D, 128], [1, 512]]),
                      reads=[("wdst", slot), ("wb",)], writes=[("wdst", slot)])
                for dc4 in range(4):
                    s.op("tensor", lambda e, fs=fs, dc4=dc4, slot=slot: e.matmul(
                         PDv(dc4)[:, 0:TT],
                         sb["wdst"][:, slot * 2048 + dc4 * 128: slot * 2048 + (dc4 + 1) * 128],
                         acts[:, fs * TT:(fs + 1) * TT],
                         start=False, stop=(fs == 7 and dc4 == 3), skip_group_check=True),
                         reads=[("wdst", slot), ("acts", fs)], writes=[("PD", dc4)])
            for dc4 in range(4):
                dc = dcg * 4 + dc4
                s.op("vector", lambda e, dc=dc, dc4=dc4: e.tensor_copy(
                     outacc[:, dc * TT:(dc + 1) * TT], PDv(dc4)[:, 0:TT]),
                     reads=[("PD", dc4)], writes=[("outacc", dc)])
        if dbg and b == 0:
            s.dma("sync", t["dbg_out"][:, 0:DCH * TT], outacc[:],
                  reads=[("outacc", d_) for d_ in range(DCH)], writes=["dbgk"])
        for dc in range(DCH):
            s.op("vector", lambda e, dc=dc: e.tensor_add(
                 outacc[:, dc * TT:(dc + 1) * TT], outacc[:, dc * TT:(dc + 1) * TT],
                 x3f[:, dc * TT:(dc + 1) * TT]),
                 reads=[("outacc", dc), ("x3f", dc)], writes=[("outacc", dc)])
            s.op("vector", lambda e, dc=dc: e.tensor_copy(
                 sb["rb"][:], outacc[:, dc * TT:(dc + 1) * TT]),
                 reads=[("outacc", dc), "rb"], writes=["rb"])
            s.op("vector", lambda e: e.tensor_mul(sb["sqb"][:], sb["rb"][:], sb["rb"][:]),
                 reads=["rb"], writes=["sqb"])
            s.op("tensor", lambda e, dc=dc: e.matmul(
                 ap(PD, 2 * 512, [[2048, 1], [1, TT]]), sb["onescol"][:], sb["rb"][:],
                 start=(dc == 0), stop=(dc == DCH - 1), skip_group_check=True),
                 reads=["rb", "onescol"], writes=[("PD", 2)])
            s.op("tensor", lambda e, dc=dc: e.matmul(
                 ap(PD, 3 * 512, [[2048, 1], [1, TT]]), sb["onescol"][:], sb["sqb"][:],
                 start=(dc == 0), stop=(dc == DCH - 1), skip_group_check=True),
                 reads=["sqb", "onescol"], writes=[("PD", 3)])
        s.op("vector", lambda e: e.tensor_scalar_mul(xio[0:1, 0:TT], PDv(2)[0:1, 0:TT], 1.0 / D),
             reads=[("PD", 2)], writes=[("xio", 0)])
        s.op("vector", lambda e: e.tensor_scalar_mul(xio[0:1, TT:2 * TT], PDv(3)[0:1, 0:TT], 1.0 / D),
             reads=[("PD", 3)], writes=[("xio", 1)])
        s.op("vector", lambda e: e.tensor_mul(sb["wEm"][0:1, 0:TT],
             xio[0:1, 0:TT], xio[0:1, 0:TT]),
             reads=[("xio", 0)], writes=["wEm"])
        s.op("vector", lambda e: e.scalar_tensor_tensor(
             xio[0:1, TT:2 * TT], xio[0:1, TT:2 * TT], 1e-5, sb["wEm"][0:1, 0:TT],
             op0=ALU.add, op1=ALU.subtract),
             reads=[("xio", 1), "wEm"], writes=[("xio", 1)])
        s.op("scalar", lambda e: e.activation(sb["wEm"][0:1, 0:TT], xio[0:1, TT:2 * TT], AF.Sqrt),
             reads=[("xio", 1)], writes=["wEm"])
        s.op("vector", lambda e: e.reciprocal(xio[0:1, TT:2 * TT], sb["wEm"][0:1, 0:TT]),
             reads=["wEm"], writes=[("xio", 1)])
        s.op("tensor", lambda e: e.matmul(PA[:, 0:TT], sb["onesrow"][:], xio[0:1, 0:TT],
                                          start=True, stop=True),
             reads=[("xio", 0), "onesrow"], writes=["PA"])
        s.op("tensor", lambda e: e.matmul(PB[:, 0:TT], sb["onesrow"][:], xio[0:1, TT:2 * TT],
                                          start=True, stop=True),
             reads=[("xio", 1), "onesrow"], writes=["PB"])
        for dc in range(DCH):
            s.op("vector", lambda e, dc=dc: e.tensor_sub(
                 sb["x4s"][:, 0:TT], outacc[:, dc * TT:(dc + 1) * TT], PA[:, 0:TT]),
                 reads=[("outacc", dc), "PA", ("x4s", 0)], writes=[("x4s", 0)])
            s.op("vector", lambda e: e.tensor_mul(
                 sb["x4s"][:, TT:2 * TT], sb["x4s"][:, 0:TT], PB[:, 0:TT]),
                 reads=[("x4s", 0), "PB", ("x4s", 1)], writes=[("x4s", 1)])
            s.dma("sync", t["x4"][b, dc * 128:(dc + 1) * 128, :], sb["x4s"][:, TT:2 * TT],
                  reads=[("x4s", 1)], writes=[("x4", b, dc)], force_inc=True)
    s.final_wait()


import numpy as np
from contextlib import ExitStack
from concourse.bass_utils import run_bass_kernel_spmd

NCORES = 8
TLC = 512


def _bar(s, scratch, barscr, bi):
    allk = list(s.plan.last_writer.keys()) if s.mode == "plan" else ()
    s.op("vector", lambda e: e.memset(scratch, 0.0), writes=allk)
    s.dma("sync", barscr[0:1, bi * 32 + 0:bi * 32 + 8], scratch, writes=allk)
    s.dma("gpsimd", barscr[0:1, bi * 32 + 8:bi * 32 + 16], scratch, writes=allk)
    s.dma("scalar", barscr[0:1, bi * 32 + 16:bi * 32 + 24], scratch, writes=allk)


def kernel(**inp):
    f32 = lambda a: np.ascontiguousarray(np.asarray(a, dtype=np.float32))
    x = f32(inp["L0_attn_x"])
    xt = np.ascontiguousarray(x.transpose(0, 2, 1))
    xpad = np.concatenate([np.zeros((8, 1024, 64), np.float32), xt], axis=2)

    ts_ = np.arange(128)[:, None]
    us_ = np.arange(192)[None, :]
    band = np.where((us_ >= ts_ + 1) & (us_ <= ts_ + 64), 0.0, -1e30).astype(np.float32)
    mask0 = (band + np.where(us_ < 64, -1e30, 0.0)).astype(np.float32)

    nc = bass.Bass("TRN2", target_bir_lowering=False)
    x1h = nc.dram_tensor("x1h", [8, 1024, TLC], F32, kind="Internal").ap()
    x2h = nc.dram_tensor("x2h", [8, 1024, TLC], F32, kind="Internal").ap()
    x3h = nc.dram_tensor("x3h", [8, 1024, TLC], F32, kind="Internal").ap()
    barscr = nc.dram_tensor("barscr", [1, 128], F32, kind="Internal").ap()
    tA = declare_dram_a(nc, TLC, x1=x1h)
    tB = declare_dram_b(nc, TLC, 8, x1=x1h, x2=x2h)
    tC = declare_dram_c(nc, TLC, x2=x2h, x3=x3h)
    tD = declare_dram_d(nc, TLC, 8, x3=x3h)
    with ExitStack() as cA:
        sbA, psA = alloc_bufs_a(cA, nc, TLC, suffix="A")
    with ExitStack() as cB:
        sbB, psB = alloc_bufs_b(cB, nc, TLC, suffix="B")
    with ExitStack() as cC:
        sbC, psC = alloc_bufs_c(cC, nc, TLC, suffix="C")
    with ExitStack() as cD:
        sbD, psD = alloc_bufs_d(cD, nc, TLC, suffix="D")

    def emit_all(s):
        emit_a(s, tA, sbA, psA, TLC, final=False)
        _bar(s, sbA["onesrow"][0:1, 0:8], barscr, 0)
        emit_b(s, tB, sbB, psB, TLC, 8, final=False)
        _bar(s, sbB["onesrow"][0:1, 0:8], barscr, 1)
        emit_c(s, tC, sbC, psC, TLC, final=False)
        _bar(s, sbC["onesrow"][0:1, 0:8], barscr, 2)
        emit_d(s, tD, sbD, psD, TLC, 8)

    play(nc, emit_all)

    wAll = dict(Wdisp=f32(inp["L0_attn_W_disp"]), Wval=f32(inp["L0_attn_W_val"]),
                Wcpj=f32(inp["L0_attn_c_proj_w"]), relpos=f32(inp["L0_attn_rel_pos_emb"]),
                ppw=f32(inp["L0_attn_pos_proj_w"]), sfw=f32(inp["L0_attn_strain_fused_w"]),
                sfb=f32(inp["L0_attn_strain_fused_b"]), bow=f32(inp["L0_attn_bond_out_w"]),
                dow=f32(inp["L0_attn_damage_out_w"]), dob=f32(inp["L0_attn_damage_out_b"]),
                normw=f32(inp["L0_ffn_norm_w"]), w1=f32(inp["L0_ffn_w1"]),
                w3=f32(inp["L0_ffn_w3"]), w2=f32(inp["L0_ffn_w2"]),
                gatew=f32(inp["L0_ffn_gate_w"]), a1=f32(inp["L0_ffn_a1"]),
                b1=f32(inp["L0_ffn_b1"]), a3=f32(inp["L0_ffn_a3"]),
                b3=f32(inp["L0_ffn_b3"]), a2=f32(inp["L0_ffn_a2"]),
                b2=f32(inp["L0_ffn_b2"]),
                Wy=f32(inp["L1_attn_W_y"]), Wq=f32(inp["L1_attn_W_q"]),
                Wk=f32(inp["L1_attn_W_k"]), wb=f32(inp["L1_attn_w_b"]),
                outw=f32(inp["L1_attn_out_w"]), lng=f32(inp["L1_attn_ln_g"]),
                lnb=f32(inp["L1_attn_ln_b"]),
                routerw=f32(inp["L1_ffn_router_w"]), wgate=f32(inp["L1_ffn_w_gate"]),
                wup=f32(inp["L1_ffn_w_up"]), wdown=f32(inp["L1_ffn_w_down"]),
                wsgate=f32(inp["L1_ffn_ws_gate"]), wsup=f32(inp["L1_ffn_ws_up"]),
                wsdown=f32(inp["L1_ffn_ws_down"]))
    in_maps = []
    for c in range(NCORES):
        m = dict(wAll)
        m["xA"] = np.ascontiguousarray(xpad[:, :, c * TLC: c * TLC + TLC + 64])
        m["maskA"] = mask0 if c == 0 else band
        m["maskB"] = band
        in_maps.append(m)
    res = run_bass_kernel_spmd(nc, in_maps, core_ids=list(range(NCORES)))
    out = np.concatenate([r["x4"] for r in res.results], axis=2)
    return np.ascontiguousarray(out.transpose(0, 2, 1))


# version 05 (final)
# speedup vs baseline: 1.0144x; 1.0104x over previous
import numpy as np
import concourse.bass as bass
import concourse.mybir as mybir
import numpy as np
import concourse.bass as bass
import concourse.mybir as mybir

F32 = mybir.dt.float32
BF16 = mybir.dt.bfloat16
AF = mybir.ActivationFunctionType
ALU = mybir.AluOpType
AX = mybir.AxisListType

ENGINES = ("tensor", "vector", "scalar", "gpsimd", "sync")
DMA_QUEUES = ("sync", "gpsimd", "scalar")


class Plan:
    def __init__(self):
        self.n = 0
        self.op_engine = []
        self.op_isdma = []
        self.needs_inc = []
        self.deps = []
        self.last_writer = {}
        self.readers = {}
        self.eng_seq = {}
        self.counts = {e: 0 for e in ENGINES}
        self.dma_counts = {e: 0 for e in DMA_QUEUES}
        self.inc_rank = {}

    def record(self, engine, isdma, reads, writes, sreads, force_inc=False):
        i = self.n
        self.n += 1
        self.op_engine.append(engine)
        self.op_isdma.append(isdma)
        self.needs_inc.append(bool(force_inc))

        def same_pipe(j):
            return (not isdma and not self.op_isdma[j]
                    and engine == "tensor" and self.op_engine[j] == "tensor")

        dep = set()
        for r in list(reads) + list(sreads):
            for w in self.last_writer.get(r, ()):
                forced = r in sreads
                if not same_pipe(w) or forced:
                    dep.add(w)
            self.readers.setdefault(r, []).append(i)
        for r in writes:
            for w in self.last_writer.get(r, ()):
                if not same_pipe(w):
                    dep.add(w)
            for rd in self.readers.get(r, []):
                if rd != i and not same_pipe(rd):
                    dep.add(rd)
            prev = self.last_writer.get(r, ())
            self.last_writer[r] = [i]
            self.readers[r] = []
        self.deps.append(sorted(dep))
        for d in dep:
            self.needs_inc[d] = True
        return i

    def finalize(self):
        counters = {}
        self.rank = [None] * self.n
        for i in range(self.n):
            g = (self.op_engine[i], self.op_isdma[i])
            if self.needs_inc[i]:
                counters[g] = counters.get(g, 0) + 1
                self.rank[i] = counters[g]
        self.group_totals = counters
        self.flush_after = set()
        for i in range(self.n):
            for d in self.deps[i]:
                if self.op_isdma[d]:
                    self.flush_after.add(d)


class Sched:
    def __init__(self, nc, mode, eng, plan: Plan, sems):
        self.nc = nc
        self.mode = mode
        self.eng = eng
        self.plan = plan
        self.sems = sems
        self.idx = 0
        self.hwm = {}

    def _emit_waits(self, i):
        need = {}
        for d in self.plan.deps[i]:
            g = (self.plan.op_engine[d], self.plan.op_isdma[d])
            v = self.plan.rank[d] * (16 if g[1] else 1)
            if need.get(g, 0) < v:
                need[g] = v
        for g, v in sorted(need.items(), key=lambda kv: str(kv[0])):
            if self.hwm.get(g, 0) >= v:
                continue
            self.hwm[g] = v
            self.eng.wait_ge(self.sems[g], v)

    def op(self, engine, fn, reads=(), writes=(), sreads=(), force_inc=False):
        if self.mode == "plan":
            return self.plan.record(engine, False, reads, writes, sreads, force_inc)
        i = self.idx
        self.idx += 1
        assert self.plan.op_engine[i] == engine and not self.plan.op_isdma[i]
        if self.mode == engine:
            self._emit_waits(i)
            ins = fn(self.eng)
            if self.plan.needs_inc[i]:
                ins.then_inc(self.sems[(engine, False)], 1)
        return i

    def dma(self, queue, out, in_, reads=(), writes=(), force_inc=True, **kw):
        if self.mode == "plan":
            return self.plan.record(queue, True, reads, writes, (), force_inc)
        i = self.idx
        self.idx += 1
        assert self.plan.op_engine[i] == queue and self.plan.op_isdma[i]
        if self.mode == queue:
            self._emit_waits(i)
            ins = self.eng.dma_start(out=out, in_=in_, **kw)
            if self.plan.needs_inc[i]:
                ins.then_inc(self.sems[(queue, True)], 16)
            if i in self.plan.flush_after:
                g = (queue, True)
                v = 16 * self.plan.rank[i]
                if self.hwm.get(g, 0) < v:
                    self.hwm[g] = v
                    self.eng.wait_ge(self.sems[g], v)
        return i

    def final_wait(self):
        if self.mode == "plan":
            return
        if self.mode == "sync":
            for g, total in sorted(self.plan.group_totals.items(), key=lambda kv: str(kv[0])):
                v = total * (16 if g[1] else 1)
                if self.hwm.get(g, 0) < v:
                    self.eng.wait_ge(self.sems[g], v)


def play(nc, schedule_fn, *args, **kwargs):
    plan = Plan()
    schedule_fn(Sched(nc, "plan", None, plan, None), *args, **kwargs)
    plan.finalize()

    from contextlib import ExitStack
    with ExitStack() as ctx:
        sems = {}
        for e in ENGINES:
            sems[(e, False)] = ctx.enter_context(nc.semaphore(f"s_{e}"))
        for q in DMA_QUEUES:
            sems[(q, True)] = ctx.enter_context(nc.semaphore(f"s_dma_{q}"))
        with nc.Block() as block:
            @block.tensor
            def _(eng):
                schedule_fn(Sched(nc, "tensor", eng, plan, sems), *args, **kwargs)

            @block.vector
            def _(eng):
                schedule_fn(Sched(nc, "vector", eng, plan, sems), *args, **kwargs)

            @block.scalar
            def _(eng):
                schedule_fn(Sched(nc, "scalar", eng, plan, sems), *args, **kwargs)

            @block.gpsimd
            def _(eng):
                schedule_fn(Sched(nc, "gpsimd", eng, plan, sems), *args, **kwargs)

            @block.sync
            def _(eng):
                schedule_fn(Sched(nc, "sync", eng, plan, sems), *args, **kwargs)
    return plan


def ap(t, offset, dims):
    if isinstance(t, bass.AP):
        assert not isinstance(t.offset, str)
        offset = t.offset + offset
        t = t.tensor
    return bass.AP(t, offset, dims)


B, D, NH, W, HS, BDm = 8, 1024, 16, 64, 64, 16
DCH = 8
NHG = 4


def declare_dram_a(nc, TL, suffix="", dbg=False, x1=None):
    TP = TL + 64
    t = {}
    ein = lambda n, s: nc.dram_tensor(n + suffix, s, F32, kind="ExternalInput").ap()
    t["xA"] = ein("xA", [B, D, TP])
    t["Wdisp"] = ein("Wdisp", [256, 1024])
    t["Wval"] = ein("Wval", [1024, 1024])
    t["Wcpj"] = ein("Wcpj", [1024, 1024])
    t["relpos"] = ein("relpos", [64, 16])
    t["ppw"] = ein("ppw", [16, 16])
    t["sfw"] = ein("sfw", [32, 16])
    t["sfb"] = ein("sfb", [32])
    t["bow"] = ein("bow", [1, 16])
    t["dow"] = ein("dow", [1, 16])
    t["dob"] = ein("dob", [1])
    t["maskA"] = ein("maskA", [128, 192])
    t["maskB"] = ein("maskB", [128, 192])
    t["x1"] = x1 if x1 is not None else nc.dram_tensor("x1" + suffix, [B, D, TL], F32, kind="ExternalOutput").ap()
    t["wvalT"] = nc.dram_tensor("wvalT" + suffix, [1024, 1024], BF16, kind="Internal").ap()
    t["cprojT"] = nc.dram_tensor("cprojT" + suffix, [1024, 1024], BF16, kind="Internal").ap()
    t["shear"] = nc.dram_tensor("shearA" + suffix, [4, 2, 4, TL, TP], F32, kind="Internal").ap()
    if dbg:
        t["dbg_dproj"] = nc.dram_tensor("dbg_dproj" + suffix, [128, TP], BF16, kind="ExternalOutput").ap()
    if dbg:
        t["dbg_valt"] = nc.dram_tensor("dbg_valt" + suffix, [128, 1024], BF16, kind="ExternalOutput").ap()
    if dbg:
        t["dbg_raw"] = nc.dram_tensor("dbg_raw" + suffix, [128, 8 * TL], F32, kind="ExternalOutput").ap()
    if dbg:
        t["dbg_bondR"] = nc.dram_tensor("dbg_bondR" + suffix, [128, 768], F32, kind="ExternalOutput").ap()
    if dbg:
        t["dbg_dmgR"] = nc.dram_tensor("dbg_dmgR" + suffix, [128, 768], F32, kind="ExternalOutput").ap()
    if dbg:
        t["dbg_wcast"] = nc.dram_tensor("dbg_wcast" + suffix, [128, 768], BF16, kind="ExternalOutput").ap()
    if dbg:
        t["dbg_attn"] = nc.dram_tensor("dbg_attn" + suffix, [128, 8 * TL], BF16, kind="ExternalOutput").ap()
    return t


def alloc_bufs_a(ctx, nc, TL, suffix=""):
    TP = TL + 64
    TCHP = (TP + 127) // 128
    sb = {}
    s_ = lambda n, shape, dt: ctx.enter_context(nc.sbuf_tensor(n + suffix, shape, dt))
    sb["xb"] = s_("xb", [128, DCH * TP], BF16)
    sb["xio"] = s_("xio", [128, 2 * TP], F32)
    sb["valt"] = s_("valt", [128, 4 * TCHP * 1024], BF16)
    sb["dproj"] = s_("dproj", [128, 16 * TP], BF16)
    sb["gbuf"] = s_("gbuf", [128, 4 * 8 * TL], BF16)
    sb["raw"] = s_("raw", [128, 8 * TL], F32)
    sb["tanhb"] = s_("tanhb", [128, 8 * TL], F32)
    sb["bondR"] = s_("bondR", [128, 768], F32)
    sb["dmgR"] = s_("dmgR", [128, 768], F32)
    sb["S3"] = s_("S3", [128, 768], F32)
    sb["S4"] = s_("S4", [128, 768], F32)
    sb["wcast"] = s_("wcast", [128, 768], BF16)
    sb["negmax"] = s_("negmax", [128, 4], F32)
    sb["sumR"] = s_("sumR", [128, 4], F32)
    sb["recR"] = s_("recR", [128, 4], F32)
    sb["SuA"] = s_("SuA", [128, 128], BF16)
    sb["SuB"] = s_("SuB", [128, 128], BF16)
    sb["attn"] = s_("attn", [128, 4 * 8 * TL], BF16)
    sb["wrot"] = s_("wrot", [128, 2 * 512], BF16)
    sb["gpscr"] = s_("gpscr", [128, TL], BF16)
    sb["rb"] = s_("rb", [128, TL], BF16)
    sb["sqb"] = s_("sqb", [128, TL], BF16)

    sb["c_rep"] = s_("c_rep", [128, 64], F32)
    sb["posrep"] = s_("posrep", [128, 64], F32)
    sb["sfbcol"] = s_("sfbcol", [128, 1], F32)
    sb["dbh"] = s_("dbh", [128, 1], F32)
    sb["identb"] = s_("identb", [128, 128], BF16)
    sb["Wcomb"] = s_("Wcomb", [128, 8 * 512], BF16)
    sb["lhsT_dot"] = s_("lhsT_dot", [128, 8], BF16)
    sb["onescol"] = s_("onescol", [128, 1], BF16)
    sb["onesrow"] = s_("onesrow", [1, 128], F32)
    if 8 * TL < 1280:
        sb["smallS_t"] = s_("smallS_t", [128, 256], F32)
        sb["stg_t"] = s_("stg_t", [128, 256], BF16)

    ps = {}
    p_ = lambda n, shape, dt: ctx.enter_context(nc.psum_tensor(n + suffix, shape, dt))
    ps["PA"] = p_("PA", [128, 512], F32)
    ps["PB"] = p_("PB", [128, 512], F32)
    ps["PD"] = p_("PD", [128, 2048], F32)
    ps["PX"] = p_("PX", [128, 1024], BF16)
    ps["PV"] = p_("PV", [128, 512], F32)
    return sb, ps


def emit_a(s: Sched, t, sb, ps, TL, dbg=False, final=True):
    TP = TL + 64
    NTC = TL // 128
    TCHP = (TP + 127) // 128
    nc = s.nc
    PA, PB, PD, PX, PV = ps["PA"], ps["PB"], ps["PD"], ps["PX"], ps["PV"]
    xb, valt, dproj, gbuf, raw, tanhb = (sb[k] for k in ("xb", "valt", "dproj", "gbuf", "raw", "tanhb"))
    tanh_bc = tanhb[:].bitcast(BF16)
    tmp_keys = lambda: [("tmp", w) for w in range(8)]
    wldf = raw[:, 0:1024]
    wldf_keys = [("raw", 0), ("raw", 1)]
    wldb = tanh_bc[:, 0:1024]
    wldb_keys = [("tmp", 0), ("tmp", 1)]
    stg2 = gbuf[:, 0:256]
    stg2_keys = [("gbuf", 0)]
    blk8 = gbuf[:, 256:512]

    if 8 * TL >= 1280:
        smallS = raw[:, 1024:1280]
        stg = tanh_bc[:, 2048:2304]
        sKk = ("raw", 2)
        gKk = ("tmp", 4)
    else:
        smallS = sb["smallS_t"][:]
        stg = sb["stg_t"][:]
        sKk = "smallS_t"
        gKk = "stg_t"
    wrot_f32 = sb["wrot"][:].bitcast(F32)
    maskAs_v = wrot_f32[:, 0:192]
    maskBs_v = wrot_f32[:, 192:384]
    mask_keys = {"maskAs": [("wrot", 0)], "maskBs": [("wrot", 0), ("wrot", 1)]}
    valt_f32 = valt[:].bitcast(F32)
    stats = valt_f32[0:1, 0:4 * TL]
    packMR = valt_f32[0:1, 4 * TL:6 * TL]
    statsK = ("valt", 0, 0)
    packK = ("valt", 0, 1)
    xio = sb["xio"]
    xres_v = lambda slot: xio[:, slot * TP: slot * TP + TL]
    x1s_v = lambda slot: xio[:, slot * TP: slot * TP + TL]
    PDv = lambda k, n=512: PD[:, k * 512:k * 512 + n]

    def TT(engine, out, a, b_, op, reads, writes, sreads=()):
        return s.op(engine, lambda e: e.tensor_tensor(out, a, b_, op=op),
                    reads=reads, writes=writes, sreads=sreads)

    s.op("gpsimd", lambda e: e.memset(sb["identb"][:], 0.0), writes=["identb"])
    s.op("gpsimd", lambda e: e.affine_select(
        out=sb["identb"][:], in_=sb["identb"][:], compare_op=ALU.not_equal,
        fill=1.0, base=0, pattern=[[-1, 128]], channel_multiplier=1),
        reads=["identb"], writes=["identb"])
    s.op("vector", lambda e: e.memset(sb["onescol"][:], 1.0), writes=["onescol"])
    s.op("vector", lambda e: e.memset(ps["PD"][:], 0.0),
         writes=[("PD", k) for k in range(4)])
    s.op("vector", lambda e: e.memset(sb["onesrow"][:], 1.0), writes=["onesrow"])

    s.op("vector", lambda e: e.memset(wldf[:, 0:TP], 0.0), writes=wldf_keys)
    for bs in range(4):
        for half in range(2):
            for h in range(4):
                for tr in range(TL // 128):
                    s.dma("sync", t["shear"][bs, half, h, tr * 128:(tr + 1) * 128, :],
                          wldf[:, 0:TP], reads=wldf_keys, writes=[("shear", bs, half)])

    s.dma("scalar", smallS[0:16, 0:16], t["ppw"], writes=[sKk])
    s.dma("scalar", smallS[0:64, 16:32], t["relpos"], writes=[sKk])
    s.op("vector", lambda e: e.tensor_copy(stg[0:16, 0:16], smallS[0:16, 0:16]),
         reads=[sKk], writes=[gKk])
    s.op("vector", lambda e: e.tensor_copy(stg[0:64, 16:32], smallS[0:64, 16:32]),
         reads=[sKk], writes=[gKk])
    s.dma("scalar", smallS[0:32, 96:112], t["sfw"], writes=[sKk])
    s.op("vector", lambda e: e.tensor_copy(stg[0:32, 32:48], smallS[0:32, 96:112]),
         reads=[sKk], writes=[gKk])
    s.op("tensor", lambda e: e.transpose(PX[0:16, 0:16], stg[0:16, 0:16], sb["identb"][0:16, 0:16]),
         reads=[gKk, "identb"], writes=["PX"])
    s.op("tensor", lambda e: e.transpose(PX[0:16, 64:128], stg[0:64, 16:32], sb["identb"][0:64, 0:64]),
         reads=[gKk, "identb"], writes=["PX"])
    s.op("tensor", lambda e: e.transpose(PX[0:16, 128:160], stg[0:32, 32:48], sb["identb"][0:32, 0:32]),
         reads=[gKk, "identb"], writes=["PX"])
    s.op("vector", lambda e: e.tensor_copy(stg[0:16, 48:64], PX[0:16, 0:16]),
         reads=["PX"], writes=[gKk])
    s.op("vector", lambda e: e.tensor_copy(stg[0:16, 64:128], PX[0:16, 64:128]),
         reads=["PX"], writes=[gKk])
    s.op("vector", lambda e: e.tensor_copy(stg[0:16, 128:160], PX[0:16, 128:160]),
         reads=["PX"], writes=[gKk])
    s.op("tensor", lambda e: e.matmul(PA[0:16, 0:64], stg[0:16, 48:64],
                                      stg[0:16, 64:128], start=True, stop=True),
         reads=[gKk], writes=["PA"])
    s.op("vector", lambda e: e.tensor_copy(smallS[0:16, 32:96], PA[0:16, 0:64]),
         reads=["PA"], writes=[sKk])
    s.op("gpsimd", lambda e: e.memset(sb["posrep"][:], 0.0), writes=["posrep"])
    s.op("gpsimd", lambda e: e.memset(sb["sfbcol"][:], 0.0), writes=["sfbcol"])
    for h in range(4):
        s.dma("scalar", ap(sb["posrep"], (h * 32) * 64, [[64, 16], [1, 64]]),
              smallS[0:16, 32:96], reads=[sKk, "posrep"], writes=["posrep"])
        s.dma("scalar", ap(sb["sfbcol"], (h * 32), [[1, 32], [1, 1]]),
              t["sfb"].unsqueeze(1), reads=["sfbcol"], writes=["sfbcol"])
    s.op("vector", lambda e: e.tensor_scalar(sb["c_rep"][:], sb["posrep"][:],
                                             sb["sfbcol"][:], None, op0=ALU.add),
         reads=["posrep"], writes=["c_rep"], sreads=["sfbcol"])
    s.dma("scalar", smallS[0:1, 96:97], t["dob"].unsqueeze(1), reads=[sKk], writes=[sKk])
    s.op("tensor", lambda e: e.matmul(PB[:, 0:1], sb["onesrow"][:], smallS[0:1, 96:97],
                                      start=True, stop=True),
         reads=[sKk, "onesrow"], writes=["PB"])
    s.op("vector", lambda e: e.tensor_scalar_mul(sb["dbh"][:], PB[:, 0:1], 0.5),
         reads=["PB"], writes=["dbh"])
    s.op("gpsimd", lambda e: e.memset(sb["lhsT_dot"][:], 0.0), writes=["lhsT_dot"])
    s.dma("scalar", smallS[0:16, 112:113], t["bow"].rearrange("a b -> (a b)").unsqueeze(1),
          writes=[sKk])
    s.dma("scalar", smallS[0:16, 113:114], t["dow"].rearrange("a b -> (a b)").unsqueeze(1),
          writes=[sKk])
    s.op("vector", lambda e: e.tensor_copy(stg[0:16, 160:162], smallS[0:16, 112:114]),
         reads=[sKk], writes=[gKk])
    for h in range(4):
        s.dma("scalar", ap(sb["lhsT_dot"], (h * 32) * 8 + h, [[8, 16], [1, 1]]),
              stg[0:16, 160:161],
              reads=["lhsT_dot", gKk], writes=["lhsT_dot"])
        s.dma("scalar", ap(sb["lhsT_dot"], (h * 32 + 16) * 8 + 4 + h, [[8, 16], [1, 1]]),
              stg[0:16, 161:162],
              reads=["lhsT_dot", gKk], writes=["lhsT_dot"])
    s.op("gpsimd", lambda e: e.memset(blk8[:], 0.0), writes=[("gbuf", 0)])
    for h in range(8):
        s.dma("scalar", ap(gbuf, (h * 16) * (32 * TL) + 256 + h * 32, [[32 * TL, 16], [1, 32]]),
              stg[0:16, 128:160], reads=[gKk, ("gbuf", 0)], writes=[("gbuf", 0)])
    for oc in range(2):
        s.dma("sync", wldf, t["Wdisp"][oc * 128:(oc + 1) * 128, :],
              reads=wldf_keys, writes=wldf_keys)
        s.op("vector", lambda e: e.tensor_copy(wldb, wldf),
             reads=wldf_keys + wldb_keys, writes=wldb_keys)
        for dc in range(DCH):
            s.op("tensor", lambda e, dc=dc: e.matmul(PA[:, 0:256],
                 wldb[:, dc * 128:(dc + 1) * 128], blk8[:], start=True, stop=True),
                 reads=wldb_keys + [("gbuf", 0)], writes=["PA"])
            s.op("vector", lambda e, dc=dc, oc=oc: e.tensor_copy(
                 sb["Wcomb"][:, dc * 512 + oc * 256: dc * 512 + (oc + 1) * 256], PA[:, 0:256]),
                 reads=["PA"], writes=["Wcomb"])
    for (src, dst, key) in ((t["Wval"], t["wvalT"], "wvalT"), (t["Wcpj"], t["cprojT"], "cprojT")):
        for oc in range(8):
            s.dma("sync", wldf, src[oc * 128:(oc + 1) * 128, :],
                  reads=wldf_keys, writes=wldf_keys)
            s.op("vector", lambda e: e.tensor_copy(wldb, wldf),
                 reads=wldf_keys + wldb_keys, writes=wldb_keys)
            for dc in range(DCH):
                slot = dc % 2
                s.op("tensor", lambda e, dc=dc, slot=slot: e.transpose(
                     PX[:, slot * 128:(slot + 1) * 128],
                     wldb[:, dc * 128:(dc + 1) * 128], sb["identb"][:]),
                     reads=wldb_keys + ["identb"], writes=["PX"])
                s.op("vector", lambda e, slot=slot: e.tensor_copy(
                     stg2[:, slot * 128:(slot + 1) * 128] if False else
                     gbuf[:, slot * 128:(slot + 1) * 128],
                     PX[:, slot * 128:(slot + 1) * 128]),
                     reads=["PX"], writes=stg2_keys)
                s.dma("sync", dst[dc * 128:(dc + 1) * 128, oc * 128:(oc + 1) * 128],
                      gbuf[:, slot * 128:(slot + 1) * 128],
                      reads=stg2_keys, writes=[(key,)])

    for quad in range(2):
        for bs in range(4):
            b = quad * 4 + bs
            for dc in range(DCH):
                slot = dc % 2
                s.dma("sync", xio[:, slot * TP:(slot + 1) * TP],
                      t["xA"][b, dc * 128:(dc + 1) * 128, :],
                      reads=[("xio", slot)], writes=[("xio", slot)])
                s.op("gpsimd", lambda e, dc=dc, slot=slot: e.tensor_copy(
                     xb[:, dc * TP:(dc + 1) * TP], xio[:, slot * TP:(slot + 1) * TP]),
                     reads=[("xio", slot), ("xb", dc)], writes=[("xb", dc)])
            for oh in range(2):
                for dc in range(DCH):
                    slot = dc % 2
                    s.dma("sync", sb["wrot"][:, slot * 512:(slot + 1) * 512],
                          t["wvalT"][dc * 128:(dc + 1) * 128, oh * 512:(oh + 1) * 512],
                          reads=[("wrot", slot), ("wvalT",)], writes=[("wrot", slot)])
                    for tc in range(TCHP):
                        n = min(128, TP - tc * 128)
                        bank = PDv(tc) if tc < 4 else PA
                        s.op("tensor", lambda e, dc=dc, tc=tc, slot=slot, n=n, bank=bank: e.matmul(
                             bank[0:n, :],
                             xb[:, dc * TP + tc * 128: dc * TP + tc * 128 + n],
                             sb["wrot"][:, slot * 512:(slot + 1) * 512],
                             start=(dc == 0), stop=(dc == DCH - 1)),
                             reads=[("xb", dc), ("wrot", slot)],
                             writes=[("PD", tc) if tc < 4 else "PA"])
                for tc in range(TCHP):
                    n = min(128, TP - tc * 128)
                    bank = PDv(tc) if tc < 4 else PA
                    s.op("vector", lambda e, tc=tc, oh=oh, bs=bs, n=n, bank=bank: e.tensor_copy(
                         valt[0:n, (bs * TCHP + tc) * 1024 + oh * 512:
                              (bs * TCHP + tc) * 1024 + (oh + 1) * 512], bank[0:n, :]),
                         reads=[("PD", tc) if tc < 4 else "PA"],
                         writes=[("valt", bs, tc)])
            for hg in range(NHG):
                w0 = min(512, TP)
                for dc in range(DCH):
                    s.op("tensor", lambda e, dc=dc, hg=hg, w0=w0: e.matmul(
                         PDv(0, w0), sb["Wcomb"][:, dc * 512 + hg * 128: dc * 512 + (hg + 1) * 128],
                         xb[:, dc * TP: dc * TP + w0],
                         start=(dc == 0), stop=(dc == DCH - 1)),
                         reads=[("xb", dc), "Wcomb"], writes=[("PD", 0)])
                    if TP > 512:
                        s.op("tensor", lambda e, dc=dc, hg=hg: e.matmul(
                             PDv(1, TP - 512), sb["Wcomb"][:, dc * 512 + hg * 128: dc * 512 + (hg + 1) * 128],
                             xb[:, dc * TP + 512: (dc + 1) * TP],
                             start=(dc == 0), stop=(dc == DCH - 1)),
                             reads=[("xb", dc), "Wcomb"], writes=[("PD", 1)])
                s.op("vector", lambda e, bs=bs, hg=hg, w0=w0: e.tensor_copy(
                     dproj[:, (bs * 4 + hg) * TP: (bs * 4 + hg) * TP + w0], PDv(0, w0)),
                     reads=[("PD", 0)], writes=[("dproj", bs, hg)])
                if TP > 512:
                    s.op("vector", lambda e, bs=bs, hg=hg: e.tensor_copy(
                         dproj[:, (bs * 4 + hg) * TP + 512: (bs * 4 + hg + 1) * TP],
                         PDv(1, TP - 512)),
                         reads=[("PD", 1)], writes=[("dproj", bs, hg)])

        if dbg and quad == 0:
            s.dma("sync", t["dbg_dproj"], dproj[:, 0:TP], reads=[("dproj", 0, 0)],
                  writes=["dbgx"])
            s.dma("sync", t["dbg_valt"], valt[:, 0:1024], reads=[("valt", 0, 0)],
                  writes=["dbgx"])
        s.dma("sync", maskAs_v, t["maskA"], reads=[("wrot", 0)], writes=[("wrot", 0)])
        s.dma("sync", maskBs_v, t["maskB"], reads=[("wrot", 0), ("wrot", 1)],
              writes=[("wrot", 0), ("wrot", 1)])
        for hg in range(NHG):
            for wb in range(8):
                for bs in range(4):
                    dbase = (bs * 4 + hg) * TP
                    for wsub in range(8):
                        w = wb * 8 + wsub
                        in0 = ap(dproj, dbase + 1 + w, [[16 * TP, 128], [1, TL]])
                        in1 = ap(dproj, dbase + 64, [[16 * TP, 128], [1, TL]])
                        out = tanh_bc[:, wsub * TL:(wsub + 1) * TL]
                        if wsub % 2 == 0:
                            s.op("vector", lambda e, in0=in0, in1=in1, out=out, w=w:
                                 e.scalar_tensor_tensor(
                                 out, in0, sb["c_rep"][:, w:w + 1], in1,
                                 op0=ALU.add, op1=ALU.subtract),
                                 reads=[("dproj", bs, hg)], writes=[("tmp", wsub)],
                                 sreads=["c_rep"])
                        else:
                            cbc = ap(sb["c_rep"], w, [[64, 128], [0, TL]])
                            s.op("gpsimd", lambda e, in0=in0, cbc=cbc: e.tensor_tensor(
                                 sb["gpscr"][:], in0, cbc, op=ALU.add),
                                 reads=[("dproj", bs, hg), "c_rep", "gpscr"],
                                 writes=["gpscr"])
                            s.op("gpsimd", lambda e, in1=in1, out=out: e.tensor_tensor(
                                 out, sb["gpscr"][:], in1, op=ALU.subtract),
                                 reads=["gpscr", ("dproj", bs, hg)], writes=[("tmp", wsub)])
                    s.op("scalar", lambda e, bs=bs: e.activation(
                         gbuf[:, bs * 8 * TL:(bs + 1) * 8 * TL], tanh_bc[:, 0:8 * TL], AF.Gelu),
                         reads=tmp_keys(), writes=[("gbuf", bs)])
                for g4 in range(2):
                    for wsub in range(4):
                        for bs in range(4):
                            tp = None if bs == 0 else (0, 32 * bs)
                            s.op("tensor", lambda e, g4=g4, wsub=wsub, bs=bs, tp=tp: e.matmul(
                                 ap(PD, (32 * bs) * 2048 + wsub * 512, [[2048, 8], [1, TL]]),
                                 sb["lhsT_dot"][:],
                                 gbuf[:, (bs * 8 + g4 * 4 + wsub) * TL:(bs * 8 + g4 * 4 + wsub + 1) * TL],
                                 start=True, stop=True, tile_position=tp),
                                 reads=[("gbuf", bs), "lhsT_dot"],
                                 writes=[("PD", wsub)])
                    s.op("vector", lambda e, g4=g4: e.tensor_copy(
                         ap(raw, g4 * 4, [[8 * TL, 128], [1, 4], [8, TL]]),
                         ap(PD, 0, [[2048, 128], [512, 4], [1, TL]])),
                         reads=[("PD", k) for k in range(4)],
                         writes=[("raw", k) for k in range(8)])
                s.op("scalar", lambda e: e.activation(
                     tanhb[:], raw[:], AF.Tanh, bias=sb["dbh"][:, 0:1], scale=0.5),
                     reads=[("raw", k) for k in range(8)], writes=tmp_keys(),
                     sreads=["dbh"])
                if dbg and quad == 0 and hg == 0 and wb == 0:
                    s.dma("sync", t["dbg_raw"], raw[:], reads=[("raw", k) for k in range(8)],
                          writes=["dbgx"])
                for bs in range(4):
                    for half in range(2):
                        srcbuf = raw if half == 0 else tanhb
                        srckeys = [("raw", k) for k in range(8)] if half == 0 else tmp_keys()
                        src = ap(srcbuf, (32 * bs + 4 * half) * 8 * TL,
                                 [[8 * TL, 4], [8, TL], [1, 8]])
                        dst = ap(t["shear"], ((bs * 2 + half) * 4) * TL * TP + wb * 8 + 1,
                                 [[TL * TP, 4], [TP + 1, TL], [1, 8]])
                        s.dma("scalar", dst, src, reads=srckeys,
                              writes=[("shear", bs, half)])
            for bs in range(4):
                for tc in range(NTC):
                    roff = lambda half: ((bs * 2 + half) * 4) * TL * TP + tc * 128 * TP + tc * 128
                    s.dma("sync", sb["bondR"][:],
                          ap(t["shear"], roff(0), [[TP, 128], [TL * TP, 4], [1, 192]]),
                          reads=[("shear", bs, 0), "bondR"], writes=["bondR"])
                    s.dma("sync", sb["dmgR"][:],
                          ap(t["shear"], roff(1), [[TP, 128], [TL * TP, 4], [1, 192]]),
                          reads=[("shear", bs, 1), "dmgR"], writes=["dmgR"])
                    s.op("gpsimd", lambda e: e.tensor_scalar(
                         sb["S4"][:], sb["dmgR"][:], -5.0, None, op0=ALU.mult),
                         reads=["dmgR"], writes=["S4"])
                    s.op("gpsimd", lambda e: e.tensor_tensor(
                         sb["S3"][:], sb["S4"][:], sb["bondR"][:], op=ALU.add),
                         reads=["S4", "bondR"], writes=["S3"])
                    mv = maskAs_v if tc == 0 else maskBs_v
                    mks = mask_keys["maskAs" if tc == 0 else "maskBs"]
                    moff = 0 if tc == 0 else 192
                    s.op("gpsimd", lambda e, moff=moff: e.tensor_tensor(
                         sb["S4"][:], sb["S3"][:],
                         ap(wrot_f32, moff, [[512, 128], [0, 4], [1, 192]]), op=ALU.add),
                         reads=["S3"] + mks, writes=["S4"])
                    s.op("vector", lambda e: e.tensor_reduce(
                         sb["negmax"][:].unsqueeze(2),
                         sb["S4"][:].rearrange("p (h u) -> p h u", h=4),
                         axis=AX.X, op=ALU.max, negate=True),
                         reads=["S4"], writes=["negmax"])
                    s.op("gpsimd", lambda e: e.tensor_tensor(
                         sb["S3"][:], sb["S4"][:],
                         ap(sb["negmax"], 0, [[4, 128], [1, 4], [0, 192]]), op=ALU.add),
                         reads=["S4", "negmax"], writes=["S3"])
                    s.op("scalar", lambda e: e.activation(sb["S4"][:], sb["S3"][:], AF.Exp),
                         reads=["S3"], writes=["S4"])
                    s.op("vector", lambda e: e.tensor_reduce(
                         sb["sumR"][:].unsqueeze(2),
                         sb["S4"][:].rearrange("p (h u) -> p h u", h=4),
                         axis=AX.X, op=ALU.add),
                         reads=["S4"], writes=["sumR"])
                    s.op("vector", lambda e: e.reciprocal(sb["recR"][:], sb["sumR"][:]),
                         reads=["sumR"], writes=["recR"])
                    s.op("gpsimd", lambda e: e.tensor_tensor(
                         sb["wcast"][:], sb["S4"][:],
                         ap(sb["recR"], 0, [[4, 128], [1, 4], [0, 192]]), op=ALU.mult),
                         reads=["S4", "recR"], writes=["wcast"])
                    if dbg and quad == 0 and hg == 0 and bs == 0 and tc == 0:
                        s.dma("sync", t["dbg_bondR"], sb["bondR"][:], reads=["bondR"], writes=["dbgx"])
                        s.dma("sync", t["dbg_dmgR"], sb["dmgR"][:], reads=["dmgR"], writes=["dbgx"])
                        s.dma("sync", t["dbg_wcast"], sb["wcast"][:], reads=["wcast"], writes=["dbgx"])
                    for pair in range(2):
                        for hsub in range(2):
                            h = pair * 2 + hsub
                            s.op("tensor", lambda e, h=h: e.transpose(
                                 PX[:, 0:128], sb["wcast"][:, h * 192: h * 192 + 128],
                                 sb["identb"][:]),
                                 reads=["wcast", "identb"], writes=["PX"])
                            s.op("vector", lambda e: e.tensor_copy(sb["SuA"][:], PX[:, 0:128]),
                                 reads=["PX"], writes=["SuA"])
                            s.op("tensor", lambda e, h=h: e.transpose(
                                 PX[0:64, 128:256], sb["wcast"][:, h * 192 + 128: (h + 1) * 192],
                                 sb["identb"][:]),
                                 reads=["wcast", "identb"], writes=["PX"])
                            s.op("vector", lambda e: e.tensor_copy(sb["SuB"][0:64, :], PX[0:64, 128:256]),
                                 reads=["PX"], writes=["SuB"])
                            hglob = hg * 4 + h
                            vc0 = (bs * TCHP + tc) * 1024 + hglob * 64
                            vc1 = (bs * TCHP + tc + 1) * 1024 + hglob * 64
                            s.op("tensor", lambda e, vc0=vc0, hsub=hsub: e.matmul(
                                 PV[64 * hsub:64 * (hsub + 1), 0:128],
                                 valt[:, vc0:vc0 + 64], sb["SuA"][:],
                                 start=True, stop=False),
                                 reads=[("valt", bs, tc), "SuA"], writes=["PV"])
                            s.op("tensor", lambda e, vc1=vc1, hsub=hsub: e.matmul(
                                 PV[64 * hsub:64 * (hsub + 1), 0:128],
                                 valt[0:64, vc1:vc1 + 64], sb["SuB"][0:64, :],
                                 start=False, stop=True),
                                 reads=[("valt", bs, tc + 1), "SuB"], writes=["PV"])
                        c = hg * 2 + pair
                        s.op("vector", lambda e, bs=bs, c=c, tc=tc: e.tensor_copy(
                             sb["attn"][:, (bs * 8 + c) * TL + tc * 128:(bs * 8 + c) * TL + (tc + 1) * 128],
                             PV[:, 0:128]),
                             reads=["PV"], writes=[("attn", bs, c)])

        if dbg and quad == 0:
            s.dma("sync", t["dbg_attn"], sb["attn"][:, 0:8 * TL],
                  reads=[("attn", 0, c) for c in range(8)], writes=["dbgx"])
        for bs in range(4):
            b = quad * 4 + bs
            banks = [("PA", PA), ("PB", PB), (("PD", 0), PDv(0)), (("PD", 1), PDv(1))]
            for pas in range(2):
                for dc in range(DCH):
                    slot = dc % 2
                    s.dma("sync", sb["wrot"][:, slot * 512:(slot + 1) * 512],
                          t["cprojT"][dc * 128:(dc + 1) * 128, pas * 512:(pas + 1) * 512],
                          reads=[("wrot", slot), ("cprojT",)], writes=[("wrot", slot)])
                    for j, (bk, bt) in enumerate(banks):
                        s.op("tensor", lambda e, dc=dc, j=j, slot=slot, bt=bt: e.matmul(
                             bt[:, 0:TL], sb["wrot"][:, slot * 512 + j * 128: slot * 512 + (j + 1) * 128],
                             sb["attn"][:, (bs * 8 + dc) * TL:(bs * 8 + dc + 1) * TL],
                             start=(dc == 0), stop=(dc == DCH - 1)),
                             reads=[("wrot", slot), ("attn", bs, dc)], writes=[bk])
                for j, (bk, bt) in enumerate(banks):
                    oc = pas * 4 + j
                    slot = oc % 2
                    s.dma("sync", xres_v(slot),
                          t["xA"][b, oc * 128:(oc + 1) * 128, 64:TP],
                          reads=[("xio", slot)], writes=[("xio", slot)])
                    s.op("vector", lambda e, j=j, oc=oc, slot=slot, bt=bt: e.tensor_add(
                         raw[:, oc * TL:(oc + 1) * TL], bt[:, 0:TL],
                         xres_v(slot)),
                         reads=[bk, ("xio", slot)], writes=[("raw", oc)])
            for oc in range(DCH):
                s.op("vector", lambda e, oc=oc: e.tensor_copy(sb["rb"][:], raw[:, oc * TL:(oc + 1) * TL]),
                     reads=[("raw", oc)], writes=["rb"])
                s.op("vector", lambda e: e.tensor_mul(sb["sqb"][:], sb["rb"][:], sb["rb"][:]),
                     reads=["rb"], writes=["sqb"])
                s.op("tensor", lambda e, oc=oc: e.matmul(
                     ap(PD, 2 * 512, [[2048, 1], [1, TL]]), sb["onescol"][:], sb["rb"][:],
                     start=(oc == 0), stop=(oc == DCH - 1)),
                     reads=["rb", "onescol"], writes=[("PD", 2)])
                s.op("tensor", lambda e, oc=oc: e.matmul(
                     ap(PD, 3 * 512, [[2048, 1], [1, TL]]), sb["onescol"][:], sb["sqb"][:],
                     start=(oc == 0), stop=(oc == DCH - 1)),
                     reads=["sqb", "onescol"], writes=[("PD", 3)])
            s.op("vector", lambda e: e.tensor_scalar_mul(packMR[0:1, 0:TL], PDv(2)[0:1, 0:TL], 1.0 / D),
                 reads=[("PD", 2)], writes=[packK])
            s.op("vector", lambda e: e.tensor_scalar_mul(stats[0:1, 0:TL], PDv(3)[0:1, 0:TL], 1.0 / D),
                 reads=[("PD", 3)], writes=[statsK])
            s.op("vector", lambda e: e.tensor_mul(stats[0:1, TL:2 * TL],
                                                  packMR[0:1, 0:TL], packMR[0:1, 0:TL]),
                 reads=[packK], writes=[statsK])
            s.op("vector", lambda e: e.scalar_tensor_tensor(
                 stats[0:1, 2 * TL:3 * TL], stats[0:1, 0:TL], 1e-5,
                 stats[0:1, TL:2 * TL], op0=ALU.add, op1=ALU.subtract),
                 reads=[statsK], writes=[statsK])
            s.op("scalar", lambda e: e.activation(stats[0:1, 3 * TL:4 * TL],
                                                  stats[0:1, 2 * TL:3 * TL],
                                                  AF.Sqrt),
                 reads=[statsK], writes=[statsK])
            s.op("vector", lambda e: e.reciprocal(packMR[0:1, TL:2 * TL],
                                                  stats[0:1, 3 * TL:4 * TL]),
                 reads=[statsK], writes=[packK])
            s.op("tensor", lambda e: e.matmul(PA[:, 0:TL], sb["onesrow"][:],
                                              packMR[0:1, 0:TL], start=True, stop=True),
                 reads=[packK, "onesrow"], writes=["PA"])
            s.op("tensor", lambda e: e.matmul(PB[:, 0:TL], sb["onesrow"][:],
                                              packMR[0:1, TL:2 * TL], start=True, stop=True),
                 reads=[packK, "onesrow"], writes=["PB"])
            for oc in range(DCH):
                s.op("vector", lambda e, oc=oc: e.tensor_sub(
                     x1s_v(0), raw[:, oc * TL:(oc + 1) * TL], PA[:, 0:TL]),
                     reads=[("raw", oc), "PA"], writes=[("xio", 0)])
                s.op("vector", lambda e: e.tensor_mul(
                     x1s_v(1), x1s_v(0), PB[:, 0:TL]),
                     reads=[("xio", 0), "PB"], writes=[("xio", 1)])
                s.dma("sync", t["x1"][b, oc * 128:(oc + 1) * 128, :],
                      x1s_v(1),
                      reads=[("xio", 1)], writes=[("x1", b, oc)], force_inc=True)
    if final:
        s.final_wait()


D, F_, E, R = 1024, 4096, 8, 16
DCH, FCH = 8, 32


def declare_dram_b(nc, TT, NB, suffix="", dbg=False, x1=None, x2=None):
    t = {}
    ein = lambda n, s: nc.dram_tensor(n + suffix, s, F32, kind="ExternalInput").ap()
    t["x1"] = x1 if x1 is not None else ein("x1", [NB, D, TT])
    t["normw"] = ein("normw", [D])
    t["w1"] = ein("w1", [F_, D])
    t["w3"] = ein("w3", [F_, D])
    t["w2"] = ein("w2", [D, F_])
    t["gatew"] = ein("gatew", [E, D])
    t["a1"] = ein("a1", [E, R, D])
    t["b1"] = ein("b1", [E, F_, R])
    t["a3"] = ein("a3", [E, R, D])
    t["b3"] = ein("b3", [E, F_, R])
    t["a2"] = ein("a2", [E, R, F_])
    t["b2"] = ein("b2", [E, D, R])
    t["x2"] = x2 if x2 is not None else nc.dram_tensor("x2" + suffix, [NB, D, TT], F32, kind="ExternalOutput").ap()
    t["w1T"] = nc.dram_tensor("w1T" + suffix, [D, F_], BF16, kind="Internal").ap()
    t["w3T"] = nc.dram_tensor("w3T" + suffix, [D, F_], BF16, kind="Internal").ap()
    t["w2T"] = nc.dram_tensor("w2T" + suffix, [F_, D], BF16, kind="Internal").ap()
    t["b1T"] = nc.dram_tensor("b1T" + suffix, [E, R, F_], BF16, kind="Internal").ap()
    t["b3T"] = nc.dram_tensor("b3T" + suffix, [E, R, F_], BF16, kind="Internal").ap()
    if dbg:
        t["dbg_nxb"] = nc.dram_tensor("dbg_nxb" + suffix, [128, DCH * TT], BF16, kind="ExternalOutput").ap()
    if dbg:
        t["dbg_wts"] = nc.dram_tensor("dbg_wts" + suffix, [128, 32], F32, kind="ExternalOutput").ap()
    if dbg:
        t["dbg_wrep"] = nc.dram_tensor("dbg_wrep" + suffix, [128, E * TT], BF16, kind="ExternalOutput").ap()
    if dbg:
        t["dbg_base1"] = nc.dram_tensor("dbg_base1" + suffix, [128, FCH * TT], BF16, kind="ExternalOutput").ap()
    if dbg:
        t["dbg_etr1"] = nc.dram_tensor("dbg_etr1" + suffix, [128, 2 * TT], BF16, kind="ExternalOutput").ap()
    if dbg:
        t["dbg_act"] = nc.dram_tensor("dbg_act" + suffix, [128, 4 * E * TT], BF16, kind="ExternalOutput").ap()
    if dbg:
        t["dbg_out"] = nc.dram_tensor("dbg_out" + suffix, [128, DCH * TT], F32, kind="ExternalOutput").ap()
    return t


def alloc_bufs_b(ctx, nc, TT, suffix=""):
    sb, ps = {}, {}
    s_ = lambda n, shape, dt: ctx.enter_context(nc.sbuf_tensor(n + suffix, shape, dt))
    p_ = lambda n, shape, dt: ctx.enter_context(nc.psum_tensor(n + suffix, shape, dt))
    sb["x1f"] = s_("x1f", [128, DCH * TT], F32)
    sb["nxb"] = s_("nxb", [128, DCH * TT], BF16)
    sb["base1"] = s_("base1", [128, FCH * TT], BF16)
    sb["base3"] = s_("base3", [128, FCH * TT], BF16)
    sb["actg"] = s_("actg", [128, 4 * E * TT], BF16)
    sb["wst"] = s_("wst", [128, 2 * 1024], BF16)
    sb["w2st"] = s_("w2st", [128, 4 * 1024], BF16)
    sb["bst"] = s_("bst", [128, 2048], BF16)
    sb["etr1P"] = s_("etr1P", [128, 2 * TT], BF16)
    sb["etr3P"] = s_("etr3P", [128, 2 * TT], BF16)
    sb["etr2P"] = s_("etr2P", [128, 2 * TT], BF16)
    sb["a1T"] = s_("a1T", [128, 2 * 1024], BF16)
    sb["a3T"] = s_("a3T", [128, 2 * 1024], BF16)
    sb["a2T"] = s_("a2T", [128, 4096], BF16)
    sb["b2s"] = s_("b2s", [128, 2 * 1024], BF16)
    sb["wrep"] = s_("wrep", [128, E * TT], BF16)

    sb["outacc"] = s_("outacc", [128, DCH * TT], F32)
    sb["gateT"] = s_("gateT", [128, 64], BF16)
    sb["normw"] = s_("normw_sb", [128, 8], F32)
    sb["gs1"] = s_("gs1", [128, 32], F32)
    sb["gs2"] = s_("gs2", [128, 32], F32)
    sb["gm"] = s_("gm", [128, 4], F32)
    sb["gm2"] = s_("gm2", [128, 4], F32)
    sb["gmr"] = s_("gmr", [128, 4], F32)
    sb["idxw"] = s_("idxw", [128, 32], F32)
    sb["idxwi"] = s_("idxwi", [128, 32], mybir.dt.int32)
    sb["wtsE"] = s_("wtsE", [128, 32], F32)
    sb["wEm"] = s_("wEm", [128, TT], F32)
    sb["identb"] = s_("identb", [128, 128], BF16)
    sb["identf"] = s_("identf", [128, 128], F32)
    sb["onescol"] = s_("onescol", [128, 1], BF16)
    sb["onesrow"] = s_("onesrow", [1, 128], F32)
    sb["zzb"] = s_("zzb", [1, 128], BF16)
    sb["selc"] = s_("selc", [8, 1024], BF16)
    sb["wEmb"] = s_("wEmb", [8, 512], BF16)
    sb["ldx"] = s_("ldx", [128, 1024], F32)
    sb["ldb"] = s_("ldb", [128, 1024], BF16)
    sb["stg2"] = s_("stg2", [128, 256], BF16)
    sb["rb"] = s_("rb", [128, TT], BF16)
    sb["sqb"] = s_("sqb", [128, TT], BF16)
    sb["xio"] = s_("xio", [128, 2 * TT], F32)
    ps["PA"] = p_("PA", [128, 512], F32)
    ps["PB"] = p_("PB", [128, 512], F32)
    ps["PD"] = p_("PD", [128, 2048], F32)
    ps["PV"] = p_("PV", [128, 512], F32)
    ps["PX"] = p_("PX", [128, 512], F32)
    return sb, ps


def emit_transpose_prep_b(s, t, sb, PXb, src, dst, RR, CC, kpref):
    ldf, ldb, stg2 = sb["ldf"], sb["ldb"], sb["stg2"]
    for rc in range(RR // 128):
        for cb in range(CC // 1024):
            s.dma("sync", ldf[:, 0:1024], src[rc * 128:(rc + 1) * 128, cb * 1024:(cb + 1) * 1024],
                  reads=[("ldx", 0), ("ldx", 1)], writes=[("ldx", 0), ("ldx", 1)])
            s.op("vector", lambda e: e.tensor_copy(ldb[:, 0:1024], ldf[:, 0:1024]),
                 reads=[("ldx", 0), ("ldx", 1), ("ldb",)], writes=[("ldb",)])
            for q in range(8):
                slot = q % 2
                s.op("tensor", lambda e, q=q, slot=slot: e.transpose(
                     PXb[:, slot * 128:(slot + 1) * 128],
                     ldb[:, q * 128:(q + 1) * 128], sb["identb"][:]),
                     reads=[("ldb",), "identb"], writes=["PX"])
                s.op("vector", lambda e, slot=slot: e.tensor_copy(
                     stg2[:, slot * 128:(slot + 1) * 128], PXb[:, slot * 128:(slot + 1) * 128]),
                     reads=["PX"], writes=[("stg2x",)])
                s.dma("sync", dst[cb * 1024 + q * 128: cb * 1024 + (q + 1) * 128,
                                  rc * 128:(rc + 1) * 128],
                      stg2[:, slot * 128:(slot + 1) * 128],
                      reads=[("stg2x",)], writes=[("dst", kpref)])


def emit_b(s: Sched, t, sb, ps, TT, NB, stop_stage=99, dbg=False, final=True):
    nc = s.nc
    PA, PB, PD, PV, PX = ps["PA"], ps["PB"], ps["PD"], ps["PV"], ps["PX"]
    PXb = PX[:].bitcast(BF16)
    NTC = TT // 128
    PDv = lambda k, n=512: PD[:, k * 512:k * 512 + n]
    x1f, nxb, base1, base3, actg = (sb[k] for k in ("x1f", "nxb", "base1", "base3", "actg"))
    sb["ldf"] = sb["ldx"]
    sb["x2s"] = sb["ldx"]
    ldxb = sb["ldx"][:].bitcast(BF16)
    sb["silu"] = sb["rb"]
    sb["h3w"] = sb["sqb"]
    outacc, wrep, xio = sb["outacc"], sb["wrep"], sb["xio"]
    g3 = lambda a: a.rearrange("p (c e) -> p c e", e=8)

    s.op("gpsimd", lambda e: e.memset(sb["identb"][:], 0.0), writes=["identb"])
    s.op("gpsimd", lambda e: e.affine_select(
        out=sb["identb"][:], in_=sb["identb"][:], compare_op=ALU.not_equal,
        fill=1.0, base=0, pattern=[[-1, 128]], channel_multiplier=1),
        reads=["identb"], writes=["identb"])
    s.op("gpsimd", lambda e: e.memset(sb["identf"][:], 0.0), writes=["identf"])
    s.op("gpsimd", lambda e: e.affine_select(
        out=sb["identf"][:], in_=sb["identf"][:], compare_op=ALU.not_equal,
        fill=1.0, base=0, pattern=[[-1, 128]], channel_multiplier=1),
        reads=["identf"], writes=["identf"])
    s.op("vector", lambda e: e.memset(sb["onescol"][:], 1.0), writes=["onescol"])
    s.op("vector", lambda e: e.memset(sb["onesrow"][:], 1.0), writes=["onesrow"])
    s.op("vector", lambda e: e.memset(sb["zzb"][:], 0.0), writes=["zzb"])
    s.op("vector", lambda e: e.memset(sb["bst"][:], 0.0),
         writes=[("bst", w_, h, e4) for w_ in range(2) for h in range(2) for e4 in range(4)])
    s.op("vector", lambda e: e.memset(sb["b2s"][:], 0.0), writes=["b2s"])
    s.op("vector", lambda e: e.memset(PD[:], 0.0), writes=[("PD", k) for k in range(4)])
    s.op("vector", lambda e: e.memset(PV[:], 0.0), writes=["PV"])
    s.op("vector", lambda e: e.memset(PX[:], 0.0), writes=["PX"])
    s.op("gpsimd", lambda e: e.iota(sb["idxwi"][:], base=8, pattern=[[0, 4], [-1, 8]],
                                    channel_multiplier=0), writes=["idxwi"])
    s.op("vector", lambda e: e.tensor_copy(sb["idxw"][:], sb["idxwi"][:]),
         reads=["idxwi"], writes=["idxw"])
    s.op("gpsimd", lambda e: e.memset(sb["selc"][:], 0.0), writes=["selc"])
    s.op("gpsimd", lambda e: e.affine_select(
        out=sb["selc"][:], in_=sb["selc"][:], compare_op=ALU.not_equal,
        fill=1.0, base=0, pattern=[[-1, 8], [0, 128]], channel_multiplier=1),
        reads=["selc"], writes=["selc"])
    for dc in range(DCH):
        s.dma("scalar", sb["normw"][:, dc:dc + 1],
              t["normw"][dc * 128:(dc + 1) * 128].unsqueeze(1),
              reads=["normw_s"], writes=["normw_s"])
    s.dma("sync", sb["ldf"][0:8, 0:1024], t["gatew"], reads=[("ldx", 0), ("ldx", 1)], writes=[("ldx", 0), ("ldx", 1)])
    s.op("vector", lambda e: e.tensor_copy(sb["ldb"][0:8, 0:1024], sb["ldf"][0:8, 0:1024]),
         reads=[("ldx", 0), ("ldx", 1), ("ldb",)], writes=[("ldb",)])
    for dc in range(DCH):
        s.op("tensor", lambda e, dc=dc: e.transpose(
             PXb[:, 0:8], sb["ldb"][0:8, dc * 128:(dc + 1) * 128], sb["identb"][0:8, 0:8]),
             reads=[("ldb",), "identb"], writes=["PX"])
        s.op("vector", lambda e, dc=dc: e.tensor_copy(sb["gateT"][:, dc * 8:(dc + 1) * 8],
                                                      PXb[:, 0:8]),
             reads=["PX"], writes=["gateT"])
    emit_transpose_prep_b(s, t, sb, PXb, t["w1"], t["w1T"], F_, D, "w1")
    emit_transpose_prep_b(s, t, sb, PXb, t["w3"], t["w3T"], F_, D, "w3")
    emit_transpose_prep_b(s, t, sb, PXb, t["w2"], t["w2T"], D, F_, "w2")
    for (bsrc, bdst, kp) in ((t["b1"], t["b1T"], "b1"), (t["b3"], t["b3T"], "b3")):
        for e_ in range(E):
            for fb in range(4):
                src = ap(bsrc, e_ * F_ * R + fb * 1024 * R, [[16, 128], [2048, 8], [1, 16]])
                s.dma("sync", ap(sb["ldf"], 0, [[1024, 128], [1, 128]]), src,
                      reads=[("ldx", 0), ("ldx", 1)], writes=[("ldx", 0), ("ldx", 1)])
                s.op("vector", lambda e2: e2.tensor_copy(
                     ap(sb["ldb"], 0, [[1024, 128], [1, 128]]),
                     ap(sb["ldf"], 0, [[1024, 128], [1, 128]])),
                     reads=[("ldx", 0), ("ldx", 1), ("ldb",)], writes=[("ldb",)])
                for q in range(8):
                    slot = q % 2
                    tile = ap(sb["ldb"], q * 16, [[1024, 128], [1, 16]])
                    s.op("tensor", lambda e2, tile=tile, slot=slot: e2.transpose(
                         PXb[0:16, slot * 128:slot * 128 + 128], tile, sb["identb"][:]),
                         reads=[("ldb",), "identb"], writes=["PX"])
                    s.op("vector", lambda e2, slot=slot: e2.tensor_copy(
                         sb["stg2"][0:16, slot * 128:slot * 128 + 128],
                         PXb[0:16, slot * 128:slot * 128 + 128]),
                         reads=["PX"], writes=[("stg2x",)])
                    s.dma("sync", bdst[e_, :, fb * 1024 + q * 128: fb * 1024 + (q + 1) * 128],
                          sb["stg2"][0:16, slot * 128:slot * 128 + 128],
                          reads=[("stg2x",)], writes=[("dst", kp)])
    for (asrc, adst) in ((t["a1"], "a1T"), (t["a3"], "a3T")):
        for half in range(2):
            for e4 in range(4):
                s.dma("sync", ap(sb["ldf"], (e4 * 32) * 1024, [[1024, 16], [1, 1024]]),
                      asrc[half * 4 + e4], reads=[("ldx", 0), ("ldx", 1)], writes=[("ldx", 0), ("ldx", 1)])
            s.op("vector", lambda e2: e2.memset(sb["ldb"][:, 0:1024], 0.0),
                 reads=[("ldb",)], writes=[("ldb",)])
            for e4 in range(4):
                s.op("vector", lambda e2, e4=e4: e2.tensor_copy(
                     ap(sb["ldb"], (e4 * 32) * 1024, [[1024, 16], [1, 1024]]),
                     ap(sb["ldf"], (e4 * 32) * 1024, [[1024, 16], [1, 1024]])),
                     reads=[("ldx", 0), ("ldx", 1), ("ldb",)], writes=[("ldb",)])
            for dc in range(DCH):
                s.op("tensor", lambda e2, dc=dc: e2.transpose(
                     PXb[:, 0:128], sb["ldb"][:, dc * 128:(dc + 1) * 128], sb["identb"][:]),
                     reads=[("ldb",), "identb"], writes=["PX"])
                s.op("vector", lambda e2, dc=dc, half=half, adst=adst: e2.tensor_copy(
                     sb[adst][:, half * 1024 + dc * 128: half * 1024 + (dc + 1) * 128],
                     PXb[:, 0:128]),
                     reads=["PX"], writes=[adst])
    for e_ in range(E):
        for cb in range(4):
            s.dma("sync", ap(sb["ldf"], 0, [[1024, 16], [1, 1024]]),
                  t["a2"][e_, :, cb * 1024:(cb + 1) * 1024],
                  reads=[("ldx", 0), ("ldx", 1)], writes=[("ldx", 0), ("ldx", 1)])
            s.op("vector", lambda e2: e2.tensor_copy(
                 ap(sb["ldb"], 0, [[1024, 16], [1, 1024]]),
                 ap(sb["ldf"], 0, [[1024, 16], [1, 1024]])),
                 reads=[("ldx", 0), ("ldx", 1), ("ldb",)], writes=[("ldb",)])
            for q in range(8):
                fc = cb * 8 + q
                slot = q % 2
                s.op("tensor", lambda e2, q=q, slot=slot: e2.transpose(
                     PXb[:, slot * 128:slot * 128 + 16],
                     ap(sb["ldb"], q * 128, [[1024, 16], [1, 128]]),
                     sb["identb"][0:16, 0:16]),
                     reads=[("ldb",), "identb"], writes=["PX"])
                s.op("vector", lambda e2, fc=fc, e_=e_, slot=slot: e2.tensor_copy(
                     sb["a2T"][:, fc * 128 + e_ * 16: fc * 128 + (e_ + 1) * 16],
                     PXb[:, slot * 128:slot * 128 + 16]),
                     reads=["PX"], writes=["a2T"])
    for half in range(2):
        for e4 in range(4):
            e_ = half * 4 + e4
            for dgb in range(2):
                src = ap(t["b2"], e_ * D * R + dgb * 512 * R, [[16, 128], [2048, 4], [1, 16]])
                s.dma("sync", ap(sb["ldf"], 0, [[1024, 128], [1, 64]]), src,
                      reads=[("ldx", 0), ("ldx", 1)], writes=[("ldx", 0), ("ldx", 1)])
                s.op("vector", lambda e2: e2.tensor_copy(
                     ap(sb["ldb"], 0, [[1024, 128], [1, 64]]),
                     ap(sb["ldf"], 0, [[1024, 128], [1, 64]])),
                     reads=[("ldx", 0), ("ldx", 1), ("ldb",)], writes=[("ldb",)])
                for q in range(4):
                    slot = q % 2
                    tile = ap(sb["ldb"], q * 16, [[1024, 128], [1, 16]])
                    s.op("tensor", lambda e2, tile=tile, slot=slot: e2.transpose(
                         PXb[0:16, slot * 128:slot * 128 + 128], tile, sb["identb"][:]),
                         reads=[("ldb",), "identb"], writes=["PX"])
                    s.op("vector", lambda e2, e4=e4, half=half, dgb=dgb, q=q, slot=slot:
                         e2.tensor_copy(
                         sb["b2s"][e4 * 32:e4 * 32 + 16,
                                   half * 1024 + dgb * 512 + q * 128:
                                   half * 1024 + dgb * 512 + (q + 1) * 128],
                         PXb[0:16, slot * 128:slot * 128 + 128]),
                         reads=["PX"], writes=["b2s"])

    if stop_stage <= 1:
        s.dma("sync", t["dbg_wts"][:, 0:4], sb["gateT"][:, 0:8].bitcast(F32),
              reads=["gateT"], writes=["dbg"], force_inc=True)
        s.final_wait()
        return
    for b in range(NB):
        for dc in range(DCH):
            s.dma("sync", x1f[:, dc * TT:(dc + 1) * TT], t["x1"][b, dc * 128:(dc + 1) * 128, :],
                  reads=[("x1f", dc)], writes=[("x1f", dc)])
            s.op("vector", lambda e, dc=dc: e.tensor_copy(
                 sb["rb"][:], x1f[:, dc * TT:(dc + 1) * TT]),
                 reads=[("x1f", dc), "rb"], writes=["rb"])
            s.op("vector", lambda e: e.tensor_mul(sb["sqb"][:], sb["rb"][:], sb["rb"][:]),
                 reads=["rb"], writes=["sqb"])
            s.op("tensor", lambda e, dc=dc: e.matmul(
                 ap(PD, 2 * 512, [[2048, 1], [1, TT]]), sb["onescol"][:], sb["sqb"][:],
                 start=(dc == 0), stop=(dc == DCH - 1)),
                 reads=["sqb", "onescol"], writes=[("PD", 2)])
        s.op("vector", lambda e: e.tensor_scalar(xio[0:1, 0:TT], PDv(2)[0:1, 0:TT],
                                                 1.0 / D, 1e-6, op0=ALU.mult, op1=ALU.add),
             reads=[("PD", 2)], writes=[("xio", 0)])
        s.op("scalar", lambda e: e.activation(xio[0:1, TT:2 * TT], xio[0:1, 0:TT], AF.Sqrt),
             reads=[("xio", 0)], writes=[("xio", 1)])
        s.op("vector", lambda e: e.reciprocal(xio[0:1, 0:TT], xio[0:1, TT:2 * TT]),
             reads=[("xio", 1)], writes=[("xio", 0)])
        s.op("tensor", lambda e: e.matmul(PA[:, 0:TT], sb["onesrow"][:], xio[0:1, 0:TT],
                                          start=True, stop=True),
             reads=[("xio", 0), "onesrow"], writes=["PA"])
        for dc in range(DCH):
            s.op("vector", lambda e, dc=dc: e.scalar_tensor_tensor(
                 nxb[:, dc * TT:(dc + 1) * TT], x1f[:, dc * TT:(dc + 1) * TT],
                 sb["normw"][:, dc:dc + 1], PA[:, 0:TT], op0=ALU.mult, op1=ALU.mult),
                 reads=[("x1f", dc), "PA", ("nxb", dc)], sreads=["normw_s"],
                 writes=[("nxb", dc)])
        if stop_stage <= 2:
            continue
        for dc in range(DCH):
            s.op("tensor", lambda e, dc=dc: e.matmul(
                 PB[0:8, 0:TT], sb["gateT"][:, dc * 8:(dc + 1) * 8], nxb[:, dc * TT:(dc + 1) * TT],
                 start=(dc == 0), stop=(dc == DCH - 1)),
                 reads=[("nxb", dc), "gateT"], writes=["PB"])
        s.op("vector", lambda e: e.tensor_copy(sb["wEm"][0:8, 0:TT], PB[0:8, 0:TT]),
             reads=["PB"], writes=["wEm"])
        for tc in range(NTC):
            s.op("tensor", lambda e, tc=tc: e.transpose(
                 PV[:, tc * 8:(tc + 1) * 8], sb["wEm"][0:8, tc * 128:(tc + 1) * 128],
                 sb["identf"][0:8, 0:8]),
                 reads=["wEm", "identf"], writes=["PV"])
        s.op("vector", lambda e: e.tensor_copy(sb["gs1"][:, 0:NTC * 8], PV[:, 0:NTC * 8]),
             reads=["PV"], writes=["gs1"])
        s.op("vector", lambda e: e.tensor_reduce(sb["gm"][:, 0:NTC].unsqueeze(2),
             g3(sb["gs1"][:, 0:NTC * 8]), axis=AX.X, op=ALU.max, negate=True),
             reads=["gs1"], writes=["gm"])
        s.op("vector", lambda e: e.tensor_tensor(sb["gs2"][:, 0:NTC * 8], sb["gs1"][:, 0:NTC * 8],
             ap(sb["gm"], 0, [[4, 128], [1, NTC], [0, 8]]), op=ALU.add),
             reads=["gs1", "gm"], writes=["gs2"])
        s.op("scalar", lambda e: e.activation(sb["gs1"][:, 0:NTC * 8], sb["gs2"][:, 0:NTC * 8],
                                              AF.Exp),
             reads=["gs2"], writes=["gs1"])
        s.op("vector", lambda e: e.tensor_reduce(sb["gm"][:, 0:NTC].unsqueeze(2),
             g3(sb["gs1"][:, 0:NTC * 8]), axis=AX.X, op=ALU.add),
             reads=["gs1"], writes=["gm"])
        s.op("vector", lambda e: e.reciprocal(sb["gmr"][:, 0:NTC], sb["gm"][:, 0:NTC]),
             reads=["gm"], writes=["gmr"])
        s.op("vector", lambda e: e.tensor_tensor(sb["gs2"][:, 0:NTC * 8], sb["gs1"][:, 0:NTC * 8],
             ap(sb["gmr"], 0, [[4, 128], [1, NTC], [0, 8]]), op=ALU.mult),
             reads=["gs1", "gmr"], writes=["gs2"])
        s.op("vector", lambda e: e.tensor_reduce(sb["gm"][:, 0:NTC].unsqueeze(2),
             g3(sb["gs2"][:, 0:NTC * 8]), axis=AX.X, op=ALU.max),
             reads=["gs2"], writes=["gm"])
        s.op("vector", lambda e: e.tensor_tensor(sb["gs1"][:, 0:NTC * 8], sb["gs2"][:, 0:NTC * 8],
             ap(sb["gm"], 0, [[4, 128], [1, NTC], [0, 8]]), op=ALU.is_equal),
             reads=["gs2", "gm"], writes=["gs1"])
        s.op("gpsimd", lambda e: e.tensor_tensor(sb["wtsE"][:, 0:NTC * 8], sb["gs1"][:, 0:NTC * 8],
             sb["idxw"][:, 0:NTC * 8], op=ALU.mult),
             reads=["gs1", "idxw"], writes=["wtsE"])
        s.op("vector", lambda e: e.tensor_reduce(sb["gm2"][:, 0:NTC].unsqueeze(2),
             g3(sb["wtsE"][:, 0:NTC * 8]), axis=AX.X, op=ALU.max),
             reads=["wtsE"], writes=["gm2"])
        s.op("vector", lambda e: e.tensor_tensor(sb["gs1"][:, 0:NTC * 8], sb["wtsE"][:, 0:NTC * 8],
             ap(sb["gm2"], 0, [[4, 128], [1, NTC], [0, 8]]), op=ALU.is_equal),
             reads=["wtsE", "gm2"], writes=["gs1"])
        s.op("vector", lambda e: e.scalar_tensor_tensor(sb["wtsE"][:, 0:NTC * 8],
             sb["gs1"][:, 0:NTC * 8], -1e30, sb["gs2"][:, 0:NTC * 8],
             op0=ALU.mult, op1=ALU.add),
             reads=["gs1", "gs2"], writes=["wtsE"])
        s.op("vector", lambda e: e.tensor_reduce(sb["gm2"][:, 0:NTC].unsqueeze(2),
             g3(sb["wtsE"][:, 0:NTC * 8]), axis=AX.X, op=ALU.max),
             reads=["wtsE"], writes=["gm2"])
        s.op("vector", lambda e: e.tensor_tensor(sb["wtsE"][:, 0:NTC * 8], sb["wtsE"][:, 0:NTC * 8],
             ap(sb["gm2"], 0, [[4, 128], [1, NTC], [0, 8]]), op=ALU.is_equal),
             reads=["wtsE", "gm2"], writes=["wtsE"])
        s.op("gpsimd", lambda e: e.tensor_tensor(sb["gs2"][:, 0:NTC * 8], sb["wtsE"][:, 0:NTC * 8],
             sb["idxw"][:, 0:NTC * 8], op=ALU.mult),
             reads=["wtsE", "idxw"], writes=["gs2"])
        s.op("vector", lambda e: e.tensor_reduce(sb["gmr"][:, 0:NTC].unsqueeze(2),
             g3(sb["gs2"][:, 0:NTC * 8]), axis=AX.X, op=ALU.max),
             reads=["gs2"], writes=["gmr"])
        s.op("vector", lambda e: e.tensor_tensor(sb["wtsE"][:, 0:NTC * 8], sb["gs2"][:, 0:NTC * 8],
             ap(sb["gmr"], 0, [[4, 128], [1, NTC], [0, 8]]), op=ALU.is_equal),
             reads=["gs2", "gmr"], writes=["wtsE"])
        s.op("gpsimd", lambda e: e.tensor_tensor(sb["gs1"][:, 0:NTC * 8], sb["gs1"][:, 0:NTC * 8],
             ap(sb["gm"], 0, [[4, 128], [1, NTC], [0, 8]]), op=ALU.mult),
             reads=["gs1", "gm"], writes=["gs1"])
        s.op("gpsimd", lambda e: e.tensor_tensor(sb["wtsE"][:, 0:NTC * 8], sb["wtsE"][:, 0:NTC * 8],
             ap(sb["gm2"], 0, [[4, 128], [1, NTC], [0, 8]]), op=ALU.mult),
             reads=["wtsE", "gm2"], writes=["wtsE"])
        s.op("gpsimd", lambda e: e.tensor_tensor(sb["wtsE"][:, 0:NTC * 8], sb["wtsE"][:, 0:NTC * 8],
             sb["gs1"][:, 0:NTC * 8], op=ALU.add),
             reads=["wtsE", "gs1"], writes=["wtsE"])
        s.op("vector", lambda e: e.tensor_tensor(sb["gm"][:, 0:NTC], sb["gm"][:, 0:NTC],
             sb["gm2"][:, 0:NTC], op=ALU.add),
             reads=["gm", "gm2"], writes=["gm"])
        s.op("vector", lambda e: e.reciprocal(sb["gmr"][:, 0:NTC], sb["gm"][:, 0:NTC]),
             reads=["gm"], writes=["gmr"])
        s.op("vector", lambda e: e.tensor_tensor(sb["wtsE"][:, 0:NTC * 8], sb["wtsE"][:, 0:NTC * 8],
             ap(sb["gmr"], 0, [[4, 128], [1, NTC], [0, 8]]), op=ALU.mult),
             reads=["wtsE", "gmr"], writes=["wtsE"])
        for tc in range(NTC):
            s.op("tensor", lambda e, tc=tc: e.transpose(
                 PX[0:8, tc * 128:(tc + 1) * 128],
                 sb["wtsE"][:, tc * 8:(tc + 1) * 8], sb["identf"][:]),
                 reads=["wtsE", "identf"], writes=["PX"])
        s.op("vector", lambda e: e.tensor_copy(sb["wEmb"][0:8, 0:TT], PX[0:8, 0:TT]),
             reads=["PX"], writes=["wEmb"])
        for e_ in range(E):
            s.op("tensor", lambda e, e_=e_: e.matmul(
                 PA[:, 0:TT], sb["selc"][:, e_ * 128:(e_ + 1) * 128], sb["wEmb"][0:8, 0:TT],
                 start=True, stop=True),
                 reads=["wEmb", "selc"], writes=["PA"])
            s.op("vector", lambda e, e_=e_: e.tensor_copy(
                 wrep[:, e_ * TT:(e_ + 1) * TT], PA[:, 0:TT]),
                 reads=["PA"], writes=[("wrep", e_)])
        if dbg and b == 0:
            s.dma("sync", t["dbg_nxb"], sb["nxb"][:], reads=[("nxb", dc) for dc in range(DCH)], writes=["dbg"])
            s.dma("sync", t["dbg_wts"][:, 0:NTC * 8], sb["wtsE"][:, 0:NTC * 8], reads=["wtsE"], writes=["dbg"])
            s.dma("sync", t["dbg_wrep"], sb["wrep"][:], reads=[("wrep", e_) for e_ in range(E)], writes=["dbg"])
        if stop_stage <= 3:
            continue
        for w_, (wT, bt) in enumerate(((t["w1T"], base1), (t["w3T"], base3))):
            for fc in range(FCH):
                slot = (fc * 2 + w_) % 2
                s.dma("sync", sb["wst"][:, slot * 1024:(slot + 1) * 1024],
                      ap(wT, fc * 128, [[F_, 128], [128 * F_, 8], [1, 128]]),
                      reads=[("wst", slot)], writes=[("wst", slot)])
                for dc in range(DCH):
                    s.op("tensor", lambda e, dc=dc, slot=slot: e.matmul(
                         PB[:, 0:TT], sb["wst"][:, slot * 1024 + dc * 128: slot * 1024 + (dc + 1) * 128],
                         nxb[:, dc * TT:(dc + 1) * TT],
                         start=(dc == 0), stop=(dc == DCH - 1)),
                         reads=[("wst", slot), ("nxb", dc)], writes=["PB"])
                s.op("vector", lambda e, fc=fc, bt=bt: e.tensor_copy(
                     bt[:, fc * TT:(fc + 1) * TT], PB[:, 0:TT]),
                     reads=["PB"], writes=[("base", w_, fc)])
        for half in range(2):
            for (aname, ename) in (("a1T", "etr1P"), ("a3T", "etr3P")):
                for dc in range(DCH):
                    s.op("tensor", lambda e, dc=dc, half=half, aname=aname: e.matmul(
                         PB[:, 0:TT],
                         sb[aname][:, half * 1024 + dc * 128: half * 1024 + (dc + 1) * 128],
                         nxb[:, dc * TT:(dc + 1) * TT],
                         start=(dc == 0), stop=(dc == DCH - 1)),
                         reads=[aname, ("nxb", dc)], writes=["PB"])
                s.op("vector", lambda e, half=half, ename=ename: e.tensor_scalar_mul(
                     sb[ename][:, half * TT:(half + 1) * TT], PB[:, 0:TT], 2.0),
                     reads=["PB"], writes=[(ename, half)])
        if dbg and b == 0:
            s.dma("sync", t["dbg_base1"], sb["base1"][:], reads=[("base", 0, fc) for fc in range(FCH)], writes=["dbg"])
            s.dma("sync", t["dbg_etr1"], sb["etr1P"][:], reads=[("etr1P", 0), ("etr1P", 1)], writes=["dbg"])
        for bank, bk in ((PV, "PV"), (PX, "PX")):
            s.op("tensor", lambda e, bank=bank: e.matmul(
                 bank[:, 0:TT], sb["zzb"][0:1, 0:128], sb["wEmb"][0:1, 0:TT],
                 start=True, stop=False, skip_group_check=True),
                 reads=["zzb", "wEmb"], writes=[bk])
        if stop_stage <= 4:
            continue
        for fcg in range(8):
            for w_, bT in ((0, t["b1T"]), (1, t["b3T"])):
                for e_ in range(E):
                    half, e4 = e_ // 4, e_ % 4
                    s.dma("scalar",
                          ap(sb["bst"], (e4 * 32) * 2048 + w_ * 1024 + half * 512,
                             [[2048, 16], [1, 512]]),
                          bT[e_, :, fcg * 512:(fcg + 1) * 512],
                          reads=[("bst", w_, half, e4)], writes=[("bst", w_, half, e4)])
            for fci in range(4):
                fc = fcg * 4 + fci
                for e_ in range(E):
                    half, e4 = e_ // 4, e_ % 4
                    lhs1 = ap(sb["bst"], (e4 * 32) * 2048 + 0 * 1024 + half * 512 + fci * 128,
                              [[2048, 32], [1, 128]])
                    lhs3 = ap(sb["bst"], (e4 * 32) * 2048 + 1 * 1024 + half * 512 + fci * 128,
                              [[2048, 32], [1, 128]])
                    rhs1 = ap(sb["etr1P"], (e4 * 32) * (2 * TT) + half * TT, [[2 * TT, 32], [1, TT]])
                    rhs3 = ap(sb["etr3P"], (e4 * 32) * (2 * TT) + half * TT, [[2 * TT, 32], [1, TT]])
                    tp = (32 * e4, 0) if e4 > 0 else None
                    s.op("tensor", lambda e, lhs1=lhs1, rhs1=rhs1, tp=tp: e.matmul(
                         PA[:, 0:TT], lhs1, rhs1, start=True, stop=False,
                         tile_position=tp, skip_group_check=True),
                         reads=[("bst", 0, e_ // 4, e4), ("etr1P", e_ // 4)], writes=["PA"])
                    s.op("tensor", lambda e, fc=fc: e.matmul(
                         PA[:, 0:TT], sb["identb"][:], base1[:, fc * TT:(fc + 1) * TT],
                         start=False, stop=True, skip_group_check=True),
                         reads=["identb", ("base", 0, fc)], writes=["PA"])
                    s.op("scalar", lambda e: e.activation(sb["silu"][:], PA[:, 0:TT], AF.Silu),
                         reads=["PA"], writes=["rb"])
                    s.op("tensor", lambda e, lhs3=lhs3, rhs3=rhs3, tp=tp: e.matmul(
                         PB[:, 0:TT], lhs3, rhs3, start=True, stop=False,
                         tile_position=tp, skip_group_check=True),
                         reads=[("bst", 1, e_ // 4, e4), ("etr3P", e_ // 4)], writes=["PB"])
                    s.op("tensor", lambda e, fc=fc: e.matmul(
                         PB[:, 0:TT], sb["identb"][:], base3[:, fc * TT:(fc + 1) * TT],
                         start=False, stop=True, skip_group_check=True),
                         reads=["identb", ("base", 1, fc)], writes=["PB"])
                    s.op("vector", lambda e, e_=e_: e.tensor_tensor(
                         sb["h3w"][:], PB[:, 0:TT], wrep[:, e_ * TT:(e_ + 1) * TT], op=ALU.mult),
                         reads=["PB", ("wrep", e_), "sqb"], writes=["sqb"])
                    s.op("gpsimd", lambda e, fci=fci, e_=e_: e.tensor_tensor(
                         actg[:, (fci * 8 + e_) * TT:(fci * 8 + e_ + 1) * TT],
                         sb["h3w"][:], sb["silu"][:], op=ALU.mult),
                         reads=["sqb", "rb"], writes=[("actg", fci, e_)])
                    bank = PV if half == 0 else PX
                    bk = "PV" if half == 0 else "PX"
                    s.op("tensor", lambda e, fc=fc, e_=e_, fci=fci, bank=bank, e4=e4: e.matmul(
                         ap(bank, (e4 * 32) * 512, [[512, 16], [1, TT]]),
                         sb["a2T"][:, fc * 128 + (e_ % 8) * 16: fc * 128 + (e_ % 8 + 1) * 16],
                         actg[:, (fci * 8 + e_) * TT:(fci * 8 + e_ + 1) * TT],
                         start=False, stop=(fc == FCH - 1 and e4 == 3), skip_group_check=True,
                         tile_position=((0, 32 * e4) if e4 > 0 else None)),
                         reads=["a2T", ("actg", fci, e_)], writes=[bk])
                for e_ in range(E):
                    s.op("tensor", lambda e, fci=fci, e_=e_: e.matmul(
                         PA[:, 0:TT], sb["identb"][:],
                         actg[:, (fci * 8 + e_) * TT:(fci * 8 + e_ + 1) * TT],
                         start=(e_ == 0), stop=(e_ == E - 1), skip_group_check=True),
                         reads=["identb", ("actg", fci, e_)], writes=["PA"])
                s.op("vector", lambda e, fci=fci: e.tensor_copy(
                     ldxb[:, fci * TT:(fci + 1) * TT], PA[:, 0:TT]),
                     reads=["PA"], writes=[("ldx", fci // 2)])
            for dcg in range(2):
                for dc4 in range(4):
                    dc = dcg * 4 + dc4
                    for fci in range(4):
                        fc = fcg * 4 + fci
                        slot = fci
                        if dcg == 0 and dc4 == 0:
                            s.dma("sync", sb["w2st"][:, slot * 1024:(slot + 1) * 1024],
                                  ap(t["w2T"], fc * 128 * D, [[D, 128], [1, 1024]]),
                                  reads=[("w2st", slot)], writes=[("w2st", slot)])
                        s.op("tensor", lambda e, dc=dc, fci=fci, slot=slot, dc4=dc4:
                             e.matmul(
                             PDv(dc4)[:, 0:TT],
                             sb["w2st"][:, slot * 1024 + dc * 128: slot * 1024 + (dc + 1) * 128],
                             ldxb[:, fci * TT:(fci + 1) * TT],
                             start=(fci == 0), stop=(fci == 3),
                             skip_group_check=True),
                             reads=[("w2st", slot), ("ldx", fci // 2)], writes=[("PD", dc4)])
                    if fcg == 0:
                        s.op("vector", lambda e, dc=dc, dc4=dc4: e.tensor_copy(
                             outacc[:, dc * TT:(dc + 1) * TT], PDv(dc4)[:, 0:TT]),
                             reads=[("PD", dc4)], writes=[("outacc", dc)])
                    else:
                        s.op("vector", lambda e, dc=dc, dc4=dc4: e.tensor_add(
                             outacc[:, dc * TT:(dc + 1) * TT], PDv(dc4)[:, 0:TT],
                             outacc[:, dc * TT:(dc + 1) * TT]),
                             reads=[("PD", dc4), ("outacc", dc)], writes=[("outacc", dc)])
        if dbg and b == 0:
            s.dma("sync", t["dbg_act"], sb["actg"][:],
                  reads=[("actg", fci, e_) for fci in range(4) for e_ in range(E)], writes=["dbg"])
        if stop_stage <= 5:
            continue
        for half, (bank, bk) in enumerate(((PV, "PV"), (PX, "PX"))):
            s.op("vector", lambda e, half=half, bank=bank: e.tensor_scalar_mul(
                 sb["etr2P"][:, half * TT:(half + 1) * TT], bank[:, 0:TT], 2.0),
                 reads=[bk], writes=[("etr2P", half)])
        if stop_stage <= 7:
            continue
        for dc in range(DCH):
            for half in range(2):
                s.op("tensor", lambda e, dc=dc, half=half: e.matmul(
                     PA[:, 0:TT],
                     sb["b2s"][:, half * 1024 + dc * 128: half * 1024 + (dc + 1) * 128],
                     ap(sb["etr2P"], half * TT, [[2 * TT, 128], [1, TT]]),
                     start=(half == 0), stop=(half == 1), skip_group_check=True),
                     reads=["b2s", ("etr2P", half)], writes=["PA"])
            s.op("vector", lambda e, dc=dc: e.tensor_add(
                 outacc[:, dc * TT:(dc + 1) * TT], PA[:, 0:TT], outacc[:, dc * TT:(dc + 1) * TT]),
                 reads=["PA", ("outacc", dc)], writes=[("outacc", dc)])
        if dbg and b == 0:
            s.dma("sync", t["dbg_out"], sb["outacc"][:],
                  reads=[("outacc", dc) for dc in range(DCH)], writes=["dbg"])
        if stop_stage <= 6:
            continue
        for dc in range(DCH):
            s.op("vector", lambda e, dc=dc: e.tensor_add(
                 outacc[:, dc * TT:(dc + 1) * TT], outacc[:, dc * TT:(dc + 1) * TT],
                 x1f[:, dc * TT:(dc + 1) * TT]),
                 reads=[("outacc", dc), ("x1f", dc)], writes=[("outacc", dc)])
            s.op("vector", lambda e, dc=dc: e.tensor_copy(
                 sb["rb"][:], outacc[:, dc * TT:(dc + 1) * TT]),
                 reads=[("outacc", dc), "rb"], writes=["rb"])
            s.op("vector", lambda e: e.tensor_mul(sb["sqb"][:], sb["rb"][:], sb["rb"][:]),
                 reads=["rb"], writes=["sqb"])
            s.op("tensor", lambda e, dc=dc: e.matmul(
                 ap(PD, 2 * 512, [[2048, 1], [1, TT]]), sb["onescol"][:], sb["rb"][:],
                 start=(dc == 0), stop=(dc == DCH - 1)),
                 reads=["rb", "onescol"], writes=[("PD", 2)])
            s.op("tensor", lambda e, dc=dc: e.matmul(
                 ap(PD, 3 * 512, [[2048, 1], [1, TT]]), sb["onescol"][:], sb["sqb"][:],
                 start=(dc == 0), stop=(dc == DCH - 1)),
                 reads=["sqb", "onescol"], writes=[("PD", 3)])
        if stop_stage <= 8:
            continue
        s.op("vector", lambda e: e.tensor_scalar_mul(xio[0:1, 0:TT], PDv(2)[0:1, 0:TT], 1.0 / D),
             reads=[("PD", 2)], writes=[("xio", 0)])
        s.op("vector", lambda e: e.tensor_scalar_mul(xio[0:1, TT:2 * TT], PDv(3)[0:1, 0:TT], 1.0 / D),
             reads=[("PD", 3)], writes=[("xio", 1)])
        s.op("vector", lambda e: e.tensor_mul(sb["wEm"][0:1, 0:TT],
             xio[0:1, 0:TT], xio[0:1, 0:TT]),
             reads=[("xio", 0)], writes=["wEm"])
        s.op("vector", lambda e: e.scalar_tensor_tensor(
             xio[0:1, TT:2 * TT], xio[0:1, TT:2 * TT], 1e-5, sb["wEm"][0:1, 0:TT],
             op0=ALU.add, op1=ALU.subtract),
             reads=[("xio", 1), "wEm"], writes=[("xio", 1)])
        s.op("scalar", lambda e: e.activation(sb["wEm"][0:1, 0:TT], xio[0:1, TT:2 * TT], AF.Sqrt),
             reads=[("xio", 1)], writes=["wEm"])
        s.op("vector", lambda e: e.reciprocal(xio[0:1, TT:2 * TT], sb["wEm"][0:1, 0:TT]),
             reads=["wEm"], writes=[("xio", 1)])
        s.op("tensor", lambda e: e.matmul(PA[:, 0:TT], sb["onesrow"][:], xio[0:1, 0:TT],
                                          start=True, stop=True),
             reads=[("xio", 0), "onesrow"], writes=["PA"])
        s.op("tensor", lambda e: e.matmul(PB[:, 0:TT], sb["onesrow"][:], xio[0:1, TT:2 * TT],
                                          start=True, stop=True),
             reads=[("xio", 1), "onesrow"], writes=["PB"])
        if stop_stage <= 9:
            continue
        for dc in range(DCH):
            s.op("vector", lambda e, dc=dc: e.tensor_sub(
                 sb["x2s"][:, 0:TT], outacc[:, dc * TT:(dc + 1) * TT], PA[:, 0:TT]),
                 reads=[("outacc", dc), "PA", ("ldx", 0)], writes=[("ldx", 0)])
            s.op("vector", lambda e: e.tensor_mul(
                 sb["x2s"][:, TT:2 * TT], sb["x2s"][:, 0:TT], PB[:, 0:TT]),
                 reads=[("ldx", 0), "PB", ("ldx", 1)], writes=[("ldx", 1)])
            if stop_stage > 10 and (stop_stage > 11 or dc == 0):
                s.dma("sync", t["x2"][b, dc * 128:(dc + 1) * 128, :], sb["x2s"][:, TT:2 * TT],
                      reads=[("ldx", 1)], writes=[("x2", b, dc)], force_inc=True)
    if stop_stage < 99:
        s.dma("sync", t["dbg_wts"][:, 0:8], sb["gs1"][:, 0:8],
              reads=["gs1"], writes=["dbg2"], force_inc=True)
    if final:
        s.final_wait()


D, NH, HD, SL = 1024, 16, 64, 8
DCH = 8


def declare_dram_c(nc, TL, suffix="", dbg=False, x2=None, x3=None):
    t = {}
    ein = lambda n, s: nc.dram_tensor(n + suffix, s, F32, kind="ExternalInput").ap()
    t["x2"] = x2 if x2 is not None else ein("x2", [SL, D, TL])
    t["Wy"] = ein("Wy", [1, NH, HD, HD])
    t["Wq"] = ein("Wq", [1, NH, HD, HD])
    t["Wk"] = ein("Wk", [1, NH, HD, HD])
    t["wb"] = ein("wb", [1, NH, HD, 4])
    t["outw"] = ein("outw", [D, D])
    t["lng"] = ein("lng", [D])
    t["lnb"] = ein("lnb", [D])
    t["x3"] = x3 if x3 is not None else nc.dram_tensor("x3" + suffix, [SL, D, TL], F32, kind="ExternalOutput").ap()
    t["outwT"] = nc.dram_tensor("outwT" + suffix, [D, D], BF16, kind="Internal").ap()
    if dbg:
        t["dbg_bxs"] = nc.dram_tensor("dbg_bxs" + suffix, [128, SL * 256], F32, kind="ExternalOutput").ap()
    if dbg:
        t["dbg_ys"] = nc.dram_tensor("dbg_ys" + suffix, [128, SL * DCH * 128], BF16, kind="ExternalOutput").ap()
    if dbg:
        t["dbg_attn"] = nc.dram_tensor("dbg_attn" + suffix, [128, SL * DCH * 128], F32, kind="ExternalOutput").ap()
    if dbg:
        t["dbg_sm"] = nc.dram_tensor("dbg_sm" + suffix, [128, 192], F32, kind="ExternalOutput").ap()
    if dbg:
        t["dbg_rbuf"] = nc.dram_tensor("dbg_rbuf" + suffix, [128, DCH * 128], F32, kind="ExternalOutput").ap()
    if dbg:
        t["dbg_ln"] = nc.dram_tensor("dbg_ln" + suffix, [128, 128], F32, kind="ExternalOutput").ap()
    return t


def alloc_bufs_c(ctx, nc, TL, suffix=""):
    sb, ps = {}, {}
    s_ = lambda n, shape, dt: ctx.enter_context(nc.sbuf_tensor(n + suffix, shape, dt))
    p_ = lambda n, shape, dt: ctx.enter_context(nc.psum_tensor(n + suffix, shape, dt))
    sb["x2f"] = s_("x2f", [128, SL * DCH * 128], F32)
    sb["dTf"] = s_("dTf", [128, 128], F32)
    sb["ysfm"] = s_("ysfm", [128, SL * DCH * 128], BF16)
    sb["Xp"] = s_("Xp", [128, SL * HD], F32)
    sb["Kp"] = s_("Kp", [128, SL * HD], F32)
    sb["Uall"] = s_("Uall", [128, 256 * SL], F32)
    sb["bxs"] = s_("bxs", [128, SL * 256], F32)
    sb["bxd"] = s_("bxd", [128, 256], F32)
    sb["ya"] = s_("ya", [128, 256], F32)
    sb["ua"] = s_("ua", [128, 256], F32)
    sb["sm"] = s_("sm", [128, 192], F32)
    sb["beta"] = s_("beta", [128, 4], F32)
    sb["dbuf"] = s_("dbuf", [128, 64], F32)
    sb["ybf"] = s_("ybf", [128, 64], BF16)
    sb["nmx"] = s_("nmx", [128, 3], F32)
    sb["smr"] = s_("smr", [128, 3], F32)
    sb["tmp1"] = s_("tmp1", [128, SL * HD], F32)
    sb["tmp2"] = s_("tmp2", [128, 256 * SL], F32)
    sb["prj1"] = s_("prj1", [128, 128], F32)
    sb["prj2"] = s_("prj2", [128, 128], F32)
    sb["Wcat1"] = s_("Wcat1", [128, NH * 128], F32)
    sb["Wcat2"] = s_("Wcat2", [128, NH * 68], F32)
    sb["owst"] = s_("owst", [128, 2 * 1024], BF16)
    sb["identb"] = s_("identb", [128, 128], BF16)
    sb["identf"] = s_("identf", [128, 128], F32)
    sb["rbuf"] = s_("rbuf", [128, DCH * 128], F32)
    sb["lngc"] = s_("lngc", [128, 8], F32)
    sb["lnbc"] = s_("lnbc", [128, 8], F32)
    sb["rb"] = s_("rb", [128, 128], BF16)
    sb["sqb"] = s_("sqb", [128, 128], BF16)
    sb["stats"] = s_("stats", [1, 512], F32)
    sb["onescol"] = s_("onescol", [128, 1], BF16)
    sb["onesrow"] = s_("onesrow", [1, 128], F32)
    sb["ldf"] = s_("ldf", [128, 1024], F32)
    sb["ldb"] = s_("ldb", [128, 1024], BF16)
    sb["stg2"] = s_("stg2", [128, 256], BF16)
    sb["x3s"] = s_("x3s", [128, 2 * 128], F32)
    sb["attnb"] = s_("attnb", [128, SL * DCH * 128], F32)
    sb["gx"] = s_("gx", [128, 8], F32)
    ps["PA"] = p_("PA", [128, 512], F32)
    ps["PB"] = p_("PB", [128, 512], F32)
    ps["PD"] = p_("PD", [128, 2048], F32)
    ps["PV"] = p_("PV", [128, 512], F32)
    ps["PX"] = p_("PX", [128, 512], F32)
    return sb, ps


def emit_c(s: Sched, t, sb, ps, TL, dbg=False, final=True):
    nc = s.nc
    PA, PB, PD, PV, PX = ps["PA"], ps["PB"], ps["PD"], ps["PV"], ps["PX"]
    PXb = PX[:].bitcast(BF16)
    NTC = TL // 128
    PDv = lambda k, n=512: PD[:, k * 512:k * 512 + n]
    x2f, ysfm = sb["x2f"], sb["ysfm"]
    Xp, Kp, Uall, bxs = sb["Xp"], sb["Kp"], sb["Uall"], sb["bxs"]

    s.op("gpsimd", lambda e: e.memset(sb["identb"][:], 0.0), writes=["identb"])
    s.op("gpsimd", lambda e: e.affine_select(
        out=sb["identb"][:], in_=sb["identb"][:], compare_op=ALU.not_equal,
        fill=1.0, base=0, pattern=[[-1, 128]], channel_multiplier=1),
        reads=["identb"], writes=["identb"])
    s.op("gpsimd", lambda e: e.memset(sb["identf"][:], 0.0), writes=["identf"])
    s.op("gpsimd", lambda e: e.affine_select(
        out=sb["identf"][:], in_=sb["identf"][:], compare_op=ALU.not_equal,
        fill=1.0, base=0, pattern=[[-1, 128]], channel_multiplier=1),
        reads=["identf"], writes=["identf"])
    s.op("vector", lambda e: e.memset(sb["onescol"][:], 1.0), writes=["onescol"])
    s.op("vector", lambda e: e.memset(sb["Wcat1"][:], 0.0), writes=["Wcat1"])
    s.op("vector", lambda e: e.memset(sb["Wcat2"][:], 0.0), writes=["Wcat2"])
    s.op("vector", lambda e: e.memset(sb["onesrow"][:], 1.0), writes=["onesrow"])
    s.op("vector", lambda e: e.memset(PD[:], 0.0), writes=[("PD", k) for k in range(4)])
    for dc in range(DCH):
        s.dma("scalar", sb["lngc"][:, dc:dc + 1], t["lng"][dc * 128:(dc + 1) * 128].unsqueeze(1),
              reads=["lngc"], writes=["lngc"])
        s.dma("scalar", sb["lnbc"][:, dc:dc + 1], t["lnb"][dc * 128:(dc + 1) * 128].unsqueeze(1),
              reads=["lnbc"], writes=["lnbc"])
    for rc in range(8):
        s.dma("sync", sb["ldf"][:, 0:1024], t["outw"][rc * 128:(rc + 1) * 128, :],
              reads=["ldf"], writes=["ldf"])
        s.op("vector", lambda e: e.tensor_copy(sb["ldb"][:, 0:1024], sb["ldf"][:, 0:1024]),
             reads=["ldf", "ldb"], writes=["ldb"])
        for q in range(8):
            slot = q % 2
            s.op("tensor", lambda e, q=q, slot=slot: e.transpose(
                 PXb[:, slot * 128:(slot + 1) * 128],
                 sb["ldb"][:, q * 128:(q + 1) * 128], sb["identb"][:]),
                 reads=["ldb", "identb"], writes=["PX"])
            s.op("vector", lambda e, slot=slot: e.tensor_copy(
                 sb["stg2"][:, slot * 128:(slot + 1) * 128], PXb[:, slot * 128:(slot + 1) * 128]),
                 reads=["PX"], writes=[("stg2", slot)])
            s.dma("sync", t["outwT"][q * 128:(q + 1) * 128, rc * 128:(rc + 1) * 128],
                  sb["stg2"][:, slot * 128:(slot + 1) * 128],
                  reads=[("stg2", slot)], writes=["outwT_d"])
    for (wsrc, part) in ((t["Wy"], 0), (t["Wq"], 1), (t["Wk"], 2)):
        for ch in range(8):
            src = ap(wsrc, ch * 128 * 64, [[64, 128], [1, 64]])
            s.dma("sync", sb["ldf"][:, 0:64], src, reads=["ldf"], writes=["ldf"])
            for hsub in range(2):
                h = ch * 2 + hsub
                s.op("tensor", lambda e, hsub=hsub: e.transpose(
                     PX[0:64, 0:64], sb["ldf"][hsub * 64:(hsub + 1) * 64, 0:64],
                     sb["identf"][hsub * 64:(hsub + 1) * 64, hsub * 64:(hsub + 1) * 64]),
                     reads=["ldf", "identf"], writes=["PX"])
                for cpy in range(2):
                    if part < 2:
                        dst = sb["Wcat1"][cpy * 64:(cpy + 1) * 64,
                                          h * 128 + part * 64: h * 128 + (part + 1) * 64]
                        key = "Wcat1"
                    else:
                        dst = sb["Wcat2"][cpy * 64:(cpy + 1) * 64, h * 68: h * 68 + 64]
                        key = "Wcat2"
                    s.op("vector", lambda e, dst=dst: e.tensor_copy(dst, PX[0:64, 0:64]),
                         reads=["PX"], writes=[key])
    for ch in range(4):
        src = ap(t["wb"], ch * 2 * 64 * 4, [[4, 128], [1, 4]])
        s.dma("sync", sb["ldf"][:, 0:4], src, reads=["ldf"], writes=["ldf"])
        for hsub in range(2):
            h = ch * 2 + hsub
            for cpy in range(2):
                s.op("vector", lambda e, hsub=hsub, h=h, cpy=cpy: e.tensor_copy(
                     sb["Wcat2"][cpy * 64:(cpy + 1) * 64, h * 68 + 64: h * 68 + 68],
                     sb["ldf"][hsub * 64:(hsub + 1) * 64, 0:4]),
                     reads=["ldf"], writes=["Wcat2"])

    for tc in range(NTC):
        for b in range(SL):
            s.dma("sync", x2f[:, b * 1024:(b + 1) * 1024],
                  ap(t["x2"], b * D * TL + tc * 128, [[TL, 128], [128 * TL, 8], [1, 128]]),
                  reads=[("x2f", b)], writes=[("x2f", b)])
        for h in range(NH):
            hb = (h % 2) * 64
            dc = h // 2
            idsl = sb["identf"][hb:hb + 64, hb:hb + 64]
            s.op("vector", lambda e: e.memset(sb["bxs"][:], 0.0),
                 reads=["bxs"], writes=["bxs"])
            s.op("vector", lambda e: e.memset(sb["bxd"][:], 0.0),
                 reads=["bxd"], writes=["bxd"])
            for st in range(SL):
                xsl = lambda arr: arr[hb:hb + 64, (st * 8 + dc) * 128:(st * 8 + dc) * 128 + 128]
                s.op("tensor", lambda e, st=st: e.transpose(
                     PDv(0, 64)[:, 0:64], xsl(x2f), idsl),
                     reads=[("x2f", st), "identf"], writes=[("PD", 0)])
                s.op("vector", lambda e, st=st: e.tensor_copy(
                     Xp[:, st * HD:(st + 1) * HD], PDv(0, 64)[:, 0:64]),
                     reads=[("PD", 0)], writes=["Xp"])
                s.op("tensor", lambda e, st=st: e.matmul(
                     PA[:, 0:128], sb["Wcat1"][hb:hb + 64, h * 128:(h + 1) * 128],
                     xsl(x2f), start=True, stop=True),
                     reads=[("x2f", st), "Wcat1"], writes=["PA"])
                s.op("tensor", lambda e, st=st: e.matmul(
                     PB[0:68, 0:128], sb["Wcat2"][hb:hb + 64, h * 68:(h + 1) * 68],
                     xsl(x2f), start=True, stop=True),
                     reads=[("x2f", st), "Wcat2"], writes=["PB"])
                s.op("vector", lambda e: e.tensor_copy(sb["prj1"][:], PA[:, 0:128]),
                     reads=["PA"], writes=["prj1"])
                s.op("vector", lambda e: e.tensor_copy(sb["prj2"][0:68, :], PB[0:68, 0:128]),
                     reads=["PB"], writes=["prj2"])
                s.op("tensor", lambda e: e.transpose(PDv(0)[:, 0:128], sb["prj1"][:],
                                                     sb["identf"][:]),
                     reads=["prj1", "identf"], writes=[("PD", 0)])
                s.op("tensor", lambda e: e.transpose(PDv(1, 68)[:, 0:68], sb["prj2"][0:68, :],
                                                     sb["identf"][0:68, 0:68]),
                     reads=["prj2", "identf"], writes=[("PD", 1)])
                s.op("vector", lambda e, st=st: e.tensor_copy(
                     bxs[:, st * 256: st * 256 + 128], PDv(0)[:, 0:128]),
                     reads=[("PD", 0)], writes=["bxs"])
                s.op("vector", lambda e, st=st: e.tensor_copy(
                     bxs[:, st * 256 + 128: st * 256 + 196], PDv(1, 68)[:, 0:68]),
                     reads=[("PD", 1)], writes=["bxs"])
            if dbg and tc == 0 and h == 0:
                s.dma("sync", t["dbg_bxs"], bxs[:], reads=["bxs"], writes=["dbgk"])
            for st in range(SL):
                if st > 0:
                    s.op("gpsimd", lambda e, st=st: e.tensor_tensor(
                         sb["tmp1"][:, 0:st * HD], Kp[:, 0:st * HD],
                         ap(Xp, st * HD, [[SL * HD, 128], [0, st], [1, HD]]), op=ALU.mult),
                         reads=["Kp", "Xp", "tmp1"], writes=["tmp1"])
                    s.op("vector", lambda e, st=st: e.tensor_reduce(
                         ap(sb["gx"], 0, [[8, 128], [1, st], [0, 1]]),
                         sb["tmp1"][:, 0:st * HD].rearrange("p (s d) -> p s d", d=HD),
                         axis=AX.X, op=ALU.add),
                         reads=["tmp1"], writes=["gx"])
                    s.op("gpsimd", lambda e, st=st: e.tensor_tensor(
                         sb["tmp2"][:, 0:256 * st],
                         ap(Uall, 0, [[256 * SL, 128], [SL, 256], [1, st]]),
                         ap(sb["gx"], 0, [[8, 128], [0, 256], [1, st]]), op=ALU.mult),
                         reads=["Uall", "gx", "tmp2"], writes=["tmp2"])
                    s.op("vector", lambda e, st=st: e.tensor_reduce(
                         ap(sb["ua"], 0, [[256, 128], [1, 256], [0, 1]]),
                         sb["tmp2"][:, 0:256 * st].rearrange("p (o s) -> p o s", o=256),
                         axis=AX.X, op=ALU.add),
                         reads=["tmp2"], writes=["ua"])
                    s.op("vector", lambda e, st=st: e.tensor_add(
                         sb["ya"][:], sb["ua"][:, 0:256], bxs[:, st * 256:(st + 1) * 256]),
                         reads=["ua", "bxs", "ya"], writes=["ya"])
                else:
                    s.op("vector", lambda e: e.tensor_copy(sb["ya"][:], bxs[:, 0:256]),
                         reads=["bxs", "ya"], writes=["ya"])
                s.op("vector", lambda e: e.tensor_reduce(
                     sb["nmx"][:].unsqueeze(2),
                     sb["ya"][:, 0:192].rearrange("p (c d) -> p c d", d=HD),
                     axis=AX.X, op=ALU.max, negate=True),
                     reads=["ya"], writes=["nmx"])
                s.op("gpsimd", lambda e: e.tensor_tensor(
                     sb["sm"][:], sb["ya"][:, 0:192],
                     ap(sb["nmx"], 0, [[3, 128], [1, 3], [0, HD]]), op=ALU.add),
                     reads=["ya", "nmx", "sm"], writes=["sm"])
                s.op("scalar", lambda e: e.activation(sb["sm"][:], sb["sm"][:], AF.Exp),
                     reads=["sm"], writes=["sm"])
                s.op("vector", lambda e: e.tensor_reduce(
                     sb["smr"][:].unsqueeze(2),
                     sb["sm"][:].rearrange("p (c d) -> p c d", d=HD),
                     axis=AX.X, op=ALU.add),
                     reads=["sm"], writes=["smr"])
                s.op("vector", lambda e: e.reciprocal(sb["smr"][:], sb["smr"][:]),
                     reads=["smr"], writes=["smr"])
                s.op("gpsimd", lambda e: e.tensor_tensor(
                     sb["sm"][:], sb["sm"][:],
                     ap(sb["smr"], 0, [[3, 128], [1, 3], [0, HD]]), op=ALU.mult),
                     reads=["sm", "smr"], writes=["sm"])
                s.op("scalar", lambda e: e.activation(sb["beta"][:], sb["ya"][:, 192:196],
                                                      AF.Sigmoid),
                     reads=["ya"], writes=["beta"])
                if dbg and tc == 0 and h == 0 and st == 0:
                    s.dma("sync", t["dbg_sm"], sb["sm"][:], reads=["sm"], writes=["dbgk"])
                s.op("vector", lambda e, st=st: e.tensor_copy(
                     Kp[:, st * HD:(st + 1) * HD], sb["sm"][:, 128:192]),
                     reads=["sm", "Kp"], writes=["Kp"])
                s.op("vector", lambda e: e.tensor_sub(sb["dbuf"][:], sb["sm"][:, 64:128],
                                                      sb["sm"][:, 128:192]),
                     reads=["sm", "dbuf"], writes=["dbuf"])
                s.op("vector", lambda e: e.tensor_copy(sb["ybf"][:], sb["sm"][:, 0:64]),
                     reads=["sm", "ybf"], writes=["ybf"])
                s.op("tensor", lambda e: e.transpose(PXb[0:64, 0:128], sb["ybf"][:],
                                                     sb["identb"][:]),
                     reads=["ybf", "identb"], writes=["PX"])
                s.op("vector", lambda e, st=st: e.tensor_copy(
                     ysfm[hb:hb + 64, (st * 8 + dc) * 128:(st * 8 + dc) * 128 + 128],
                     PXb[0:64, 0:128]),
                     reads=["PX"], writes=[("ysfm", st, dc)])
                s.op("tensor", lambda e: e.transpose(PX[0:64, 128:256], sb["dbuf"][:], sb["identf"][:]),
                     reads=["dbuf", "identf"], writes=["PX"])
                s.op("vector", lambda e: e.tensor_copy(sb["dTf"][0:64, 0:128], PX[0:64, 128:256]),
                     reads=["PX"], writes=["dTf"])
                s.op("tensor", lambda e, h=h: e.matmul(
                     PA[:, 0:128], sb["Wcat1"][0:64, h * 128:(h + 1) * 128],
                     sb["dTf"][0:64, 0:128], start=True, stop=True),
                     reads=["dTf", "Wcat1"], writes=["PA"])
                s.op("tensor", lambda e, h=h: e.matmul(
                     PB[0:68, 0:128], sb["Wcat2"][0:64, h * 68:(h + 1) * 68],
                     sb["dTf"][0:64, 0:128], start=True, stop=True),
                     reads=["dTf", "Wcat2"], writes=["PB"])
                s.op("vector", lambda e: e.tensor_copy(sb["prj1"][:], PA[:, 0:128]),
                     reads=["PA"], writes=["prj1"])
                s.op("vector", lambda e: e.tensor_copy(sb["prj2"][0:68, :], PB[0:68, 0:128]),
                     reads=["PB"], writes=["prj2"])
                s.op("tensor", lambda e: e.transpose(PDv(0)[:, 0:128], sb["prj1"][:],
                                                     sb["identf"][:]),
                     reads=["prj1", "identf"], writes=[("PD", 0)])
                s.op("tensor", lambda e: e.transpose(PDv(1, 68)[:, 0:68], sb["prj2"][0:68, :],
                                                     sb["identf"][0:68, 0:68]),
                     reads=["prj2", "identf"], writes=[("PD", 1)])
                s.op("vector", lambda e: e.tensor_copy(sb["bxd"][:, 0:128], PDv(0)[:, 0:128]),
                     reads=[("PD", 0)], writes=["bxd"])
                s.op("vector", lambda e: e.tensor_copy(sb["bxd"][:, 128:196], PDv(1, 68)[:, 0:68]),
                     reads=[("PD", 1)], writes=["bxd"])
                if st > 0:
                    s.op("gpsimd", lambda e, st=st: e.tensor_tensor(
                         sb["tmp1"][:, 0:st * HD], Kp[:, 0:st * HD],
                         ap(sb["dbuf"], 0, [[64, 128], [0, st], [1, HD]]), op=ALU.mult),
                         reads=["Kp", "dbuf", "tmp1"], writes=["tmp1"])
                    s.op("vector", lambda e, st=st: e.tensor_reduce(
                         ap(sb["gx"], 0, [[8, 128], [1, st], [0, 1]]),
                         sb["tmp1"][:, 0:st * HD].rearrange("p (s d) -> p s d", d=HD),
                         axis=AX.X, op=ALU.add),
                         reads=["tmp1"], writes=["gx"])
                    s.op("gpsimd", lambda e, st=st: e.tensor_tensor(
                         sb["tmp2"][:, 0:256 * st],
                         ap(Uall, 0, [[256 * SL, 128], [SL, 256], [1, st]]),
                         ap(sb["gx"], 0, [[8, 128], [0, 256], [1, st]]), op=ALU.mult),
                         reads=["Uall", "gx", "tmp2"], writes=["tmp2"])
                    s.op("vector", lambda e, st=st: e.tensor_reduce(
                         ap(sb["ua"], 0, [[256, 128], [1, 256], [0, 1]]),
                         sb["tmp2"][:, 0:256 * st].rearrange("p (o s) -> p o s", o=256),
                         axis=AX.X, op=ALU.add),
                         reads=["tmp2"], writes=["ua"])
                    s.op("vector", lambda e: e.tensor_add(sb["ya"][:], sb["ua"][:, 0:256],
                                                          sb["bxd"][:]),
                         reads=["ua", "bxd", "ya"], writes=["ya"])
                else:
                    s.op("vector", lambda e: e.tensor_copy(sb["ya"][:], sb["bxd"][:]),
                         reads=["bxd", "ya"], writes=["ya"])
                s.op("gpsimd", lambda e, st=st: e.tensor_tensor(
                     ap(Uall, st, [[256 * SL, 128], [64 * SL, 4], [SL, 64]]),
                     ap(sb["ya"], 0, [[256, 128], [64, 4], [1, 64]]),
                     ap(sb["beta"], 0, [[4, 128], [1, 4], [0, 64]]),
                     op=ALU.mult),
                     reads=["ya", "beta", "Uall"], writes=["Uall"])
        if dbg and tc == 0:
            s.dma("sync", t["dbg_ys"], ysfm[:],
                  reads=[("ysfm", b_, d_) for b_ in range(SL) for d_ in range(DCH)], writes=["dbgk"])
        for oc in range(DCH):
            slot = oc % 2
            s.dma("sync", sb["owst"][:, slot * 1024:(slot + 1) * 1024],
                  ap(t["outwT"], oc * 128, [[D, 128], [128 * D, 8], [1, 128]]),
                  reads=[("owst", slot)], writes=[("owst", slot)])
            for b in range(SL):
                pk = ("PD", 2 + (b % 2))
                bank = PDv(2 + (b % 2))
                for dc2 in range(DCH):
                    s.op("tensor", lambda e, b=b, dc2=dc2, slot=slot, bank=bank: e.matmul(
                         bank[:, (b // 2) * 128:(b // 2) * 128 + 128],
                         sb["owst"][:, slot * 1024 + dc2 * 128: slot * 1024 + (dc2 + 1) * 128],
                         ysfm[:, (b * 8 + dc2) * 128:(b * 8 + dc2) * 128 + 128],
                         start=(dc2 == 0), stop=(dc2 == DCH - 1)),
                         reads=[("owst", slot), ("ysfm", b, dc2)], writes=[pk])
                s.op("vector", lambda e, oc=oc, b=b, bank=bank: e.tensor_copy(
                     sb["attnb"][:, (b * 8 + oc) * 128:(b * 8 + oc) * 128 + 128],
                     bank[:, (b // 2) * 128:(b // 2) * 128 + 128]),
                     reads=[pk], writes=[("attnb", b, oc)])
        if dbg and tc == 0:
            s.dma("sync", t["dbg_attn"], sb["attnb"][:],
                  reads=[("attnb", b_, o_) for b_ in range(SL) for o_ in range(DCH)], writes=["dbgk"])
        for b in range(SL):
            for dc2 in range(DCH):
                s.op("vector", lambda e, b=b, dc2=dc2: e.tensor_copy(
                     sb["rb"][:], x2f[:, b * 1024 + dc2 * 128: b * 1024 + (dc2 + 1) * 128]),
                     reads=[("x2f", b), "rb"], writes=["rb"])
                s.op("vector", lambda e: e.tensor_mul(sb["sqb"][:], sb["rb"][:], sb["rb"][:]),
                     reads=["rb"], writes=["sqb"])
                s.op("tensor", lambda e, dc2=dc2: e.matmul(
                     PA[0:1, 0:128], sb["onescol"][:], sb["rb"][:],
                     start=(dc2 == 0), stop=(dc2 == DCH - 1), skip_group_check=True),
                     reads=["rb", "onescol"], writes=["PA"])
                s.op("tensor", lambda e, dc2=dc2: e.matmul(
                     PB[0:1, 0:128], sb["onescol"][:], sb["sqb"][:],
                     start=(dc2 == 0), stop=(dc2 == DCH - 1), skip_group_check=True),
                     reads=["sqb", "onescol"], writes=["PB"])
            s.op("vector", lambda e: e.tensor_scalar_mul(sb["stats"][0:1, 0:128],
                 PA[0:1, 0:128], 1.0 / D), reads=["PA"], writes=["st0"])
            s.op("vector", lambda e: e.tensor_scalar_mul(sb["stats"][0:1, 128:256],
                 PB[0:1, 0:128], 1.0 / D), reads=["PB"], writes=["st1"])
            s.op("vector", lambda e: e.tensor_mul(sb["stats"][0:1, 256:384],
                 sb["stats"][0:1, 0:128], sb["stats"][0:1, 0:128]),
                 reads=["st0"], writes=["st2"])
            s.op("vector", lambda e: e.scalar_tensor_tensor(
                 sb["stats"][0:1, 128:256], sb["stats"][0:1, 128:256], 1e-5,
                 sb["stats"][0:1, 256:384], op0=ALU.add, op1=ALU.subtract),
                 reads=["st1", "st2"], writes=["st1"])
            s.op("scalar", lambda e: e.activation(sb["stats"][0:1, 256:384],
                 sb["stats"][0:1, 128:256], AF.Sqrt),
                 reads=["st1"], writes=["st2"])
            s.op("vector", lambda e: e.reciprocal(sb["stats"][0:1, 128:256],
                 sb["stats"][0:1, 256:384]),
                 reads=["st2"], writes=["st1"])
            s.op("tensor", lambda e: e.matmul(PV[:, 256:384], sb["onesrow"][:],
                 sb["stats"][0:1, 0:128], start=True, stop=True),
                 reads=["st0", "onesrow"], writes=["PV"])
            s.op("tensor", lambda e: e.matmul(PV[:, 384:512], sb["onesrow"][:],
                 sb["stats"][0:1, 128:256], start=True, stop=True),
                 reads=["st1", "onesrow"], writes=["PV"])
            for dc2 in range(DCH):
                s.op("vector", lambda e, b=b, dc2=dc2: e.tensor_sub(
                     sb["x3s"][:, 0:128],
                     x2f[:, b * 1024 + dc2 * 128: b * 1024 + (dc2 + 1) * 128],
                     PV[:, 256:384]),
                     reads=[("x2f", b), "PV", "x3s"], writes=["x3s"])
                s.op("vector", lambda e: e.tensor_mul(sb["x3s"][:, 128:256],
                     sb["x3s"][:, 0:128], PV[:, 384:512]),
                     reads=["x3s", "PV"], writes=["x3s2"])
                s.op("vector", lambda e, dc2=dc2: e.tensor_scalar(
                     sb["x3s"][:, 0:128], sb["x3s"][:, 128:256],
                     sb["lngc"][:, dc2:dc2 + 1], sb["lnbc"][:, dc2:dc2 + 1],
                     op0=ALU.mult, op1=ALU.add),
                     reads=["x3s2"], sreads=["lngc", "lnbc"], writes=["x3s"])
                if dbg and tc == 0 and b == 0 and dc2 == 0:
                    s.dma("sync", t["dbg_ln"], sb["x3s"][:, 0:128], reads=["x3s"], writes=["dbgk"])
                s.op("gpsimd", lambda e, b=b, dc2=dc2: e.tensor_tensor(
                     sb["x3s"][:, 128:256], sb["x3s"][:, 0:128],
                     sb["attnb"][:, (b * 8 + dc2) * 128:(b * 8 + dc2) * 128 + 128],
                     op=ALU.add),
                     reads=["x3s", ("attnb", b, dc2)], writes=["x3s2"])
                s.op("gpsimd", lambda e, b=b, dc2=dc2: e.tensor_tensor(
                     sb["rbuf"][:, dc2 * 128:(dc2 + 1) * 128], sb["x3s"][:, 128:256],
                     x2f[:, b * 1024 + dc2 * 128: b * 1024 + (dc2 + 1) * 128],
                     op=ALU.add),
                     reads=["x3s2", ("x2f", b), ("rbuf", dc2)], writes=[("rbuf", dc2)])
                s.op("vector", lambda e, dc2=dc2: e.tensor_copy(
                     sb["rb"][:], sb["rbuf"][:, dc2 * 128:(dc2 + 1) * 128]),
                     reads=[("rbuf", dc2), "rb"], writes=["rb"])
                s.op("vector", lambda e: e.tensor_mul(sb["sqb"][:], sb["rb"][:], sb["rb"][:]),
                     reads=["rb"], writes=["sqb"])
                s.op("tensor", lambda e, dc2=dc2: e.matmul(
                     PA[0:1, 0:128], sb["onescol"][:], sb["rb"][:],
                     start=(dc2 == 0), stop=(dc2 == DCH - 1), skip_group_check=True),
                     reads=["rb", "onescol"], writes=["PA"])
                s.op("tensor", lambda e, dc2=dc2: e.matmul(
                     PB[0:1, 0:128], sb["onescol"][:], sb["sqb"][:],
                     start=(dc2 == 0), stop=(dc2 == DCH - 1), skip_group_check=True),
                     reads=["sqb", "onescol"], writes=["PB"])
            if dbg and tc == 0 and b == 0:
                s.dma("sync", t["dbg_rbuf"], sb["rbuf"][:],
                      reads=[("rbuf", d_) for d_ in range(DCH)], writes=["dbgk"])
            s.op("vector", lambda e: e.tensor_scalar_mul(sb["stats"][0:1, 0:128],
                 PA[0:1, 0:128], 1.0 / D), reads=["PA"], writes=["st0"])
            s.op("vector", lambda e: e.tensor_scalar_mul(sb["stats"][0:1, 128:256],
                 PB[0:1, 0:128], 1.0 / D), reads=["PB"], writes=["st1"])
            s.op("vector", lambda e: e.tensor_mul(sb["stats"][0:1, 256:384],
                 sb["stats"][0:1, 0:128], sb["stats"][0:1, 0:128]),
                 reads=["st0"], writes=["st2"])
            s.op("vector", lambda e: e.scalar_tensor_tensor(
                 sb["stats"][0:1, 128:256], sb["stats"][0:1, 128:256], 1e-5,
                 sb["stats"][0:1, 256:384], op0=ALU.add, op1=ALU.subtract),
                 reads=["st1", "st2"], writes=["st1"])
            s.op("scalar", lambda e: e.activation(sb["stats"][0:1, 256:384],
                 sb["stats"][0:1, 128:256], AF.Sqrt),
                 reads=["st1"], writes=["st2"])
            s.op("vector", lambda e: e.reciprocal(sb["stats"][0:1, 128:256],
                 sb["stats"][0:1, 256:384]),
                 reads=["st2"], writes=["st1"])
            s.op("tensor", lambda e: e.matmul(PX[:, 256:384], sb["onesrow"][:],
                 sb["stats"][0:1, 0:128], start=True, stop=True),
                 reads=["st0", "onesrow"], writes=["PX"])
            s.op("tensor", lambda e: e.matmul(PX[:, 384:512], sb["onesrow"][:],
                 sb["stats"][0:1, 128:256], start=True, stop=True),
                 reads=["st1", "onesrow"], writes=["PX"])
            for dc2 in range(DCH):
                s.op("vector", lambda e, dc2=dc2: e.tensor_sub(
                     sb["x3s"][:, 0:128], sb["rbuf"][:, dc2 * 128:(dc2 + 1) * 128],
                     PX[:, 256:384]),
                     reads=[("rbuf", dc2), "PX", "x3s"], writes=["x3s"])
                s.op("vector", lambda e: e.tensor_mul(sb["x3s"][:, 128:256],
                     sb["x3s"][:, 0:128], PX[:, 384:512]),
                     reads=["x3s", "PX"], writes=["x3s2"])
                s.dma("sync", ap(t["x3"], b * D * TL + dc2 * 128 * TL + tc * 128,
                                 [[TL, 128], [1, 128]]),
                      sb["x3s"][:, 128:256],
                      reads=["x3s2"], writes=[("x3", tc, b, dc2)], force_inc=True)
    if final:
        s.final_wait()


D, E, FE, FS = 1024, 8, 512, 1024
DCH = 8


def declare_dram_d(nc, TT, NB, suffix="", dbg=False, x3=None):
    t = {}
    ein = lambda n, s: nc.dram_tensor(n + suffix, s, F32, kind="ExternalInput").ap()
    t["x3"] = x3 if x3 is not None else ein("x3", [NB, D, TT])
    t["routerw"] = ein("routerw", [D, E])
    t["wgate"] = ein("wgate", [E, D, FE])
    t["wup"] = ein("wup", [E, D, FE])
    t["wdown"] = ein("wdown", [E, FE, D])
    t["wsgate"] = ein("wsgate", [D, FS])
    t["wsup"] = ein("wsup", [D, FS])
    t["wsdown"] = ein("wsdown", [FS, D])
    t["x4"] = nc.dram_tensor("x4" + suffix, [NB, D, TT], F32, kind="ExternalOutput").ap()
    if dbg:
        t["dbg_wts"] = nc.dram_tensor("dbg_wts" + suffix, [128, 32], F32, kind="ExternalOutput").ap()
    if dbg:
        t["dbg_act"] = nc.dram_tensor("dbg_act" + suffix, [128, 512], BF16, kind="ExternalOutput").ap()
    if dbg:
        t["dbg_out"] = nc.dram_tensor("dbg_out" + suffix, [128, DCH * 512], F32, kind="ExternalOutput").ap()
    t["wgate_b"] = nc.dram_tensor("wgate_b" + suffix, [E, D, FE], BF16, kind="Internal").ap()
    t["wup_b"] = nc.dram_tensor("wup_b" + suffix, [E, D, FE], BF16, kind="Internal").ap()
    t["wdown_b"] = nc.dram_tensor("wdown_b" + suffix, [E, FE, D], BF16, kind="Internal").ap()
    t["wsgate_b"] = nc.dram_tensor("wsgate_b" + suffix, [D, FS], BF16, kind="Internal").ap()
    t["wsup_b"] = nc.dram_tensor("wsup_b" + suffix, [D, FS], BF16, kind="Internal").ap()
    t["wsdown_b"] = nc.dram_tensor("wsdown_b" + suffix, [FS, D], BF16, kind="Internal").ap()
    return t


def alloc_bufs_d(ctx, nc, TT, suffix=""):
    sb, ps = {}, {}
    s_ = lambda n, shape, dt: ctx.enter_context(nc.sbuf_tensor(n + suffix, shape, dt))
    p_ = lambda n, shape, dt: ctx.enter_context(nc.psum_tensor(n + suffix, shape, dt))
    NTC = TT // 128
    sb["x3f"] = s_("x3f", [128, DCH * TT], F32)
    sb["xb"] = s_("xb", [128, DCH * TT], BF16)
    sb["actg"] = s_("actg", [128, E * 4 * TT], BF16)
    sb["acts"] = s_("acts", [128, 8 * TT], BF16)
    sb["wst"] = s_("wst", [128, 4 * 1024], BF16)
    sb["wdst"] = s_("wdst", [128, 2 * 2048], BF16)
    sb["wrep"] = s_("wrep", [128, E * TT], BF16)
    sb["outacc"] = s_("outacc", [128, DCH * TT], F32)
    sb["routT"] = s_("routT", [128, 64], BF16)
    sb["gs1"] = s_("gs1", [128, 32], F32)
    sb["gs2"] = s_("gs2", [128, 32], F32)
    sb["wts"] = s_("wts", [128, 32], F32)
    sb["gm"] = s_("gm", [128, 4], F32)
    sb["gm2"] = s_("gm2", [128, 4], F32)
    sb["gmr"] = s_("gmr", [128, 4], F32)
    sb["idxu"] = s_("idxu", [128, 32], F32)
    sb["idxui"] = s_("idxui", [128, 32], mybir.dt.int32)
    sb["wEm"] = s_("wEm", [128, TT], F32)
    sb["wEmb"] = s_("wEmb", [8, 512], BF16)
    sb["selc"] = s_("selc", [8, 1024], BF16)
    sb["identf"] = s_("identf", [128, 128], F32)
    sb["onescol"] = s_("onescol", [128, 1], BF16)
    sb["onesrow"] = s_("onesrow", [1, 128], F32)
    sb["silu"] = s_("silu", [128, TT], BF16)
    sb["h3w"] = s_("h3w", [128, TT], BF16)
    sb["ldf"] = s_("ldf", [128, 2048], F32)
    sb["ldb"] = s_("ldb", [128, 2048], BF16)
    sb["rb"] = s_("rb", [128, TT], BF16)
    sb["sqb"] = s_("sqb", [128, TT], BF16)
    sb["xio"] = s_("xio", [128, 2 * TT], F32)
    sb["x4s"] = s_("x4s", [128, 2 * TT], F32)
    ps["PA"] = p_("PA", [128, 512], F32)
    ps["PB"] = p_("PB", [128, 512], F32)
    ps["PD"] = p_("PD", [128, 2048], F32)
    ps["PV"] = p_("PV", [128, 512], F32)
    ps["PX"] = p_("PX", [128, 512], F32)
    return sb, ps


def emit_d(s: Sched, t, sb, ps, TT, NB, dbg=False):
    nc = s.nc
    PA, PB, PD, PV, PX = ps["PA"], ps["PB"], ps["PD"], ps["PV"], ps["PX"]
    NTC = TT // 128
    PDv = lambda k, n=512: PD[:, k * 512:k * 512 + n]
    x3f, xb, actg, acts = sb["x3f"], sb["xb"], sb["actg"], sb["acts"]
    outacc, wrep, xio = sb["outacc"], sb["wrep"], sb["xio"]
    g3 = lambda a: a.rearrange("p (c e) -> p c e", e=8)

    s.op("gpsimd", lambda e: e.memset(sb["identf"][:], 0.0), writes=["identf"])
    s.op("gpsimd", lambda e: e.affine_select(
        out=sb["identf"][:], in_=sb["identf"][:], compare_op=ALU.not_equal,
        fill=1.0, base=0, pattern=[[-1, 128]], channel_multiplier=1),
        reads=["identf"], writes=["identf"])
    s.op("vector", lambda e: e.memset(sb["onescol"][:], 1.0), writes=["onescol"])
    s.op("vector", lambda e: e.memset(sb["onesrow"][:], 1.0), writes=["onesrow"])
    s.op("vector", lambda e: e.memset(PD[:], 0.0), writes=[("PD", k) for k in range(4)])
    s.op("gpsimd", lambda e: e.iota(sb["idxui"][:], base=1, pattern=[[0, 4], [1, 8]],
                                    channel_multiplier=0), writes=["idxui"])
    s.op("vector", lambda e: e.tensor_copy(sb["idxu"][:], sb["idxui"][:]),
         reads=["idxui"], writes=["idxu"])
    s.op("gpsimd", lambda e: e.memset(sb["selc"][:], 0.0), writes=["selc"])
    s.op("gpsimd", lambda e: e.affine_select(
        out=sb["selc"][:], in_=sb["selc"][:], compare_op=ALU.not_equal,
        fill=1.0, base=0, pattern=[[-1, 8], [0, 128]], channel_multiplier=1),
        reads=["selc"], writes=["selc"])
    for (fsrc, fdst, kp) in (("wgate", "wgate_b", "cg"), ("wup", "wup_b", "cu"),
                             ("wdown", "wdown_b", "cd"), ("wsgate", "wsgate_b", "csg"),
                             ("wsup", "wsup_b", "csu"), ("wsdown", "wsdown_b", "csd")):
        total = 1
        for d_ in t[fsrc].shape:
            total *= d_
        nblk = total // (128 * 2048)
        for blk in range(nblk):
            s.dma("sync", sb["ldf"][:, 0:2048],
                  ap(t[fsrc], blk * 128 * 2048, [[2048, 128], [1, 2048]]),
                  reads=["ldf"], writes=["ldf"])
            s.op("vector", lambda e: e.tensor_copy(sb["ldb"][:, 0:2048], sb["ldf"][:, 0:2048]),
                 reads=["ldf", "ldb"], writes=["ldb"])
            s.dma("sync", ap(t[fdst], blk * 128 * 2048, [[2048, 128], [1, 2048]]),
                  sb["ldb"][:, 0:2048],
                  reads=["ldb"], writes=[("wb",)])
    for dc in range(DCH):
        s.dma("sync", sb["ldf"][:, 0:8], t["routerw"][dc * 128:(dc + 1) * 128, :],
              reads=["ldf"], writes=["ldf"])
        s.op("vector", lambda e, dc=dc: e.tensor_copy(sb["routT"][:, dc * 8:(dc + 1) * 8],
                                                      sb["ldf"][:, 0:8]),
             reads=["ldf"], writes=["routT"])

    for b in range(NB):
        for dc in range(DCH):
            s.dma("sync", x3f[:, dc * TT:(dc + 1) * TT], t["x3"][b, dc * 128:(dc + 1) * 128, :],
                  reads=[("x3f", dc)], writes=[("x3f", dc)])
            s.op("vector", lambda e, dc=dc: e.tensor_copy(
                 xb[:, dc * TT:(dc + 1) * TT], x3f[:, dc * TT:(dc + 1) * TT]),
                 reads=[("x3f", dc), ("xb", dc)], writes=[("xb", dc)])
        for dc in range(DCH):
            s.op("tensor", lambda e, dc=dc: e.matmul(
                 PB[0:8, 0:TT], sb["routT"][:, dc * 8:(dc + 1) * 8], xb[:, dc * TT:(dc + 1) * TT],
                 start=(dc == 0), stop=(dc == DCH - 1)),
                 reads=[("xb", dc), "routT"], writes=["PB"])
        s.op("vector", lambda e: e.tensor_copy(sb["wEm"][0:8, 0:TT], PB[0:8, 0:TT]),
             reads=["PB"], writes=["wEm"])
        for tc in range(NTC):
            s.op("tensor", lambda e, tc=tc: e.transpose(
                 PV[:, tc * 8:(tc + 1) * 8], sb["wEm"][0:8, tc * 128:(tc + 1) * 128],
                 sb["identf"][0:8, 0:8]),
                 reads=["wEm", "identf"], writes=["PV"])
        s.op("vector", lambda e: e.tensor_copy(sb["gs1"][:, 0:NTC * 8], PV[:, 0:NTC * 8]),
             reads=["PV"], writes=["gs1"])
        s.op("vector", lambda e: e.tensor_reduce(sb["gm"][:, 0:NTC].unsqueeze(2),
             g3(sb["gs1"][:, 0:NTC * 8]), axis=AX.X, op=ALU.max, negate=True),
             reads=["gs1"], writes=["gm"])
        s.op("vector", lambda e: e.tensor_tensor(sb["gs2"][:, 0:NTC * 8], sb["gs1"][:, 0:NTC * 8],
             ap(sb["gm"], 0, [[4, 128], [1, NTC], [0, 8]]), op=ALU.add),
             reads=["gs1", "gm"], writes=["gs2"])
        s.op("scalar", lambda e: e.activation(sb["gs1"][:, 0:NTC * 8], sb["gs2"][:, 0:NTC * 8],
                                              AF.Exp),
             reads=["gs2"], writes=["gs1"])
        s.op("vector", lambda e: e.tensor_reduce(sb["gm"][:, 0:NTC].unsqueeze(2),
             g3(sb["gs1"][:, 0:NTC * 8]), axis=AX.X, op=ALU.add),
             reads=["gs1"], writes=["gm"])
        s.op("vector", lambda e: e.reciprocal(sb["gmr"][:, 0:NTC], sb["gm"][:, 0:NTC]),
             reads=["gm"], writes=["gmr"])
        s.op("vector", lambda e: e.tensor_tensor(sb["wts"][:, 0:NTC * 8], sb["gs1"][:, 0:NTC * 8],
             ap(sb["gmr"], 0, [[4, 128], [1, NTC], [0, 8]]), op=ALU.mult),
             reads=["gs1", "gmr"], writes=["wts"])
        s.op("vector", lambda e: e.tensor_copy(sb["gs2"][:, 0:NTC * 8], sb["wts"][:, 0:NTC * 8]),
             reads=["wts"], writes=["gs2"])
        for it in range(2):
            s.op("vector", lambda e: e.tensor_reduce(sb["gm"][:, 0:NTC].unsqueeze(2),
                 g3(sb["gs2"][:, 0:NTC * 8]), axis=AX.X, op=ALU.min),
                 reads=["gs2"], writes=["gm"])
            s.op("vector", lambda e: e.tensor_tensor(sb["gs1"][:, 0:NTC * 8],
                 sb["gs2"][:, 0:NTC * 8],
                 ap(sb["gm"], 0, [[4, 128], [1, NTC], [0, 8]]), op=ALU.is_equal),
                 reads=["gs2", "gm"], writes=["gs1"])
            s.op("gpsimd", lambda e: e.tensor_tensor(sb["gs1"][:, 0:NTC * 8],
                 sb["gs1"][:, 0:NTC * 8], sb["idxu"][:, 0:NTC * 8], op=ALU.mult),
                 reads=["gs1", "idxu"], writes=["gs1"])
            s.op("vector", lambda e: e.tensor_reduce(sb["gm2"][:, 0:NTC].unsqueeze(2),
                 g3(sb["gs1"][:, 0:NTC * 8]), axis=AX.X, op=ALU.max),
                 reads=["gs1"], writes=["gm2"])
            s.op("vector", lambda e: e.tensor_tensor(sb["gs1"][:, 0:NTC * 8],
                 sb["gs1"][:, 0:NTC * 8],
                 ap(sb["gm2"], 0, [[4, 128], [1, NTC], [0, 8]]), op=ALU.is_equal),
                 reads=["gs1", "gm2"], writes=["gs1"])
            s.op("vector", lambda e: e.scalar_tensor_tensor(sb["gs2"][:, 0:NTC * 8],
                 sb["gs1"][:, 0:NTC * 8], 1e30, sb["gs2"][:, 0:NTC * 8],
                 op0=ALU.mult, op1=ALU.add),
                 reads=["gs1", "gs2"], writes=["gs2"])
            s.op("gpsimd", lambda e: e.tensor_tensor(
                 sb["gs1"][:, 0:NTC * 8], sb["gs1"][:, 0:NTC * 8], sb["wts"][:, 0:NTC * 8],
                 op=ALU.mult),
                 reads=["gs1", "wts"], writes=["gs1"])
            s.op("vector", lambda e: e.tensor_sub(sb["wts"][:, 0:NTC * 8],
                 sb["wts"][:, 0:NTC * 8], sb["gs1"][:, 0:NTC * 8]),
                 reads=["wts", "gs1"], writes=["wts"])
        if dbg and b == 0:
            s.dma("sync", t["dbg_wts"][:, 0:NTC * 8], sb["wts"][:, 0:NTC * 8],
                  reads=["wts"], writes=["dbgk"])
        for tc in range(NTC):
            s.op("tensor", lambda e, tc=tc: e.transpose(
                 PX[0:8, tc * 128:(tc + 1) * 128],
                 sb["wts"][:, tc * 8:(tc + 1) * 8], sb["identf"][:]),
                 reads=["wts", "identf"], writes=["PX"])
        s.op("vector", lambda e: e.tensor_copy(sb["wEmb"][0:8, 0:TT], PX[0:8, 0:TT]),
             reads=["PX"], writes=["wEmb"])
        for e_ in range(E):
            s.op("tensor", lambda e, e_=e_: e.matmul(
                 PA[:, 0:TT], sb["selc"][:, e_ * 128:(e_ + 1) * 128], sb["wEmb"][0:8, 0:TT],
                 start=True, stop=True),
                 reads=["wEmb", "selc"], writes=["PA"])
            s.op("vector", lambda e, e_=e_: e.tensor_copy(
                 wrep[:, e_ * TT:(e_ + 1) * TT], PA[:, 0:TT]),
                 reads=["PA"], writes=[("wrep", e_)])
        for e_ in range(E):
            for fc in range(4):
                slot = fc % 2
                s.dma("sync", sb["wst"][:, slot * 1024:(slot + 1) * 1024],
                      ap(t["wgate_b"], e_ * D * FE + fc * 128, [[FE, 128], [128 * FE, 8], [1, 128]]),
                      reads=[("wst", slot), ("wb",)], writes=[("wst", slot)])
                s.dma("sync", sb["wst"][:, (2 + slot) * 1024:(3 + slot) * 1024],
                      ap(t["wup_b"], e_ * D * FE + fc * 128, [[FE, 128], [128 * FE, 8], [1, 128]]),
                      reads=[("wst", 2 + slot), ("wb",)], writes=[("wst", 2 + slot)])
                for dc in range(DCH):
                    s.op("tensor", lambda e, dc=dc, slot=slot: e.matmul(
                         PA[:, 0:TT], sb["wst"][:, slot * 1024 + dc * 128: slot * 1024 + (dc + 1) * 128],
                         xb[:, dc * TT:(dc + 1) * TT],
                         start=(dc == 0), stop=(dc == DCH - 1)),
                         reads=[("wst", slot), ("xb", dc)], writes=["PA"])
                    s.op("tensor", lambda e, dc=dc, slot=slot: e.matmul(
                         PB[:, 0:TT], sb["wst"][:, (2 + slot) * 1024 + dc * 128: (2 + slot) * 1024 + (dc + 1) * 128],
                         xb[:, dc * TT:(dc + 1) * TT],
                         start=(dc == 0), stop=(dc == DCH - 1)),
                         reads=[("wst", 2 + slot), ("xb", dc)], writes=["PB"])
                s.op("scalar", lambda e: e.activation(sb["silu"][:], PA[:, 0:TT], AF.Silu),
                     reads=["PA"], writes=["silu"])
                s.op("vector", lambda e, e_=e_: e.tensor_tensor(
                     sb["h3w"][:], PB[:, 0:TT], wrep[:, e_ * TT:(e_ + 1) * TT], op=ALU.mult),
                     reads=["PB", ("wrep", e_), "h3w"], writes=["h3w"])
                s.op("gpsimd", lambda e, e_=e_, fc=fc: e.tensor_tensor(
                     actg[:, (e_ * 4 + fc) * TT:(e_ * 4 + fc + 1) * TT],
                     sb["h3w"][:], sb["silu"][:], op=ALU.mult),
                     reads=["h3w", "silu"], writes=[("actg", e_, fc)])
        if dbg and b == 0:
            s.dma("sync", t["dbg_act"][:, 0:TT], actg[:, 0:TT], reads=[("actg", 0, 0)], writes=["dbgk"])
        for fc in range(8):
            slot = fc % 2
            s.dma("sync", sb["wst"][:, slot * 1024:(slot + 1) * 1024],
                  ap(t["wsgate_b"], fc * 128, [[FS, 128], [128 * FS, 8], [1, 128]]),
                  reads=[("wst", slot), ("wb",)], writes=[("wst", slot)])
            s.dma("sync", sb["wst"][:, (2 + slot) * 1024:(3 + slot) * 1024],
                  ap(t["wsup_b"], fc * 128, [[FS, 128], [128 * FS, 8], [1, 128]]),
                  reads=[("wst", 2 + slot), ("wb",)], writes=[("wst", 2 + slot)])
            for dc in range(DCH):
                s.op("tensor", lambda e, dc=dc, slot=slot: e.matmul(
                     PA[:, 0:TT], sb["wst"][:, slot * 1024 + dc * 128: slot * 1024 + (dc + 1) * 128],
                     xb[:, dc * TT:(dc + 1) * TT],
                     start=(dc == 0), stop=(dc == DCH - 1)),
                     reads=[("wst", slot), ("xb", dc)], writes=["PA"])
                s.op("tensor", lambda e, dc=dc, slot=slot: e.matmul(
                     PB[:, 0:TT], sb["wst"][:, (2 + slot) * 1024 + dc * 128: (2 + slot) * 1024 + (dc + 1) * 128],
                     xb[:, dc * TT:(dc + 1) * TT],
                     start=(dc == 0), stop=(dc == DCH - 1)),
                     reads=[("wst", 2 + slot), ("xb", dc)], writes=["PB"])
            s.op("scalar", lambda e: e.activation(sb["silu"][:], PA[:, 0:TT], AF.Silu),
                 reads=["PA"], writes=["silu"])
            s.op("vector", lambda e: e.tensor_copy(sb["h3w"][:], PB[:, 0:TT]),
                 reads=["PB", "h3w"], writes=["h3w"])
            s.op("gpsimd", lambda e, fc=fc: e.tensor_tensor(
                 acts[:, fc * TT:(fc + 1) * TT], sb["h3w"][:], sb["silu"][:], op=ALU.mult),
                 reads=["h3w", "silu"], writes=[("acts", fc)])
        for dcg in range(2):
            for e_ in range(E):
                slot = e_ % 2
                s.dma("sync", sb["wdst"][:, slot * 2048:(slot + 1) * 2048],
                      ap(t["wdown_b"], e_ * FE * D + dcg * 512,
                         [[D, 128], [128 * D, 4], [1, 512]]),
                      reads=[("wdst", slot), ("wb",)], writes=[("wdst", slot)])
                for dc4 in range(4):
                    for fc in range(4):
                        s.op("tensor", lambda e, e_=e_, fc=fc, dc4=dc4, slot=slot: e.matmul(
                             PDv(dc4)[:, 0:TT],
                             sb["wdst"][:, slot * 2048 + fc * 512 + dc4 * 128:
                                        slot * 2048 + fc * 512 + (dc4 + 1) * 128],
                             actg[:, (e_ * 4 + fc) * TT:(e_ * 4 + fc + 1) * TT],
                             start=(e_ == 0 and fc == 0), stop=False, skip_group_check=True),
                             reads=[("wdst", slot), ("actg", e_, fc)], writes=[("PD", dc4)])
            for fs in range(8):
                slot = fs % 2
                s.dma("sync", sb["wdst"][:, slot * 2048: slot * 2048 + 512],
                      ap(t["wsdown_b"], fs * 128 * D + dcg * 512,
                         [[D, 128], [1, 512]]),
                      reads=[("wdst", slot), ("wb",)], writes=[("wdst", slot)])
                for dc4 in range(4):
                    s.op("tensor", lambda e, fs=fs, dc4=dc4, slot=slot: e.matmul(
                         PDv(dc4)[:, 0:TT],
                         sb["wdst"][:, slot * 2048 + dc4 * 128: slot * 2048 + (dc4 + 1) * 128],
                         acts[:, fs * TT:(fs + 1) * TT],
                         start=False, stop=(fs == 7 and dc4 == 3), skip_group_check=True),
                         reads=[("wdst", slot), ("acts", fs)], writes=[("PD", dc4)])
            for dc4 in range(4):
                dc = dcg * 4 + dc4
                s.op("vector", lambda e, dc=dc, dc4=dc4: e.tensor_copy(
                     outacc[:, dc * TT:(dc + 1) * TT], PDv(dc4)[:, 0:TT]),
                     reads=[("PD", dc4)], writes=[("outacc", dc)])
        if dbg and b == 0:
            s.dma("sync", t["dbg_out"][:, 0:DCH * TT], outacc[:],
                  reads=[("outacc", d_) for d_ in range(DCH)], writes=["dbgk"])
        for dc in range(DCH):
            s.op("vector", lambda e, dc=dc: e.tensor_add(
                 outacc[:, dc * TT:(dc + 1) * TT], outacc[:, dc * TT:(dc + 1) * TT],
                 x3f[:, dc * TT:(dc + 1) * TT]),
                 reads=[("outacc", dc), ("x3f", dc)], writes=[("outacc", dc)])
            s.op("vector", lambda e, dc=dc: e.tensor_copy(
                 sb["rb"][:], outacc[:, dc * TT:(dc + 1) * TT]),
                 reads=[("outacc", dc), "rb"], writes=["rb"])
            s.op("vector", lambda e: e.tensor_mul(sb["sqb"][:], sb["rb"][:], sb["rb"][:]),
                 reads=["rb"], writes=["sqb"])
            s.op("tensor", lambda e, dc=dc: e.matmul(
                 ap(PD, 2 * 512, [[2048, 1], [1, TT]]), sb["onescol"][:], sb["rb"][:],
                 start=(dc == 0), stop=(dc == DCH - 1), skip_group_check=True),
                 reads=["rb", "onescol"], writes=[("PD", 2)])
            s.op("tensor", lambda e, dc=dc: e.matmul(
                 ap(PD, 3 * 512, [[2048, 1], [1, TT]]), sb["onescol"][:], sb["sqb"][:],
                 start=(dc == 0), stop=(dc == DCH - 1), skip_group_check=True),
                 reads=["sqb", "onescol"], writes=[("PD", 3)])
        s.op("vector", lambda e: e.tensor_scalar_mul(xio[0:1, 0:TT], PDv(2)[0:1, 0:TT], 1.0 / D),
             reads=[("PD", 2)], writes=[("xio", 0)])
        s.op("vector", lambda e: e.tensor_scalar_mul(xio[0:1, TT:2 * TT], PDv(3)[0:1, 0:TT], 1.0 / D),
             reads=[("PD", 3)], writes=[("xio", 1)])
        s.op("vector", lambda e: e.tensor_mul(sb["wEm"][0:1, 0:TT],
             xio[0:1, 0:TT], xio[0:1, 0:TT]),
             reads=[("xio", 0)], writes=["wEm"])
        s.op("vector", lambda e: e.scalar_tensor_tensor(
             xio[0:1, TT:2 * TT], xio[0:1, TT:2 * TT], 1e-5, sb["wEm"][0:1, 0:TT],
             op0=ALU.add, op1=ALU.subtract),
             reads=[("xio", 1), "wEm"], writes=[("xio", 1)])
        s.op("scalar", lambda e: e.activation(sb["wEm"][0:1, 0:TT], xio[0:1, TT:2 * TT], AF.Sqrt),
             reads=[("xio", 1)], writes=["wEm"])
        s.op("vector", lambda e: e.reciprocal(xio[0:1, TT:2 * TT], sb["wEm"][0:1, 0:TT]),
             reads=["wEm"], writes=[("xio", 1)])
        s.op("tensor", lambda e: e.matmul(PA[:, 0:TT], sb["onesrow"][:], xio[0:1, 0:TT],
                                          start=True, stop=True),
             reads=[("xio", 0), "onesrow"], writes=["PA"])
        s.op("tensor", lambda e: e.matmul(PB[:, 0:TT], sb["onesrow"][:], xio[0:1, TT:2 * TT],
                                          start=True, stop=True),
             reads=[("xio", 1), "onesrow"], writes=["PB"])
        for dc in range(DCH):
            s.op("vector", lambda e, dc=dc: e.tensor_sub(
                 sb["x4s"][:, 0:TT], outacc[:, dc * TT:(dc + 1) * TT], PA[:, 0:TT]),
                 reads=[("outacc", dc), "PA", ("x4s", 0)], writes=[("x4s", 0)])
            s.op("vector", lambda e: e.tensor_mul(
                 sb["x4s"][:, TT:2 * TT], sb["x4s"][:, 0:TT], PB[:, 0:TT]),
                 reads=[("x4s", 0), "PB", ("x4s", 1)], writes=[("x4s", 1)])
            s.dma("sync", t["x4"][b, dc * 128:(dc + 1) * 128, :], sb["x4s"][:, TT:2 * TT],
                  reads=[("x4s", 1)], writes=[("x4", b, dc)], force_inc=True)
    s.final_wait()


import numpy as np
from contextlib import ExitStack
from concourse.bass_utils import run_bass_kernel_spmd

NCORES = 8
TLC = 512


def _bar(s, scratch, barscr, bi):
    allk = list(s.plan.last_writer.keys()) if s.mode == "plan" else ()
    s.op("vector", lambda e: e.memset(scratch, 0.0), writes=allk)
    s.dma("sync", barscr[0:1, bi * 32 + 0:bi * 32 + 8], scratch, writes=allk)
    s.dma("gpsimd", barscr[0:1, bi * 32 + 8:bi * 32 + 16], scratch, writes=allk)
    s.dma("scalar", barscr[0:1, bi * 32 + 16:bi * 32 + 24], scratch, writes=allk)


def kernel(**inp):
    f32 = lambda a: np.ascontiguousarray(np.asarray(a, dtype=np.float32))
    x = f32(inp["L0_attn_x"])
    xt = np.ascontiguousarray(x.transpose(0, 2, 1))
    xpad = np.concatenate([np.zeros((8, 1024, 64), np.float32), xt], axis=2)

    ts_ = np.arange(128)[:, None]
    us_ = np.arange(192)[None, :]
    band = np.where((us_ >= ts_ + 1) & (us_ <= ts_ + 64), 0.0, -1e30).astype(np.float32)
    mask0 = (band + np.where(us_ < 64, -1e30, 0.0)).astype(np.float32)

    nc = bass.Bass("TRN2", target_bir_lowering=False)
    x1h = nc.dram_tensor("x1h", [8, 1024, TLC], F32, kind="Internal").ap()
    x2h = nc.dram_tensor("x2h", [8, 1024, TLC], F32, kind="Internal").ap()
    x3h = nc.dram_tensor("x3h", [8, 1024, TLC], F32, kind="Internal").ap()
    barscr = nc.dram_tensor("barscr", [1, 128], F32, kind="Internal").ap()
    tA = declare_dram_a(nc, TLC, x1=x1h)
    tB = declare_dram_b(nc, TLC, 8, x1=x1h, x2=x2h)
    tC = declare_dram_c(nc, TLC, x2=x2h, x3=x3h)
    tD = declare_dram_d(nc, TLC, 8, x3=x3h)
    with ExitStack() as cA:
        sbA, psA = alloc_bufs_a(cA, nc, TLC, suffix="A")
    with ExitStack() as cB:
        sbB, psB = alloc_bufs_b(cB, nc, TLC, suffix="B")
    with ExitStack() as cC:
        sbC, psC = alloc_bufs_c(cC, nc, TLC, suffix="C")
    with ExitStack() as cD:
        sbD, psD = alloc_bufs_d(cD, nc, TLC, suffix="D")

    def emit_all(s):
        emit_a(s, tA, sbA, psA, TLC, final=False)
        _bar(s, sbA["onesrow"][0:1, 0:8], barscr, 0)
        emit_b(s, tB, sbB, psB, TLC, 8, final=False)
        _bar(s, sbB["onesrow"][0:1, 0:8], barscr, 1)
        emit_c(s, tC, sbC, psC, TLC, final=False)
        _bar(s, sbC["onesrow"][0:1, 0:8], barscr, 2)
        emit_d(s, tD, sbD, psD, TLC, 8)

    play(nc, emit_all)

    wAll = dict(Wdisp=f32(inp["L0_attn_W_disp"]), Wval=f32(inp["L0_attn_W_val"]),
                Wcpj=f32(inp["L0_attn_c_proj_w"]), relpos=f32(inp["L0_attn_rel_pos_emb"]),
                ppw=f32(inp["L0_attn_pos_proj_w"]), sfw=f32(inp["L0_attn_strain_fused_w"]),
                sfb=f32(inp["L0_attn_strain_fused_b"]), bow=f32(inp["L0_attn_bond_out_w"]),
                dow=f32(inp["L0_attn_damage_out_w"]), dob=f32(inp["L0_attn_damage_out_b"]),
                normw=f32(inp["L0_ffn_norm_w"]), w1=f32(inp["L0_ffn_w1"]),
                w3=f32(inp["L0_ffn_w3"]), w2=f32(inp["L0_ffn_w2"]),
                gatew=f32(inp["L0_ffn_gate_w"]), a1=f32(inp["L0_ffn_a1"]),
                b1=f32(inp["L0_ffn_b1"]), a3=f32(inp["L0_ffn_a3"]),
                b3=f32(inp["L0_ffn_b3"]), a2=f32(inp["L0_ffn_a2"]),
                b2=f32(inp["L0_ffn_b2"]),
                Wy=f32(inp["L1_attn_W_y"]), Wq=f32(inp["L1_attn_W_q"]),
                Wk=f32(inp["L1_attn_W_k"]), wb=f32(inp["L1_attn_w_b"]),
                outw=f32(inp["L1_attn_out_w"]), lng=f32(inp["L1_attn_ln_g"]),
                lnb=f32(inp["L1_attn_ln_b"]),
                routerw=f32(inp["L1_ffn_router_w"]), wgate=f32(inp["L1_ffn_w_gate"]),
                wup=f32(inp["L1_ffn_w_up"]), wdown=f32(inp["L1_ffn_w_down"]),
                wsgate=f32(inp["L1_ffn_ws_gate"]), wsup=f32(inp["L1_ffn_ws_up"]),
                wsdown=f32(inp["L1_ffn_ws_down"]))
    in_maps = []
    for c in range(NCORES):
        m = dict(wAll)
        m["xA"] = np.ascontiguousarray(xpad[:, :, c * TLC: c * TLC + TLC + 64])
        m["maskA"] = mask0 if c == 0 else band
        m["maskB"] = band
        in_maps.append(m)
    res = run_bass_kernel_spmd(nc, in_maps, core_ids=list(range(NCORES)))
    out = np.concatenate([r["x4"] for r in res.results], axis=2)
    return np.ascontiguousarray(out.transpose(0, 2, 1))
